# version 04
# baseline (speedup 1.0000x reference)
import numpy as np
from contextlib import ExitStack

import concourse.bass as bass
import concourse.mybir as mybir
from concourse.bass_utils import run_bass_kernel_spmd

F32 = mybir.dt.float32
BF16 = mybir.dt.bfloat16
U8 = mybir.dt.uint8
AF = mybir.ActivationFunctionType
ALU = mybir.AluOpType
AXX = mybir.AxisListType.X

D = 4096
T = 1024
NTT = 8
NIC = 32
NH = 8
KV = 1024
S = 4096
NKC = 32
EPS = 1e-5
QSCALE = float(1.0 / np.sqrt(128.0))

DEBUG = False


class Sem:
    __slots__ = ("h", "n")

    def __init__(self, h):
        self.h = h
        self.n = 0


class Builder:
    ENGINES = ("sync", "gpsimd", "tensor", "vector", "scalar")

    def __init__(self):
        self.ops = {e: [] for e in self.ENGINES}

    def emit(self, eng, fn, waits=(), inc=None):
        ws = []
        for s, v in waits:
            assert v <= s.n, f"wait on future sem value {v} > {s.n} ({eng})"
            if v > 0:
                ws.append((s.h, v))
        if inc is not None:
            s, k = inc
            s.n += k
            inc = (s.h, k)
        self.ops[eng].append((fn, ws, inc))

    def replay(self, block):
        def mk(eng_name):
            def run(e):
                seen = {}
                for fn, ws, inc in self.ops[eng_name]:
                    merged = {}
                    for h, v in ws:
                        if v > merged.get(h.name, (None, -1))[1]:
                            merged[h.name] = (h, v)
                    for h, v in merged.values():
                        if seen.get(h.name, -1) >= v:
                            continue
                        seen[h.name] = v
                        e.wait_ge(h, v)
                    inst = fn(e) if fn is not None else None
                    if inc is not None:
                        h, k = inc
                        if inst is None:
                            inst = e.nop()
                        inst.then_inc(h, k)
            return run

        for eng_name in self.ENGINES:
            if self.ops[eng_name]:
                getattr(block, eng_name)(mk(eng_name))


def build_program(debug=DEBUG, stage="full"):
    nc = bass.Bass("TRN2", target_bir_lowering=False)

    x_in = nc.dram_tensor("x", [T, D], F32, kind="ExternalInput").ap()
    wq = nc.dram_tensor("wq", [D, D], F32, kind="ExternalInput").ap()
    wk = nc.dram_tensor("wk", [D, KV], F32, kind="ExternalInput").ap()
    wv = nc.dram_tensor("wv", [D, KV], F32, kind="ExternalInput").ap()
    wo = nc.dram_tensor("wo", [KV, D], F32, kind="ExternalInput").ap()
    bw = nc.dram_tensor("bw", [D, D], F32, kind="ExternalInput").ap()
    bb = nc.dram_tensor("bb", [D], F32, kind="ExternalInput").ap()
    A_in = nc.dram_tensor("A", [64, D], F32, kind="ExternalInput").ap()
    B_in = nc.dram_tensor("B", [16, D, 64], F32, kind="ExternalInput").ap()
    rw = nc.dram_tensor("rw", [16, D], F32, kind="ExternalInput").ap()
    idf_in = nc.dram_tensor("idf", [128, 128], F32, kind="ExternalInput").ap()
    idb_in = nc.dram_tensor("idb", [128, 128], BF16, kind="ExternalInput").ap()
    out_d = nc.dram_tensor("out", [T, D], F32, kind="ExternalOutput").ap()

    wqe_d = nc.dram_tensor("wqe_d", [D, KV], BF16, kind="Internal").ap()
    bwT_d = nc.dram_tensor("bwT_d", [16, D, 256], BF16, kind="Internal").ap()
    bfT_d = nc.dram_tensor("bfT_d", [16, KV, 256], BF16, kind="Internal").ap()
    kv_in = nc.dram_tensor("kv_in", [2 * T, KV], BF16, kind="Internal").ap()
    kv_out = nc.dram_tensor("kv_out", [8 * T, KV], BF16, kind="Internal").ap()
    x1sp = nc.dram_tensor("x1sp", [T, D], F32, kind="Internal").ap()
    h2sp = nc.dram_tensor("h2sp", [T, D], BF16, kind="Internal").ap()

    dbg = {}
    dbg_written = set()
    if debug:
        for nm, shp, dt in [
            ("dbg_qT", [128, NH * T], F32), ("dbg_kT", [128, NH * T], F32),
            ("dbg_v", [128, NTT * KV], F32), ("dbg_vals", [128, NH * T], F32),
            ("dbg_h2", [T, D], F32), ("dbg_res", [T, 80], F32),
            ("dbg_H", [T, KV], F32), ("dbg_stage", [T, D], F32),
            ("dbg_h1b", [128, D], F32), ("dbg_h1T", [128, NIC * T], F32),
            ("dbg_wqe", [D, KV], F32),
        ]:
            dbg[nm] = nc.dram_tensor(nm, shp, dt, kind="ExternalOutput").ap()

    ctx = ExitStack()
    CAP = 212_832
    arena = ctx.enter_context(nc.sbuf_tensor("arena", [128, CAP], U8))
    base = nc.lookup_mloc(arena).addr

    def place(name, shape, dtype, off):
        per = int(np.prod(shape[1:])) * mybir.dt.size(dtype)
        assert off % 32 == 0 and off + per <= CAP, (name, off, per)
        h = nc.virtual_sbuf_tensor(name, shape, dtype)
        nc._tensor(h.name, h.shape, h.dtype, type="SB")
        m = nc.lookup_mloc(h)
        m.allocated = True
        m.addr = base + off
        return h.ap()

    def place2(name, shape, dtype, off):
        per = int(np.prod(shape[1:])) * mybir.dt.size(dtype)
        return [place(f"{name}{i}", shape, dtype, off + i * per) for i in range(2)]

    ident_f = place("ident_f", [128, 128], F32, 0)
    ident_b = place("ident_b", [128, 128], BF16, 512)
    st1 = place("st1", [128, 64], F32, 768)
    st2 = place("st2", [128, 64], F32, 1024)
    D_sb = place("D_sb", [128, 128], BF16, 1280)
    rtr = place("rtr", [128, 160], F32, 1536)
    res16 = place("res16", [128, 4 * 64], BF16, 2176)
    H_sb = place("H_sb", [128, KV], BF16, 2688)
    arT = place("arT", [128, NIC * 80], BF16, 4736)
    bb16 = place("bb16", [128, D], BF16, 9856)
    ln_sb = place("ln_sb", [128, D], F32, 18048)
    sq_sb = place("sq_sb", [128, D], BF16, 34432)
    vals_sb = place("vals_sb", [128, NH * T], BF16, 42624)
    qT = place("qT", [128, NH * T], BF16, 59008)
    P_sb = place("P_sb", [128, S], BF16, 75392)
    OB = 83584
    FR = OB + 20480

    wq_stage = place2("wq_stage", [128, KV], F32, OB)
    wqe_sb = place("wqe_sb", [128, KV], BF16, OB + 8192)
    f1_sb = place("f1_sb", [128, 256], F32, OB + 10240)
    bw_in = place2("bw_in", [128, 512], F32, OB + 11264)
    bwT_stage = place("bwT_stage", [128, 8 * 256], BF16, OB + 15360)
    bft_sb = place("bft_sb", [128, 8 * D], BF16, FR)
    b_stage = place2("b_stage", [128, 2048], F32, FR + 65536)
    ar_stage = place("ar_stage", [64, D], BF16, FR + 81920)
    r_stage = place("r_stage", [16, D], BF16, FR + 90112)
    h1T = place("h1T", [128, NIC * T], BF16, FR)
    x_buf = place("x_buf", [128, D], F32, FR + 65536)
    h1b = place("h1b", [128, D], BF16, FR + 81920)
    kT_own = place("kT_own", [128, NH * T], BF16, FR + 65536)
    v_own = place("v_own", [128, NTT * KV], BF16, FR + 81920)
    ws_buf = place2("ws_buf", [128, 2 * KV], BF16, FR + 98304)
    v_sb = place("v_sb", [128, NKC * KV], BF16, FR)
    PT_half = place("PT_half", [128, NKC * 512], BF16, FR + 65536)
    kh_sb = place("kh_sb", [128, S], BF16, FR + 98304)
    wo_sb = place("wo_sb", [128, 8 * D], BF16, FR)
    atst = place2("atst", [128, D], BF16, FR + 65536)
    x5_buf = place("x5_buf", [128, D], F32, FR + 81920)
    h2b = place("h2b", [128, D], BF16, FR + 98304)
    stg_sb = place("stg_sb", [128, 4 * D], BF16, FR)
    h2T = place("h2T", [128, NIC * 512], BF16, FR + 32768)
    bwT_s = place2("bwT_s", [128, NIC * 256], BF16, FR + 65536)
    bfT_s = place2("bfT_s", [128, 8 * 256], BF16, 75392)
    x1buf = place("x1buf", [128, D], F32, 59008)
    HT_sb = place("HT_sb", [128, 8 * 512], BF16, 42624)
    assert FR + 106496 <= OB + 129248

    PS = ctx.enter_context(nc.psum_tensor("PS", [128, 4096], F32))
    _psh = nc.virtual_psum_tensor("PSb", [128, 8192], BF16)
    nc._tensor(_psh.name, _psh.shape, _psh.dtype, type="PSUM")
    _psm = nc.lookup_mloc(_psh)
    _psm.allocated = True
    _psm.addr = nc.lookup_mloc(PS.tensor).addr if hasattr(PS, "tensor") else 0
    PSb = _psh.ap()

    sems = {nm: Sem(ctx.enter_context(nc.semaphore(nm)))
            for nm in ("LD", "GD", "PE", "VE", "SE", "GE", "CC")}
    LD, GD, PE, VE, SE, GE, CC = (sems[k] for k in
                                  ("LD", "GD", "PE", "VE", "SE", "GE", "CC"))

    B = Builder()

    def _finish():
        if stage != "full":
            B.emit("gpsimd", lambda e: e.memset(ln_sb[:, :], 0.0), (), (GE, 1))
            zt = (GE, GE.n)
            for _tt in range(NTT):
                dma("gpsimd", out_d[_tt * 128:(_tt + 1) * 128, :], ln_sb[:, :], (zt,), st="g_fill")
            if debug:
                for _nm, _ap in dbg.items():
                    if _nm in dbg_written:
                        continue
                    _views = ([_ap] if _ap.shape[0] == 128 else
                              [_ap[_r * 128:(_r + 1) * 128]
                               for _r in range(_ap.shape[0] // 128)])
                    for _v in _views:
                        nfree = int(np.prod(_v.shape[1:]))
                        _off = 0
                        while _off < nfree:
                            _w = min(4096, nfree - _off)
                            dma("gpsimd", _v[:, _off:_off + _w], ln_sb[:, 0:_w], (zt,), st="g_fill")
                            _off += _w
        allw = tuple((s, s.n) for s in dstreams.values()) + ((CC, CC.n),)
        B.emit("gpsimd", None, allw)
        if any(fn is not None or ws for fn, ws, inc in B.ops["sync"]):
            B.emit("sync", None, allw)
        with nc.Block() as block:
            B.replay(block)
        ctx.close()
        return nc

    dstreams = {}

    def dsem(name):
        if name not in dstreams:
            dstreams[name] = Sem(ctx.enter_context(nc.semaphore(name)))
        return dstreams[name]

    def dma(q, out, in_, waits=(), st=None):
        s = dsem(st if st is not None else ("s_misc" if q == "sync" else "g_misc"))
        B.emit(q, lambda e, o=out, i=in_: e.dma_start(out=o, in_=i), waits, (s, 16))
        return (s, s.n)

    dma("sync", ident_f[:, :], idf_in[:, :])
    ident_ready = dma("sync", ident_b[:, :], idb_in[:, :])

    if stage == "Z0":
        return _finish()
    dma("gpsimd", bb16[:, :], bb[None, :].broadcast_to([128, D]))

    if stage == "P0a":
        return _finish()
    dma("gpsimd", ar_stage[0:64, :], A_in[:, :])
    t_r = dma("gpsimd", r_stage[0:16, :], rw[:, :])
    bb_ready = t_a = t_r
    ar_ev_tok = None
    for ic in range(NIC):
        if ar_ev_tok is not None:
            B.emit("tensor", None, (ar_ev_tok,))
        B.emit("tensor", lambda e, ic=ic: e.transpose(
            PSb[:, 0:64], ar_stage[0:64, ic * 128:(ic + 1) * 128], ident_b[0:64, 0:64]),
            (t_a, ident_ready) if ic == 0 else ())
        B.emit("tensor", lambda e, ic=ic: e.transpose(
            PSb[:, 64:80], r_stage[0:16, ic * 128:(ic + 1) * 128], ident_b[0:16, 0:16]),
            (t_r,) if ic == 0 else (), (PE, 1))
        B.emit("scalar", lambda e, ic=ic: e.copy(
            arT[:, ic * 80:(ic + 1) * 80], PSb[:, 0:80]), ((PE, PE.n),), (SE, 1))
        ar_ev_tok = (SE, SE.n)
    ar_done = ar_ev_tok

    if stage == "P0b":
        return _finish()
    b_pe = {}
    bft_last = None
    for e_i in range(16):
        erc, lane = e_i // 2, (e_i % 2) * 64
        srcv = B_in[e_i].rearrange("(ot p) r -> p ot r", p=128)
        lw = (b_pe[e_i - 2],) if e_i >= 2 else ()
        t_b = dma("sync", b_stage[e_i % 2][:, :].rearrange(
            "p (ot r) -> p ot r", ot=32), srcv, lw, st=f"s_bst{e_i % 2}")
        for grp in range(8):
            if bft_last is not None:
                B.emit("tensor", None, (bft_last,))
            for j in range(4):
                ot = grp * 4 + j
                B.emit("tensor", lambda e, e_i=e_i, ot=ot, j=j: e.transpose(
                    PS[0:64, 1024 + j * 128:1024 + (j + 1) * 128],
                    b_stage[e_i % 2][:, ot * 64:(ot + 1) * 64], ident_f[:, :]),
                    (t_b, ident_ready) if (grp == 0 and j == 0) else (),
                    (PE, 1) if j == 3 else None)
            b_pe[e_i] = (PE, PE.n)
            B.emit("vector", lambda e, erc=erc, lane=lane, grp=grp: e.tensor_copy(
                bft_sb[lane:lane + 64, erc * D + grp * 512:erc * D + (grp + 1) * 512],
                PS[0:64, 1024:1536]), ((PE, PE.n),), (VE, 1))
            bft_last = (VE, VE.n)
    bft_st = []
    for oe in range(16):
        src = bft_sb[:, :].rearrange("p (erc o) -> p erc o", erc=8)[
            :, :, oe * 256:(oe + 1) * 256]
        dst = bfT_d[oe].rearrange("(erc p) o -> p erc o", p=128)
        bft_st.append(dma("sync", dst, src, (bft_last,), st="s_bftst"))
    bft_done = bft_st[-1]

    if stage == "P0c":
        return _finish()
    fold_state = {"done": None}

    def fold_gen():
        mm_toks = []
        ev_tok = None
        st_tok = None
        u = 0
        for ic in range(NIC):
            for qq in range(4):
                bi = u % 2
                lw = (mm_toks[u - 2],) if u >= 2 else ()
                t_w = dma("sync", wq_stage[bi],
                          wq[ic * 128:(ic + 1) * 128, qq * 1024:(qq + 1) * 1024], lw,
                          st=f"s_wq{bi}")
                for hl in range(2):
                    h = 2 * qq + hl
                    for g in range(4):
                        ws = [t_w]
                        if hl == 0 and g == 0:
                            ws.append(ident_ready)
                            if ev_tok is not None and qq == 0:
                                ws.append(ev_tok)
                        B.emit("tensor", lambda e, bi=bi, hl=hl, g=g, h=h: e.matmul(
                            PS[:, 1536 + h * 128:1536 + (h + 1) * 128],
                            ident_f[:, :],
                            wq_stage[bi][:, hl * 512 + g * 128: hl * 512 + (g + 1) * 128],
                            start=(g == 0), stop=(g == 3)),
                            tuple(ws), (PE, 1) if (hl == 1 and g == 3) else None)
                mm_toks.append((PE, PE.n))
                u += 1
                yield
            ew = [(PE, PE.n)]
            if st_tok is not None:
                ew.append(st_tok)
            B.emit("vector", lambda e: e.tensor_copy(wqe_sb[:, :], PS[:, 1536:2560]),
                   tuple(ew), (VE, 1))
            ev_tok = (VE, VE.n)
            st_tok = dma("sync", wqe_d[ic * 128:(ic + 1) * 128, :], wqe_sb[:, :],
                         (ev_tok,), st="s_wqest")
        fold_state["done"] = st_tok

    fold_it = fold_gen()

    def fold_step(k=1):
        for _ in range(k):
            if fold_state["done"] is None:
                next(fold_it, None)

    fold_step(16)

    bw_state = {"done": None}

    def bw_gen():
        ld_free = {}
        bank_free = {}
        stg_tok = None
        st_toks = []
        unit = 0
        for sl in range(16):
            for ihq in range(4):
                evs = []
                for otl in range(2):
                    ot = sl * 2 + otl
                    for iq2 in range(2):
                        iq = ihq * 2 + iq2
                        bi = unit % 2
                        lw = []
                        if unit >= 2:
                            lw.append(ld_free[unit - 2])
                        if unit == 0:
                            lw.append(ident_ready)
                        t_bw = dma("sync", bw_in[bi],
                                   bw[ot * 128:(ot + 1) * 128,
                                      iq * 512:(iq + 1) * 512], tuple(lw),
                                   st=f"s_bw{bi}")
                        bank = 3072 + (unit % 2) * 512
                        gw = [t_bw]
                        if bank in bank_free:
                            gw.append(bank_free[bank])
                        if iq2 == 0 and otl == 0 and stg_tok is not None:
                            gw.append(stg_tok)
                        for j in range(4):
                            B.emit("tensor", lambda e, j=j, bank=bank, bi=bi: e.transpose(
                                PS[:, bank + j * 128:bank + (j + 1) * 128],
                                bw_in[bi][:, j * 128:(j + 1) * 128], ident_f[:, :]),
                                tuple(gw) if j == 0 else (),
                                (PE, 1) if j == 3 else None)
                        pe_tok = (PE, PE.n)
                        dst = bwT_stage[:, :].rearrange(
                            "p (icl o) -> p icl o", icl=8)[
                            :, iq2 * 4:(iq2 + 1) * 4, otl * 128:(otl + 1) * 128]
                        B.emit("vector", lambda e, dst=dst, bank=bank: e.tensor_copy(
                            dst, PS[:, bank:bank + 512].rearrange(
                                "p (a b) -> p a b", a=4)), (pe_tok,), (VE, 1))
                        bank_free[bank] = (VE, VE.n)
                        evs.append((VE, VE.n))
                        ld_free[unit] = (VE, VE.n)
                        unit += 1
                        yield
                dstv = bwT_d[sl][ihq * 1024:(ihq + 1) * 1024, :].rearrange(
                    "(icl p) o -> p icl o", p=128)
                stg_tok = dma("sync", dstv,
                              bwT_stage[:, :].rearrange("p (icl o) -> p icl o", icl=8),
                              tuple(evs), st="s_bwst")
                st_toks.append(stg_tok)
        bw_state["done"] = st_toks[-1]

    bw_it = bw_gen()

    def bw_step(k=1):
        for _ in range(k):
            if bw_state["done"] is None:
                next(bw_it, None)

    if stage == "P0":
        return _finish()
    def vdrain():
        B.emit("vector", None, ((VE, VE.n),))

    def ln_stats(src_ap):
        B.emit("vector", lambda e: e.reduce_sum(st1[:, 0:1], src_ap, axis=AXX), (), (VE, 1))
        vdrain()
        B.emit("scalar", lambda e: e.activation(
            sq_sb[:, :], src_ap, AF.Square, accum_out=st1[:, 1:2]), (), (SE, 1))
        sq_tok = (SE, SE.n)
        B.emit("vector", lambda e: e.tensor_scalar(
            st1[:, 2:3], st1[:, 0:1], -1.0 / D, None, op0=ALU.mult), (), (VE, 1))
        vdrain()
        B.emit("vector", lambda e: e.tensor_scalar(
            st1[:, 3:4], st1[:, 1:2], 1.0 / D, None, op0=ALU.mult), (sq_tok,), (VE, 1))
        vdrain()
        B.emit("vector", lambda e: e.tensor_mul(st1[:, 4:5], st1[:, 2:3], st1[:, 2:3]),
               (), (VE, 1))
        vdrain()
        B.emit("vector", lambda e: e.tensor_scalar(
            st1[:, 5:6], st1[:, 3:4], st1[:, 4:5], EPS, op0=ALU.subtract, op1=ALU.add),
            (), (VE, 1))
        vdrain()
        B.emit("vector", lambda e: e.reciprocal(st1[:, 6:7], st1[:, 5:6]), (), (VE, 1))
        B.emit("scalar", lambda e: e.activation(st1[:, 7:8], st1[:, 6:7], AF.Sqrt),
               ((VE, VE.n),), (SE, 1))
        rstd_tok = (SE, SE.n)
        B.emit("scalar", None, (rstd_tok,))
        B.emit("vector", lambda e: e.tensor_mul(st1[:, 8:9], st1[:, 2:3], st1[:, 7:8]),
               (rstd_tok,), (VE, 1))
        vdrain()
        return rstd_tok, (VE, VE.n)

    x_free = None
    h1T_last = None
    a1_bar = ([(VE, VE.n), (SE, SE.n), (GE, GE.n), (PE, PE.n)] +
              [(s, s.n) for s in dstreams.values()])
    for tt in range(NTT):
        w = ((x_free,) if x_free else tuple(a1_bar))
        t_x = dma("sync", x_buf[:, :], x_in[tt * 128:(tt + 1) * 128, :], w, st="s_x")
        B.emit("vector", None, (t_x,))
        B.emit("scalar", None, (t_x,))
        rstd_tok, nb_tok = ln_stats(x_buf[:, :])
        nw = [nb_tok]
        if tt > 0:
            nw.append(h1_pe_prev)
        B.emit("scalar", lambda e: e.activation(
            h1b[:, :], x_buf[:, :], AF.Identity, bias=st1[:, 8:9], scale=st1[:, 7:8]),
            tuple(nw), (SE, 1))
        h1_tok = (SE, SE.n)
        x_free = h1_tok
        ev2 = {}
        for grp in range(8):
            bank = (grp % 2) * 512
            gw = [h1_tok] if grp == 0 else []
            if tt == 0 and grp == 0:
                gw += list(a1_bar)
            if grp >= 2:
                gw.append(ev2[grp - 2])
            elif tt > 0:
                gw.append(h1T_prev[grp])
            pb = (grp % 2) * 1024
            for j in range(4):
                ic = grp * 4 + j
                B.emit("tensor", lambda e, ic=ic, pb=pb, j=j: e.transpose(
                    PSb[:, pb + j * 128:pb + (j + 1) * 128],
                    h1b[:, ic * 128:(ic + 1) * 128], ident_b[:, :]),
                    tuple(gw) if j == 0 else (), (PE, 1) if j == 3 else None)
            pe_tok = (PE, PE.n)
            dst = h1T[:, :].rearrange("p (ic t) -> p ic t", ic=NIC)[
                :, grp * 4:(grp + 1) * 4, tt * 128:(tt + 1) * 128]
            B.emit("vector", lambda e, dst=dst, pb=pb: e.tensor_copy(
                dst, PSb[:, pb:pb + 512].rearrange("p (a b) -> p a b", a=4)),
                (pe_tok,), (VE, 1))
            ev2[grp] = (VE, VE.n)
        h1T_prev = dict(ev2)
        h1T_prev[0] = ev2[6]
        h1T_prev[1] = ev2[7]
        h1T_last = ev2[7]
        h1_pe_prev = (PE, PE.n)
        if debug and tt == 0:
            h1_pe_prev = dma("gpsimd", dbg["dbg_h1b"], h1b[:, :], (h1_tok, ev2[7]), st="g_dbg")
            dbg_written.add("dbg_h1b")
        fold_step(14)
        bw_step(8)
    fold_step(4)
    h1T_done = h1T_last
    if debug:
        for _q in range(8):
            dma("gpsimd", dbg["dbg_h1T"][:, _q * 4096:(_q + 1) * 4096],
                h1T[:, _q * 4096:(_q + 1) * 4096], (h1T_done,), st="g_dbg")
        dbg_written.add("dbg_h1T")

    if stage == "A1":
        return _finish()
    def proj_pass(kind, out_tiles, wsrc, wcols, cast, out_sb, out_off, scale=None,
                  extra_wait=(), hook=None):
        ntile = len(out_tiles)
        c0, c1 = wcols
        ncol = c1 - c0
        q = "gpsimd" if cast else "sync"
        blk_tok = {}
        blk_pe = {}

        NB = 16

        def load_blk(icb, lw):
            src = wsrc.rearrange("(ic p) o -> p ic o", p=128)[
                :, icb * 2:(icb + 1) * 2, c0:c1]
            dstv = ws_buf[icb % 2][:, 0:2 * ncol].rearrange("p (ic o) -> p ic o", ic=2)
            blk_tok[icb] = dma(q, dstv, src, lw,
                               st=f"ws{'g' if cast else 's'}{icb % 2}")

        load_blk(0, extra_wait)
        load_blk(1, extra_wait)
        for ic in range(NIC):
            icb, j = ic // 2, ic % 2
            if j == 0 and hook:
                hook()
            wt = blk_tok[icb]
            for ti in range(ntile):
                if kind == "fm":
                    st_ap = ws_buf[icb % 2][:, j * ncol + ti * 128:
                                            j * ncol + (ti + 1) * 128]
                    for half in range(2):
                        last = (ic == NIC - 1 and ti == ntile - 1 and half == 1)
                        B.emit("tensor", lambda e, st_ap=st_ap, ic=ic, ti=ti, half=half: e.matmul(
                            PS[:, (2 * ti + half) * 512:(2 * ti + half + 1) * 512],
                            st_ap,
                            h1T[:, ic * T + half * 512: ic * T + (half + 1) * 512],
                            start=(ic == 0), stop=(ic == NIC - 1)),
                            ((wt, h1T_done) + tuple(extra_wait))
                            if (ic == 0 and ti == 0 and half == 0) else (wt,),
                            (PE, 1) if (last or (j == 1 and ti == ntile - 1 and half == 1))
                            else None)
                else:
                    tile = out_tiles[ti]
                    last = (ic == NIC - 1 and ti == ntile - 1)
                    B.emit("tensor", lambda e, ic=ic, tile=tile, ti=ti, icb=icb, j=j: e.matmul(
                        PS[:, ti * 512:(ti + 1) * 512],
                        h1T[:, ic * T + tile * 128:ic * T + (tile + 1) * 128],
                        ws_buf[icb % 2][:, j * ncol:(j + 1) * ncol],
                        start=(ic == 0), stop=(ic == NIC - 1)),
                        ((wt, h1T_done) + tuple(extra_wait))
                        if (ic == 0 and ti == 0) else (wt,),
                        (PE, 1) if (last or (j == 1 and ti == ntile - 1)) else None)
            if j == 1:
                blk_pe[icb] = (PE, PE.n)
                if icb + 2 < NB:
                    load_blk(icb + 2, (blk_pe[icb],))
        mm_tok = (PE, PE.n)
        ev = None
        for ti in range(ntile):
            tile = out_tiles[ti]
            if kind == "fm":
                psl = PS[:, ti * 1024:(ti + 1) * 1024]
                dst = out_sb[:, out_off + tile * T: out_off + (tile + 1) * T]
            else:
                psl = PS[:, ti * 512:(ti + 1) * 512]
                dst = out_sb[:, tile * KV + out_off: tile * KV + out_off + ncol]
            if scale is None:
                B.emit("vector", lambda e, dst=dst, psl=psl: e.tensor_copy(dst, psl),
                       (mm_tok,), (VE, 1))
            else:
                B.emit("vector", lambda e, dst=dst, psl=psl: e.tensor_scalar(
                    dst, psl, scale, None, op0=ALU.mult), (mm_tok,), (VE, 1))
            ev = (VE, VE.n)
        return ev

    def a2_hook():
        bw_step(2)

    kev = proj_pass("fm", [0, 1, 2], wk, (0, 384), True, kT_own, 0, hook=a2_hook)
    kev = proj_pass("fm", [3, 4, 5], wk, (384, 768), True, kT_own, 0,
                    extra_wait=(kev,), hook=a2_hook)
    kev = proj_pass("fm", [6, 7], wk, (768, 1024), True, kT_own, 0,
                    extra_wait=(kev,), hook=a2_hook)
    vev = kev
    for tg in range(2):
        for vf in range(2):
            vev = proj_pass("tm", [tg * 4 + i for i in range(4)], wv,
                            (vf * 512, (vf + 1) * 512), True, v_own, vf * 512,
                            extra_wait=(vev,), hook=a2_hook)
    assert fold_state["done"] is not None, "fold must complete in P0+A1"
    if debug:
        for _q in range(4):
            dma("gpsimd", dbg["dbg_wqe"][_q * 1024:(_q + 1) * 1024, :],
                wqe_d[_q * 1024:(_q + 1) * 1024, :], (fold_state["done"],),
                st="g_dbg")
        dbg_written.add("dbg_wqe")
    qev = proj_pass("fm", [0, 1, 2], wqe_d, (0, 384), False, qT, 0, scale=QSCALE,
                    extra_wait=(vev, fold_state["done"]), hook=a2_hook)
    qev = proj_pass("fm", [3, 4, 5], wqe_d, (384, 768), False, qT, 0, scale=QSCALE,
                    extra_wait=(qev,), hook=a2_hook)
    qev = proj_pass("fm", [6, 7], wqe_d, (768, 1024), False, qT, 0, scale=QSCALE,
                    extra_wait=(qev,), hook=a2_hook)
    q_done = qev

    dbg_toks = []
    if debug:
        dbg_toks.append(dma("gpsimd", dbg["dbg_kT"], kT_own[:, :], (kev,), st="g_dbg"))
        dbg_toks.append(dma("gpsimd", dbg["dbg_v"], v_own[:, :], (vev,), st="g_dbg"))
        dbg_toks.append(dma("gpsimd", dbg["dbg_qT"], qT[:, :], (q_done,), st="g_dbg"))
        dbg_written.update(["dbg_kT", "dbg_v", "dbg_qT"])

    st_toks = []
    for kc in range(8):
        st_toks.append(dma("sync", kv_in[kc * 128:(kc + 1) * 128, :],
                           kT_own[:, kc * T:(kc + 1) * T], (kev,), st="s_kvst"))
    for tt in range(8):
        st_toks.append(dma("sync", kv_in[T + tt * 128: T + (tt + 1) * 128, :],
                           v_own[:, tt * KV:(tt + 1) * KV], (vev,), st="s_kvst"))
    if stage == "A2":
        return _finish()
    B.emit("gpsimd", lambda e: e.collective_compute(
        "AllGather", ALU.bypass,
        replica_groups=[[0, 1, 2, 3], [4, 5, 6, 7]],
        ins=[kv_in[:, :]], outs=[kv_out[:, :]]),
        tuple(st_toks) + tuple(dbg_toks), (CC, 1))
    cc_done = (CC, CC.n)

    bw_step(200)
    bwT_done = bw_state["done"]

    if stage == "AG":
        return _finish()
    v_toks = []
    for r in range(4):
        src = kv_out[2048 * r + 1024: 2048 * r + 2048, :].rearrange(
            "(kc p) f -> p kc f", p=128)
        dstv = v_sb[:, :].rearrange("p (kc f) -> p kc f", kc=NKC)[
            :, r * 8:(r + 1) * 8, :]
        v_toks.append(dma("gpsimd", dstv, src, (cc_done, q_done), st="g_vsb"))
    v_ready = v_toks[-1]

    kh_tok = None
    last_smm = None
    prev_av_ev = None
    prev_D_pe = None
    prev_exp = {}
    prev_pt_v = None
    prev_pt_s = None
    CH = [(0, 1536), (1536, 3072), (3072, 4096)]
    for h in range(NH):
        kw = [cc_done] if kh_tok is None else [last_smm]
        kts = []
        for r in range(4):
            kts.append(dma("gpsimd", kh_sb[:, r * 1024:(r + 1) * 1024],
                           kv_out[2048 * r + h * 128: 2048 * r + (h + 1) * 128, :],
                           tuple(kw) if r == 0 else (), st="g_kh"))
        kh_tok = kts[-1]
        for qh in range(2):
            pt_v_last = None
            pt_s_last = None
            for qtl in range(4):
                qt = qh * 4 + qtl
                for ci, (c0, c1) in enumerate(CH):
                    nmm = (c1 - c0) // 512
                    for m in range(nmm):
                        ws = [kh_tok, q_done]
                        if m == 0 and ci in prev_exp:
                            ws.append(prev_exp[ci])
                        if ci == 0 and m == 0:
                            if prev_pt_v is not None:
                                ws += [prev_pt_v, prev_pt_s]
                            if prev_av_ev is not None:
                                ws.append(prev_av_ev)
                        B.emit("tensor", lambda e, h=h, qt=qt, c0=c0, m=m: e.matmul(
                            PS[:, c0 + m * 512:c0 + (m + 1) * 512],
                            qT[:, h * T + qt * 128:h * T + (qt + 1) * 128],
                            kh_sb[:, c0 + m * 512:c0 + (m + 1) * 512],
                            start=True, stop=True),
                            tuple(ws), (PE, 1) if (ci == 2 and m == nmm - 1) else None)
                last_smm = (PE, PE.n)
                sub = PS[:, :].rearrange("p (a b) -> p b a", b=8)[:, 0, :]
                B.emit("vector", lambda e, sub=sub: e.reduce_max(
                    st2[:, 0:1], sub, axis=AXX, negate=True), (last_smm,), (VE, 1))
                negm_tok = (VE, VE.n)
                exp_toks = []
                for ci, (c0, c1) in enumerate(CH):
                    ws = [negm_tok]
                    if prev_pt_v is not None:
                        ws += [prev_pt_v, prev_pt_s]
                    B.emit("scalar", lambda e, c0=c0, c1=c1, ci=ci: e.activation(
                        P_sb[:, c0:c1], PS[:, c0:c1], AF.Exp,
                        bias=st2[:, 0:1], scale=1.0,
                        accum_out=st2[:, 1 + ci:2 + ci]), tuple(ws), (SE, 1))
                    exp_toks.append((SE, SE.n))
                    prev_exp[ci] = (SE, SE.n)
                B.emit("vector", lambda e: e.tensor_add(
                    st2[:, 4:5], st2[:, 1:2], st2[:, 2:3]), (exp_toks[-1],), (VE, 1))
                B.emit("vector", None, ((VE, VE.n),))
                B.emit("vector", lambda e: e.tensor_add(
                    st2[:, 4:5], st2[:, 4:5], st2[:, 3:4]), (), (VE, 1))
                B.emit("vector", None, ((VE, VE.n),))
                B.emit("vector", lambda e: e.reciprocal(
                    st2[:, 5:6], st2[:, 4:5]), (), (VE, 1))
                B.emit("vector", None, ((VE, VE.n),))
                dw = [prev_D_pe] if prev_D_pe is not None else []
                B.emit("vector", lambda e: e.tensor_scalar(
                    D_sb[:, :], ident_b[:, :], st2[:, 5:6], None, op0=ALU.mult),
                    tuple(dw), (VE, 1))
                D_tok = (VE, VE.n)
                evs_v = {}
                evs_s = {}
                for grp in range(8):
                    pb = 512 + (grp % 2) * 512
                    gw = [D_tok, exp_toks[-1]] if grp == 0 else []
                    if grp >= 2:
                        gw.append(evs_v[grp - 2] if (grp - 2) % 2 == 0
                                  else evs_s[grp - 2])
                    for j in range(4):
                        kc = grp * 4 + j
                        B.emit("tensor", lambda e, kc=kc, pb=pb, j=j: e.matmul(
                            PS[:, pb + j * 128:pb + (j + 1) * 128],
                            P_sb[:, kc * 128:(kc + 1) * 128], D_sb[:, :],
                            start=True, stop=True),
                            tuple(gw) if j == 0 else (),
                            (PE, 1) if j == 3 else None)
                    pe_tok = (PE, PE.n)
                    dst = PT_half[:, :].rearrange("p (kc t) -> p kc t", kc=NKC)[
                        :, grp * 4:(grp + 1) * 4, qtl * 128:(qtl + 1) * 128]
                    psrc = PS[:, pb:pb + 512].rearrange("p (a b) -> p a b", a=4)
                    if grp % 2 == 0:
                        B.emit("vector", lambda e, dst=dst, psrc=psrc: e.tensor_copy(
                            dst, psrc), (pe_tok,), (VE, 1))
                        evs_v[grp] = (VE, VE.n)
                    else:
                        B.emit("scalar", lambda e, dst=dst, psrc=psrc: e.copy(
                            dst, psrc), (pe_tok,), (SE, 1))
                        evs_s[grp] = (SE, SE.n)
                pt_v_last = evs_v[6]
                pt_s_last = evs_s[7]
                prev_pt_v, prev_pt_s = pt_v_last, pt_s_last
                prev_D_pe = (PE, PE.n)
            for kc in range(NKC):
                ws = []
                if kc == 0:
                    ws = [pt_v_last, pt_s_last, v_ready]
                    if prev_av_ev is not None:
                        ws.append(prev_av_ev)
                B.emit("tensor", lambda e, kc=kc, h=h: e.matmul(
                    PS[:, 0:512],
                    v_sb[:, kc * KV + h * 128: kc * KV + (h + 1) * 128],
                    PT_half[:, kc * 512:(kc + 1) * 512],
                    start=(kc == 0), stop=(kc == NKC - 1)),
                    tuple(ws), (PE, 1) if kc == NKC - 1 else None)
            av_tok = (PE, PE.n)
            B.emit("vector", lambda e, h=h, qh=qh: e.tensor_copy(
                vals_sb[:, h * T + qh * 512: h * T + (qh + 1) * 512], PS[:, 0:512]),
                (av_tok,), (VE, 1))
            prev_av_ev = (VE, VE.n)
    attn_done = prev_av_ev

    if debug:
        dma("gpsimd", dbg["dbg_vals"], vals_sb[:, :], (attn_done,), st="g_dbg")
        dbg_written.add("dbg_vals")

    if stage == "A4":
        return _finish()
    wo_toks = []
    for kvc in range(8):
        wo_toks.append(dma("gpsimd", wo_sb[:, kvc * D:(kvc + 1) * D],
                           wo[kvc * 128:(kvc + 1) * 128, :],
                           (attn_done,) if kvc == 0 else (wo_toks[-1],),
                           st="g_wo"))
    atst_free = [None, None]
    x5_free = None
    prev_ev_tok = None
    a5_h2_st = []
    for tt in range(NTT):
        bi = tt % 2
        for kvc in range(8):
            ws = []
            if kvc == 0:
                ws = [wo_toks[0], attn_done]
                if atst_free[bi] is not None:
                    ws.append(atst_free[bi])
                if prev_ev_tok is not None:
                    ws.append(prev_ev_tok)
            else:
                ws = [wo_toks[kvc]]
            for ob in range(8):
                B.emit("tensor", lambda e, kvc=kvc, ob=ob, tt=tt: e.matmul(
                    PS[:, ob * 512:(ob + 1) * 512],
                    vals_sb[:, kvc * T + tt * 128: kvc * T + (tt + 1) * 128],
                    wo_sb[:, kvc * D + ob * 512: kvc * D + (ob + 1) * 512],
                    start=(kvc == 0), stop=(kvc == 7)),
                    tuple(ws) if ob == 0 else (),
                    (PE, 1) if (kvc == 7 and ob == 7) else None)
        wo_tok = (PE, PE.n)
        B.emit("vector", lambda e, bi=bi: e.tensor_copy(
            atst[bi][:, 0:2048], PS[:, 0:2048]), (wo_tok,), (VE, 1))
        B.emit("vector", lambda e, bi=bi: e.tensor_copy(
            atst[bi][:, 2048:4096], PS[:, 2048:4096]), (), (VE, 1))
        ev_tok = (VE, VE.n)
        prev_ev_tok = ev_tok
        B.emit("vector", None, (ev_tok,))
        B.emit("scalar", None, (ev_tok,))
        rstd_tok, nb_tok = ln_stats(atst[bi][:, :])
        lnw = [nb_tok]
        if tt > 0:
            lnw.append(prev_x1_tok)
        B.emit("scalar", lambda e, bi=bi: e.activation(
            ln_sb[:, :], atst[bi][:, :], AF.Identity, bias=st1[:, 8:9], scale=st1[:, 7:8]),
            tuple(lnw), (SE, 1))
        ln_tok = (SE, SE.n)
        atst_free[bi] = ln_tok
        xw = [attn_done] if tt == 0 else [x5_free, prev_t_h2]
        t_x = dma("sync", x5_buf[:, :], x_in[tt * 128:(tt + 1) * 128, :], tuple(xw), st="s_x5")
        B.emit("vector", lambda e: e.tensor_add(x5_buf[:, :], x5_buf[:, :], ln_sb[:, :]),
               (t_x, ln_tok), (VE, 1))
        x1_tok = (VE, VE.n)
        prev_x1_tok = x1_tok
        dma("sync", x1sp[tt * 128:(tt + 1) * 128, :], x5_buf[:, :], (x1_tok,), st="s_sp")
        B.emit("vector", None, (x1_tok,))
        B.emit("scalar", None, (x1_tok,))
        rstd2, nb2 = ln_stats(x5_buf[:, :])
        B.emit("vector", lambda e: e.tensor_scalar(
            h2b[:, :], x5_buf[:, :], st1[:, 2:3], st1[:, 7:8],
            op0=ALU.add, op1=ALU.mult), (rstd2, nb2), (VE, 1))
        h2_tok = (VE, VE.n)
        x5_free = h2_tok
        a5_h2_st.append(dma("sync", h2sp[tt * 128:(tt + 1) * 128, :], h2b[:, :],
                            (h2_tok,), st="s_sp"))
        prev_t_h2 = a5_h2_st[-1]
        if debug:
            dma("gpsimd", dbg["dbg_h2"][tt * 128:(tt + 1) * 128, :], h2b[:, :], (h2_tok,), st="g_dbg")
            dbg_written.add("dbg_h2")
    a5_done = a5_h2_st[-1]

    if stage == "A5":
        return _finish()
    h2rd_free = None
    stage_evs = []
    ep_done = []
    slot_free = [None, None]
    x1_free = None
    for hf in range(2):
        tr_last = None
        for ttl in range(4):
            tt = hf * 4 + ttl
            w = [a5_done]
            if h2rd_free is not None:
                w.append(h2rd_free)
            t_h2 = dma("sync", h2b[:, :], h2sp[tt * 128:(tt + 1) * 128, :], tuple(w), st="s_h2rd")
            evs = {}
            for grp in range(8):
                bank = (grp % 2) * 512
                gw = [t_h2] if grp == 0 else []
                if hf == 1 and ttl == 0 and grp == 0:
                    gw.append(ep_done[-1])
                if grp >= 2:
                    gw.append(evs[grp - 2])
                elif ttl > 0:
                    gw.append(prev_evs[grp + 6])
                pb = (grp % 2) * 1024
                for j in range(4):
                    ic = grp * 4 + j
                    B.emit("tensor", lambda e, ic=ic, pb=pb, j=j: e.transpose(
                        PSb[:, pb + j * 128:pb + (j + 1) * 128],
                        h2b[:, ic * 128:(ic + 1) * 128], ident_b[:, :]),
                        tuple(gw) if j == 0 else (), (PE, 1) if j == 3 else None)
                pe_tok = (PE, PE.n)
                dst = h2T[:, :].rearrange("p (ic t) -> p ic t", ic=NIC)[
                    :, grp * 4:(grp + 1) * 4, ttl * 128:(ttl + 1) * 128]
                B.emit("vector", lambda e, dst=dst, pb=pb: e.tensor_copy(
                    dst, PSb[:, pb:pb + 512].rearrange("p (a b) -> p a b", a=4)),
                    (pe_tok,), (VE, 1))
                evs[grp] = (VE, VE.n)
            prev_evs = evs
            h2rd_free = evs[7]
            tr_last = evs[7]
        h2T_tok = tr_last

        for ttl in range(4):
            for ic in range(NIC):
                B.emit("tensor", lambda e, ic=ic, ttl=ttl: e.matmul(
                    PS[:, ttl * 512: ttl * 512 + 80],
                    h2T[:, ic * 512 + ttl * 128: ic * 512 + (ttl + 1) * 128],
                    arT[:, ic * 80:(ic + 1) * 80],
                    start=(ic == 0), stop=(ic == NIC - 1)),
                    (h2T_tok, ar_done) if (ttl == 0 and ic == 0) else (),
                    (PE, 1) if ic == NIC - 1 else None)
        ar_mm = (PE, PE.n)
        for ttl in range(4):
            B.emit("vector", lambda e, ttl=ttl: e.tensor_copy(
                res16[:, ttl * 64:(ttl + 1) * 64], PS[:, ttl * 512: ttl * 512 + 64]),
                (ar_mm,), (VE, 1))
            B.emit("vector", lambda e, ttl=ttl: e.tensor_copy(
                rtr[:, ttl * 16:(ttl + 1) * 16], PS[:, ttl * 512 + 64: ttl * 512 + 80]),
                (), (VE, 1))
        ar_ev = (VE, VE.n)
        if debug:
            dma("gpsimd", dbg["dbg_res"][hf * 512:(hf + 1) * 512, 0:64].rearrange(
                "(ttl p) r -> p ttl r", p=128),
                res16[:, :].rearrange("p (ttl r) -> p ttl r", ttl=4), (ar_ev,),
                st="g_dbg")
            dma("gpsimd", dbg["dbg_res"][hf * 512:(hf + 1) * 512, 64:80].rearrange(
                "(ttl p) r -> p ttl r", p=128),
                rtr[:, 0:64].rearrange("p (ttl r) -> p ttl r", ttl=4), (ar_ev,),
                st="g_dbg")

        ht_last = None
        for ttl in range(4):
            lg = rtr[:, ttl * 16:(ttl + 1) * 16]
            B.emit("vector", lambda e, lg=lg: e.reduce_max(
                rtr[:, 64:65], lg, axis=AXX, negate=True), (ar_ev,), (VE, 1))
            B.emit("scalar", lambda e, lg=lg: e.activation(
                rtr[:, 80:96], lg, AF.Exp, bias=rtr[:, 64:65], scale=1.0,
                accum_out=rtr[:, 65:66]), ((VE, VE.n),), (SE, 1))
            B.emit("vector", lambda e: e.reciprocal(rtr[:, 66:67], rtr[:, 65:66]),
                   ((SE, SE.n),), (VE, 1))
            B.emit("vector", None, ((VE, VE.n),))
            B.emit("vector", lambda e: e.tensor_scalar(
                rtr[:, 96:112], rtr[:, 80:96], rtr[:, 66:67], None, op0=ALU.mult),
                (), (VE, 1))
            B.emit("vector", None, ((VE, VE.n),))
            B.emit("vector", lambda e: e.reduce_max(
                rtr[:, 67:68], rtr[:, 96:112], axis=AXX), (), (VE, 1))
            B.emit("vector", None, ((VE, VE.n),))
            B.emit("vector", lambda e: e.tensor_scalar(
                rtr[:, 112:128], rtr[:, 96:112], rtr[:, 67:68], None, op0=ALU.is_equal),
                (), (VE, 1))
            B.emit("vector", None, ((VE, VE.n),))
            B.emit("vector", lambda e: e.tensor_sub(
                rtr[:, 128:144], rtr[:, 96:112], rtr[:, 112:128]), (), (VE, 1))
            B.emit("vector", None, ((VE, VE.n),))
            B.emit("vector", lambda e: e.reduce_max(
                rtr[:, 68:69], rtr[:, 128:144], axis=AXX), (), (VE, 1))
            B.emit("vector", None, ((VE, VE.n),))
            B.emit("vector", lambda e: e.tensor_scalar(
                rtr[:, 144:160], rtr[:, 128:144], rtr[:, 68:69], None, op0=ALU.is_equal),
                (), (VE, 1))
            B.emit("vector", None, ((VE, VE.n),))
            B.emit("vector", lambda e: e.tensor_scalar(
                rtr[:, 69:70], rtr[:, 67:68], rtr[:, 68:69], 1e-6,
                op0=ALU.add, op1=ALU.add), (), (VE, 1))
            B.emit("vector", None, ((VE, VE.n),))
            B.emit("vector", lambda e: e.reciprocal(rtr[:, 70:71], rtr[:, 69:70]),
                   (), (VE, 1))
            B.emit("vector", None, ((VE, VE.n),))
            B.emit("vector", lambda e: e.tensor_scalar(
                rtr[:, 71:72], rtr[:, 67:68], rtr[:, 70:71], 0.5,
                op0=ALU.mult, op1=ALU.mult), (), (VE, 1))
            B.emit("vector", None, ((VE, VE.n),))
            B.emit("vector", lambda e: e.tensor_scalar(
                rtr[:, 72:73], rtr[:, 68:69], rtr[:, 70:71], 0.5,
                op0=ALU.mult, op1=ALU.mult), (), (VE, 1))
            B.emit("vector", None, ((VE, VE.n),))
            B.emit("vector", lambda e: e.tensor_scalar(
                rtr[:, 112:128], rtr[:, 112:128], rtr[:, 71:72], None, op0=ALU.mult),
                (), (VE, 1))
            B.emit("vector", None, ((VE, VE.n),))
            B.emit("vector", lambda e: e.tensor_scalar(
                rtr[:, 144:160], rtr[:, 144:160], rtr[:, 72:73], None, op0=ALU.mult),
                (), (VE, 1))
            B.emit("vector", None, ((VE, VE.n),))
            B.emit("vector", lambda e: e.tensor_add(
                rtr[:, 112:128], rtr[:, 112:128], rtr[:, 144:160]), (), (VE, 1))
            B.emit("vector", None, ((VE, VE.n),))
            for e_i in range(16):
                B.emit("vector", lambda e, e_i=e_i, ttl=ttl: e.tensor_scalar(
                    H_sb[:, e_i * 64:(e_i + 1) * 64],
                    res16[:, ttl * 64:(ttl + 1) * 64],
                    rtr[:, 112 + e_i:113 + e_i], None, op0=ALU.mult),
                    (), (VE, 1) if e_i == 15 else None)
            h_tok = (VE, VE.n)
            if debug:
                dma("gpsimd", dbg["dbg_H"][(hf * 4 + ttl) * 128:(hf * 4 + ttl + 1) * 128, :],
                    H_sb[:, :], (h_tok,), st="g_dbg")
            for grp in range(2):
                pb = grp * 1024
                gw = [h_tok, ar_ev] if grp == 0 else []
                for j in range(4):
                    erc = grp * 4 + j
                    B.emit("tensor", lambda e, erc=erc, pb=pb, j=j: e.transpose(
                        PSb[:, pb + j * 128:pb + (j + 1) * 128],
                        H_sb[:, erc * 128:(erc + 1) * 128], ident_b[:, :]),
                        tuple(gw) if j == 0 else (), (PE, 1) if j == 3 else None)
                pe_tok = (PE, PE.n)
                dst = HT_sb[:, :].rearrange("p (erc t) -> p erc t", erc=8)[
                    :, grp * 4:(grp + 1) * 4, ttl * 128:(ttl + 1) * 128]
                B.emit("vector", lambda e, dst=dst, pb=pb: e.tensor_copy(
                    dst, PSb[:, pb:pb + 512].rearrange("p (a b) -> p a b", a=4)),
                    (pe_tok,), (VE, 1))
                ht_last = (VE, VE.n)
        ht_done = ht_last

        for oe in range(16):
            gi = hf * 16 + oe
            si = gi % 2
            w = [bwT_done]
            if slot_free[si] is not None:
                w.append(slot_free[si])
            if gi == 0:
                w.append(a5_done)
            t_w1 = dma("sync", bwT_s[si][:, :].rearrange("p (ic o) -> p ic o", ic=NIC),
                       bwT_d[oe].rearrange("(ic p) o -> p ic o", p=128), tuple(w),
                       st=f"s_sl{si}")
            t_w2 = dma("sync", bfT_s[si][:, :].rearrange("p (erc o) -> p erc o", erc=8),
                       bfT_d[oe].rearrange("(erc p) o -> p erc o", p=128),
                       (bft_done,) + ((slot_free[si],) if slot_free[si] else ()),
                       st=f"s_sl{si}")
            goff = (gi % 2) * 2048
            for ic in range(NIC):
                for ttl in range(4):
                    ws = []
                    if ic == 0 and ttl == 0:
                        ws = [t_w1, t_w2, ht_done]
                        if len(stage_evs) >= 2:
                            ws.append(stage_evs[gi - 2])
                    B.emit("tensor", lambda e, ic=ic, ttl=ttl, si=si, goff=goff: e.matmul(
                        PS[:, goff + ttl * 512: goff + ttl * 512 + 256],
                        h2T[:, ic * 512 + ttl * 128: ic * 512 + (ttl + 1) * 128],
                        bwT_s[si][:, ic * 256:(ic + 1) * 256],
                        start=(ic == 0), stop=False),
                        tuple(ws), None)
            for erc in range(8):
                for ttl in range(4):
                    ws = [t_w2] if (erc == 0 and ttl == 0) else []
                    B.emit("tensor", lambda e, erc=erc, ttl=ttl, si=si, goff=goff: e.matmul(
                        PS[:, goff + ttl * 512: goff + ttl * 512 + 256],
                        HT_sb[:, erc * 512 + ttl * 128: erc * 512 + (ttl + 1) * 128],
                        bfT_s[si][:, erc * 256:(erc + 1) * 256],
                        start=False, stop=(erc == 7)),
                        tuple(ws), (PE, 1) if (erc == 7 and ttl == 3) else None)
            mm_tok = (PE, PE.n)
            slot_free[si] = mm_tok
            for ttl in range(4):
                B.emit("vector", lambda e, ttl=ttl, oe=oe, goff=goff: e.tensor_tensor(
                    stg_sb[:, ttl * D + oe * 256: ttl * D + (oe + 1) * 256],
                    PS[:, goff + ttl * 512: goff + ttl * 512 + 256],
                    bb16[:, oe * 256:(oe + 1) * 256], op=ALU.add),
                    (mm_tok, bb_ready), (VE, 1) if ttl == 3 else None)
            stage_evs.append((VE, VE.n))
        pass_done = stage_evs[-1]

        for ttl in range(4):
            tt = hf * 4 + ttl
            stg = stg_sb[:, ttl * D:(ttl + 1) * D]
            B.emit("vector", None, (pass_done,))
            B.emit("scalar", None, (pass_done,))
            rstd3, nb3 = ln_stats(stg)
            lnw = [nb3] + ([ep_done[-1]] if ep_done else [])
            B.emit("scalar", lambda e, stg=stg: e.activation(
                ln_sb[:, :], stg, AF.Identity, bias=st1[:, 8:9], scale=st1[:, 7:8]),
                tuple(lnw), (SE, 1))
            ln_tok = (SE, SE.n)
            xw = [a5_done]
            if x1_free:
                xw.append(x1_free)
            t_x1 = dma("sync", x1buf[:, :], x1sp[tt * 128:(tt + 1) * 128, :], tuple(xw), st="s_x1rd")
            B.emit("vector", lambda e: e.tensor_add(
                x1buf[:, :], x1buf[:, :], ln_sb[:, :]), (t_x1, ln_tok), (VE, 1))
            fin_tok = (VE, VE.n)
            x1_free = dma("sync", out_d[tt * 128:(tt + 1) * 128, :], x1buf[:, :],
                          (fin_tok,), st="s_out")
            ep_done.append(fin_tok)
        if debug:
            dma("gpsimd", dbg["dbg_stage"][hf * 512:(hf + 1) * 512, :].rearrange(
                "(ttl p) o -> p ttl o", p=128),
                stg_sb[:, :].rearrange("p (ttl o) -> p ttl o", ttl=4), (pass_done,),
                st="g_dbg")

    return _finish()


_NC_CACHE = {}


def _get_nc(debug=DEBUG):
    if debug not in _NC_CACHE:
        _NC_CACHE[debug] = build_program(debug)
    return _NC_CACHE[debug]


def kernel(L0_attn_x, L0_attn_wq, L0_attn_wk, L0_attn_wv, L0_attn_wo,
           L0_ffn_base_w, L0_ffn_base_b, L0_ffn_A, L0_ffn_B, L0_ffn_router_w):
    x = np.ascontiguousarray(np.asarray(L0_attn_x, dtype=np.float32))
    wq_ = np.ascontiguousarray(np.asarray(L0_attn_wq, dtype=np.float32))
    wk_ = np.ascontiguousarray(np.asarray(L0_attn_wk, dtype=np.float32))
    wv_ = np.ascontiguousarray(np.asarray(L0_attn_wv, dtype=np.float32))
    wo_ = np.ascontiguousarray(np.asarray(L0_attn_wo, dtype=np.float32))
    bw_ = np.ascontiguousarray(np.asarray(L0_ffn_base_w, dtype=np.float32))
    bb_ = np.ascontiguousarray(np.asarray(L0_ffn_base_b, dtype=np.float32))
    A_ = np.ascontiguousarray(np.asarray(L0_ffn_A, dtype=np.float32))
    B_ = np.ascontiguousarray(np.asarray(L0_ffn_B, dtype=np.float32))
    rw_ = np.ascontiguousarray(np.asarray(L0_ffn_router_w, dtype=np.float32))

    nc = _get_nc()
    xf = x.reshape(8192, 4096)
    import ml_dtypes
    idf = np.eye(128, dtype=np.float32)
    idb = np.eye(128).astype(ml_dtypes.bfloat16)
    in_maps = [{
        "x": xf[c * 1024:(c + 1) * 1024], "wq": wq_, "wk": wk_, "wv": wv_,
        "wo": wo_, "bw": bw_, "bb": bb_, "A": A_, "B": B_, "rw": rw_,
        "idf": idf, "idb": idb,
    } for c in range(8)]
    res = run_bass_kernel_spmd(nc, in_maps, core_ids=list(range(8)))
    shards = [res.results[c]["out"] for c in range(8)]
    return np.concatenate(shards, axis=0).reshape(2, 4096, 4096).astype(np.float32)


if __name__ == "__main__":
    build_program()
    print("build ok")


# version 05
# speedup vs baseline: 1.0503x; 1.0503x over previous
import numpy as np
from contextlib import ExitStack

import concourse.bass as bass
import concourse.mybir as mybir
from concourse.bass_utils import run_bass_kernel_spmd

F32 = mybir.dt.float32
BF16 = mybir.dt.bfloat16
U8 = mybir.dt.uint8
AF = mybir.ActivationFunctionType
ALU = mybir.AluOpType
AXX = mybir.AxisListType.X

D = 4096
T = 1024
NTT = 8
NIC = 32
NH = 8
KV = 1024
S = 4096
NKC = 32
EPS = 1e-5
QSCALE = float(1.0 / np.sqrt(128.0))

DEBUG = False


class Sem:
    __slots__ = ("h", "n")

    def __init__(self, h):
        self.h = h
        self.n = 0


class Builder:
    ENGINES = ("sync", "gpsimd", "tensor", "vector", "scalar")

    def __init__(self):
        self.ops = {e: [] for e in self.ENGINES}

    def emit(self, eng, fn, waits=(), inc=None):
        ws = []
        for s, v in waits:
            assert v <= s.n, f"wait on future sem value {v} > {s.n} ({eng})"
            if v > 0:
                ws.append((s.h, v))
        if inc is not None:
            s, k = inc
            s.n += k
            inc = (s.h, k)
        self.ops[eng].append((fn, ws, inc))

    def replay(self, block):
        def mk(eng_name):
            def run(e):
                seen = {}
                for fn, ws, inc in self.ops[eng_name]:
                    merged = {}
                    for h, v in ws:
                        if v > merged.get(h.name, (None, -1))[1]:
                            merged[h.name] = (h, v)
                    for h, v in merged.values():
                        if seen.get(h.name, -1) >= v:
                            continue
                        seen[h.name] = v
                        e.wait_ge(h, v)
                    inst = fn(e) if fn is not None else None
                    if inc is not None:
                        h, k = inc
                        if inst is None:
                            inst = e.nop()
                        inst.then_inc(h, k)
            return run

        for eng_name in self.ENGINES:
            if self.ops[eng_name]:
                getattr(block, eng_name)(mk(eng_name))


def build_program(debug=DEBUG, stage="full"):
    nc = bass.Bass("TRN2", target_bir_lowering=False)

    x_in = nc.dram_tensor("x", [T, D], F32, kind="ExternalInput").ap()
    wq = nc.dram_tensor("wq", [D, D], F32, kind="ExternalInput").ap()
    wk = nc.dram_tensor("wk", [D, KV], F32, kind="ExternalInput").ap()
    wv = nc.dram_tensor("wv", [D, KV], F32, kind="ExternalInput").ap()
    wo = nc.dram_tensor("wo", [KV, D], F32, kind="ExternalInput").ap()
    bw = nc.dram_tensor("bw", [D, D], F32, kind="ExternalInput").ap()
    bb = nc.dram_tensor("bb", [D], F32, kind="ExternalInput").ap()
    A_in = nc.dram_tensor("A", [64, D], F32, kind="ExternalInput").ap()
    B_in = nc.dram_tensor("B", [16, D, 64], F32, kind="ExternalInput").ap()
    rw = nc.dram_tensor("rw", [16, D], F32, kind="ExternalInput").ap()
    idf_in = nc.dram_tensor("idf", [128, 128], F32, kind="ExternalInput").ap()
    idb_in = nc.dram_tensor("idb", [128, 128], BF16, kind="ExternalInput").ap()
    out_d = nc.dram_tensor("out", [T, D], F32, kind="ExternalOutput").ap()

    wqe_d = nc.dram_tensor("wqe_d", [D, KV], BF16, kind="Internal").ap()
    bwT_d = nc.dram_tensor("bwT_d", [16, D, 256], BF16, kind="Internal").ap()
    bfT_d = nc.dram_tensor("bfT_d", [16, KV, 256], BF16, kind="Internal").ap()
    kv_in = nc.dram_tensor("kv_in", [2 * T, KV], BF16, kind="Internal").ap()
    kv_out = nc.dram_tensor("kv_out", [8 * T, KV], BF16, kind="Internal").ap()
    x1sp = nc.dram_tensor("x1sp", [T, D], F32, kind="Internal").ap()
    h2sp = nc.dram_tensor("h2sp", [T, D], BF16, kind="Internal").ap()

    dbg = {}
    dbg_written = set()
    if debug:
        for nm, shp, dt in [
            ("dbg_qT", [128, NH * T], F32), ("dbg_kT", [128, NH * T], F32),
            ("dbg_v", [128, NTT * KV], F32), ("dbg_vals", [128, NH * T], F32),
            ("dbg_h2", [T, D], F32), ("dbg_res", [T, 80], F32),
            ("dbg_H", [T, KV], F32), ("dbg_stage", [T, D], F32),
            ("dbg_h1b", [128, D], F32), ("dbg_h1T", [128, NIC * T], F32),
            ("dbg_wqe", [D, KV], F32),
        ]:
            dbg[nm] = nc.dram_tensor(nm, shp, dt, kind="ExternalOutput").ap()

    ctx = ExitStack()
    CAP = 212_832
    arena = ctx.enter_context(nc.sbuf_tensor("arena", [128, CAP], U8))
    base = nc.lookup_mloc(arena).addr

    def place(name, shape, dtype, off):
        per = int(np.prod(shape[1:])) * mybir.dt.size(dtype)
        assert off % 32 == 0 and off + per <= CAP, (name, off, per)
        h = nc.virtual_sbuf_tensor(name, shape, dtype)
        nc._tensor(h.name, h.shape, h.dtype, type="SB")
        m = nc.lookup_mloc(h)
        m.allocated = True
        m.addr = base + off
        return h.ap()

    def place2(name, shape, dtype, off):
        per = int(np.prod(shape[1:])) * mybir.dt.size(dtype)
        return [place(f"{name}{i}", shape, dtype, off + i * per) for i in range(2)]

    ident_f = place("ident_f", [128, 128], F32, 0)
    ident_b = place("ident_b", [128, 128], BF16, 512)
    st1 = place("st1", [128, 64], F32, 768)
    st2 = place("st2", [128, 64], F32, 1024)
    D_sb = place("D_sb", [128, 128], BF16, 1280)
    rtr = place("rtr", [128, 160], F32, 1536)
    res16 = place("res16", [128, 4 * 64], BF16, 2176)
    H_sb = place("H_sb", [128, KV], BF16, 2688)
    arT = place("arT", [128, NIC * 80], BF16, 4736)
    bb16 = place("bb16", [128, D], BF16, 9856)
    ln_sb = place("ln_sb", [128, D], F32, 18048)
    sq_sb = place("sq_sb", [128, D], BF16, 34432)
    vals_sb = place("vals_sb", [128, NH * T], BF16, 42624)
    qT = place("qT", [128, NH * T], BF16, 59008)
    P_sb = place("P_sb", [128, S], BF16, 75392)
    OB = 83584
    FR = OB + 20480

    wq_stage = place2("wq_stage", [128, 2 * KV], F32, 18048)
    wqe_sb = place("wqe_sb", [128, KV], BF16, OB)
    bw_in = place2("bw_in", [128, 512], F32, OB + 2048)
    bwT_stage = place("bwT_stage", [128, 8 * 256], BF16, OB + 6144)
    bft_sb = place("bft_sb", [128, 8 * D], BF16, FR)
    b_stage = place2("b_stage", [128, 2048], F32, FR + 65536)
    ar_stage = place("ar_stage", [64, D], BF16, FR + 81920)
    r_stage = place("r_stage", [16, D], BF16, FR + 90112)
    h1T = place("h1T", [128, NIC * T], BF16, FR)
    x_buf = place("x_buf", [128, D], F32, FR + 65536)
    h1b = place("h1b", [128, D], BF16, FR + 81920)
    kT_own = place("kT_own", [128, NH * T], BF16, FR + 65536)
    v_own = place("v_own", [128, NTT * KV], BF16, FR + 81920)
    ws_buf = place2("ws_buf", [128, 2 * KV], BF16, FR + 98304)
    v_sb = place("v_sb", [128, NKC * KV], BF16, FR)
    PT_half = place("PT_half", [128, NKC * 512], BF16, FR + 65536)
    kh_sb = place("kh_sb", [128, S], BF16, FR + 98304)
    wo_sb = place("wo_sb", [128, 8 * D], BF16, FR)
    atst = place2("atst", [128, D], BF16, FR + 65536)
    x5_buf = place("x5_buf", [128, D], F32, FR + 81920)
    h2b = place("h2b", [128, D], BF16, FR + 98304)
    stg_sb = place("stg_sb", [128, 4 * D], BF16, FR)
    h2T = place("h2T", [128, NIC * 512], BF16, FR + 32768)
    bwT_s = place2("bwT_s", [128, NIC * 256], BF16, FR + 65536)
    bfT_s = place2("bfT_s", [128, 8 * 256], BF16, 75392)
    x1buf = place("x1buf", [128, D], F32, 59008)
    HT_sb = place("HT_sb", [128, 8 * 512], BF16, 42624)
    assert FR + 106496 <= OB + 129248

    PS = ctx.enter_context(nc.psum_tensor("PS", [128, 4096], F32))
    _psh = nc.virtual_psum_tensor("PSb", [128, 8192], BF16)
    nc._tensor(_psh.name, _psh.shape, _psh.dtype, type="PSUM")
    _psm = nc.lookup_mloc(_psh)
    _psm.allocated = True
    _psm.addr = nc.lookup_mloc(PS.tensor).addr if hasattr(PS, "tensor") else 0
    PSb = _psh.ap()

    sems = {nm: Sem(ctx.enter_context(nc.semaphore(nm)))
            for nm in ("LD", "GD", "PE", "VE", "SE", "GE", "CC")}
    LD, GD, PE, VE, SE, GE, CC = (sems[k] for k in
                                  ("LD", "GD", "PE", "VE", "SE", "GE", "CC"))

    B = Builder()

    def _finish():
        if stage != "full":
            B.emit("gpsimd", lambda e: e.memset(ln_sb[:, :], 0.0), (), (GE, 1))
            zt = (GE, GE.n)
            for _tt in range(NTT):
                dma("gpsimd", out_d[_tt * 128:(_tt + 1) * 128, :], ln_sb[:, :], (zt,), st="g_fill")
            if debug:
                for _nm, _ap in dbg.items():
                    if _nm in dbg_written:
                        continue
                    _views = ([_ap] if _ap.shape[0] == 128 else
                              [_ap[_r * 128:(_r + 1) * 128]
                               for _r in range(_ap.shape[0] // 128)])
                    for _v in _views:
                        nfree = int(np.prod(_v.shape[1:]))
                        _off = 0
                        while _off < nfree:
                            _w = min(4096, nfree - _off)
                            dma("gpsimd", _v[:, _off:_off + _w], ln_sb[:, 0:_w], (zt,), st="g_fill")
                            _off += _w
        allw = tuple((s, s.n) for s in dstreams.values()) + ((CC, CC.n),)
        B.emit("gpsimd", None, allw)
        if any(fn is not None or ws for fn, ws, inc in B.ops["sync"]):
            B.emit("sync", None, allw)
        with nc.Block() as block:
            B.replay(block)
        ctx.close()
        return nc

    dstreams = {}

    def dsem(name):
        if name not in dstreams:
            dstreams[name] = Sem(ctx.enter_context(nc.semaphore(name)))
        return dstreams[name]

    def dma(q, out, in_, waits=(), st=None):
        s = dsem(st if st is not None else ("s_misc" if q == "sync" else "g_misc"))
        B.emit(q, lambda e, o=out, i=in_: e.dma_start(out=o, in_=i), waits, (s, 16))
        return (s, s.n)

    dma("sync", ident_f[:, :], idf_in[:, :])
    ident_ready = dma("sync", ident_b[:, :], idb_in[:, :])

    if stage == "Z0":
        return _finish()
    dma("gpsimd", bb16[:, :], bb[None, :].broadcast_to([128, D]))

    if stage == "P0a":
        return _finish()
    dma("gpsimd", ar_stage[0:64, :], A_in[:, :])
    t_r = dma("gpsimd", r_stage[0:16, :], rw[:, :])
    bb_ready = t_a = t_r
    ar_ev_tok = None
    for ic in range(NIC):
        if ar_ev_tok is not None:
            B.emit("tensor", None, (ar_ev_tok,))
        B.emit("tensor", lambda e, ic=ic: e.transpose(
            PSb[:, 0:64], ar_stage[0:64, ic * 128:(ic + 1) * 128], ident_b[0:64, 0:64]),
            (t_a, ident_ready) if ic == 0 else ())
        B.emit("tensor", lambda e, ic=ic: e.transpose(
            PSb[:, 64:80], r_stage[0:16, ic * 128:(ic + 1) * 128], ident_b[0:16, 0:16]),
            (t_r,) if ic == 0 else (), (PE, 1))
        B.emit("scalar", lambda e, ic=ic: e.copy(
            arT[:, ic * 80:(ic + 1) * 80], PSb[:, 0:80]), ((PE, PE.n),), (SE, 1))
        ar_ev_tok = (SE, SE.n)
    ar_done = ar_ev_tok

    if stage == "P0b":
        return _finish()
    b_pe = {}
    bft_last = None
    for e_i in range(16):
        erc, lane = e_i // 2, (e_i % 2) * 64
        srcv = B_in[e_i].rearrange("(ot p) r -> p ot r", p=128)
        lw = (b_pe[e_i - 2],) if e_i >= 2 else ()
        t_b = dma("sync", b_stage[e_i % 2][:, :].rearrange(
            "p (ot r) -> p ot r", ot=32), srcv, lw, st=f"s_bst{e_i % 2}")
        for grp in range(8):
            if bft_last is not None:
                B.emit("tensor", None, (bft_last,))
            for j in range(4):
                ot = grp * 4 + j
                B.emit("tensor", lambda e, e_i=e_i, ot=ot, j=j: e.transpose(
                    PS[0:64, 1024 + j * 128:1024 + (j + 1) * 128],
                    b_stage[e_i % 2][:, ot * 64:(ot + 1) * 64], ident_f[:, :]),
                    (t_b, ident_ready) if (grp == 0 and j == 0) else (),
                    (PE, 1) if j == 3 else None)
            b_pe[e_i] = (PE, PE.n)
            B.emit("vector", lambda e, erc=erc, lane=lane, grp=grp: e.tensor_copy(
                bft_sb[lane:lane + 64, erc * D + grp * 512:erc * D + (grp + 1) * 512],
                PS[0:64, 1024:1536]), ((PE, PE.n),), (VE, 1))
            bft_last = (VE, VE.n)
    bft_st = []
    for oe in range(16):
        src = bft_sb[:, :].rearrange("p (erc o) -> p erc o", erc=8)[
            :, :, oe * 256:(oe + 1) * 256]
        dst = bfT_d[oe].rearrange("(erc p) o -> p erc o", p=128)
        bft_st.append(dma("sync", dst, src, (bft_last,), st="s_bftst"))
    bft_done = bft_st[-1]

    if stage == "P0c":
        return _finish()
    fold_state = {"done": None}

    def fold_gen():
        mm_toks = []
        ev_tok = None
        st_tok = None
        u = 0
        for ic in range(NIC):
            for hh in range(2):
                bi = u % 2
                lw = (mm_toks[u - 2],) if u >= 2 else ()
                t_w = dma("sync", wq_stage[bi],
                          wq[ic * 128:(ic + 1) * 128, hh * 2048:(hh + 1) * 2048], lw,
                          st=f"s_wq{bi}")
                for hl in range(4):
                    h = 4 * hh + hl
                    for g in range(4):
                        ws = [t_w]
                        if hl == 0 and g == 0:
                            ws.append(ident_ready)
                            if ev_tok is not None and hh == 0:
                                ws.append(ev_tok)
                        B.emit("tensor", lambda e, bi=bi, hl=hl, g=g, h=h: e.matmul(
                            PS[:, 1536 + h * 128:1536 + (h + 1) * 128],
                            ident_f[:, :],
                            wq_stage[bi][:, hl * 512 + g * 128: hl * 512 + (g + 1) * 128],
                            start=(g == 0), stop=(g == 3)),
                            tuple(ws), (PE, 1) if (hl == 3 and g == 3) else None)
                mm_toks.append((PE, PE.n))
                u += 1
                yield
            ew = [(PE, PE.n)]
            if st_tok is not None:
                ew.append(st_tok)
            B.emit("vector", lambda e: e.tensor_copy(wqe_sb[:, :], PS[:, 1536:2560]),
                   tuple(ew), (VE, 1))
            ev_tok = (VE, VE.n)
            st_tok = dma("sync", wqe_d[ic * 128:(ic + 1) * 128, :], wqe_sb[:, :],
                         (ev_tok,), st="s_wqest")
        fold_state["done"] = st_tok

    fold_it = fold_gen()

    def fold_step(k=1):
        for _ in range(k):
            if fold_state["done"] is None:
                next(fold_it, None)

    fold_step(12)

    bw_state = {"done": None}

    def bw_gen():
        ld_free = {}
        bank_free = {}
        stg_tok = None
        st_toks = []
        unit = 0
        for sl in range(16):
            for ihq in range(4):
                evs = []
                for otl in range(2):
                    ot = sl * 2 + otl
                    for iq2 in range(2):
                        iq = ihq * 2 + iq2
                        bi = unit % 2
                        lw = []
                        if unit >= 2:
                            lw.append(ld_free[unit - 2])
                        if unit == 0:
                            lw.append(ident_ready)
                        t_bw = dma("sync", bw_in[bi],
                                   bw[ot * 128:(ot + 1) * 128,
                                      iq * 512:(iq + 1) * 512], tuple(lw),
                                   st=f"s_bw{bi}")
                        bank = 3072 + (unit % 2) * 512
                        gw = [t_bw]
                        if bank in bank_free:
                            gw.append(bank_free[bank])
                        if iq2 == 0 and otl == 0 and stg_tok is not None:
                            gw.append(stg_tok)
                        for j in range(4):
                            B.emit("tensor", lambda e, j=j, bank=bank, bi=bi: e.transpose(
                                PS[:, bank + j * 128:bank + (j + 1) * 128],
                                bw_in[bi][:, j * 128:(j + 1) * 128], ident_f[:, :]),
                                tuple(gw) if j == 0 else (),
                                (PE, 1) if j == 3 else None)
                        pe_tok = (PE, PE.n)
                        dst = bwT_stage[:, :].rearrange(
                            "p (icl o) -> p icl o", icl=8)[
                            :, iq2 * 4:(iq2 + 1) * 4, otl * 128:(otl + 1) * 128]
                        B.emit("vector", lambda e, dst=dst, bank=bank: e.tensor_copy(
                            dst, PS[:, bank:bank + 512].rearrange(
                                "p (a b) -> p a b", a=4)), (pe_tok,), (VE, 1))
                        bank_free[bank] = (VE, VE.n)
                        evs.append((VE, VE.n))
                        ld_free[unit] = (VE, VE.n)
                        unit += 1
                        yield
                dstv = bwT_d[sl][ihq * 1024:(ihq + 1) * 1024, :].rearrange(
                    "(icl p) o -> p icl o", p=128)
                stg_tok = dma("sync", dstv,
                              bwT_stage[:, :].rearrange("p (icl o) -> p icl o", icl=8),
                              tuple(evs), st="s_bwst")
                st_toks.append(stg_tok)
        bw_state["done"] = st_toks[-1]

    bw_it = bw_gen()

    def bw_step(k=1):
        for _ in range(k):
            if bw_state["done"] is None:
                next(bw_it, None)

    if stage == "P0":
        return _finish()
    def vdrain():
        B.emit("vector", None, ((VE, VE.n),))

    def ln_stats(src_ap):
        B.emit("vector", lambda e: e.reduce_sum(st1[:, 0:1], src_ap, axis=AXX), (), (VE, 1))
        vdrain()
        B.emit("scalar", lambda e: e.activation(
            sq_sb[:, :], src_ap, AF.Square, accum_out=st1[:, 1:2]), (), (SE, 1))
        sq_tok = (SE, SE.n)
        B.emit("vector", lambda e: e.tensor_scalar(
            st1[:, 2:3], st1[:, 0:1], -1.0 / D, None, op0=ALU.mult), (), (VE, 1))
        vdrain()
        B.emit("vector", lambda e: e.tensor_scalar(
            st1[:, 3:4], st1[:, 1:2], 1.0 / D, None, op0=ALU.mult), (sq_tok,), (VE, 1))
        vdrain()
        B.emit("vector", lambda e: e.tensor_mul(st1[:, 4:5], st1[:, 2:3], st1[:, 2:3]),
               (), (VE, 1))
        vdrain()
        B.emit("vector", lambda e: e.tensor_scalar(
            st1[:, 5:6], st1[:, 3:4], st1[:, 4:5], EPS, op0=ALU.subtract, op1=ALU.add),
            (), (VE, 1))
        vdrain()
        B.emit("vector", lambda e: e.reciprocal(st1[:, 6:7], st1[:, 5:6]), (), (VE, 1))
        B.emit("scalar", lambda e: e.activation(st1[:, 7:8], st1[:, 6:7], AF.Sqrt),
               ((VE, VE.n),), (SE, 1))
        rstd_tok = (SE, SE.n)
        B.emit("scalar", None, (rstd_tok,))
        B.emit("vector", lambda e: e.tensor_mul(st1[:, 8:9], st1[:, 2:3], st1[:, 7:8]),
               (rstd_tok,), (VE, 1))
        vdrain()
        return rstd_tok, (VE, VE.n)

    x_free = None
    h1T_last = None
    a1_bar = ([(VE, VE.n), (SE, SE.n), (GE, GE.n), (PE, PE.n)] +
              [(s, s.n) for s in dstreams.values()])
    for tt in range(NTT):
        w = ((x_free,) if x_free else tuple(a1_bar))
        t_x = dma("sync", x_buf[:, :], x_in[tt * 128:(tt + 1) * 128, :], w, st="s_x")
        B.emit("vector", None, (t_x,))
        B.emit("scalar", None, (t_x,))
        rstd_tok, nb_tok = ln_stats(x_buf[:, :])
        nw = [nb_tok]
        if tt > 0:
            nw.append(h1_pe_prev)
        B.emit("scalar", lambda e: e.activation(
            h1b[:, :], x_buf[:, :], AF.Identity, bias=st1[:, 8:9], scale=st1[:, 7:8]),
            tuple(nw), (SE, 1))
        h1_tok = (SE, SE.n)
        x_free = h1_tok
        ev2 = {}
        for grp in range(8):
            bank = (grp % 2) * 512
            gw = [h1_tok] if grp == 0 else []
            if tt == 0 and grp == 0:
                gw += list(a1_bar)
            if grp >= 2:
                gw.append(ev2[grp - 2])
            elif tt > 0:
                gw.append(h1T_prev[grp])
            pb = (grp % 2) * 1024
            for j in range(4):
                ic = grp * 4 + j
                B.emit("tensor", lambda e, ic=ic, pb=pb, j=j: e.transpose(
                    PSb[:, pb + j * 128:pb + (j + 1) * 128],
                    h1b[:, ic * 128:(ic + 1) * 128], ident_b[:, :]),
                    tuple(gw) if j == 0 else (), (PE, 1) if j == 3 else None)
            pe_tok = (PE, PE.n)
            dst = h1T[:, :].rearrange("p (ic t) -> p ic t", ic=NIC)[
                :, grp * 4:(grp + 1) * 4, tt * 128:(tt + 1) * 128]
            B.emit("vector", lambda e, dst=dst, pb=pb: e.tensor_copy(
                dst, PSb[:, pb:pb + 512].rearrange("p (a b) -> p a b", a=4)),
                (pe_tok,), (VE, 1))
            ev2[grp] = (VE, VE.n)
        h1T_prev = dict(ev2)
        h1T_prev[0] = ev2[6]
        h1T_prev[1] = ev2[7]
        h1T_last = ev2[7]
        h1_pe_prev = (PE, PE.n)
        if debug and tt == 0:
            h1_pe_prev = dma("gpsimd", dbg["dbg_h1b"], h1b[:, :], (h1_tok, ev2[7]), st="g_dbg")
            dbg_written.add("dbg_h1b")
        fold_step(7)
        bw_step(8)
    fold_step(4)
    h1T_done = h1T_last
    if debug:
        for _q in range(8):
            dma("gpsimd", dbg["dbg_h1T"][:, _q * 4096:(_q + 1) * 4096],
                h1T[:, _q * 4096:(_q + 1) * 4096], (h1T_done,), st="g_dbg")
        dbg_written.add("dbg_h1T")

    if stage == "A1":
        return _finish()
    def proj_pass(kind, out_tiles, wsrc, wcols, cast, out_sb, out_off, scale=None,
                  extra_wait=(), hook=None):
        ntile = len(out_tiles)
        c0, c1 = wcols
        ncol = c1 - c0
        q = "gpsimd" if cast else "sync"
        blk_tok = {}
        blk_pe = {}

        NB = 16

        def load_blk(icb, lw):
            src = wsrc.rearrange("(ic p) o -> p ic o", p=128)[
                :, icb * 2:(icb + 1) * 2, c0:c1]
            dstv = ws_buf[icb % 2][:, 0:2 * ncol].rearrange("p (ic o) -> p ic o", ic=2)
            blk_tok[icb] = dma(q, dstv, src, lw,
                               st=f"ws{'g' if cast else 's'}{icb % 2}")

        load_blk(0, extra_wait)
        load_blk(1, extra_wait)
        for ic in range(NIC):
            icb, j = ic // 2, ic % 2
            if j == 0 and hook:
                hook()
            wt = blk_tok[icb]
            for ti in range(ntile):
                if kind == "fm":
                    st_ap = ws_buf[icb % 2][:, j * ncol + ti * 128:
                                            j * ncol + (ti + 1) * 128]
                    for half in range(2):
                        last = (ic == NIC - 1 and ti == ntile - 1 and half == 1)
                        B.emit("tensor", lambda e, st_ap=st_ap, ic=ic, ti=ti, half=half: e.matmul(
                            PS[:, (2 * ti + half) * 512:(2 * ti + half + 1) * 512],
                            st_ap,
                            h1T[:, ic * T + half * 512: ic * T + (half + 1) * 512],
                            start=(ic == 0), stop=(ic == NIC - 1)),
                            ((wt, h1T_done) + tuple(extra_wait))
                            if (ic == 0 and ti == 0 and half == 0) else (wt,),
                            (PE, 1) if (last or (j == 1 and ti == ntile - 1 and half == 1))
                            else None)
                else:
                    tile = out_tiles[ti]
                    last = (ic == NIC - 1 and ti == ntile - 1)
                    B.emit("tensor", lambda e, ic=ic, tile=tile, ti=ti, icb=icb, j=j: e.matmul(
                        PS[:, ti * 512:(ti + 1) * 512],
                        h1T[:, ic * T + tile * 128:ic * T + (tile + 1) * 128],
                        ws_buf[icb % 2][:, j * ncol:(j + 1) * ncol],
                        start=(ic == 0), stop=(ic == NIC - 1)),
                        ((wt, h1T_done) + tuple(extra_wait))
                        if (ic == 0 and ti == 0) else (wt,),
                        (PE, 1) if (last or (j == 1 and ti == ntile - 1)) else None)
            if j == 1:
                blk_pe[icb] = (PE, PE.n)
                if icb + 2 < NB:
                    load_blk(icb + 2, (blk_pe[icb],))
        mm_tok = (PE, PE.n)
        ev = None
        for ti in range(ntile):
            tile = out_tiles[ti]
            if kind == "fm":
                psl = PS[:, ti * 1024:(ti + 1) * 1024]
                dst = out_sb[:, out_off + tile * T: out_off + (tile + 1) * T]
            else:
                psl = PS[:, ti * 512:(ti + 1) * 512]
                dst = out_sb[:, tile * KV + out_off: tile * KV + out_off + ncol]
            if scale is None:
                B.emit("vector", lambda e, dst=dst, psl=psl: e.tensor_copy(dst, psl),
                       (mm_tok,), (VE, 1))
            else:
                B.emit("vector", lambda e, dst=dst, psl=psl: e.tensor_scalar(
                    dst, psl, scale, None, op0=ALU.mult), (mm_tok,), (VE, 1))
            ev = (VE, VE.n)
        return ev

    def a2_hook():
        bw_step(2)

    kev = proj_pass("fm", [0, 1, 2], wk, (0, 384), True, kT_own, 0, hook=a2_hook)
    kev = proj_pass("fm", [3, 4, 5], wk, (384, 768), True, kT_own, 0,
                    extra_wait=(kev,), hook=a2_hook)
    kev = proj_pass("fm", [6, 7], wk, (768, 1024), True, kT_own, 0,
                    extra_wait=(kev,), hook=a2_hook)
    vev = kev
    for tg in range(2):
        for vf in range(2):
            vev = proj_pass("tm", [tg * 4 + i for i in range(4)], wv,
                            (vf * 512, (vf + 1) * 512), True, v_own, vf * 512,
                            extra_wait=(vev,), hook=a2_hook)
    assert fold_state["done"] is not None, "fold must complete in P0+A1"
    if debug:
        for _q in range(4):
            dma("gpsimd", dbg["dbg_wqe"][_q * 1024:(_q + 1) * 1024, :],
                wqe_d[_q * 1024:(_q + 1) * 1024, :], (fold_state["done"],),
                st="g_dbg")
        dbg_written.add("dbg_wqe")
    qev = proj_pass("fm", [0, 1, 2], wqe_d, (0, 384), False, qT, 0, scale=QSCALE,
                    extra_wait=(vev, fold_state["done"]), hook=a2_hook)
    qev = proj_pass("fm", [3, 4, 5], wqe_d, (384, 768), False, qT, 0, scale=QSCALE,
                    extra_wait=(qev,), hook=a2_hook)
    qev = proj_pass("fm", [6, 7], wqe_d, (768, 1024), False, qT, 0, scale=QSCALE,
                    extra_wait=(qev,), hook=a2_hook)
    q_done = qev

    dbg_toks = []
    if debug:
        dbg_toks.append(dma("gpsimd", dbg["dbg_kT"], kT_own[:, :], (kev,), st="g_dbg"))
        dbg_toks.append(dma("gpsimd", dbg["dbg_v"], v_own[:, :], (vev,), st="g_dbg"))
        dbg_toks.append(dma("gpsimd", dbg["dbg_qT"], qT[:, :], (q_done,), st="g_dbg"))
        dbg_written.update(["dbg_kT", "dbg_v", "dbg_qT"])

    st_toks = []
    for kc in range(8):
        st_toks.append(dma("sync", kv_in[kc * 128:(kc + 1) * 128, :],
                           kT_own[:, kc * T:(kc + 1) * T], (kev,), st="s_kvst"))
    for tt in range(8):
        st_toks.append(dma("sync", kv_in[T + tt * 128: T + (tt + 1) * 128, :],
                           v_own[:, tt * KV:(tt + 1) * KV], (vev,), st="s_kvst"))
    if stage == "A2":
        return _finish()
    B.emit("gpsimd", lambda e: e.collective_compute(
        "AllGather", ALU.bypass,
        replica_groups=[[0, 1, 2, 3], [4, 5, 6, 7]],
        ins=[kv_in[:, :]], outs=[kv_out[:, :]]),
        tuple(st_toks) + tuple(dbg_toks), (CC, 1))
    cc_done = (CC, CC.n)

    bw_step(200)
    bwT_done = bw_state["done"]

    if stage == "AG":
        return _finish()
    v_toks = []
    for r in range(4):
        src = kv_out[2048 * r + 1024: 2048 * r + 2048, :].rearrange(
            "(kc p) f -> p kc f", p=128)
        dstv = v_sb[:, :].rearrange("p (kc f) -> p kc f", kc=NKC)[
            :, r * 8:(r + 1) * 8, :]
        v_toks.append(dma("gpsimd", dstv, src, (cc_done, q_done), st="g_vsb"))
    v_ready = v_toks[-1]

    kh_tok = None
    last_smm = None
    prev_av_ev = None
    prev_D_pe = None
    prev_exp = None
    prev_pt_v = None
    prev_pt_s = None
    for h in range(NH):
        kw = [cc_done] if kh_tok is None else [last_smm]
        kts = []
        for r in range(4):
            kts.append(dma("gpsimd", kh_sb[:, r * 1024:(r + 1) * 1024],
                           kv_out[2048 * r + h * 128: 2048 * r + (h + 1) * 128, :],
                           tuple(kw) if r == 0 else (), st="g_kh"))
        kh_tok = kts[-1]
        for qh in range(2):
            pt_v_last = None
            pt_s_last = None
            for qtl in range(4):
                qt = qh * 4 + qtl
                for m in range(8):
                    ws = [kh_tok, q_done]
                    if m == 0:
                        if prev_exp is not None:
                            ws.append(prev_exp)
                        if prev_pt_v is not None:
                            ws += [prev_pt_v, prev_pt_s]
                        if prev_av_ev is not None:
                            ws.append(prev_av_ev)
                    B.emit("tensor", lambda e, h=h, qt=qt, m=m: e.matmul(
                        PS[:, m * 512:(m + 1) * 512],
                        qT[:, h * T + qt * 128:h * T + (qt + 1) * 128],
                        kh_sb[:, m * 512:(m + 1) * 512],
                        start=True, stop=True),
                        tuple(ws), (PE, 1) if m == 7 else None)
                last_smm = (PE, PE.n)
                sub = PS[:, :].rearrange("p (a b) -> p b a", b=8)[:, 0, :]
                B.emit("vector", lambda e, sub=sub: e.reduce_max(
                    st2[:, 0:1], sub, axis=AXX, negate=True), (last_smm,), (VE, 1))
                negm_tok = (VE, VE.n)
                B.emit("scalar", lambda e: e.activation(
                    P_sb[:, :], PS[:, :], AF.Exp,
                    bias=st2[:, 0:1], scale=1.0,
                    accum_out=st2[:, 1:2]), (negm_tok,), (SE, 1))
                exp_tok = (SE, SE.n)
                prev_exp = exp_tok
                B.emit("vector", lambda e: e.reciprocal(
                    st2[:, 5:6], st2[:, 1:2]), (exp_tok,), (VE, 1))
                B.emit("vector", None, ((VE, VE.n),))
                dw = [prev_D_pe] if prev_D_pe is not None else []
                B.emit("vector", lambda e: e.tensor_scalar(
                    D_sb[:, :], ident_b[:, :], st2[:, 5:6], None, op0=ALU.mult),
                    tuple(dw), (VE, 1))
                D_tok = (VE, VE.n)
                evs_v = {}
                evs_s = {}
                for grp in range(8):
                    pb = 512 + (grp % 2) * 512
                    gw = [D_tok] if grp == 0 else []
                    if grp >= 2:
                        gw.append(evs_v[grp - 2] if (grp - 2) % 2 == 0
                                  else evs_s[grp - 2])
                    for j in range(4):
                        kc = grp * 4 + j
                        B.emit("tensor", lambda e, kc=kc, pb=pb, j=j: e.matmul(
                            PS[:, pb + j * 128:pb + (j + 1) * 128],
                            P_sb[:, kc * 128:(kc + 1) * 128], D_sb[:, :],
                            start=True, stop=True),
                            tuple(gw) if j == 0 else (),
                            (PE, 1) if j == 3 else None)
                    pe_tok = (PE, PE.n)
                    dst = PT_half[:, :].rearrange("p (kc t) -> p kc t", kc=NKC)[
                        :, grp * 4:(grp + 1) * 4, qtl * 128:(qtl + 1) * 128]
                    psrc = PS[:, pb:pb + 512].rearrange("p (a b) -> p a b", a=4)
                    if grp % 2 == 0:
                        B.emit("vector", lambda e, dst=dst, psrc=psrc: e.tensor_copy(
                            dst, psrc), (pe_tok,), (VE, 1))
                        evs_v[grp] = (VE, VE.n)
                    else:
                        B.emit("scalar", lambda e, dst=dst, psrc=psrc: e.copy(
                            dst, psrc), (pe_tok,), (SE, 1))
                        evs_s[grp] = (SE, SE.n)
                pt_v_last = evs_v[6]
                pt_s_last = evs_s[7]
                prev_pt_v, prev_pt_s = pt_v_last, pt_s_last
                prev_D_pe = (PE, PE.n)
            for kc in range(NKC):
                ws = []
                if kc == 0:
                    ws = [pt_v_last, pt_s_last, v_ready]
                    if prev_av_ev is not None:
                        ws.append(prev_av_ev)
                B.emit("tensor", lambda e, kc=kc, h=h: e.matmul(
                    PS[:, 0:512],
                    v_sb[:, kc * KV + h * 128: kc * KV + (h + 1) * 128],
                    PT_half[:, kc * 512:(kc + 1) * 512],
                    start=(kc == 0), stop=(kc == NKC - 1)),
                    tuple(ws), (PE, 1) if kc == NKC - 1 else None)
            av_tok = (PE, PE.n)
            B.emit("vector", lambda e, h=h, qh=qh: e.tensor_copy(
                vals_sb[:, h * T + qh * 512: h * T + (qh + 1) * 512], PS[:, 0:512]),
                (av_tok,), (VE, 1))
            prev_av_ev = (VE, VE.n)
    attn_done = prev_av_ev

    if debug:
        dma("gpsimd", dbg["dbg_vals"], vals_sb[:, :], (attn_done,), st="g_dbg")
        dbg_written.add("dbg_vals")

    if stage == "A4":
        return _finish()
    wo_toks = []
    for kvc in range(8):
        wo_toks.append(dma("gpsimd", wo_sb[:, kvc * D:(kvc + 1) * D],
                           wo[kvc * 128:(kvc + 1) * 128, :],
                           (attn_done,) if kvc == 0 else (wo_toks[-1],),
                           st="g_wo"))
    atst_free = [None, None]
    x5_free = None
    prev_ev_tok = None
    a5_h2_st = []
    for tt in range(NTT):
        bi = tt % 2
        for kvc in range(8):
            ws = []
            if kvc == 0:
                ws = [wo_toks[0], attn_done]
                if atst_free[bi] is not None:
                    ws.append(atst_free[bi])
                if prev_ev_tok is not None:
                    ws.append(prev_ev_tok)
            else:
                ws = [wo_toks[kvc]]
            for ob in range(8):
                B.emit("tensor", lambda e, kvc=kvc, ob=ob, tt=tt: e.matmul(
                    PS[:, ob * 512:(ob + 1) * 512],
                    vals_sb[:, kvc * T + tt * 128: kvc * T + (tt + 1) * 128],
                    wo_sb[:, kvc * D + ob * 512: kvc * D + (ob + 1) * 512],
                    start=(kvc == 0), stop=(kvc == 7)),
                    tuple(ws) if ob == 0 else (),
                    (PE, 1) if (kvc == 7 and ob == 7) else None)
        wo_tok = (PE, PE.n)
        B.emit("vector", lambda e, bi=bi: e.tensor_copy(
            atst[bi][:, 0:2048], PS[:, 0:2048]), (wo_tok,), (VE, 1))
        B.emit("vector", lambda e, bi=bi: e.tensor_copy(
            atst[bi][:, 2048:4096], PS[:, 2048:4096]), (), (VE, 1))
        ev_tok = (VE, VE.n)
        prev_ev_tok = ev_tok
        B.emit("vector", None, (ev_tok,))
        B.emit("scalar", None, (ev_tok,))
        rstd_tok, nb_tok = ln_stats(atst[bi][:, :])
        lnw = [nb_tok]
        if tt > 0:
            lnw.append(prev_x1_tok)
        B.emit("scalar", lambda e, bi=bi: e.activation(
            ln_sb[:, :], atst[bi][:, :], AF.Identity, bias=st1[:, 8:9], scale=st1[:, 7:8]),
            tuple(lnw), (SE, 1))
        ln_tok = (SE, SE.n)
        atst_free[bi] = ln_tok
        xw = [attn_done] if tt == 0 else [x5_free, prev_t_h2]
        t_x = dma("sync", x5_buf[:, :], x_in[tt * 128:(tt + 1) * 128, :], tuple(xw), st="s_x5")
        B.emit("vector", lambda e: e.tensor_add(x5_buf[:, :], x5_buf[:, :], ln_sb[:, :]),
               (t_x, ln_tok), (VE, 1))
        x1_tok = (VE, VE.n)
        prev_x1_tok = x1_tok
        dma("sync", x1sp[tt * 128:(tt + 1) * 128, :], x5_buf[:, :], (x1_tok,), st="s_sp")
        B.emit("vector", None, (x1_tok,))
        B.emit("scalar", None, (x1_tok,))
        rstd2, nb2 = ln_stats(x5_buf[:, :])
        B.emit("vector", lambda e: e.tensor_scalar(
            h2b[:, :], x5_buf[:, :], st1[:, 2:3], st1[:, 7:8],
            op0=ALU.add, op1=ALU.mult), (rstd2, nb2), (VE, 1))
        h2_tok = (VE, VE.n)
        x5_free = h2_tok
        a5_h2_st.append(dma("sync", h2sp[tt * 128:(tt + 1) * 128, :], h2b[:, :],
                            (h2_tok,), st="s_sp"))
        prev_t_h2 = a5_h2_st[-1]
        if debug:
            dma("gpsimd", dbg["dbg_h2"][tt * 128:(tt + 1) * 128, :], h2b[:, :], (h2_tok,), st="g_dbg")
            dbg_written.add("dbg_h2")
    a5_done = a5_h2_st[-1]

    if stage == "A5":
        return _finish()
    h2rd_free = None
    stage_evs = []
    ep_done = []
    slot_free = [None, None]
    x1_free = None
    for hf in range(2):
        tr_last = None
        for ttl in range(4):
            tt = hf * 4 + ttl
            w = [a5_done]
            if h2rd_free is not None:
                w.append(h2rd_free)
            t_h2 = dma("sync", h2b[:, :], h2sp[tt * 128:(tt + 1) * 128, :], tuple(w), st="s_h2rd")
            evs = {}
            for grp in range(8):
                bank = (grp % 2) * 512
                gw = [t_h2] if grp == 0 else []
                if hf == 1 and ttl == 0 and grp == 0:
                    gw.append(ep_done[-1])
                if grp >= 2:
                    gw.append(evs[grp - 2])
                elif ttl > 0:
                    gw.append(prev_evs[grp + 6])
                pb = (grp % 2) * 1024
                for j in range(4):
                    ic = grp * 4 + j
                    B.emit("tensor", lambda e, ic=ic, pb=pb, j=j: e.transpose(
                        PSb[:, pb + j * 128:pb + (j + 1) * 128],
                        h2b[:, ic * 128:(ic + 1) * 128], ident_b[:, :]),
                        tuple(gw) if j == 0 else (), (PE, 1) if j == 3 else None)
                pe_tok = (PE, PE.n)
                dst = h2T[:, :].rearrange("p (ic t) -> p ic t", ic=NIC)[
                    :, grp * 4:(grp + 1) * 4, ttl * 128:(ttl + 1) * 128]
                B.emit("vector", lambda e, dst=dst, pb=pb: e.tensor_copy(
                    dst, PSb[:, pb:pb + 512].rearrange("p (a b) -> p a b", a=4)),
                    (pe_tok,), (VE, 1))
                evs[grp] = (VE, VE.n)
            prev_evs = evs
            h2rd_free = evs[7]
            tr_last = evs[7]
        h2T_tok = tr_last

        for ttl in range(4):
            for ic in range(NIC):
                B.emit("tensor", lambda e, ic=ic, ttl=ttl: e.matmul(
                    PS[:, ttl * 512: ttl * 512 + 80],
                    h2T[:, ic * 512 + ttl * 128: ic * 512 + (ttl + 1) * 128],
                    arT[:, ic * 80:(ic + 1) * 80],
                    start=(ic == 0), stop=(ic == NIC - 1)),
                    (h2T_tok, ar_done) if (ttl == 0 and ic == 0) else (),
                    (PE, 1) if ic == NIC - 1 else None)
        ar_mm = (PE, PE.n)
        for ttl in range(4):
            B.emit("vector", lambda e, ttl=ttl: e.tensor_copy(
                res16[:, ttl * 64:(ttl + 1) * 64], PS[:, ttl * 512: ttl * 512 + 64]),
                (ar_mm,), (VE, 1))
            B.emit("vector", lambda e, ttl=ttl: e.tensor_copy(
                rtr[:, ttl * 16:(ttl + 1) * 16], PS[:, ttl * 512 + 64: ttl * 512 + 80]),
                (), (VE, 1))
        ar_ev = (VE, VE.n)
        if debug:
            dma("gpsimd", dbg["dbg_res"][hf * 512:(hf + 1) * 512, 0:64].rearrange(
                "(ttl p) r -> p ttl r", p=128),
                res16[:, :].rearrange("p (ttl r) -> p ttl r", ttl=4), (ar_ev,),
                st="g_dbg")
            dma("gpsimd", dbg["dbg_res"][hf * 512:(hf + 1) * 512, 64:80].rearrange(
                "(ttl p) r -> p ttl r", p=128),
                rtr[:, 0:64].rearrange("p (ttl r) -> p ttl r", ttl=4), (ar_ev,),
                st="g_dbg")

        ht_last = None
        for ttl in range(4):
            lg = rtr[:, ttl * 16:(ttl + 1) * 16]
            B.emit("vector", lambda e, lg=lg: e.reduce_max(
                rtr[:, 64:65], lg, axis=AXX, negate=True), (ar_ev,), (VE, 1))
            B.emit("scalar", lambda e, lg=lg: e.activation(
                rtr[:, 80:96], lg, AF.Exp, bias=rtr[:, 64:65], scale=1.0,
                accum_out=rtr[:, 65:66]), ((VE, VE.n),), (SE, 1))
            B.emit("vector", lambda e: e.reciprocal(rtr[:, 66:67], rtr[:, 65:66]),
                   ((SE, SE.n),), (VE, 1))
            B.emit("vector", None, ((VE, VE.n),))
            B.emit("vector", lambda e: e.tensor_scalar(
                rtr[:, 96:112], rtr[:, 80:96], rtr[:, 66:67], None, op0=ALU.mult),
                (), (VE, 1))
            B.emit("vector", None, ((VE, VE.n),))
            B.emit("vector", lambda e: e.reduce_max(
                rtr[:, 67:68], rtr[:, 96:112], axis=AXX), (), (VE, 1))
            B.emit("vector", None, ((VE, VE.n),))
            B.emit("vector", lambda e: e.tensor_scalar(
                rtr[:, 112:128], rtr[:, 96:112], rtr[:, 67:68], None, op0=ALU.is_equal),
                (), (VE, 1))
            B.emit("vector", None, ((VE, VE.n),))
            B.emit("vector", lambda e: e.tensor_sub(
                rtr[:, 128:144], rtr[:, 96:112], rtr[:, 112:128]), (), (VE, 1))
            B.emit("vector", None, ((VE, VE.n),))
            B.emit("vector", lambda e: e.reduce_max(
                rtr[:, 68:69], rtr[:, 128:144], axis=AXX), (), (VE, 1))
            B.emit("vector", None, ((VE, VE.n),))
            B.emit("vector", lambda e: e.tensor_scalar(
                rtr[:, 144:160], rtr[:, 128:144], rtr[:, 68:69], None, op0=ALU.is_equal),
                (), (VE, 1))
            B.emit("vector", None, ((VE, VE.n),))
            B.emit("vector", lambda e: e.tensor_scalar(
                rtr[:, 69:70], rtr[:, 67:68], rtr[:, 68:69], 1e-6,
                op0=ALU.add, op1=ALU.add), (), (VE, 1))
            B.emit("vector", None, ((VE, VE.n),))
            B.emit("vector", lambda e: e.reciprocal(rtr[:, 70:71], rtr[:, 69:70]),
                   (), (VE, 1))
            B.emit("vector", None, ((VE, VE.n),))
            B.emit("vector", lambda e: e.tensor_scalar(
                rtr[:, 71:72], rtr[:, 67:68], rtr[:, 70:71], 0.5,
                op0=ALU.mult, op1=ALU.mult), (), (VE, 1))
            B.emit("vector", None, ((VE, VE.n),))
            B.emit("vector", lambda e: e.tensor_scalar(
                rtr[:, 72:73], rtr[:, 68:69], rtr[:, 70:71], 0.5,
                op0=ALU.mult, op1=ALU.mult), (), (VE, 1))
            B.emit("vector", None, ((VE, VE.n),))
            B.emit("vector", lambda e: e.tensor_scalar(
                rtr[:, 112:128], rtr[:, 112:128], rtr[:, 71:72], None, op0=ALU.mult),
                (), (VE, 1))
            B.emit("vector", None, ((VE, VE.n),))
            B.emit("vector", lambda e: e.tensor_scalar(
                rtr[:, 144:160], rtr[:, 144:160], rtr[:, 72:73], None, op0=ALU.mult),
                (), (VE, 1))
            B.emit("vector", None, ((VE, VE.n),))
            B.emit("vector", lambda e: e.tensor_add(
                rtr[:, 112:128], rtr[:, 112:128], rtr[:, 144:160]), (), (VE, 1))
            B.emit("vector", None, ((VE, VE.n),))
            for e_i in range(16):
                B.emit("vector", lambda e, e_i=e_i, ttl=ttl: e.tensor_scalar(
                    H_sb[:, e_i * 64:(e_i + 1) * 64],
                    res16[:, ttl * 64:(ttl + 1) * 64],
                    rtr[:, 112 + e_i:113 + e_i], None, op0=ALU.mult),
                    (), (VE, 1) if e_i == 15 else None)
            h_tok = (VE, VE.n)
            if debug:
                dma("gpsimd", dbg["dbg_H"][(hf * 4 + ttl) * 128:(hf * 4 + ttl + 1) * 128, :],
                    H_sb[:, :], (h_tok,), st="g_dbg")
            for grp in range(2):
                pb = grp * 1024
                gw = [h_tok, ar_ev] if grp == 0 else []
                for j in range(4):
                    erc = grp * 4 + j
                    B.emit("tensor", lambda e, erc=erc, pb=pb, j=j: e.transpose(
                        PSb[:, pb + j * 128:pb + (j + 1) * 128],
                        H_sb[:, erc * 128:(erc + 1) * 128], ident_b[:, :]),
                        tuple(gw) if j == 0 else (), (PE, 1) if j == 3 else None)
                pe_tok = (PE, PE.n)
                dst = HT_sb[:, :].rearrange("p (erc t) -> p erc t", erc=8)[
                    :, grp * 4:(grp + 1) * 4, ttl * 128:(ttl + 1) * 128]
                B.emit("vector", lambda e, dst=dst, pb=pb: e.tensor_copy(
                    dst, PSb[:, pb:pb + 512].rearrange("p (a b) -> p a b", a=4)),
                    (pe_tok,), (VE, 1))
                ht_last = (VE, VE.n)
        ht_done = ht_last

        for oe in range(16):
            gi = hf * 16 + oe
            si = gi % 2
            w = [bwT_done]
            if slot_free[si] is not None:
                w.append(slot_free[si])
            if gi == 0:
                w.append(a5_done)
            t_w1 = dma("sync", bwT_s[si][:, :].rearrange("p (ic o) -> p ic o", ic=NIC),
                       bwT_d[oe].rearrange("(ic p) o -> p ic o", p=128), tuple(w),
                       st=f"s_sl{si}")
            t_w2 = dma("sync", bfT_s[si][:, :].rearrange("p (erc o) -> p erc o", erc=8),
                       bfT_d[oe].rearrange("(erc p) o -> p erc o", p=128),
                       (bft_done,) + ((slot_free[si],) if slot_free[si] else ()),
                       st=f"s_sl{si}")
            goff = (gi % 2) * 2048
            for ic in range(NIC):
                for ttl in range(4):
                    ws = []
                    if ic == 0 and ttl == 0:
                        ws = [t_w1, t_w2, ht_done]
                        if len(stage_evs) >= 2:
                            ws.append(stage_evs[gi - 2])
                    B.emit("tensor", lambda e, ic=ic, ttl=ttl, si=si, goff=goff: e.matmul(
                        PS[:, goff + ttl * 512: goff + ttl * 512 + 256],
                        h2T[:, ic * 512 + ttl * 128: ic * 512 + (ttl + 1) * 128],
                        bwT_s[si][:, ic * 256:(ic + 1) * 256],
                        start=(ic == 0), stop=False),
                        tuple(ws), None)
            for erc in range(8):
                for ttl in range(4):
                    ws = [t_w2] if (erc == 0 and ttl == 0) else []
                    B.emit("tensor", lambda e, erc=erc, ttl=ttl, si=si, goff=goff: e.matmul(
                        PS[:, goff + ttl * 512: goff + ttl * 512 + 256],
                        HT_sb[:, erc * 512 + ttl * 128: erc * 512 + (ttl + 1) * 128],
                        bfT_s[si][:, erc * 256:(erc + 1) * 256],
                        start=False, stop=(erc == 7)),
                        tuple(ws), (PE, 1) if (erc == 7 and ttl == 3) else None)
            mm_tok = (PE, PE.n)
            slot_free[si] = mm_tok
            for ttl in range(4):
                B.emit("vector", lambda e, ttl=ttl, oe=oe, goff=goff: e.tensor_tensor(
                    stg_sb[:, ttl * D + oe * 256: ttl * D + (oe + 1) * 256],
                    PS[:, goff + ttl * 512: goff + ttl * 512 + 256],
                    bb16[:, oe * 256:(oe + 1) * 256], op=ALU.add),
                    (mm_tok, bb_ready), (VE, 1) if ttl == 3 else None)
            stage_evs.append((VE, VE.n))
        pass_done = stage_evs[-1]

        for ttl in range(4):
            tt = hf * 4 + ttl
            stg = stg_sb[:, ttl * D:(ttl + 1) * D]
            B.emit("vector", None, (pass_done,))
            B.emit("scalar", None, (pass_done,))
            rstd3, nb3 = ln_stats(stg)
            lnw = [nb3] + ([ep_done[-1]] if ep_done else [])
            B.emit("scalar", lambda e, stg=stg: e.activation(
                ln_sb[:, :], stg, AF.Identity, bias=st1[:, 8:9], scale=st1[:, 7:8]),
                tuple(lnw), (SE, 1))
            ln_tok = (SE, SE.n)
            xw = [a5_done]
            if x1_free:
                xw.append(x1_free)
            t_x1 = dma("sync", x1buf[:, :], x1sp[tt * 128:(tt + 1) * 128, :], tuple(xw), st="s_x1rd")
            B.emit("vector", lambda e: e.tensor_add(
                x1buf[:, :], x1buf[:, :], ln_sb[:, :]), (t_x1, ln_tok), (VE, 1))
            fin_tok = (VE, VE.n)
            x1_free = dma("sync", out_d[tt * 128:(tt + 1) * 128, :], x1buf[:, :],
                          (fin_tok,), st="s_out")
            ep_done.append(fin_tok)
        if debug:
            dma("gpsimd", dbg["dbg_stage"][hf * 512:(hf + 1) * 512, :].rearrange(
                "(ttl p) o -> p ttl o", p=128),
                stg_sb[:, :].rearrange("p (ttl o) -> p ttl o", ttl=4), (pass_done,),
                st="g_dbg")

    return _finish()


_NC_CACHE = {}


def _get_nc(debug=DEBUG):
    if debug not in _NC_CACHE:
        _NC_CACHE[debug] = build_program(debug)
    return _NC_CACHE[debug]


def kernel(L0_attn_x, L0_attn_wq, L0_attn_wk, L0_attn_wv, L0_attn_wo,
           L0_ffn_base_w, L0_ffn_base_b, L0_ffn_A, L0_ffn_B, L0_ffn_router_w):
    x = np.ascontiguousarray(np.asarray(L0_attn_x, dtype=np.float32))
    wq_ = np.ascontiguousarray(np.asarray(L0_attn_wq, dtype=np.float32))
    wk_ = np.ascontiguousarray(np.asarray(L0_attn_wk, dtype=np.float32))
    wv_ = np.ascontiguousarray(np.asarray(L0_attn_wv, dtype=np.float32))
    wo_ = np.ascontiguousarray(np.asarray(L0_attn_wo, dtype=np.float32))
    bw_ = np.ascontiguousarray(np.asarray(L0_ffn_base_w, dtype=np.float32))
    bb_ = np.ascontiguousarray(np.asarray(L0_ffn_base_b, dtype=np.float32))
    A_ = np.ascontiguousarray(np.asarray(L0_ffn_A, dtype=np.float32))
    B_ = np.ascontiguousarray(np.asarray(L0_ffn_B, dtype=np.float32))
    rw_ = np.ascontiguousarray(np.asarray(L0_ffn_router_w, dtype=np.float32))

    nc = _get_nc()
    xf = x.reshape(8192, 4096)
    import ml_dtypes
    idf = np.eye(128, dtype=np.float32)
    idb = np.eye(128).astype(ml_dtypes.bfloat16)
    in_maps = [{
        "x": xf[c * 1024:(c + 1) * 1024], "wq": wq_, "wk": wk_, "wv": wv_,
        "wo": wo_, "bw": bw_, "bb": bb_, "A": A_, "B": B_, "rw": rw_,
        "idf": idf, "idb": idb,
    } for c in range(8)]
    res = run_bass_kernel_spmd(nc, in_maps, core_ids=list(range(8)))
    shards = [res.results[c]["out"] for c in range(8)]
    return np.concatenate(shards, axis=0).reshape(2, 4096, 4096).astype(np.float32)


if __name__ == "__main__":
    build_program()
    print("build ok")


# version 06
# speedup vs baseline: 1.0751x; 1.0236x over previous
import numpy as np
from contextlib import ExitStack

import concourse.bass as bass
import concourse.mybir as mybir
from concourse.bass_utils import run_bass_kernel_spmd

F32 = mybir.dt.float32
BF16 = mybir.dt.bfloat16
U8 = mybir.dt.uint8
AF = mybir.ActivationFunctionType
ALU = mybir.AluOpType
AXX = mybir.AxisListType.X

D = 4096
T = 1024
NTT = 8
NIC = 32
NH = 8
KV = 1024
S = 4096
NKC = 32
EPS = 1e-5
QSCALE = float(1.0 / np.sqrt(128.0))

DEBUG = False


class Sem:
    __slots__ = ("h", "n")

    def __init__(self, h):
        self.h = h
        self.n = 0


class Builder:
    ENGINES = ("sync", "gpsimd", "tensor", "vector", "scalar")

    def __init__(self):
        self.ops = {e: [] for e in self.ENGINES}

    def emit(self, eng, fn, waits=(), inc=None):
        ws = []
        for s, v in waits:
            assert v <= s.n, f"wait on future sem value {v} > {s.n} ({eng})"
            if v > 0:
                ws.append((s.h, v))
        if inc is not None:
            s, k = inc
            s.n += k
            inc = (s.h, k)
        self.ops[eng].append((fn, ws, inc))

    def replay(self, block):
        def mk(eng_name):
            def run(e):
                seen = {}
                for fn, ws, inc in self.ops[eng_name]:
                    merged = {}
                    for h, v in ws:
                        if v > merged.get(h.name, (None, -1))[1]:
                            merged[h.name] = (h, v)
                    for h, v in merged.values():
                        if seen.get(h.name, -1) >= v:
                            continue
                        seen[h.name] = v
                        e.wait_ge(h, v)
                    inst = fn(e) if fn is not None else None
                    if inc is not None:
                        h, k = inc
                        if inst is None:
                            inst = e.nop()
                        inst.then_inc(h, k)
            return run

        for eng_name in self.ENGINES:
            if self.ops[eng_name]:
                getattr(block, eng_name)(mk(eng_name))


def build_program(debug=DEBUG, stage="full"):
    nc = bass.Bass("TRN2", target_bir_lowering=False)

    x_in = nc.dram_tensor("x", [T, D], F32, kind="ExternalInput").ap()
    wq = nc.dram_tensor("wq", [D, D], F32, kind="ExternalInput").ap()
    wk = nc.dram_tensor("wk", [D, KV], F32, kind="ExternalInput").ap()
    wv = nc.dram_tensor("wv", [D, KV], F32, kind="ExternalInput").ap()
    wo = nc.dram_tensor("wo", [KV, D], F32, kind="ExternalInput").ap()
    bw = nc.dram_tensor("bw", [D, D], F32, kind="ExternalInput").ap()
    bb = nc.dram_tensor("bb", [D], F32, kind="ExternalInput").ap()
    A_in = nc.dram_tensor("A", [64, D], F32, kind="ExternalInput").ap()
    B_in = nc.dram_tensor("B", [16, D, 64], F32, kind="ExternalInput").ap()
    rw = nc.dram_tensor("rw", [16, D], F32, kind="ExternalInput").ap()
    idf_in = nc.dram_tensor("idf", [128, 128], F32, kind="ExternalInput").ap()
    idb_in = nc.dram_tensor("idb", [128, 128], BF16, kind="ExternalInput").ap()
    out_d = nc.dram_tensor("out", [T, D], F32, kind="ExternalOutput").ap()

    wqe_d = nc.dram_tensor("wqe_d", [D, KV], BF16, kind="Internal").ap()
    bwT_d = nc.dram_tensor("bwT_d", [16, D, 256], BF16, kind="Internal").ap()
    bfT_d = nc.dram_tensor("bfT_d", [16, KV, 256], BF16, kind="Internal").ap()
    kv_in = nc.dram_tensor("kv_in", [2 * T, KV], BF16, kind="Internal").ap()
    kv_out = nc.dram_tensor("kv_out", [8 * T, KV], BF16, kind="Internal").ap()
    x1sp = nc.dram_tensor("x1sp", [T, D], F32, kind="Internal").ap()
    h2sp = nc.dram_tensor("h2sp", [T, D], BF16, kind="Internal").ap()

    dbg = {}
    dbg_written = set()
    if debug:
        for nm, shp, dt in [
            ("dbg_qT", [128, NH * T], F32), ("dbg_kT", [128, NH * T], F32),
            ("dbg_v", [128, NTT * KV], F32), ("dbg_vals", [128, NH * T], F32),
            ("dbg_h2", [T, D], F32), ("dbg_res", [T, 80], F32),
            ("dbg_H", [T, KV], F32), ("dbg_stage", [T, D], F32),
            ("dbg_h1b", [128, D], F32), ("dbg_h1T", [128, NIC * T], F32),
            ("dbg_wqe", [D, KV], F32),
        ]:
            dbg[nm] = nc.dram_tensor(nm, shp, dt, kind="ExternalOutput").ap()

    ctx = ExitStack()
    CAP = 212_832
    arena = ctx.enter_context(nc.sbuf_tensor("arena", [128, CAP], U8))
    base = nc.lookup_mloc(arena).addr

    def place(name, shape, dtype, off):
        per = int(np.prod(shape[1:])) * mybir.dt.size(dtype)
        assert off % 32 == 0 and off + per <= CAP, (name, off, per)
        h = nc.virtual_sbuf_tensor(name, shape, dtype)
        nc._tensor(h.name, h.shape, h.dtype, type="SB")
        m = nc.lookup_mloc(h)
        m.allocated = True
        m.addr = base + off
        return h.ap()

    def place2(name, shape, dtype, off):
        per = int(np.prod(shape[1:])) * mybir.dt.size(dtype)
        return [place(f"{name}{i}", shape, dtype, off + i * per) for i in range(2)]

    ident_f = place("ident_f", [128, 128], F32, 0)
    ident_b = place("ident_b", [128, 128], BF16, 512)
    st1 = place("st1", [128, 64], F32, 768)
    st2 = place("st2", [128, 64], F32, 1024)
    D_sb = place("D_sb", [128, 128], BF16, 1280)
    rtr = place("rtr", [128, 160], F32, 1536)
    res16 = place("res16", [128, 4 * 64], BF16, 2176)
    H_sb = place("H_sb", [128, KV], BF16, 2688)
    arT = place("arT", [128, NIC * 80], BF16, 4736)
    bb16 = place("bb16", [128, D], BF16, 9856)
    ln_sb = place("ln_sb", [128, D], F32, 18048)
    sq_sb = place("sq_sb", [128, D], BF16, 34432)
    vals_sb = place("vals_sb", [128, NH * T], BF16, 42624)
    qT = place("qT", [128, NH * T], BF16, 59008)
    P_sb = place("P_sb", [128, S], BF16, 75392)
    OB = 83584
    FR = OB + 20480

    wq_stage = place2("wq_stage", [128, 2 * KV], F32, 18048)
    wqe_sb = place("wqe_sb", [128, KV], BF16, OB)
    bw_in = place2("bw_in", [128, 512], F32, OB + 2048)
    bwT_stage = place("bwT_stage", [128, 8 * 256], BF16, OB + 6144)
    bft_sb = place("bft_sb", [128, 8 * D], BF16, FR)
    b_stage = place2("b_stage", [128, 2048], F32, FR + 65536)
    ar_stage = place("ar_stage", [64, D], BF16, FR + 81920)
    r_stage = place("r_stage", [16, D], BF16, FR + 90112)
    h1T = place("h1T", [128, NIC * T], BF16, FR)
    x_buf = place("x_buf", [128, D], F32, FR + 65536)
    h1b = place("h1b", [128, D], BF16, FR + 81920)
    kT_own = place("kT_own", [128, NH * T], BF16, FR + 65536)
    v_own = place("v_own", [128, NTT * KV], BF16, FR + 81920)
    ws_buf = place2("ws_buf", [128, 4 * KV], BF16, 18048)
    v_sb = place("v_sb", [128, NKC * KV], BF16, FR)
    PT_half = place("PT_half", [128, NKC * 512], BF16, FR + 65536)
    kh_sb = place("kh_sb", [128, S], BF16, FR + 98304)
    wo_sb = place("wo_sb", [128, 8 * D], BF16, FR)
    atst = place2("atst", [128, D], BF16, FR + 65536)
    x5_buf = place("x5_buf", [128, D], F32, FR + 81920)
    h2b = place("h2b", [128, D], BF16, FR + 98304)
    stg_sb = place("stg_sb", [128, 4 * D], BF16, FR)
    h2T = place("h2T", [128, NIC * 512], BF16, FR + 32768)
    bwT_s = place2("bwT_s", [128, NIC * 256], BF16, FR + 65536)
    bfT_s = place2("bfT_s", [128, 8 * 256], BF16, 75392)
    x1buf = place("x1buf", [128, D], F32, 59008)
    HT_sb = place("HT_sb", [128, 8 * 512], BF16, 42624)
    assert FR + 106496 <= OB + 129248

    PS = ctx.enter_context(nc.psum_tensor("PS", [128, 4096], F32))
    _psh = nc.virtual_psum_tensor("PSb", [128, 8192], BF16)
    nc._tensor(_psh.name, _psh.shape, _psh.dtype, type="PSUM")
    _psm = nc.lookup_mloc(_psh)
    _psm.allocated = True
    _psm.addr = nc.lookup_mloc(PS.tensor).addr if hasattr(PS, "tensor") else 0
    PSb = _psh.ap()

    sems = {nm: Sem(ctx.enter_context(nc.semaphore(nm)))
            for nm in ("LD", "GD", "PE", "VE", "SE", "GE", "CC")}
    LD, GD, PE, VE, SE, GE, CC = (sems[k] for k in
                                  ("LD", "GD", "PE", "VE", "SE", "GE", "CC"))

    B = Builder()

    def _finish():
        if stage != "full":
            B.emit("gpsimd", lambda e: e.memset(ln_sb[:, :], 0.0), (), (GE, 1))
            zt = (GE, GE.n)
            for _tt in range(NTT):
                dma("gpsimd", out_d[_tt * 128:(_tt + 1) * 128, :], ln_sb[:, :], (zt,), st="g_fill")
            if debug:
                for _nm, _ap in dbg.items():
                    if _nm in dbg_written:
                        continue
                    _views = ([_ap] if _ap.shape[0] == 128 else
                              [_ap[_r * 128:(_r + 1) * 128]
                               for _r in range(_ap.shape[0] // 128)])
                    for _v in _views:
                        nfree = int(np.prod(_v.shape[1:]))
                        _off = 0
                        while _off < nfree:
                            _w = min(4096, nfree - _off)
                            dma("gpsimd", _v[:, _off:_off + _w], ln_sb[:, 0:_w], (zt,), st="g_fill")
                            _off += _w
        allw = tuple((s, s.n) for s in dstreams.values()) + ((CC, CC.n),)
        B.emit("gpsimd", None, allw)
        if any(fn is not None or ws for fn, ws, inc in B.ops["sync"]):
            B.emit("sync", None, allw)
        with nc.Block() as block:
            B.replay(block)
        ctx.close()
        return nc

    dstreams = {}

    def dsem(name):
        if name not in dstreams:
            dstreams[name] = Sem(ctx.enter_context(nc.semaphore(name)))
        return dstreams[name]

    def dma(q, out, in_, waits=(), st=None):
        s = dsem(st if st is not None else ("s_misc" if q == "sync" else "g_misc"))
        B.emit(q, lambda e, o=out, i=in_: e.dma_start(out=o, in_=i), waits, (s, 16))
        return (s, s.n)

    dma("sync", ident_f[:, :], idf_in[:, :])
    ident_ready = dma("sync", ident_b[:, :], idb_in[:, :])

    if stage == "Z0":
        return _finish()
    dma("gpsimd", bb16[:, :], bb[None, :].broadcast_to([128, D]))

    if stage == "P0a":
        return _finish()
    dma("gpsimd", ar_stage[0:64, :], A_in[:, :])
    t_r = dma("gpsimd", r_stage[0:16, :], rw[:, :])
    bb_ready = t_a = t_r
    ar_ev_tok = None
    for ic in range(NIC):
        if ar_ev_tok is not None:
            B.emit("tensor", None, (ar_ev_tok,))
        B.emit("tensor", lambda e, ic=ic: e.transpose(
            PSb[:, 0:64], ar_stage[0:64, ic * 128:(ic + 1) * 128], ident_b[0:64, 0:64]),
            (t_a, ident_ready) if ic == 0 else ())
        B.emit("tensor", lambda e, ic=ic: e.transpose(
            PSb[:, 64:80], r_stage[0:16, ic * 128:(ic + 1) * 128], ident_b[0:16, 0:16]),
            (t_r,) if ic == 0 else (), (PE, 1))
        B.emit("scalar", lambda e, ic=ic: e.copy(
            arT[:, ic * 80:(ic + 1) * 80], PSb[:, 0:80]), ((PE, PE.n),), (SE, 1))
        ar_ev_tok = (SE, SE.n)
    ar_done = ar_ev_tok

    if stage == "P0b":
        return _finish()
    b_pe = {}
    bft_last = None
    for e_i in range(16):
        erc, lane = e_i // 2, (e_i % 2) * 64
        srcv = B_in[e_i].rearrange("(ot p) r -> p ot r", p=128)
        lw = (b_pe[e_i - 2],) if e_i >= 2 else ()
        t_b = dma("sync", b_stage[e_i % 2][:, :].rearrange(
            "p (ot r) -> p ot r", ot=32), srcv, lw, st=f"s_bst{e_i % 2}")
        for grp in range(8):
            if bft_last is not None:
                B.emit("tensor", None, (bft_last,))
            for j in range(4):
                ot = grp * 4 + j
                B.emit("tensor", lambda e, e_i=e_i, ot=ot, j=j: e.transpose(
                    PS[0:64, 1024 + j * 128:1024 + (j + 1) * 128],
                    b_stage[e_i % 2][:, ot * 64:(ot + 1) * 64], ident_f[:, :]),
                    (t_b, ident_ready) if (grp == 0 and j == 0) else (),
                    (PE, 1) if j == 3 else None)
            b_pe[e_i] = (PE, PE.n)
            B.emit("vector", lambda e, erc=erc, lane=lane, grp=grp: e.tensor_copy(
                bft_sb[lane:lane + 64, erc * D + grp * 512:erc * D + (grp + 1) * 512],
                PS[0:64, 1024:1536]), ((PE, PE.n),), (VE, 1))
            bft_last = (VE, VE.n)
    bft_st = []
    for oe in range(16):
        src = bft_sb[:, :].rearrange("p (erc o) -> p erc o", erc=8)[
            :, :, oe * 256:(oe + 1) * 256]
        dst = bfT_d[oe].rearrange("(erc p) o -> p erc o", p=128)
        bft_st.append(dma("sync", dst, src, (bft_last,), st="s_bftst"))
    bft_done = bft_st[-1]

    if stage == "P0c":
        return _finish()
    fold_state = {"done": None}

    def fold_gen():
        mm_toks = []
        ev_tok = None
        st_tok = None
        u = 0
        for ic in range(NIC):
            for hh in range(2):
                bi = u % 2
                lw = (mm_toks[u - 2],) if u >= 2 else ()
                t_w = dma("sync", wq_stage[bi],
                          wq[ic * 128:(ic + 1) * 128, hh * 2048:(hh + 1) * 2048], lw,
                          st=f"s_wq{bi}")
                for hl in range(4):
                    h = 4 * hh + hl
                    for g in range(4):
                        ws = [t_w]
                        if hl == 0 and g == 0:
                            ws.append(ident_ready)
                            if ev_tok is not None and hh == 0:
                                ws.append(ev_tok)
                        B.emit("tensor", lambda e, bi=bi, hl=hl, g=g, h=h: e.matmul(
                            PS[:, 1536 + h * 128:1536 + (h + 1) * 128],
                            ident_f[:, :],
                            wq_stage[bi][:, hl * 512 + g * 128: hl * 512 + (g + 1) * 128],
                            start=(g == 0), stop=(g == 3)),
                            tuple(ws), (PE, 1) if (hl == 3 and g == 3) else None)
                mm_toks.append((PE, PE.n))
                u += 1
                yield
            ew = [(PE, PE.n)]
            if st_tok is not None:
                ew.append(st_tok)
            B.emit("vector", lambda e: e.tensor_copy(wqe_sb[:, :], PS[:, 1536:2560]),
                   tuple(ew), (VE, 1))
            ev_tok = (VE, VE.n)
            st_tok = dma("sync", wqe_d[ic * 128:(ic + 1) * 128, :], wqe_sb[:, :],
                         (ev_tok,), st="s_wqest")
        fold_state["done"] = st_tok

    fold_it = fold_gen()

    def fold_step(k=1):
        for _ in range(k):
            if fold_state["done"] is None:
                next(fold_it, None)

    fold_step(12)

    bw_state = {"done": None}

    def bw_gen():
        ld_free = {}
        bank_free = {}
        stg_tok = None
        st_toks = []
        unit = 0
        for sl in range(16):
            for ihq in range(4):
                evs = []
                for otl in range(2):
                    ot = sl * 2 + otl
                    for iq2 in range(2):
                        iq = ihq * 2 + iq2
                        bi = unit % 2
                        lw = []
                        if unit >= 2:
                            lw.append(ld_free[unit - 2])
                        if unit == 0:
                            lw.append(ident_ready)
                        t_bw = dma("sync", bw_in[bi],
                                   bw[ot * 128:(ot + 1) * 128,
                                      iq * 512:(iq + 1) * 512], tuple(lw),
                                   st=f"s_bw{bi}")
                        bank = 3072 + (unit % 2) * 512
                        gw = [t_bw]
                        if bank in bank_free:
                            gw.append(bank_free[bank])
                        if iq2 == 0 and otl == 0 and stg_tok is not None:
                            gw.append(stg_tok)
                        for j in range(4):
                            B.emit("tensor", lambda e, j=j, bank=bank, bi=bi: e.transpose(
                                PS[:, bank + j * 128:bank + (j + 1) * 128],
                                bw_in[bi][:, j * 128:(j + 1) * 128], ident_f[:, :]),
                                tuple(gw) if j == 0 else (),
                                (PE, 1) if j == 3 else None)
                        pe_tok = (PE, PE.n)
                        dst = bwT_stage[:, :].rearrange(
                            "p (icl o) -> p icl o", icl=8)[
                            :, iq2 * 4:(iq2 + 1) * 4, otl * 128:(otl + 1) * 128]
                        B.emit("vector", lambda e, dst=dst, bank=bank: e.tensor_copy(
                            dst, PS[:, bank:bank + 512].rearrange(
                                "p (a b) -> p a b", a=4)), (pe_tok,), (VE, 1))
                        bank_free[bank] = (VE, VE.n)
                        evs.append((VE, VE.n))
                        ld_free[unit] = (VE, VE.n)
                        unit += 1
                        yield
                dstv = bwT_d[sl][ihq * 1024:(ihq + 1) * 1024, :].rearrange(
                    "(icl p) o -> p icl o", p=128)
                stg_tok = dma("sync", dstv,
                              bwT_stage[:, :].rearrange("p (icl o) -> p icl o", icl=8),
                              tuple(evs), st="s_bwst")
                st_toks.append(stg_tok)
        bw_state["done"] = st_toks[-1]

    bw_it = bw_gen()

    def bw_step(k=1):
        for _ in range(k):
            if bw_state["done"] is None:
                next(bw_it, None)

    if stage == "P0":
        return _finish()
    def vdrain():
        B.emit("vector", None, ((VE, VE.n),))

    def ln_stats(src_ap):
        B.emit("vector", lambda e: e.reduce_sum(st1[:, 0:1], src_ap, axis=AXX), (), (VE, 1))
        vdrain()
        B.emit("scalar", lambda e: e.activation(
            sq_sb[:, :], src_ap, AF.Square, accum_out=st1[:, 1:2]), (), (SE, 1))
        sq_tok = (SE, SE.n)
        B.emit("vector", lambda e: e.tensor_scalar(
            st1[:, 2:3], st1[:, 0:1], -1.0 / D, None, op0=ALU.mult), (), (VE, 1))
        vdrain()
        B.emit("vector", lambda e: e.tensor_scalar(
            st1[:, 3:4], st1[:, 1:2], 1.0 / D, None, op0=ALU.mult), (sq_tok,), (VE, 1))
        vdrain()
        B.emit("vector", lambda e: e.tensor_mul(st1[:, 4:5], st1[:, 2:3], st1[:, 2:3]),
               (), (VE, 1))
        vdrain()
        B.emit("vector", lambda e: e.tensor_scalar(
            st1[:, 5:6], st1[:, 3:4], st1[:, 4:5], EPS, op0=ALU.subtract, op1=ALU.add),
            (), (VE, 1))
        vdrain()
        B.emit("vector", lambda e: e.reciprocal(st1[:, 6:7], st1[:, 5:6]), (), (VE, 1))
        B.emit("scalar", lambda e: e.activation(st1[:, 7:8], st1[:, 6:7], AF.Sqrt),
               ((VE, VE.n),), (SE, 1))
        rstd_tok = (SE, SE.n)
        B.emit("scalar", None, (rstd_tok,))
        B.emit("vector", lambda e: e.tensor_mul(st1[:, 8:9], st1[:, 2:3], st1[:, 7:8]),
               (rstd_tok,), (VE, 1))
        vdrain()
        return rstd_tok, (VE, VE.n)

    x_free = None
    h1T_last = None
    a1_bar = ([(VE, VE.n), (SE, SE.n), (GE, GE.n), (PE, PE.n)] +
              [(s, s.n) for s in dstreams.values()])
    for tt in range(NTT):
        w = ((x_free,) if x_free else tuple(a1_bar))
        t_x = dma("sync", x_buf[:, :], x_in[tt * 128:(tt + 1) * 128, :], w, st="s_x")
        B.emit("vector", None, (t_x,))
        B.emit("scalar", None, (t_x,))
        rstd_tok, nb_tok = ln_stats(x_buf[:, :])
        nw = [nb_tok]
        if tt > 0:
            nw.append(h1_pe_prev)
        B.emit("scalar", lambda e: e.activation(
            h1b[:, :], x_buf[:, :], AF.Identity, bias=st1[:, 8:9], scale=st1[:, 7:8]),
            tuple(nw), (SE, 1))
        h1_tok = (SE, SE.n)
        x_free = h1_tok
        ev2 = {}
        for grp in range(8):
            bank = (grp % 2) * 512
            gw = [h1_tok] if grp == 0 else []
            if tt == 0 and grp == 0:
                gw += list(a1_bar)
            if grp >= 2:
                gw.append(ev2[grp - 2])
            elif tt > 0:
                gw.append(h1T_prev[grp])
            pb = (grp % 2) * 1024
            for j in range(4):
                ic = grp * 4 + j
                B.emit("tensor", lambda e, ic=ic, pb=pb, j=j: e.transpose(
                    PSb[:, pb + j * 128:pb + (j + 1) * 128],
                    h1b[:, ic * 128:(ic + 1) * 128], ident_b[:, :]),
                    tuple(gw) if j == 0 else (), (PE, 1) if j == 3 else None)
            pe_tok = (PE, PE.n)
            dst = h1T[:, :].rearrange("p (ic t) -> p ic t", ic=NIC)[
                :, grp * 4:(grp + 1) * 4, tt * 128:(tt + 1) * 128]
            B.emit("vector", lambda e, dst=dst, pb=pb: e.tensor_copy(
                dst, PSb[:, pb:pb + 512].rearrange("p (a b) -> p a b", a=4)),
                (pe_tok,), (VE, 1))
            ev2[grp] = (VE, VE.n)
        h1T_prev = dict(ev2)
        h1T_prev[0] = ev2[6]
        h1T_prev[1] = ev2[7]
        h1T_last = ev2[7]
        h1_pe_prev = (PE, PE.n)
        if debug and tt == 0:
            h1_pe_prev = dma("gpsimd", dbg["dbg_h1b"], h1b[:, :], (h1_tok, ev2[7]), st="g_dbg")
            dbg_written.add("dbg_h1b")
        fold_step(7)
        bw_step(8)
    fold_step(4)
    h1T_done = h1T_last
    if debug:
        for _q in range(8):
            dma("gpsimd", dbg["dbg_h1T"][:, _q * 4096:(_q + 1) * 4096],
                h1T[:, _q * 4096:(_q + 1) * 4096], (h1T_done,), st="g_dbg")
        dbg_written.add("dbg_h1T")

    if stage == "A1":
        return _finish()
    def proj_pass(kind, out_tiles, wsrc, wcols, cast, out_sb, out_off, scale=None,
                  extra_wait=(), hook=None):
        ntile = len(out_tiles)
        c0, c1 = wcols
        ncol = c1 - c0
        q = "gpsimd" if cast else "sync"
        blk_tok = {}
        blk_pe = {}

        NB = 8
        BL = 4

        def load_blk(icb, lw):
            src = wsrc.rearrange("(ic p) o -> p ic o", p=128)[
                :, icb * BL:(icb + 1) * BL, c0:c1]
            dstv = ws_buf[icb % 2][:, 0:BL * ncol].rearrange(
                "p (ic o) -> p ic o", ic=BL)
            blk_tok[icb] = dma(q, dstv, src, lw,
                               st=f"ws{'g' if cast else 's'}{icb % 2}")

        load_blk(0, extra_wait)
        load_blk(1, extra_wait)
        for ic in range(NIC):
            icb, j = ic // BL, ic % BL
            if j == 0 and hook:
                hook()
            wt = blk_tok[icb]
            for ti in range(ntile):
                if kind == "fm":
                    st_ap = ws_buf[icb % 2][:, j * ncol + ti * 128:
                                            j * ncol + (ti + 1) * 128]
                    for half in range(2):
                        last = (ic == NIC - 1 and ti == ntile - 1 and half == 1)
                        B.emit("tensor", lambda e, st_ap=st_ap, ic=ic, ti=ti, half=half: e.matmul(
                            PS[:, (2 * ti + half) * 512:(2 * ti + half + 1) * 512],
                            st_ap,
                            h1T[:, ic * T + half * 512: ic * T + (half + 1) * 512],
                            start=(ic == 0), stop=(ic == NIC - 1)),
                            ((wt, h1T_done) + tuple(extra_wait))
                            if (ic == 0 and ti == 0 and half == 0) else (wt,),
                            (PE, 1) if (last or (j == BL - 1 and ti == ntile - 1 and half == 1))
                            else None)
                else:
                    tile = out_tiles[ti]
                    last = (ic == NIC - 1 and ti == ntile - 1)
                    B.emit("tensor", lambda e, ic=ic, tile=tile, ti=ti, icb=icb, j=j: e.matmul(
                        PS[:, ti * 512:(ti + 1) * 512],
                        h1T[:, ic * T + tile * 128:ic * T + (tile + 1) * 128],
                        ws_buf[icb % 2][:, j * ncol:(j + 1) * ncol],
                        start=(ic == 0), stop=(ic == NIC - 1)),
                        ((wt, h1T_done) + tuple(extra_wait))
                        if (ic == 0 and ti == 0) else (wt,),
                        (PE, 1) if (last or (j == BL - 1 and ti == ntile - 1)) else None)
            if j == BL - 1:
                blk_pe[icb] = (PE, PE.n)
                if icb + 2 < NB:
                    load_blk(icb + 2, (blk_pe[icb],))
        mm_tok = (PE, PE.n)
        ev = None
        for ti in range(ntile):
            tile = out_tiles[ti]
            if kind == "fm":
                psl = PS[:, ti * 1024:(ti + 1) * 1024]
                dst = out_sb[:, out_off + tile * T: out_off + (tile + 1) * T]
            else:
                psl = PS[:, ti * 512:(ti + 1) * 512]
                dst = out_sb[:, tile * KV + out_off: tile * KV + out_off + ncol]
            if scale is None:
                B.emit("vector", lambda e, dst=dst, psl=psl: e.tensor_copy(dst, psl),
                       (mm_tok,), (VE, 1))
            else:
                B.emit("vector", lambda e, dst=dst, psl=psl: e.tensor_scalar(
                    dst, psl, scale, None, op0=ALU.mult), (mm_tok,), (VE, 1))
            ev = (VE, VE.n)
        return ev

    def a2_hook():
        bw_step(2)

    kev = proj_pass("fm", [0, 1, 2], wk, (0, 384), True, kT_own, 0,
                    extra_wait=(fold_state["done"],), hook=a2_hook)
    kev = proj_pass("fm", [3, 4, 5], wk, (384, 768), True, kT_own, 0,
                    extra_wait=(kev,), hook=a2_hook)
    kev = proj_pass("fm", [6, 7], wk, (768, 1024), True, kT_own, 0,
                    extra_wait=(kev,), hook=a2_hook)
    vev = kev
    for tg in range(2):
        for vf in range(2):
            vev = proj_pass("tm", [tg * 4 + i for i in range(4)], wv,
                            (vf * 512, (vf + 1) * 512), True, v_own, vf * 512,
                            extra_wait=(vev,), hook=a2_hook)
    assert fold_state["done"] is not None, "fold must complete in P0+A1"
    if debug:
        for _q in range(4):
            dma("gpsimd", dbg["dbg_wqe"][_q * 1024:(_q + 1) * 1024, :],
                wqe_d[_q * 1024:(_q + 1) * 1024, :], (fold_state["done"],),
                st="g_dbg")
        dbg_written.add("dbg_wqe")
    st_toks = []
    for kc in range(8):
        st_toks.append(dma("sync", kv_in[kc * 128:(kc + 1) * 128, :],
                           kT_own[:, kc * T:(kc + 1) * T], (kev,), st="s_kvst"))
    for tt in range(8):
        st_toks.append(dma("sync", kv_in[T + tt * 128: T + (tt + 1) * 128, :],
                           v_own[:, tt * KV:(tt + 1) * KV], (vev,), st="s_kvst"))
    B.emit("gpsimd", lambda e: e.collective_compute(
        "AllGather", ALU.bypass,
        replica_groups=[[0, 1, 2, 3], [4, 5, 6, 7]],
        ins=[kv_in[:, :]], outs=[kv_out[:, :]]),
        tuple(st_toks), (CC, 1))
    cc_done = (CC, CC.n)
    qev = proj_pass("fm", [0, 1, 2], wqe_d, (0, 384), False, qT, 0, scale=QSCALE,
                    extra_wait=(vev, fold_state["done"]), hook=a2_hook)
    qev = proj_pass("fm", [3, 4, 5], wqe_d, (384, 768), False, qT, 0, scale=QSCALE,
                    extra_wait=(qev,), hook=a2_hook)
    qev = proj_pass("fm", [6, 7], wqe_d, (768, 1024), False, qT, 0, scale=QSCALE,
                    extra_wait=(qev,), hook=a2_hook)
    q_done = qev

    dbg_toks = []
    if debug:
        dbg_toks.append(dma("gpsimd", dbg["dbg_kT"], kT_own[:, :], (kev,), st="g_dbg"))
        dbg_toks.append(dma("gpsimd", dbg["dbg_v"], v_own[:, :], (vev,), st="g_dbg"))
        dbg_toks.append(dma("gpsimd", dbg["dbg_qT"], qT[:, :], (q_done,), st="g_dbg"))
        dbg_written.update(["dbg_kT", "dbg_v", "dbg_qT"])

    if stage == "A2":
        return _finish()

    bw_step(200)
    bwT_done = bw_state["done"]

    if stage == "AG":
        return _finish()
    v_toks = []
    for r in range(4):
        src = kv_out[2048 * r + 1024: 2048 * r + 2048, :].rearrange(
            "(kc p) f -> p kc f", p=128)
        dstv = v_sb[:, :].rearrange("p (kc f) -> p kc f", kc=NKC)[
            :, r * 8:(r + 1) * 8, :]
        v_toks.append(dma("gpsimd", dstv, src,
                          (cc_done, q_done) + tuple(dbg_toks), st="g_vsb"))
    v_ready = v_toks[-1]

    kh_tok = None
    last_smm = None
    prev_av_ev = None
    prev_D_pe = None
    prev_exp = None
    prev_pt_v = None
    prev_pt_s = None
    for h in range(NH):
        kw = [cc_done] if kh_tok is None else [last_smm]
        kts = []
        for r in range(4):
            kts.append(dma("gpsimd", kh_sb[:, r * 1024:(r + 1) * 1024],
                           kv_out[2048 * r + h * 128: 2048 * r + (h + 1) * 128, :],
                           tuple(kw) if r == 0 else (), st="g_kh"))
        kh_tok = kts[-1]
        for qh in range(2):
            pt_v_last = None
            pt_s_last = None
            for qtl in range(4):
                qt = qh * 4 + qtl
                for m in range(8):
                    ws = [kh_tok, q_done]
                    if m == 0:
                        if prev_exp is not None:
                            ws.append(prev_exp)
                        if prev_pt_v is not None:
                            ws += [prev_pt_v, prev_pt_s]
                        if prev_av_ev is not None:
                            ws.append(prev_av_ev)
                    B.emit("tensor", lambda e, h=h, qt=qt, m=m: e.matmul(
                        PS[:, m * 512:(m + 1) * 512],
                        qT[:, h * T + qt * 128:h * T + (qt + 1) * 128],
                        kh_sb[:, m * 512:(m + 1) * 512],
                        start=True, stop=True),
                        tuple(ws), (PE, 1) if m == 7 else None)
                last_smm = (PE, PE.n)
                sub = PS[:, :].rearrange("p (a b) -> p b a", b=8)[:, 0, :]
                B.emit("vector", lambda e, sub=sub: e.reduce_max(
                    st2[:, 0:1], sub, axis=AXX, negate=True), (last_smm,), (VE, 1))
                negm_tok = (VE, VE.n)
                B.emit("scalar", lambda e: e.activation(
                    P_sb[:, :], PS[:, :], AF.Exp,
                    bias=st2[:, 0:1], scale=1.0,
                    accum_out=st2[:, 1:2]), (negm_tok,), (SE, 1))
                exp_tok = (SE, SE.n)
                prev_exp = exp_tok
                B.emit("vector", lambda e: e.reciprocal(
                    st2[:, 5:6], st2[:, 1:2]), (exp_tok,), (VE, 1))
                B.emit("vector", None, ((VE, VE.n),))
                dw = [prev_D_pe] if prev_D_pe is not None else []
                B.emit("vector", lambda e: e.tensor_scalar(
                    D_sb[:, :], ident_b[:, :], st2[:, 5:6], None, op0=ALU.mult),
                    tuple(dw), (VE, 1))
                D_tok = (VE, VE.n)
                evs_v = {}
                evs_s = {}
                for grp in range(8):
                    pb = 512 + (grp % 2) * 512
                    gw = [D_tok] if grp == 0 else []
                    if grp >= 2:
                        gw.append(evs_v[grp - 2] if (grp - 2) % 2 == 0
                                  else evs_s[grp - 2])
                    for j in range(4):
                        kc = grp * 4 + j
                        B.emit("tensor", lambda e, kc=kc, pb=pb, j=j: e.matmul(
                            PS[:, pb + j * 128:pb + (j + 1) * 128],
                            P_sb[:, kc * 128:(kc + 1) * 128], D_sb[:, :],
                            start=True, stop=True),
                            tuple(gw) if j == 0 else (),
                            (PE, 1) if j == 3 else None)
                    pe_tok = (PE, PE.n)
                    dst = PT_half[:, :].rearrange("p (kc t) -> p kc t", kc=NKC)[
                        :, grp * 4:(grp + 1) * 4, qtl * 128:(qtl + 1) * 128]
                    psrc = PS[:, pb:pb + 512].rearrange("p (a b) -> p a b", a=4)
                    if grp % 2 == 0:
                        B.emit("vector", lambda e, dst=dst, psrc=psrc: e.tensor_copy(
                            dst, psrc), (pe_tok,), (VE, 1))
                        evs_v[grp] = (VE, VE.n)
                    else:
                        B.emit("scalar", lambda e, dst=dst, psrc=psrc: e.copy(
                            dst, psrc), (pe_tok,), (SE, 1))
                        evs_s[grp] = (SE, SE.n)
                pt_v_last = evs_v[6]
                pt_s_last = evs_s[7]
                prev_pt_v, prev_pt_s = pt_v_last, pt_s_last
                prev_D_pe = (PE, PE.n)
            for kc in range(NKC):
                ws = []
                if kc == 0:
                    ws = [pt_v_last, pt_s_last, v_ready]
                    if prev_av_ev is not None:
                        ws.append(prev_av_ev)
                B.emit("tensor", lambda e, kc=kc, h=h: e.matmul(
                    PS[:, 0:512],
                    v_sb[:, kc * KV + h * 128: kc * KV + (h + 1) * 128],
                    PT_half[:, kc * 512:(kc + 1) * 512],
                    start=(kc == 0), stop=(kc == NKC - 1)),
                    tuple(ws), (PE, 1) if kc == NKC - 1 else None)
            av_tok = (PE, PE.n)
            B.emit("vector", lambda e, h=h, qh=qh: e.tensor_copy(
                vals_sb[:, h * T + qh * 512: h * T + (qh + 1) * 512], PS[:, 0:512]),
                (av_tok,), (VE, 1))
            prev_av_ev = (VE, VE.n)
    attn_done = prev_av_ev

    if debug:
        dma("gpsimd", dbg["dbg_vals"], vals_sb[:, :], (attn_done,), st="g_dbg")
        dbg_written.add("dbg_vals")

    if stage == "A4":
        return _finish()
    wo_toks = []
    for kvc in range(8):
        wo_toks.append(dma("gpsimd", wo_sb[:, kvc * D:(kvc + 1) * D],
                           wo[kvc * 128:(kvc + 1) * 128, :],
                           (attn_done,) if kvc == 0 else (wo_toks[-1],),
                           st="g_wo"))
    atst_free = [None, None]
    x5_free = None
    prev_ev_tok = None
    a5_h2_st = []
    for tt in range(NTT):
        bi = tt % 2
        for kvc in range(8):
            ws = []
            if kvc == 0:
                ws = [wo_toks[0], attn_done]
                if atst_free[bi] is not None:
                    ws.append(atst_free[bi])
                if prev_ev_tok is not None:
                    ws.append(prev_ev_tok)
            else:
                ws = [wo_toks[kvc]]
            for ob in range(8):
                B.emit("tensor", lambda e, kvc=kvc, ob=ob, tt=tt: e.matmul(
                    PS[:, ob * 512:(ob + 1) * 512],
                    vals_sb[:, kvc * T + tt * 128: kvc * T + (tt + 1) * 128],
                    wo_sb[:, kvc * D + ob * 512: kvc * D + (ob + 1) * 512],
                    start=(kvc == 0), stop=(kvc == 7)),
                    tuple(ws) if ob == 0 else (),
                    (PE, 1) if (kvc == 7 and ob == 7) else None)
        wo_tok = (PE, PE.n)
        B.emit("vector", lambda e, bi=bi: e.tensor_copy(
            atst[bi][:, 0:2048], PS[:, 0:2048]), (wo_tok,), (VE, 1))
        B.emit("vector", lambda e, bi=bi: e.tensor_copy(
            atst[bi][:, 2048:4096], PS[:, 2048:4096]), (), (VE, 1))
        ev_tok = (VE, VE.n)
        prev_ev_tok = ev_tok
        B.emit("vector", None, (ev_tok,))
        B.emit("scalar", None, (ev_tok,))
        rstd_tok, nb_tok = ln_stats(atst[bi][:, :])
        lnw = [nb_tok]
        if tt > 0:
            lnw.append(prev_x1_tok)
        B.emit("scalar", lambda e, bi=bi: e.activation(
            ln_sb[:, :], atst[bi][:, :], AF.Identity, bias=st1[:, 8:9], scale=st1[:, 7:8]),
            tuple(lnw), (SE, 1))
        ln_tok = (SE, SE.n)
        atst_free[bi] = ln_tok
        xw = [attn_done] if tt == 0 else [x5_free, prev_t_h2]
        t_x = dma("sync", x5_buf[:, :], x_in[tt * 128:(tt + 1) * 128, :], tuple(xw), st="s_x5")
        B.emit("vector", lambda e: e.tensor_add(x5_buf[:, :], x5_buf[:, :], ln_sb[:, :]),
               (t_x, ln_tok), (VE, 1))
        x1_tok = (VE, VE.n)
        prev_x1_tok = x1_tok
        dma("sync", x1sp[tt * 128:(tt + 1) * 128, :], x5_buf[:, :], (x1_tok,), st="s_sp")
        B.emit("vector", None, (x1_tok,))
        B.emit("scalar", None, (x1_tok,))
        rstd2, nb2 = ln_stats(x5_buf[:, :])
        B.emit("vector", lambda e: e.tensor_scalar(
            h2b[:, :], x5_buf[:, :], st1[:, 2:3], st1[:, 7:8],
            op0=ALU.add, op1=ALU.mult), (rstd2, nb2), (VE, 1))
        h2_tok = (VE, VE.n)
        x5_free = h2_tok
        a5_h2_st.append(dma("sync", h2sp[tt * 128:(tt + 1) * 128, :], h2b[:, :],
                            (h2_tok,), st="s_sp"))
        prev_t_h2 = a5_h2_st[-1]
        if debug:
            dma("gpsimd", dbg["dbg_h2"][tt * 128:(tt + 1) * 128, :], h2b[:, :], (h2_tok,), st="g_dbg")
            dbg_written.add("dbg_h2")
    a5_done = a5_h2_st[-1]

    if stage == "A5":
        return _finish()
    h2rd_free = None
    stage_evs = []
    ep_done = []
    slot_free = [None, None]
    x1_free = None
    for hf in range(2):
        tr_last = None
        for ttl in range(4):
            tt = hf * 4 + ttl
            w = [a5_done]
            if h2rd_free is not None:
                w.append(h2rd_free)
            t_h2 = dma("sync", h2b[:, :], h2sp[tt * 128:(tt + 1) * 128, :], tuple(w), st="s_h2rd")
            evs = {}
            for grp in range(8):
                bank = (grp % 2) * 512
                gw = [t_h2] if grp == 0 else []
                if hf == 1 and ttl == 0 and grp == 0:
                    gw.append(ep_done[-1])
                if grp >= 2:
                    gw.append(evs[grp - 2])
                elif ttl > 0:
                    gw.append(prev_evs[grp + 6])
                pb = (grp % 2) * 1024
                for j in range(4):
                    ic = grp * 4 + j
                    B.emit("tensor", lambda e, ic=ic, pb=pb, j=j: e.transpose(
                        PSb[:, pb + j * 128:pb + (j + 1) * 128],
                        h2b[:, ic * 128:(ic + 1) * 128], ident_b[:, :]),
                        tuple(gw) if j == 0 else (), (PE, 1) if j == 3 else None)
                pe_tok = (PE, PE.n)
                dst = h2T[:, :].rearrange("p (ic t) -> p ic t", ic=NIC)[
                    :, grp * 4:(grp + 1) * 4, ttl * 128:(ttl + 1) * 128]
                B.emit("vector", lambda e, dst=dst, pb=pb: e.tensor_copy(
                    dst, PSb[:, pb:pb + 512].rearrange("p (a b) -> p a b", a=4)),
                    (pe_tok,), (VE, 1))
                evs[grp] = (VE, VE.n)
            prev_evs = evs
            h2rd_free = evs[7]
            tr_last = evs[7]
        h2T_tok = tr_last

        for ttl in range(4):
            for ic in range(NIC):
                B.emit("tensor", lambda e, ic=ic, ttl=ttl: e.matmul(
                    PS[:, ttl * 512: ttl * 512 + 80],
                    h2T[:, ic * 512 + ttl * 128: ic * 512 + (ttl + 1) * 128],
                    arT[:, ic * 80:(ic + 1) * 80],
                    start=(ic == 0), stop=(ic == NIC - 1)),
                    (h2T_tok, ar_done) if (ttl == 0 and ic == 0) else (),
                    (PE, 1) if ic == NIC - 1 else None)
        ar_mm = (PE, PE.n)
        for ttl in range(4):
            B.emit("vector", lambda e, ttl=ttl: e.tensor_copy(
                res16[:, ttl * 64:(ttl + 1) * 64], PS[:, ttl * 512: ttl * 512 + 64]),
                (ar_mm,), (VE, 1))
            B.emit("vector", lambda e, ttl=ttl: e.tensor_copy(
                rtr[:, ttl * 16:(ttl + 1) * 16], PS[:, ttl * 512 + 64: ttl * 512 + 80]),
                (), (VE, 1))
        ar_ev = (VE, VE.n)
        if debug:
            dma("gpsimd", dbg["dbg_res"][hf * 512:(hf + 1) * 512, 0:64].rearrange(
                "(ttl p) r -> p ttl r", p=128),
                res16[:, :].rearrange("p (ttl r) -> p ttl r", ttl=4), (ar_ev,),
                st="g_dbg")
            dma("gpsimd", dbg["dbg_res"][hf * 512:(hf + 1) * 512, 64:80].rearrange(
                "(ttl p) r -> p ttl r", p=128),
                rtr[:, 0:64].rearrange("p (ttl r) -> p ttl r", ttl=4), (ar_ev,),
                st="g_dbg")

        ht_last = None
        for ttl in range(4):
            lg = rtr[:, ttl * 16:(ttl + 1) * 16]
            B.emit("vector", lambda e, lg=lg: e.reduce_max(
                rtr[:, 64:65], lg, axis=AXX, negate=True), (ar_ev,), (VE, 1))
            B.emit("scalar", lambda e, lg=lg: e.activation(
                rtr[:, 80:96], lg, AF.Exp, bias=rtr[:, 64:65], scale=1.0,
                accum_out=rtr[:, 65:66]), ((VE, VE.n),), (SE, 1))
            B.emit("vector", lambda e: e.reciprocal(rtr[:, 66:67], rtr[:, 65:66]),
                   ((SE, SE.n),), (VE, 1))
            B.emit("vector", None, ((VE, VE.n),))
            B.emit("vector", lambda e: e.tensor_scalar(
                rtr[:, 96:112], rtr[:, 80:96], rtr[:, 66:67], None, op0=ALU.mult),
                (), (VE, 1))
            B.emit("vector", None, ((VE, VE.n),))
            B.emit("vector", lambda e: e.reduce_max(
                rtr[:, 67:68], rtr[:, 96:112], axis=AXX), (), (VE, 1))
            B.emit("vector", None, ((VE, VE.n),))
            B.emit("vector", lambda e: e.tensor_scalar(
                rtr[:, 112:128], rtr[:, 96:112], rtr[:, 67:68], None, op0=ALU.is_equal),
                (), (VE, 1))
            B.emit("vector", None, ((VE, VE.n),))
            B.emit("vector", lambda e: e.tensor_sub(
                rtr[:, 128:144], rtr[:, 96:112], rtr[:, 112:128]), (), (VE, 1))
            B.emit("vector", None, ((VE, VE.n),))
            B.emit("vector", lambda e: e.reduce_max(
                rtr[:, 68:69], rtr[:, 128:144], axis=AXX), (), (VE, 1))
            B.emit("vector", None, ((VE, VE.n),))
            B.emit("vector", lambda e: e.tensor_scalar(
                rtr[:, 144:160], rtr[:, 128:144], rtr[:, 68:69], None, op0=ALU.is_equal),
                (), (VE, 1))
            B.emit("vector", None, ((VE, VE.n),))
            B.emit("vector", lambda e: e.tensor_scalar(
                rtr[:, 69:70], rtr[:, 67:68], rtr[:, 68:69], 1e-6,
                op0=ALU.add, op1=ALU.add), (), (VE, 1))
            B.emit("vector", None, ((VE, VE.n),))
            B.emit("vector", lambda e: e.reciprocal(rtr[:, 70:71], rtr[:, 69:70]),
                   (), (VE, 1))
            B.emit("vector", None, ((VE, VE.n),))
            B.emit("vector", lambda e: e.tensor_scalar(
                rtr[:, 71:72], rtr[:, 67:68], rtr[:, 70:71], 0.5,
                op0=ALU.mult, op1=ALU.mult), (), (VE, 1))
            B.emit("vector", None, ((VE, VE.n),))
            B.emit("vector", lambda e: e.tensor_scalar(
                rtr[:, 72:73], rtr[:, 68:69], rtr[:, 70:71], 0.5,
                op0=ALU.mult, op1=ALU.mult), (), (VE, 1))
            B.emit("vector", None, ((VE, VE.n),))
            B.emit("vector", lambda e: e.tensor_scalar(
                rtr[:, 112:128], rtr[:, 112:128], rtr[:, 71:72], None, op0=ALU.mult),
                (), (VE, 1))
            B.emit("vector", None, ((VE, VE.n),))
            B.emit("vector", lambda e: e.tensor_scalar(
                rtr[:, 144:160], rtr[:, 144:160], rtr[:, 72:73], None, op0=ALU.mult),
                (), (VE, 1))
            B.emit("vector", None, ((VE, VE.n),))
            B.emit("vector", lambda e: e.tensor_add(
                rtr[:, 112:128], rtr[:, 112:128], rtr[:, 144:160]), (), (VE, 1))
            B.emit("vector", None, ((VE, VE.n),))
            for e_i in range(16):
                B.emit("vector", lambda e, e_i=e_i, ttl=ttl: e.tensor_scalar(
                    H_sb[:, e_i * 64:(e_i + 1) * 64],
                    res16[:, ttl * 64:(ttl + 1) * 64],
                    rtr[:, 112 + e_i:113 + e_i], None, op0=ALU.mult),
                    (), (VE, 1) if e_i == 15 else None)
            h_tok = (VE, VE.n)
            if debug:
                dma("gpsimd", dbg["dbg_H"][(hf * 4 + ttl) * 128:(hf * 4 + ttl + 1) * 128, :],
                    H_sb[:, :], (h_tok,), st="g_dbg")
            for grp in range(2):
                pb = grp * 1024
                gw = [h_tok, ar_ev] if grp == 0 else []
                for j in range(4):
                    erc = grp * 4 + j
                    B.emit("tensor", lambda e, erc=erc, pb=pb, j=j: e.transpose(
                        PSb[:, pb + j * 128:pb + (j + 1) * 128],
                        H_sb[:, erc * 128:(erc + 1) * 128], ident_b[:, :]),
                        tuple(gw) if j == 0 else (), (PE, 1) if j == 3 else None)
                pe_tok = (PE, PE.n)
                dst = HT_sb[:, :].rearrange("p (erc t) -> p erc t", erc=8)[
                    :, grp * 4:(grp + 1) * 4, ttl * 128:(ttl + 1) * 128]
                B.emit("vector", lambda e, dst=dst, pb=pb: e.tensor_copy(
                    dst, PSb[:, pb:pb + 512].rearrange("p (a b) -> p a b", a=4)),
                    (pe_tok,), (VE, 1))
                ht_last = (VE, VE.n)
        ht_done = ht_last

        for oe in range(16):
            gi = hf * 16 + oe
            si = gi % 2
            w = [bwT_done]
            if slot_free[si] is not None:
                w.append(slot_free[si])
            if gi == 0:
                w.append(a5_done)
            t_w1 = dma("sync", bwT_s[si][:, :].rearrange("p (ic o) -> p ic o", ic=NIC),
                       bwT_d[oe].rearrange("(ic p) o -> p ic o", p=128), tuple(w),
                       st=f"s_sl{si}")
            t_w2 = dma("sync", bfT_s[si][:, :].rearrange("p (erc o) -> p erc o", erc=8),
                       bfT_d[oe].rearrange("(erc p) o -> p erc o", p=128),
                       (bft_done,) + ((slot_free[si],) if slot_free[si] else ()),
                       st=f"s_sl{si}")
            goff = (gi % 2) * 2048
            for ic in range(NIC):
                for ttl in range(4):
                    ws = []
                    if ic == 0 and ttl == 0:
                        ws = [t_w1, t_w2, ht_done]
                        if len(stage_evs) >= 2:
                            ws.append(stage_evs[gi - 2])
                    B.emit("tensor", lambda e, ic=ic, ttl=ttl, si=si, goff=goff: e.matmul(
                        PS[:, goff + ttl * 512: goff + ttl * 512 + 256],
                        h2T[:, ic * 512 + ttl * 128: ic * 512 + (ttl + 1) * 128],
                        bwT_s[si][:, ic * 256:(ic + 1) * 256],
                        start=(ic == 0), stop=False),
                        tuple(ws), None)
            for erc in range(8):
                for ttl in range(4):
                    ws = [t_w2] if (erc == 0 and ttl == 0) else []
                    B.emit("tensor", lambda e, erc=erc, ttl=ttl, si=si, goff=goff: e.matmul(
                        PS[:, goff + ttl * 512: goff + ttl * 512 + 256],
                        HT_sb[:, erc * 512 + ttl * 128: erc * 512 + (ttl + 1) * 128],
                        bfT_s[si][:, erc * 256:(erc + 1) * 256],
                        start=False, stop=(erc == 7)),
                        tuple(ws), (PE, 1) if (erc == 7 and ttl == 3) else None)
            mm_tok = (PE, PE.n)
            slot_free[si] = mm_tok
            for ttl in range(4):
                B.emit("vector", lambda e, ttl=ttl, oe=oe, goff=goff: e.tensor_tensor(
                    stg_sb[:, ttl * D + oe * 256: ttl * D + (oe + 1) * 256],
                    PS[:, goff + ttl * 512: goff + ttl * 512 + 256],
                    bb16[:, oe * 256:(oe + 1) * 256], op=ALU.add),
                    (mm_tok, bb_ready), (VE, 1) if ttl == 3 else None)
            stage_evs.append((VE, VE.n))
        pass_done = stage_evs[-1]

        for ttl in range(4):
            tt = hf * 4 + ttl
            stg = stg_sb[:, ttl * D:(ttl + 1) * D]
            B.emit("vector", None, (pass_done,))
            B.emit("scalar", None, (pass_done,))
            rstd3, nb3 = ln_stats(stg)
            lnw = [nb3] + ([ep_done[-1]] if ep_done else [])
            B.emit("scalar", lambda e, stg=stg: e.activation(
                ln_sb[:, :], stg, AF.Identity, bias=st1[:, 8:9], scale=st1[:, 7:8]),
                tuple(lnw), (SE, 1))
            ln_tok = (SE, SE.n)
            xw = [a5_done]
            if x1_free:
                xw.append(x1_free)
            t_x1 = dma("sync", x1buf[:, :], x1sp[tt * 128:(tt + 1) * 128, :], tuple(xw), st="s_x1rd")
            B.emit("vector", lambda e: e.tensor_add(
                x1buf[:, :], x1buf[:, :], ln_sb[:, :]), (t_x1, ln_tok), (VE, 1))
            fin_tok = (VE, VE.n)
            x1_free = dma("sync", out_d[tt * 128:(tt + 1) * 128, :], x1buf[:, :],
                          (fin_tok,), st="s_out")
            ep_done.append(fin_tok)
        if debug:
            dma("gpsimd", dbg["dbg_stage"][hf * 512:(hf + 1) * 512, :].rearrange(
                "(ttl p) o -> p ttl o", p=128),
                stg_sb[:, :].rearrange("p (ttl o) -> p ttl o", ttl=4), (pass_done,),
                st="g_dbg")

    return _finish()


_NC_CACHE = {}


def _get_nc(debug=DEBUG):
    if debug not in _NC_CACHE:
        _NC_CACHE[debug] = build_program(debug)
    return _NC_CACHE[debug]


def kernel(L0_attn_x, L0_attn_wq, L0_attn_wk, L0_attn_wv, L0_attn_wo,
           L0_ffn_base_w, L0_ffn_base_b, L0_ffn_A, L0_ffn_B, L0_ffn_router_w):
    x = np.ascontiguousarray(np.asarray(L0_attn_x, dtype=np.float32))
    wq_ = np.ascontiguousarray(np.asarray(L0_attn_wq, dtype=np.float32))
    wk_ = np.ascontiguousarray(np.asarray(L0_attn_wk, dtype=np.float32))
    wv_ = np.ascontiguousarray(np.asarray(L0_attn_wv, dtype=np.float32))
    wo_ = np.ascontiguousarray(np.asarray(L0_attn_wo, dtype=np.float32))
    bw_ = np.ascontiguousarray(np.asarray(L0_ffn_base_w, dtype=np.float32))
    bb_ = np.ascontiguousarray(np.asarray(L0_ffn_base_b, dtype=np.float32))
    A_ = np.ascontiguousarray(np.asarray(L0_ffn_A, dtype=np.float32))
    B_ = np.ascontiguousarray(np.asarray(L0_ffn_B, dtype=np.float32))
    rw_ = np.ascontiguousarray(np.asarray(L0_ffn_router_w, dtype=np.float32))

    nc = _get_nc()
    xf = x.reshape(8192, 4096)
    import ml_dtypes
    idf = np.eye(128, dtype=np.float32)
    idb = np.eye(128).astype(ml_dtypes.bfloat16)
    in_maps = [{
        "x": xf[c * 1024:(c + 1) * 1024], "wq": wq_, "wk": wk_, "wv": wv_,
        "wo": wo_, "bw": bw_, "bb": bb_, "A": A_, "B": B_, "rw": rw_,
        "idf": idf, "idb": idb,
    } for c in range(8)]
    res = run_bass_kernel_spmd(nc, in_maps, core_ids=list(range(8)))
    shards = [res.results[c]["out"] for c in range(8)]
    return np.concatenate(shards, axis=0).reshape(2, 4096, 4096).astype(np.float32)


if __name__ == "__main__":
    build_program()
    print("build ok")


# version 07
# speedup vs baseline: 1.0962x; 1.0196x over previous
import numpy as np
from contextlib import ExitStack

import concourse.bass as bass
import concourse.mybir as mybir
from concourse.bass_utils import run_bass_kernel_spmd

F32 = mybir.dt.float32
BF16 = mybir.dt.bfloat16
U8 = mybir.dt.uint8
AF = mybir.ActivationFunctionType
ALU = mybir.AluOpType
AXX = mybir.AxisListType.X

D = 4096
T = 1024
NTT = 8
NIC = 32
NH = 8
KV = 1024
S = 4096
NKC = 32
EPS = 1e-5
QSCALE = float(1.0 / np.sqrt(128.0))

DEBUG = False


class Sem:
    __slots__ = ("h", "n")

    def __init__(self, h):
        self.h = h
        self.n = 0


class Builder:
    ENGINES = ("sync", "gpsimd", "tensor", "vector", "scalar")

    def __init__(self):
        self.ops = {e: [] for e in self.ENGINES}

    def emit(self, eng, fn, waits=(), inc=None):
        ws = []
        for s, v in waits:
            assert v <= s.n, f"wait on future sem value {v} > {s.n} ({eng})"
            if v > 0:
                ws.append((s.h, v))
        if inc is not None:
            s, k = inc
            s.n += k
            inc = (s.h, k)
        self.ops[eng].append((fn, ws, inc))

    def replay(self, block):
        def mk(eng_name):
            def run(e):
                seen = {}
                for fn, ws, inc in self.ops[eng_name]:
                    merged = {}
                    for h, v in ws:
                        if v > merged.get(h.name, (None, -1))[1]:
                            merged[h.name] = (h, v)
                    for h, v in merged.values():
                        if seen.get(h.name, -1) >= v:
                            continue
                        seen[h.name] = v
                        e.wait_ge(h, v)
                    inst = fn(e) if fn is not None else None
                    if inc is not None:
                        h, k = inc
                        if inst is None:
                            inst = e.nop()
                        inst.then_inc(h, k)
            return run

        for eng_name in self.ENGINES:
            if self.ops[eng_name]:
                getattr(block, eng_name)(mk(eng_name))


def build_program(debug=DEBUG, stage="full"):
    nc = bass.Bass("TRN2", target_bir_lowering=False)

    x_in = nc.dram_tensor("x", [T, D], F32, kind="ExternalInput").ap()
    wq = nc.dram_tensor("wq", [D, D], F32, kind="ExternalInput").ap()
    wk = nc.dram_tensor("wk", [D, KV], F32, kind="ExternalInput").ap()
    wv = nc.dram_tensor("wv", [D, KV], F32, kind="ExternalInput").ap()
    wo = nc.dram_tensor("wo", [KV, D], F32, kind="ExternalInput").ap()
    bw = nc.dram_tensor("bw", [D, D], F32, kind="ExternalInput").ap()
    bb = nc.dram_tensor("bb", [D], F32, kind="ExternalInput").ap()
    A_in = nc.dram_tensor("A", [64, D], F32, kind="ExternalInput").ap()
    B_in = nc.dram_tensor("B", [16, D, 64], F32, kind="ExternalInput").ap()
    rw = nc.dram_tensor("rw", [16, D], F32, kind="ExternalInput").ap()
    idf_in = nc.dram_tensor("idf", [128, 128], F32, kind="ExternalInput").ap()
    idb_in = nc.dram_tensor("idb", [128, 128], BF16, kind="ExternalInput").ap()
    out_d = nc.dram_tensor("out", [T, D], F32, kind="ExternalOutput").ap()

    wqe_d = nc.dram_tensor("wqe_d", [D, KV], BF16, kind="Internal").ap()
    bwT_d = nc.dram_tensor("bwT_d", [16, D, 256], BF16, kind="Internal").ap()
    bfT_d = nc.dram_tensor("bfT_d", [16, KV, 256], BF16, kind="Internal").ap()
    kv_in = nc.dram_tensor("kv_in", [2 * T, KV], BF16, kind="Internal").ap()
    kv_out = nc.dram_tensor("kv_out", [8 * T, KV], BF16, kind="Internal").ap()
    x1sp = nc.dram_tensor("x1sp", [T, D], F32, kind="Internal").ap()
    h2sp = nc.dram_tensor("h2sp", [T, D], BF16, kind="Internal").ap()

    dbg = {}
    dbg_written = set()
    if debug:
        for nm, shp, dt in [
            ("dbg_qT", [128, NH * T], F32), ("dbg_kT", [128, NH * T], F32),
            ("dbg_v", [128, NTT * KV], F32), ("dbg_vals", [128, NH * T], F32),
            ("dbg_h2", [T, D], F32), ("dbg_res", [T, 80], F32),
            ("dbg_H", [T, KV], F32), ("dbg_stage", [T, D], F32),
            ("dbg_h1b", [128, D], F32), ("dbg_h1T", [128, NIC * T], F32),
            ("dbg_wqe", [D, KV], F32),
        ]:
            dbg[nm] = nc.dram_tensor(nm, shp, dt, kind="ExternalOutput").ap()

    ctx = ExitStack()
    CAP = 212_832
    arena = ctx.enter_context(nc.sbuf_tensor("arena", [128, CAP], U8))
    base = nc.lookup_mloc(arena).addr

    def place(name, shape, dtype, off):
        per = int(np.prod(shape[1:])) * mybir.dt.size(dtype)
        assert off % 32 == 0 and off + per <= CAP, (name, off, per)
        h = nc.virtual_sbuf_tensor(name, shape, dtype)
        nc._tensor(h.name, h.shape, h.dtype, type="SB")
        m = nc.lookup_mloc(h)
        m.allocated = True
        m.addr = base + off
        return h.ap()

    def place2(name, shape, dtype, off):
        per = int(np.prod(shape[1:])) * mybir.dt.size(dtype)
        return [place(f"{name}{i}", shape, dtype, off + i * per) for i in range(2)]

    ident_f = place("ident_f", [128, 128], F32, 0)
    ident_b = place("ident_b", [128, 128], BF16, 512)
    st1 = place("st1", [128, 64], F32, 768)
    st2 = place("st2", [128, 64], F32, 1024)
    D_sb = place("D_sb", [128, 256], BF16, 34432)
    rtr = place("rtr", [128, 160], F32, 1536)
    res16 = place("res16", [128, 4 * 64], BF16, 2176)
    H_sb = place("H_sb", [128, KV], BF16, 2688)
    arT = place("arT", [128, NIC * 80], BF16, 4736)
    bb16 = place("bb16", [128, D], BF16, 9856)
    ln_sb = place("ln_sb", [128, D], F32, 18048)
    sq_sb = place("sq_sb", [128, D], BF16, 34432)
    vals_sb = place("vals_sb", [128, NH * T], BF16, 42624)
    qT = place("qT", [128, NH * T], BF16, 59008)
    P_sb = place("P_sb", [128, S], BF16, 75392)
    OB = 83584
    FR = OB + 20480

    wq_stage = place2("wq_stage", [128, 2 * KV], F32, 18048)
    wqe_sb = place("wqe_sb", [128, KV], BF16, OB)
    bw_in = place2("bw_in", [128, 512], F32, OB + 2048)
    bwT_stage = place("bwT_stage", [128, 8 * 256], BF16, OB + 6144)
    bft_sb = place("bft_sb", [128, 8 * D], BF16, FR)
    b_stage = place2("b_stage", [128, 2048], F32, FR + 65536)
    ar_stage = place("ar_stage", [64, D], BF16, FR + 81920)
    r_stage = place("r_stage", [16, D], BF16, FR + 90112)
    h1T = place("h1T", [128, NIC * T], BF16, FR)
    x_buf = place("x_buf", [128, D], F32, FR + 65536)
    h1b = place("h1b", [128, D], BF16, FR + 81920)
    kT_own = place("kT_own", [128, NH * T], BF16, FR + 65536)
    v_own = place("v_own", [128, NTT * KV], BF16, FR + 81920)
    ws_buf = place2("ws_buf", [128, 4 * KV], BF16, 18048)
    v_sb = place("v_sb", [128, NKC * KV], BF16, FR)
    PT_half = place("PT_half", [128, NKC * 512], BF16, FR + 65536)
    kh_sb = place("kh_sb", [128, S], BF16, FR + 98304)
    wo_sb = place("wo_sb", [128, 8 * D], BF16, FR)
    atst = place2("atst", [128, D], BF16, FR + 65536)
    x5_buf = place("x5_buf", [128, D], F32, FR + 81920)
    h2b = place("h2b", [128, D], BF16, FR + 98304)
    stg_sb = place("stg_sb", [128, 4 * D], BF16, FR)
    h2T = place("h2T", [128, NIC * 512], BF16, FR + 32768)
    bwT_s = place2("bwT_s", [128, NIC * 256], BF16, FR + 65536)
    bfT_s = place2("bfT_s", [128, 8 * 256], BF16, 75392)
    x1buf = place("x1buf", [128, D], F32, 59008)
    HT_sb = place("HT_sb", [128, 8 * 512], BF16, 42624)
    assert FR + 106496 <= OB + 129248

    PS = ctx.enter_context(nc.psum_tensor("PS", [128, 4096], F32))
    _psh = nc.virtual_psum_tensor("PSb", [128, 8192], BF16)
    nc._tensor(_psh.name, _psh.shape, _psh.dtype, type="PSUM")
    _psm = nc.lookup_mloc(_psh)
    _psm.allocated = True
    _psm.addr = nc.lookup_mloc(PS.tensor).addr if hasattr(PS, "tensor") else 0
    PSb = _psh.ap()

    sems = {nm: Sem(ctx.enter_context(nc.semaphore(nm)))
            for nm in ("LD", "GD", "PE", "VE", "SE", "GE", "CC")}
    LD, GD, PE, VE, SE, GE, CC = (sems[k] for k in
                                  ("LD", "GD", "PE", "VE", "SE", "GE", "CC"))

    B = Builder()

    def _finish():
        if stage != "full":
            B.emit("gpsimd", lambda e: e.memset(ln_sb[:, :], 0.0), (), (GE, 1))
            zt = (GE, GE.n)
            for _tt in range(NTT):
                dma("gpsimd", out_d[_tt * 128:(_tt + 1) * 128, :], ln_sb[:, :], (zt,), st="g_fill")
            if debug:
                for _nm, _ap in dbg.items():
                    if _nm in dbg_written:
                        continue
                    _views = ([_ap] if _ap.shape[0] == 128 else
                              [_ap[_r * 128:(_r + 1) * 128]
                               for _r in range(_ap.shape[0] // 128)])
                    for _v in _views:
                        nfree = int(np.prod(_v.shape[1:]))
                        _off = 0
                        while _off < nfree:
                            _w = min(4096, nfree - _off)
                            dma("gpsimd", _v[:, _off:_off + _w], ln_sb[:, 0:_w], (zt,), st="g_fill")
                            _off += _w
        allw = tuple((s, s.n) for s in dstreams.values()) + ((CC, CC.n),)
        B.emit("gpsimd", None, allw)
        if any(fn is not None or ws for fn, ws, inc in B.ops["sync"]):
            B.emit("sync", None, allw)
        with nc.Block() as block:
            B.replay(block)
        ctx.close()
        return nc

    dstreams = {}

    def dsem(name):
        if name not in dstreams:
            dstreams[name] = Sem(ctx.enter_context(nc.semaphore(name)))
        return dstreams[name]

    def dma(q, out, in_, waits=(), st=None):
        s = dsem(st if st is not None else ("s_misc" if q == "sync" else "g_misc"))
        B.emit(q, lambda e, o=out, i=in_: e.dma_start(out=o, in_=i), waits, (s, 16))
        return (s, s.n)

    dma("sync", ident_f[:, :], idf_in[:, :])
    ident_ready = dma("sync", ident_b[:, :], idb_in[:, :])

    if stage == "Z0":
        return _finish()
    dma("gpsimd", bb16[:, :], bb[None, :].broadcast_to([128, D]))

    if stage == "P0a":
        return _finish()
    dma("gpsimd", ar_stage[0:64, :], A_in[:, :])
    t_r = dma("gpsimd", r_stage[0:16, :], rw[:, :])
    bb_ready = t_a = t_r
    ar_ev_tok = None
    for ic in range(NIC):
        if ar_ev_tok is not None:
            B.emit("tensor", None, (ar_ev_tok,))
        B.emit("tensor", lambda e, ic=ic: e.transpose(
            PSb[:, 0:64], ar_stage[0:64, ic * 128:(ic + 1) * 128], ident_b[0:64, 0:64]),
            (t_a, ident_ready) if ic == 0 else ())
        B.emit("tensor", lambda e, ic=ic: e.transpose(
            PSb[:, 64:80], r_stage[0:16, ic * 128:(ic + 1) * 128], ident_b[0:16, 0:16]),
            (t_r,) if ic == 0 else (), (PE, 1))
        B.emit("scalar", lambda e, ic=ic: e.copy(
            arT[:, ic * 80:(ic + 1) * 80], PSb[:, 0:80]), ((PE, PE.n),), (SE, 1))
        ar_ev_tok = (SE, SE.n)
    ar_done = ar_ev_tok

    if stage == "P0b":
        return _finish()
    b_pe = {}
    bft_last = None
    for e_i in range(16):
        erc, lane = e_i // 2, (e_i % 2) * 64
        srcv = B_in[e_i].rearrange("(ot p) r -> p ot r", p=128)
        lw = (b_pe[e_i - 2],) if e_i >= 2 else ()
        t_b = dma("sync", b_stage[e_i % 2][:, :].rearrange(
            "p (ot r) -> p ot r", ot=32), srcv, lw, st=f"s_bst{e_i % 2}")
        for grp in range(8):
            if bft_last is not None:
                B.emit("tensor", None, (bft_last,))
            for j in range(4):
                ot = grp * 4 + j
                B.emit("tensor", lambda e, e_i=e_i, ot=ot, j=j: e.transpose(
                    PS[0:64, 1024 + j * 128:1024 + (j + 1) * 128],
                    b_stage[e_i % 2][:, ot * 64:(ot + 1) * 64], ident_f[:, :]),
                    (t_b, ident_ready) if (grp == 0 and j == 0) else (),
                    (PE, 1) if j == 3 else None)
            b_pe[e_i] = (PE, PE.n)
            B.emit("vector", lambda e, erc=erc, lane=lane, grp=grp: e.tensor_copy(
                bft_sb[lane:lane + 64, erc * D + grp * 512:erc * D + (grp + 1) * 512],
                PS[0:64, 1024:1536]), ((PE, PE.n),), (VE, 1))
            bft_last = (VE, VE.n)
    bft_st = []
    for oe in range(16):
        src = bft_sb[:, :].rearrange("p (erc o) -> p erc o", erc=8)[
            :, :, oe * 256:(oe + 1) * 256]
        dst = bfT_d[oe].rearrange("(erc p) o -> p erc o", p=128)
        bft_st.append(dma("sync", dst, src, (bft_last,), st="s_bftst"))
    bft_done = bft_st[-1]

    if stage == "P0c":
        return _finish()
    fold_state = {"done": None}

    def fold_gen():
        mm_toks = []
        ev_tok = None
        st_tok = None
        u = 0
        for ic in range(NIC):
            for hh in range(2):
                bi = u % 2
                lw = (mm_toks[u - 2],) if u >= 2 else ()
                t_w = dma("sync", wq_stage[bi],
                          wq[ic * 128:(ic + 1) * 128, hh * 2048:(hh + 1) * 2048], lw,
                          st=f"s_wq{bi}")
                for hl in range(4):
                    h = 4 * hh + hl
                    for g in range(4):
                        ws = [t_w]
                        if hl == 0 and g == 0:
                            ws.append(ident_ready)
                            if ev_tok is not None and hh == 0:
                                ws.append(ev_tok)
                        B.emit("tensor", lambda e, bi=bi, hl=hl, g=g, h=h: e.matmul(
                            PS[:, 1536 + h * 128:1536 + (h + 1) * 128],
                            ident_f[:, :],
                            wq_stage[bi][:, hl * 512 + g * 128: hl * 512 + (g + 1) * 128],
                            start=(g == 0), stop=(g == 3)),
                            tuple(ws), (PE, 1) if (hl == 3 and g == 3) else None)
                mm_toks.append((PE, PE.n))
                u += 1
                yield
            ew = [(PE, PE.n)]
            if st_tok is not None:
                ew.append(st_tok)
            B.emit("vector", lambda e: e.tensor_copy(wqe_sb[:, :], PS[:, 1536:2560]),
                   tuple(ew), (VE, 1))
            ev_tok = (VE, VE.n)
            st_tok = dma("sync", wqe_d[ic * 128:(ic + 1) * 128, :], wqe_sb[:, :],
                         (ev_tok,), st="s_wqest")
        fold_state["done"] = st_tok

    fold_it = fold_gen()

    def fold_step(k=1):
        for _ in range(k):
            if fold_state["done"] is None:
                next(fold_it, None)

    fold_step(12)

    bw_state = {"done": None}

    def bw_gen():
        ld_free = {}
        bank_free = {}
        stg_tok = None
        st_toks = []
        unit = 0
        for sl in range(16):
            for ihq in range(4):
                evs = []
                for otl in range(2):
                    ot = sl * 2 + otl
                    for iq2 in range(2):
                        iq = ihq * 2 + iq2
                        bi = unit % 2
                        lw = []
                        if unit >= 2:
                            lw.append(ld_free[unit - 2])
                        if unit == 0:
                            lw.append(ident_ready)
                        t_bw = dma("sync", bw_in[bi],
                                   bw[ot * 128:(ot + 1) * 128,
                                      iq * 512:(iq + 1) * 512], tuple(lw),
                                   st=f"s_bw{bi}")
                        bank = 3072 + (unit % 2) * 512
                        gw = [t_bw]
                        if bank in bank_free:
                            gw.append(bank_free[bank])
                        if iq2 == 0 and otl == 0 and stg_tok is not None:
                            gw.append(stg_tok)
                        for j in range(4):
                            B.emit("tensor", lambda e, j=j, bank=bank, bi=bi: e.transpose(
                                PS[:, bank + j * 128:bank + (j + 1) * 128],
                                bw_in[bi][:, j * 128:(j + 1) * 128], ident_f[:, :]),
                                tuple(gw) if j == 0 else (),
                                (PE, 1) if j == 3 else None)
                        pe_tok = (PE, PE.n)
                        dst = bwT_stage[:, :].rearrange(
                            "p (icl o) -> p icl o", icl=8)[
                            :, iq2 * 4:(iq2 + 1) * 4, otl * 128:(otl + 1) * 128]
                        B.emit("vector", lambda e, dst=dst, bank=bank: e.tensor_copy(
                            dst, PS[:, bank:bank + 512].rearrange(
                                "p (a b) -> p a b", a=4)), (pe_tok,), (VE, 1))
                        bank_free[bank] = (VE, VE.n)
                        evs.append((VE, VE.n))
                        ld_free[unit] = (VE, VE.n)
                        unit += 1
                        yield
                dstv = bwT_d[sl][ihq * 1024:(ihq + 1) * 1024, :].rearrange(
                    "(icl p) o -> p icl o", p=128)
                stg_tok = dma("sync", dstv,
                              bwT_stage[:, :].rearrange("p (icl o) -> p icl o", icl=8),
                              tuple(evs), st="s_bwst")
                st_toks.append(stg_tok)
        bw_state["done"] = st_toks[-1]

    bw_it = bw_gen()

    def bw_step(k=1):
        for _ in range(k):
            if bw_state["done"] is None:
                next(bw_it, None)

    if stage == "P0":
        return _finish()
    def vdrain():
        B.emit("vector", None, ((VE, VE.n),))

    def ln_stats(src_ap):
        B.emit("vector", lambda e: e.reduce_sum(st1[:, 0:1], src_ap, axis=AXX), (), (VE, 1))
        vdrain()
        B.emit("scalar", lambda e: e.activation(
            sq_sb[:, :], src_ap, AF.Square, accum_out=st1[:, 1:2]), (), (SE, 1))
        sq_tok = (SE, SE.n)
        B.emit("vector", lambda e: e.tensor_scalar(
            st1[:, 2:3], st1[:, 0:1], -1.0 / D, None, op0=ALU.mult), (), (VE, 1))
        vdrain()
        B.emit("vector", lambda e: e.tensor_scalar(
            st1[:, 3:4], st1[:, 1:2], 1.0 / D, None, op0=ALU.mult), (sq_tok,), (VE, 1))
        vdrain()
        B.emit("vector", lambda e: e.tensor_mul(st1[:, 4:5], st1[:, 2:3], st1[:, 2:3]),
               (), (VE, 1))
        vdrain()
        B.emit("vector", lambda e: e.tensor_scalar(
            st1[:, 5:6], st1[:, 3:4], st1[:, 4:5], EPS, op0=ALU.subtract, op1=ALU.add),
            (), (VE, 1))
        vdrain()
        B.emit("vector", lambda e: e.reciprocal(st1[:, 6:7], st1[:, 5:6]), (), (VE, 1))
        B.emit("scalar", lambda e: e.activation(st1[:, 7:8], st1[:, 6:7], AF.Sqrt),
               ((VE, VE.n),), (SE, 1))
        rstd_tok = (SE, SE.n)
        B.emit("scalar", None, (rstd_tok,))
        B.emit("vector", lambda e: e.tensor_mul(st1[:, 8:9], st1[:, 2:3], st1[:, 7:8]),
               (rstd_tok,), (VE, 1))
        vdrain()
        return rstd_tok, (VE, VE.n)

    x_free = None
    h1T_last = None
    a1_bar = ([(VE, VE.n), (SE, SE.n), (GE, GE.n), (PE, PE.n)] +
              [(s, s.n) for s in dstreams.values()])
    for tt in range(NTT):
        w = ((x_free,) if x_free else tuple(a1_bar))
        t_x = dma("sync", x_buf[:, :], x_in[tt * 128:(tt + 1) * 128, :], w, st="s_x")
        B.emit("vector", None, (t_x,))
        B.emit("scalar", None, (t_x,))
        rstd_tok, nb_tok = ln_stats(x_buf[:, :])
        nw = [nb_tok]
        if tt > 0:
            nw.append(h1_pe_prev)
        B.emit("scalar", lambda e: e.activation(
            h1b[:, :], x_buf[:, :], AF.Identity, bias=st1[:, 8:9], scale=st1[:, 7:8]),
            tuple(nw), (SE, 1))
        h1_tok = (SE, SE.n)
        x_free = h1_tok
        ev2 = {}
        for grp in range(8):
            bank = (grp % 2) * 512
            gw = [h1_tok] if grp == 0 else []
            if tt == 0 and grp == 0:
                gw += list(a1_bar)
            if grp >= 2:
                gw.append(ev2[grp - 2])
            elif tt > 0:
                gw.append(h1T_prev[grp])
            pb = (grp % 2) * 1024
            for j in range(4):
                ic = grp * 4 + j
                B.emit("tensor", lambda e, ic=ic, pb=pb, j=j: e.transpose(
                    PSb[:, pb + j * 128:pb + (j + 1) * 128],
                    h1b[:, ic * 128:(ic + 1) * 128], ident_b[:, :]),
                    tuple(gw) if j == 0 else (), (PE, 1) if j == 3 else None)
            pe_tok = (PE, PE.n)
            dst = h1T[:, :].rearrange("p (ic t) -> p ic t", ic=NIC)[
                :, grp * 4:(grp + 1) * 4, tt * 128:(tt + 1) * 128]
            B.emit("vector", lambda e, dst=dst, pb=pb: e.tensor_copy(
                dst, PSb[:, pb:pb + 512].rearrange("p (a b) -> p a b", a=4)),
                (pe_tok,), (VE, 1))
            ev2[grp] = (VE, VE.n)
        h1T_prev = dict(ev2)
        h1T_prev[0] = ev2[6]
        h1T_prev[1] = ev2[7]
        h1T_last = ev2[7]
        h1_pe_prev = (PE, PE.n)
        if debug and tt == 0:
            h1_pe_prev = dma("gpsimd", dbg["dbg_h1b"], h1b[:, :], (h1_tok, ev2[7]), st="g_dbg")
            dbg_written.add("dbg_h1b")
        fold_step(7)
        bw_step(8)
    fold_step(4)
    h1T_done = h1T_last
    if debug:
        for _q in range(8):
            dma("gpsimd", dbg["dbg_h1T"][:, _q * 4096:(_q + 1) * 4096],
                h1T[:, _q * 4096:(_q + 1) * 4096], (h1T_done,), st="g_dbg")
        dbg_written.add("dbg_h1T")

    if stage == "A1":
        return _finish()
    def proj_pass(kind, out_tiles, wsrc, wcols, cast, out_sb, out_off, scale=None,
                  extra_wait=(), hook=None):
        ntile = len(out_tiles)
        c0, c1 = wcols
        ncol = c1 - c0
        q = "gpsimd" if cast else "sync"
        blk_tok = {}
        blk_pe = {}

        NB = 8
        BL = 4

        def load_blk(icb, lw):
            src = wsrc.rearrange("(ic p) o -> p ic o", p=128)[
                :, icb * BL:(icb + 1) * BL, c0:c1]
            dstv = ws_buf[icb % 2][:, 0:BL * ncol].rearrange(
                "p (ic o) -> p ic o", ic=BL)
            blk_tok[icb] = dma(q, dstv, src, lw,
                               st=f"ws{'g' if cast else 's'}{icb % 2}")

        load_blk(0, extra_wait)
        load_blk(1, extra_wait)
        for ic in range(NIC):
            icb, j = ic // BL, ic % BL
            if j == 0 and hook:
                hook()
            wt = blk_tok[icb]
            for ti in range(ntile):
                if kind == "fm":
                    st_ap = ws_buf[icb % 2][:, j * ncol + ti * 128:
                                            j * ncol + (ti + 1) * 128]
                    for half in range(2):
                        last = (ic == NIC - 1 and ti == ntile - 1 and half == 1)
                        B.emit("tensor", lambda e, st_ap=st_ap, ic=ic, ti=ti, half=half: e.matmul(
                            PS[:, (2 * ti + half) * 512:(2 * ti + half + 1) * 512],
                            st_ap,
                            h1T[:, ic * T + half * 512: ic * T + (half + 1) * 512],
                            start=(ic == 0), stop=(ic == NIC - 1)),
                            ((wt, h1T_done) + tuple(extra_wait))
                            if (ic == 0 and ti == 0 and half == 0) else (wt,),
                            (PE, 1) if (last or (j == BL - 1 and ti == ntile - 1 and half == 1))
                            else None)
                else:
                    tile = out_tiles[ti]
                    last = (ic == NIC - 1 and ti == ntile - 1)
                    B.emit("tensor", lambda e, ic=ic, tile=tile, ti=ti, icb=icb, j=j: e.matmul(
                        PS[:, ti * 512:(ti + 1) * 512],
                        h1T[:, ic * T + tile * 128:ic * T + (tile + 1) * 128],
                        ws_buf[icb % 2][:, j * ncol:(j + 1) * ncol],
                        start=(ic == 0), stop=(ic == NIC - 1)),
                        ((wt, h1T_done) + tuple(extra_wait))
                        if (ic == 0 and ti == 0) else (wt,),
                        (PE, 1) if (last or (j == BL - 1 and ti == ntile - 1)) else None)
            if j == BL - 1:
                blk_pe[icb] = (PE, PE.n)
                if icb + 2 < NB:
                    load_blk(icb + 2, (blk_pe[icb],))
        mm_tok = (PE, PE.n)
        ev = None
        for ti in range(ntile):
            tile = out_tiles[ti]
            if kind == "fm":
                psl = PS[:, ti * 1024:(ti + 1) * 1024]
                dst = out_sb[:, out_off + tile * T: out_off + (tile + 1) * T]
            else:
                psl = PS[:, ti * 512:(ti + 1) * 512]
                dst = out_sb[:, tile * KV + out_off: tile * KV + out_off + ncol]
            if scale is None:
                B.emit("vector", lambda e, dst=dst, psl=psl: e.tensor_copy(dst, psl),
                       (mm_tok,), (VE, 1))
            else:
                B.emit("vector", lambda e, dst=dst, psl=psl: e.tensor_scalar(
                    dst, psl, scale, None, op0=ALU.mult), (mm_tok,), (VE, 1))
            ev = (VE, VE.n)
        return ev

    def a2_hook():
        bw_step(2)

    kev = proj_pass("fm", [0, 1, 2], wk, (0, 384), True, kT_own, 0,
                    extra_wait=(fold_state["done"],), hook=a2_hook)
    kev = proj_pass("fm", [3, 4, 5], wk, (384, 768), True, kT_own, 0,
                    extra_wait=(kev,), hook=a2_hook)
    kev = proj_pass("fm", [6, 7], wk, (768, 1024), True, kT_own, 0,
                    extra_wait=(kev,), hook=a2_hook)
    vev = kev
    for tg in range(2):
        for vf in range(2):
            vev = proj_pass("tm", [tg * 4 + i for i in range(4)], wv,
                            (vf * 512, (vf + 1) * 512), True, v_own, vf * 512,
                            extra_wait=(vev,), hook=a2_hook)
    assert fold_state["done"] is not None, "fold must complete in P0+A1"
    if debug:
        for _q in range(4):
            dma("gpsimd", dbg["dbg_wqe"][_q * 1024:(_q + 1) * 1024, :],
                wqe_d[_q * 1024:(_q + 1) * 1024, :], (fold_state["done"],),
                st="g_dbg")
        dbg_written.add("dbg_wqe")
    st_toks = []
    for kc in range(8):
        st_toks.append(dma("sync", kv_in[kc * 128:(kc + 1) * 128, :],
                           kT_own[:, kc * T:(kc + 1) * T], (kev,), st="s_kvst"))
    for tt in range(8):
        st_toks.append(dma("sync", kv_in[T + tt * 128: T + (tt + 1) * 128, :],
                           v_own[:, tt * KV:(tt + 1) * KV], (vev,), st="s_kvst"))
    B.emit("gpsimd", lambda e: e.collective_compute(
        "AllGather", ALU.bypass,
        replica_groups=[[0, 1, 2, 3], [4, 5, 6, 7]],
        ins=[kv_in[:, :]], outs=[kv_out[:, :]]),
        tuple(st_toks), (CC, 1))
    cc_done = (CC, CC.n)
    qev = proj_pass("fm", [0, 1, 2], wqe_d, (0, 384), False, qT, 0, scale=QSCALE,
                    extra_wait=(vev, fold_state["done"]), hook=a2_hook)
    qev = proj_pass("fm", [3, 4, 5], wqe_d, (384, 768), False, qT, 0, scale=QSCALE,
                    extra_wait=(qev,), hook=a2_hook)
    qev = proj_pass("fm", [6, 7], wqe_d, (768, 1024), False, qT, 0, scale=QSCALE,
                    extra_wait=(qev,), hook=a2_hook)
    q_done = qev

    dbg_toks = []
    if debug:
        dbg_toks.append(dma("gpsimd", dbg["dbg_kT"], kT_own[:, :], (kev,), st="g_dbg"))
        dbg_toks.append(dma("gpsimd", dbg["dbg_v"], v_own[:, :], (vev,), st="g_dbg"))
        dbg_toks.append(dma("gpsimd", dbg["dbg_qT"], qT[:, :], (q_done,), st="g_dbg"))
        dbg_written.update(["dbg_kT", "dbg_v", "dbg_qT"])

    if stage == "A2":
        return _finish()

    bw_step(200)
    bwT_done = bw_state["done"]

    if stage == "AG":
        return _finish()
    v_toks = []
    for r in range(4):
        src = kv_out[2048 * r + 1024: 2048 * r + 2048, :].rearrange(
            "(kc p) f -> p kc f", p=128)
        dstv = v_sb[:, :].rearrange("p (kc f) -> p kc f", kc=NKC)[
            :, r * 8:(r + 1) * 8, :]
        v_toks.append(dma("gpsimd", dstv, src,
                          (cc_done, q_done) + tuple(dbg_toks), st="g_vsb"))
    v_ready = v_toks[-1]

    kh_tok = None
    last_smm = None
    prev_av_ev = None
    slot_free = {0: None, 1: None}
    slot_exp = {0: None, 1: None}
    prev_D_pe2 = None
    for h in range(NH):
        kw = [cc_done] if kh_tok is None else [last_smm]
        kts = []
        for r in range(4):
            kts.append(dma("gpsimd", kh_sb[:, r * 1024:(r + 1) * 1024],
                           kv_out[2048 * r + h * 128: 2048 * r + (h + 1) * 128, :],
                           tuple(kw) if r == 0 else (), st="g_kh"))
        kh_tok = kts[-1]
        for qh in range(2):
            pt_evs = []
            for qtl in range(4):
                qt = qh * 4 + qtl
                exp_toks = []
                negm_toks = []
                for c in range(2):
                    base = c * 2048
                    for m in range(4):
                        ws = [kh_tok, q_done]
                        if m == 0:
                            if slot_exp[c] is not None:
                                ws.append(slot_exp[c])
                            if slot_free[c] is not None:
                                ws += slot_free[c]
                            if c == 0 and prev_av_ev is not None:
                                ws.append(prev_av_ev)
                        B.emit("tensor", lambda e, h=h, qt=qt, base=base, m=m: e.matmul(
                            PS[:, base + m * 512:base + (m + 1) * 512],
                            qT[:, h * T + qt * 128:h * T + (qt + 1) * 128],
                            kh_sb[:, base + m * 512:base + (m + 1) * 512],
                            start=True, stop=True),
                            tuple(ws), (PE, 1) if m == 3 else None)
                    smm_c = (PE, PE.n)
                    sub = PS[:, base:base + 2048].rearrange(
                        "p (a b) -> p b a", b=8)[:, 0, :]
                    B.emit("vector", lambda e, sub=sub, c=c: e.reduce_max(
                        st2[:, c:c + 1], sub, axis=AXX, negate=True),
                        (smm_c,), (VE, 1))
                    negm_toks.append((VE, VE.n))
                    B.emit("scalar", lambda e, base=base, c=c: e.activation(
                        P_sb[:, base:base + 2048], PS[:, base:base + 2048], AF.Exp,
                        bias=st2[:, c:c + 1], scale=1.0,
                        accum_out=st2[:, 2 + c:3 + c]), (negm_toks[-1],), (SE, 1))
                    exp_toks.append((SE, SE.n))
                    slot_exp[c] = exp_toks[-1]
                last_smm = (PE, PE.n)
                B.emit("vector", lambda e: e.tensor_tensor(
                    st2[:, 4:5], st2[:, 0:1], st2[:, 1:2], op=ALU.min),
                    (negm_toks[1],), (VE, 1))
                B.emit("vector", None, ((VE, VE.n),))
                B.emit("vector", lambda e: e.tensor_scalar(
                    st2[:, 6:7], st2[:, 4:5], st2[:, 0:1], None, op0=ALU.subtract),
                    (), (VE, 1))
                B.emit("vector", None, ((VE, VE.n),))
                B.emit("vector", lambda e: e.tensor_scalar(
                    st2[:, 7:8], st2[:, 4:5], st2[:, 1:2], None, op0=ALU.subtract),
                    (), (VE, 1))
                B.emit("vector", None, ((VE, VE.n),))
                B.emit("scalar", lambda e: e.activation(
                    st2[:, 8:10], st2[:, 6:8], AF.Exp), (exp_toks[1],), (SE, 1))
                sexp_tok = (SE, SE.n)
                B.emit("vector", lambda e: e.tensor_mul(
                    st2[:, 10:11], st2[:, 2:3], st2[:, 8:9]), (sexp_tok,), (VE, 1))
                B.emit("vector", None, ((VE, VE.n),))
                B.emit("vector", lambda e: e.tensor_mul(
                    st2[:, 11:12], st2[:, 3:4], st2[:, 9:10]), (), (VE, 1))
                B.emit("vector", None, ((VE, VE.n),))
                B.emit("vector", lambda e: e.tensor_add(
                    st2[:, 12:13], st2[:, 10:11], st2[:, 11:12]), (), (VE, 1))
                B.emit("vector", None, ((VE, VE.n),))
                B.emit("vector", lambda e: e.reciprocal(
                    st2[:, 13:14], st2[:, 12:13]), (), (VE, 1))
                B.emit("vector", None, ((VE, VE.n),))
                B.emit("vector", lambda e: e.tensor_scalar(
                    st2[:, 14:15], st2[:, 8:9], st2[:, 13:14], None, op0=ALU.mult),
                    (), (VE, 1))
                B.emit("vector", None, ((VE, VE.n),))
                B.emit("vector", lambda e: e.tensor_scalar(
                    st2[:, 15:16], st2[:, 9:10], st2[:, 13:14], None, op0=ALU.mult),
                    (), (VE, 1))
                B.emit("vector", None, ((VE, VE.n),))
                dw = [prev_D_pe2] if prev_D_pe2 is not None else []
                B.emit("vector", lambda e: e.tensor_scalar(
                    D_sb[:, 0:128], ident_b[:, :], st2[:, 14:15], None, op0=ALU.mult),
                    tuple(dw), (VE, 1))
                B.emit("vector", lambda e: e.tensor_scalar(
                    D_sb[:, 128:256], ident_b[:, :], st2[:, 15:16], None, op0=ALU.mult),
                    (), (VE, 1))
                D_tok = (VE, VE.n)
                for c in range(2):
                    evs = {}
                    for grp in range(4):
                        pb = c * 2048 + (grp % 2) * 512
                        gw = [D_tok] if grp == 0 else []
                        if grp >= 2:
                            gw.append(evs[grp - 2])
                        for j in range(4):
                            kc = c * 16 + grp * 4 + j
                            B.emit("tensor", lambda e, kc=kc, pb=pb, j=j, c=c: e.matmul(
                                PS[:, pb + j * 128:pb + (j + 1) * 128],
                                P_sb[:, kc * 128:(kc + 1) * 128],
                                D_sb[:, c * 128:(c + 1) * 128],
                                start=True, stop=True),
                                tuple(gw) if j == 0 else (),
                                (PE, 1) if j == 3 else None)
                        pe_tok = (PE, PE.n)
                        dst = PT_half[:, :].rearrange("p (kc t) -> p kc t", kc=NKC)[
                            :, c * 16 + grp * 4:c * 16 + (grp + 1) * 4,
                            qtl * 128:(qtl + 1) * 128]
                        psrc = PS[:, pb:pb + 512].rearrange("p (a b) -> p a b", a=4)
                        if grp % 2 == 0:
                            B.emit("vector", lambda e, dst=dst, psrc=psrc: e.tensor_copy(
                                dst, psrc), (pe_tok,), (VE, 1))
                            evs[grp] = (VE, VE.n)
                        else:
                            B.emit("scalar", lambda e, dst=dst, psrc=psrc: e.copy(
                                dst, psrc), (pe_tok,), (SE, 1))
                            evs[grp] = (SE, SE.n)
                    slot_free[c] = [evs[2], evs[3]]
                    pt_evs += [evs[2], evs[3]]
                prev_D_pe2 = (PE, PE.n)
            pt_v_last = pt_evs[-2]
            pt_s_last = pt_evs[-1]
            for kc in range(NKC):
                ws = []
                if kc == 0:
                    ws = list(pt_evs) + [v_ready]
                    if prev_av_ev is not None:
                        ws.append(prev_av_ev)
                B.emit("tensor", lambda e, kc=kc, h=h: e.matmul(
                    PS[:, 0:512],
                    v_sb[:, kc * KV + h * 128: kc * KV + (h + 1) * 128],
                    PT_half[:, kc * 512:(kc + 1) * 512],
                    start=(kc == 0), stop=(kc == NKC - 1)),
                    tuple(ws), (PE, 1) if kc == NKC - 1 else None)
            av_tok = (PE, PE.n)
            B.emit("vector", lambda e, h=h, qh=qh: e.tensor_copy(
                vals_sb[:, h * T + qh * 512: h * T + (qh + 1) * 512], PS[:, 0:512]),
                (av_tok,), (VE, 1))
            prev_av_ev = (VE, VE.n)
    attn_done = prev_av_ev

    if debug:
        dma("gpsimd", dbg["dbg_vals"], vals_sb[:, :], (attn_done,), st="g_dbg")
        dbg_written.add("dbg_vals")

    if stage == "A4":
        return _finish()
    wo_toks = []
    for kvc in range(8):
        wo_toks.append(dma("gpsimd", wo_sb[:, kvc * D:(kvc + 1) * D],
                           wo[kvc * 128:(kvc + 1) * 128, :],
                           (attn_done,) if kvc == 0 else (wo_toks[-1],),
                           st="g_wo"))
    atst_free = [None, None]
    x5_free = None
    prev_ev_tok = None
    a5_h2_st = []
    for tt in range(NTT):
        bi = tt % 2
        for kvc in range(8):
            ws = []
            if kvc == 0:
                ws = [wo_toks[0], attn_done]
                if atst_free[bi] is not None:
                    ws.append(atst_free[bi])
                if prev_ev_tok is not None:
                    ws.append(prev_ev_tok)
            else:
                ws = [wo_toks[kvc]]
            for ob in range(8):
                B.emit("tensor", lambda e, kvc=kvc, ob=ob, tt=tt: e.matmul(
                    PS[:, ob * 512:(ob + 1) * 512],
                    vals_sb[:, kvc * T + tt * 128: kvc * T + (tt + 1) * 128],
                    wo_sb[:, kvc * D + ob * 512: kvc * D + (ob + 1) * 512],
                    start=(kvc == 0), stop=(kvc == 7)),
                    tuple(ws) if ob == 0 else (),
                    (PE, 1) if (kvc == 7 and ob == 7) else None)
        wo_tok = (PE, PE.n)
        B.emit("vector", lambda e, bi=bi: e.tensor_copy(
            atst[bi][:, 0:2048], PS[:, 0:2048]), (wo_tok,), (VE, 1))
        B.emit("vector", lambda e, bi=bi: e.tensor_copy(
            atst[bi][:, 2048:4096], PS[:, 2048:4096]), (), (VE, 1))
        ev_tok = (VE, VE.n)
        prev_ev_tok = ev_tok
        B.emit("vector", None, (ev_tok,))
        B.emit("scalar", None, (ev_tok,))
        rstd_tok, nb_tok = ln_stats(atst[bi][:, :])
        lnw = [nb_tok]
        if tt > 0:
            lnw.append(prev_x1_tok)
        B.emit("scalar", lambda e, bi=bi: e.activation(
            ln_sb[:, :], atst[bi][:, :], AF.Identity, bias=st1[:, 8:9], scale=st1[:, 7:8]),
            tuple(lnw), (SE, 1))
        ln_tok = (SE, SE.n)
        atst_free[bi] = ln_tok
        xw = [attn_done] if tt == 0 else [x5_free, prev_t_h2]
        t_x = dma("sync", x5_buf[:, :], x_in[tt * 128:(tt + 1) * 128, :], tuple(xw), st="s_x5")
        B.emit("vector", lambda e: e.tensor_add(x5_buf[:, :], x5_buf[:, :], ln_sb[:, :]),
               (t_x, ln_tok), (VE, 1))
        x1_tok = (VE, VE.n)
        prev_x1_tok = x1_tok
        dma("sync", x1sp[tt * 128:(tt + 1) * 128, :], x5_buf[:, :], (x1_tok,), st="s_sp")
        B.emit("vector", None, (x1_tok,))
        B.emit("scalar", None, (x1_tok,))
        rstd2, nb2 = ln_stats(x5_buf[:, :])
        B.emit("vector", lambda e: e.tensor_scalar(
            h2b[:, :], x5_buf[:, :], st1[:, 2:3], st1[:, 7:8],
            op0=ALU.add, op1=ALU.mult), (rstd2, nb2), (VE, 1))
        h2_tok = (VE, VE.n)
        x5_free = h2_tok
        a5_h2_st.append(dma("sync", h2sp[tt * 128:(tt + 1) * 128, :], h2b[:, :],
                            (h2_tok,), st="s_sp"))
        prev_t_h2 = a5_h2_st[-1]
        if debug:
            dma("gpsimd", dbg["dbg_h2"][tt * 128:(tt + 1) * 128, :], h2b[:, :], (h2_tok,), st="g_dbg")
            dbg_written.add("dbg_h2")
    a5_done = a5_h2_st[-1]

    if stage == "A5":
        return _finish()
    h2rd_free = None
    stage_evs = []
    ep_done = []
    slot_free = [None, None]
    x1_free = None
    for hf in range(2):
        tr_last = None
        for ttl in range(4):
            tt = hf * 4 + ttl
            w = [a5_done]
            if h2rd_free is not None:
                w.append(h2rd_free)
            t_h2 = dma("sync", h2b[:, :], h2sp[tt * 128:(tt + 1) * 128, :], tuple(w), st="s_h2rd")
            evs = {}
            for grp in range(8):
                bank = (grp % 2) * 512
                gw = [t_h2] if grp == 0 else []
                if hf == 1 and ttl == 0 and grp == 0:
                    gw.append(ep_done[-1])
                if grp >= 2:
                    gw.append(evs[grp - 2])
                elif ttl > 0:
                    gw.append(prev_evs[grp + 6])
                pb = (grp % 2) * 1024
                for j in range(4):
                    ic = grp * 4 + j
                    B.emit("tensor", lambda e, ic=ic, pb=pb, j=j: e.transpose(
                        PSb[:, pb + j * 128:pb + (j + 1) * 128],
                        h2b[:, ic * 128:(ic + 1) * 128], ident_b[:, :]),
                        tuple(gw) if j == 0 else (), (PE, 1) if j == 3 else None)
                pe_tok = (PE, PE.n)
                dst = h2T[:, :].rearrange("p (ic t) -> p ic t", ic=NIC)[
                    :, grp * 4:(grp + 1) * 4, ttl * 128:(ttl + 1) * 128]
                B.emit("vector", lambda e, dst=dst, pb=pb: e.tensor_copy(
                    dst, PSb[:, pb:pb + 512].rearrange("p (a b) -> p a b", a=4)),
                    (pe_tok,), (VE, 1))
                evs[grp] = (VE, VE.n)
            prev_evs = evs
            h2rd_free = evs[7]
            tr_last = evs[7]
        h2T_tok = tr_last

        for ttl in range(4):
            for ic in range(NIC):
                B.emit("tensor", lambda e, ic=ic, ttl=ttl: e.matmul(
                    PS[:, ttl * 512: ttl * 512 + 80],
                    h2T[:, ic * 512 + ttl * 128: ic * 512 + (ttl + 1) * 128],
                    arT[:, ic * 80:(ic + 1) * 80],
                    start=(ic == 0), stop=(ic == NIC - 1)),
                    (h2T_tok, ar_done) if (ttl == 0 and ic == 0) else (),
                    (PE, 1) if ic == NIC - 1 else None)
        ar_mm = (PE, PE.n)
        for ttl in range(4):
            B.emit("vector", lambda e, ttl=ttl: e.tensor_copy(
                res16[:, ttl * 64:(ttl + 1) * 64], PS[:, ttl * 512: ttl * 512 + 64]),
                (ar_mm,), (VE, 1))
            B.emit("vector", lambda e, ttl=ttl: e.tensor_copy(
                rtr[:, ttl * 16:(ttl + 1) * 16], PS[:, ttl * 512 + 64: ttl * 512 + 80]),
                (), (VE, 1))
        ar_ev = (VE, VE.n)
        if debug:
            dma("gpsimd", dbg["dbg_res"][hf * 512:(hf + 1) * 512, 0:64].rearrange(
                "(ttl p) r -> p ttl r", p=128),
                res16[:, :].rearrange("p (ttl r) -> p ttl r", ttl=4), (ar_ev,),
                st="g_dbg")
            dma("gpsimd", dbg["dbg_res"][hf * 512:(hf + 1) * 512, 64:80].rearrange(
                "(ttl p) r -> p ttl r", p=128),
                rtr[:, 0:64].rearrange("p (ttl r) -> p ttl r", ttl=4), (ar_ev,),
                st="g_dbg")

        ht_last = None
        for ttl in range(4):
            lg = rtr[:, ttl * 16:(ttl + 1) * 16]
            B.emit("vector", lambda e, lg=lg: e.reduce_max(
                rtr[:, 64:65], lg, axis=AXX, negate=True), (ar_ev,), (VE, 1))
            B.emit("scalar", lambda e, lg=lg: e.activation(
                rtr[:, 80:96], lg, AF.Exp, bias=rtr[:, 64:65], scale=1.0,
                accum_out=rtr[:, 65:66]), ((VE, VE.n),), (SE, 1))
            B.emit("vector", lambda e: e.reciprocal(rtr[:, 66:67], rtr[:, 65:66]),
                   ((SE, SE.n),), (VE, 1))
            B.emit("vector", None, ((VE, VE.n),))
            B.emit("vector", lambda e: e.tensor_scalar(
                rtr[:, 96:112], rtr[:, 80:96], rtr[:, 66:67], None, op0=ALU.mult),
                (), (VE, 1))
            B.emit("vector", None, ((VE, VE.n),))
            B.emit("vector", lambda e: e.reduce_max(
                rtr[:, 67:68], rtr[:, 96:112], axis=AXX), (), (VE, 1))
            B.emit("vector", None, ((VE, VE.n),))
            B.emit("vector", lambda e: e.tensor_scalar(
                rtr[:, 112:128], rtr[:, 96:112], rtr[:, 67:68], None, op0=ALU.is_equal),
                (), (VE, 1))
            B.emit("vector", None, ((VE, VE.n),))
            B.emit("vector", lambda e: e.tensor_sub(
                rtr[:, 128:144], rtr[:, 96:112], rtr[:, 112:128]), (), (VE, 1))
            B.emit("vector", None, ((VE, VE.n),))
            B.emit("vector", lambda e: e.reduce_max(
                rtr[:, 68:69], rtr[:, 128:144], axis=AXX), (), (VE, 1))
            B.emit("vector", None, ((VE, VE.n),))
            B.emit("vector", lambda e: e.tensor_scalar(
                rtr[:, 144:160], rtr[:, 128:144], rtr[:, 68:69], None, op0=ALU.is_equal),
                (), (VE, 1))
            B.emit("vector", None, ((VE, VE.n),))
            B.emit("vector", lambda e: e.tensor_scalar(
                rtr[:, 69:70], rtr[:, 67:68], rtr[:, 68:69], 1e-6,
                op0=ALU.add, op1=ALU.add), (), (VE, 1))
            B.emit("vector", None, ((VE, VE.n),))
            B.emit("vector", lambda e: e.reciprocal(rtr[:, 70:71], rtr[:, 69:70]),
                   (), (VE, 1))
            B.emit("vector", None, ((VE, VE.n),))
            B.emit("vector", lambda e: e.tensor_scalar(
                rtr[:, 71:72], rtr[:, 67:68], rtr[:, 70:71], 0.5,
                op0=ALU.mult, op1=ALU.mult), (), (VE, 1))
            B.emit("vector", None, ((VE, VE.n),))
            B.emit("vector", lambda e: e.tensor_scalar(
                rtr[:, 72:73], rtr[:, 68:69], rtr[:, 70:71], 0.5,
                op0=ALU.mult, op1=ALU.mult), (), (VE, 1))
            B.emit("vector", None, ((VE, VE.n),))
            B.emit("vector", lambda e: e.tensor_scalar(
                rtr[:, 112:128], rtr[:, 112:128], rtr[:, 71:72], None, op0=ALU.mult),
                (), (VE, 1))
            B.emit("vector", None, ((VE, VE.n),))
            B.emit("vector", lambda e: e.tensor_scalar(
                rtr[:, 144:160], rtr[:, 144:160], rtr[:, 72:73], None, op0=ALU.mult),
                (), (VE, 1))
            B.emit("vector", None, ((VE, VE.n),))
            B.emit("vector", lambda e: e.tensor_add(
                rtr[:, 112:128], rtr[:, 112:128], rtr[:, 144:160]), (), (VE, 1))
            B.emit("vector", None, ((VE, VE.n),))
            for e_i in range(16):
                B.emit("vector", lambda e, e_i=e_i, ttl=ttl: e.tensor_scalar(
                    H_sb[:, e_i * 64:(e_i + 1) * 64],
                    res16[:, ttl * 64:(ttl + 1) * 64],
                    rtr[:, 112 + e_i:113 + e_i], None, op0=ALU.mult),
                    (), (VE, 1) if e_i == 15 else None)
            h_tok = (VE, VE.n)
            if debug:
                dma("gpsimd", dbg["dbg_H"][(hf * 4 + ttl) * 128:(hf * 4 + ttl + 1) * 128, :],
                    H_sb[:, :], (h_tok,), st="g_dbg")
            for grp in range(2):
                pb = grp * 1024
                gw = [h_tok, ar_ev] if grp == 0 else []
                for j in range(4):
                    erc = grp * 4 + j
                    B.emit("tensor", lambda e, erc=erc, pb=pb, j=j: e.transpose(
                        PSb[:, pb + j * 128:pb + (j + 1) * 128],
                        H_sb[:, erc * 128:(erc + 1) * 128], ident_b[:, :]),
                        tuple(gw) if j == 0 else (), (PE, 1) if j == 3 else None)
                pe_tok = (PE, PE.n)
                dst = HT_sb[:, :].rearrange("p (erc t) -> p erc t", erc=8)[
                    :, grp * 4:(grp + 1) * 4, ttl * 128:(ttl + 1) * 128]
                B.emit("vector", lambda e, dst=dst, pb=pb: e.tensor_copy(
                    dst, PSb[:, pb:pb + 512].rearrange("p (a b) -> p a b", a=4)),
                    (pe_tok,), (VE, 1))
                ht_last = (VE, VE.n)
        ht_done = ht_last

        for oe in range(16):
            gi = hf * 16 + oe
            si = gi % 2
            w = [bwT_done]
            if slot_free[si] is not None:
                w.append(slot_free[si])
            if gi == 0:
                w.append(a5_done)
            t_w1 = dma("sync", bwT_s[si][:, :].rearrange("p (ic o) -> p ic o", ic=NIC),
                       bwT_d[oe].rearrange("(ic p) o -> p ic o", p=128), tuple(w),
                       st=f"s_sl{si}")
            t_w2 = dma("sync", bfT_s[si][:, :].rearrange("p (erc o) -> p erc o", erc=8),
                       bfT_d[oe].rearrange("(erc p) o -> p erc o", p=128),
                       (bft_done,) + ((slot_free[si],) if slot_free[si] else ()),
                       st=f"s_sl{si}")
            goff = (gi % 2) * 2048
            for ic in range(NIC):
                for ttl in range(4):
                    ws = []
                    if ic == 0 and ttl == 0:
                        ws = [t_w1, t_w2, ht_done]
                        if len(stage_evs) >= 2:
                            ws.append(stage_evs[gi - 2])
                    B.emit("tensor", lambda e, ic=ic, ttl=ttl, si=si, goff=goff: e.matmul(
                        PS[:, goff + ttl * 512: goff + ttl * 512 + 256],
                        h2T[:, ic * 512 + ttl * 128: ic * 512 + (ttl + 1) * 128],
                        bwT_s[si][:, ic * 256:(ic + 1) * 256],
                        start=(ic == 0), stop=False),
                        tuple(ws), None)
            for erc in range(8):
                for ttl in range(4):
                    ws = [t_w2] if (erc == 0 and ttl == 0) else []
                    B.emit("tensor", lambda e, erc=erc, ttl=ttl, si=si, goff=goff: e.matmul(
                        PS[:, goff + ttl * 512: goff + ttl * 512 + 256],
                        HT_sb[:, erc * 512 + ttl * 128: erc * 512 + (ttl + 1) * 128],
                        bfT_s[si][:, erc * 256:(erc + 1) * 256],
                        start=False, stop=(erc == 7)),
                        tuple(ws), (PE, 1) if (erc == 7 and ttl == 3) else None)
            mm_tok = (PE, PE.n)
            slot_free[si] = mm_tok
            for ttl in range(4):
                B.emit("vector", lambda e, ttl=ttl, oe=oe, goff=goff: e.tensor_tensor(
                    stg_sb[:, ttl * D + oe * 256: ttl * D + (oe + 1) * 256],
                    PS[:, goff + ttl * 512: goff + ttl * 512 + 256],
                    bb16[:, oe * 256:(oe + 1) * 256], op=ALU.add),
                    (mm_tok, bb_ready), (VE, 1) if ttl == 3 else None)
            stage_evs.append((VE, VE.n))
        pass_done = stage_evs[-1]

        for ttl in range(4):
            tt = hf * 4 + ttl
            stg = stg_sb[:, ttl * D:(ttl + 1) * D]
            B.emit("vector", None, (pass_done,))
            B.emit("scalar", None, (pass_done,))
            rstd3, nb3 = ln_stats(stg)
            lnw = [nb3] + ([ep_done[-1]] if ep_done else [])
            B.emit("scalar", lambda e, stg=stg: e.activation(
                ln_sb[:, :], stg, AF.Identity, bias=st1[:, 8:9], scale=st1[:, 7:8]),
                tuple(lnw), (SE, 1))
            ln_tok = (SE, SE.n)
            xw = [a5_done]
            if x1_free:
                xw.append(x1_free)
            t_x1 = dma("sync", x1buf[:, :], x1sp[tt * 128:(tt + 1) * 128, :], tuple(xw), st="s_x1rd")
            B.emit("vector", lambda e: e.tensor_add(
                x1buf[:, :], x1buf[:, :], ln_sb[:, :]), (t_x1, ln_tok), (VE, 1))
            fin_tok = (VE, VE.n)
            x1_free = dma("sync", out_d[tt * 128:(tt + 1) * 128, :], x1buf[:, :],
                          (fin_tok,), st="s_out")
            ep_done.append(fin_tok)
        if debug:
            dma("gpsimd", dbg["dbg_stage"][hf * 512:(hf + 1) * 512, :].rearrange(
                "(ttl p) o -> p ttl o", p=128),
                stg_sb[:, :].rearrange("p (ttl o) -> p ttl o", ttl=4), (pass_done,),
                st="g_dbg")

    return _finish()


_NC_CACHE = {}


def _get_nc(debug=DEBUG):
    if debug not in _NC_CACHE:
        _NC_CACHE[debug] = build_program(debug)
    return _NC_CACHE[debug]


def kernel(L0_attn_x, L0_attn_wq, L0_attn_wk, L0_attn_wv, L0_attn_wo,
           L0_ffn_base_w, L0_ffn_base_b, L0_ffn_A, L0_ffn_B, L0_ffn_router_w):
    x = np.ascontiguousarray(np.asarray(L0_attn_x, dtype=np.float32))
    wq_ = np.ascontiguousarray(np.asarray(L0_attn_wq, dtype=np.float32))
    wk_ = np.ascontiguousarray(np.asarray(L0_attn_wk, dtype=np.float32))
    wv_ = np.ascontiguousarray(np.asarray(L0_attn_wv, dtype=np.float32))
    wo_ = np.ascontiguousarray(np.asarray(L0_attn_wo, dtype=np.float32))
    bw_ = np.ascontiguousarray(np.asarray(L0_ffn_base_w, dtype=np.float32))
    bb_ = np.ascontiguousarray(np.asarray(L0_ffn_base_b, dtype=np.float32))
    A_ = np.ascontiguousarray(np.asarray(L0_ffn_A, dtype=np.float32))
    B_ = np.ascontiguousarray(np.asarray(L0_ffn_B, dtype=np.float32))
    rw_ = np.ascontiguousarray(np.asarray(L0_ffn_router_w, dtype=np.float32))

    nc = _get_nc()
    xf = x.reshape(8192, 4096)
    import ml_dtypes
    idf = np.eye(128, dtype=np.float32)
    idb = np.eye(128).astype(ml_dtypes.bfloat16)
    in_maps = [{
        "x": xf[c * 1024:(c + 1) * 1024], "wq": wq_, "wk": wk_, "wv": wv_,
        "wo": wo_, "bw": bw_, "bb": bb_, "A": A_, "B": B_, "rw": rw_,
        "idf": idf, "idb": idb,
    } for c in range(8)]
    res = run_bass_kernel_spmd(nc, in_maps, core_ids=list(range(8)))
    shards = [res.results[c]["out"] for c in range(8)]
    return np.concatenate(shards, axis=0).reshape(2, 4096, 4096).astype(np.float32)


if __name__ == "__main__":
    build_program()
    print("build ok")


# version 08
# speedup vs baseline: 1.1737x; 1.0708x over previous
import numpy as np
from contextlib import ExitStack

import concourse.bass as bass
import concourse.mybir as mybir
from concourse.bass_utils import run_bass_kernel_spmd

F32 = mybir.dt.float32
BF16 = mybir.dt.bfloat16
U8 = mybir.dt.uint8
AF = mybir.ActivationFunctionType
ALU = mybir.AluOpType
AXX = mybir.AxisListType.X

D = 4096
T = 1024
NTT = 8
NIC = 32
NH = 8
KV = 1024
S = 4096
NKC = 32
EPS = 1e-5
QSCALE = float(1.0 / np.sqrt(128.0))

DEBUG = False


class Sem:
    __slots__ = ("h", "n")

    def __init__(self, h):
        self.h = h
        self.n = 0


class Builder:
    ENGINES = ("sync", "gpsimd", "tensor", "vector", "scalar")

    def __init__(self):
        self.ops = {e: [] for e in self.ENGINES}

    def emit(self, eng, fn, waits=(), inc=None):
        ws = []
        for s, v in waits:
            assert v <= s.n, f"wait on future sem value {v} > {s.n} ({eng})"
            if v > 0:
                ws.append((s.h, v))
        if inc is not None:
            s, k = inc
            s.n += k
            inc = (s.h, k)
        self.ops[eng].append((fn, ws, inc))

    def replay(self, block):
        def mk(eng_name):
            def run(e):
                seen = {}
                for fn, ws, inc in self.ops[eng_name]:
                    merged = {}
                    for h, v in ws:
                        if v > merged.get(h.name, (None, -1))[1]:
                            merged[h.name] = (h, v)
                    for h, v in merged.values():
                        if seen.get(h.name, -1) >= v:
                            continue
                        seen[h.name] = v
                        e.wait_ge(h, v)
                    inst = fn(e) if fn is not None else None
                    if inc is not None:
                        h, k = inc
                        if inst is None:
                            inst = e.nop()
                        inst.then_inc(h, k)
            return run

        for eng_name in self.ENGINES:
            if self.ops[eng_name]:
                getattr(block, eng_name)(mk(eng_name))


def build_program(debug=DEBUG, stage="full"):
    nc = bass.Bass("TRN2", target_bir_lowering=False)

    x_in = nc.dram_tensor("x", [T, D], F32, kind="ExternalInput").ap()
    wq = nc.dram_tensor("wq", [D, D], F32, kind="ExternalInput").ap()
    wk = nc.dram_tensor("wk", [D, KV], F32, kind="ExternalInput").ap()
    wv = nc.dram_tensor("wv", [D, KV], F32, kind="ExternalInput").ap()
    wo = nc.dram_tensor("wo", [KV, D], F32, kind="ExternalInput").ap()
    bw = nc.dram_tensor("bw", [D, D], F32, kind="ExternalInput").ap()
    bb = nc.dram_tensor("bb", [D], F32, kind="ExternalInput").ap()
    A_in = nc.dram_tensor("A", [64, D], F32, kind="ExternalInput").ap()
    B_in = nc.dram_tensor("B", [16, D, 64], F32, kind="ExternalInput").ap()
    rw = nc.dram_tensor("rw", [16, D], F32, kind="ExternalInput").ap()
    idf_in = nc.dram_tensor("idf", [128, 128], F32, kind="ExternalInput").ap()
    idb_in = nc.dram_tensor("idb", [128, 128], BF16, kind="ExternalInput").ap()
    out_d = nc.dram_tensor("out", [T, D], F32, kind="ExternalOutput").ap()

    wqe_d = nc.dram_tensor("wqe_d", [D, KV], BF16, kind="Internal").ap()
    bwT_d = nc.dram_tensor("bwT_d", [16, D, 256], BF16, kind="Internal").ap()
    bfT_d = nc.dram_tensor("bfT_d", [16, KV, 256], BF16, kind="Internal").ap()
    kv_in = nc.dram_tensor("kv_in", [2 * T, KV], BF16, kind="Internal").ap()
    kv_out = nc.dram_tensor("kv_out", [8 * T, KV], BF16, kind="Internal").ap()
    x1sp = nc.dram_tensor("x1sp", [T, D], F32, kind="Internal").ap()
    h2sp = nc.dram_tensor("h2sp", [T, D], BF16, kind="Internal").ap()

    dbg = {}
    dbg_written = set()
    if debug:
        for nm, shp, dt in [
            ("dbg_qT", [128, NH * T], F32), ("dbg_kT", [128, NH * T], F32),
            ("dbg_v", [128, NTT * KV], F32), ("dbg_vals", [128, NH * T], F32),
            ("dbg_h2", [T, D], F32), ("dbg_res", [T, 80], F32),
            ("dbg_H", [T, KV], F32), ("dbg_stage", [T, D], F32),
            ("dbg_h1b", [128, D], F32), ("dbg_h1T", [128, NIC * T], F32),
            ("dbg_wqe", [D, KV], F32),
        ]:
            dbg[nm] = nc.dram_tensor(nm, shp, dt, kind="ExternalOutput").ap()

    ctx = ExitStack()
    CAP = 212_832
    arena = ctx.enter_context(nc.sbuf_tensor("arena", [128, CAP], U8))
    base = nc.lookup_mloc(arena).addr

    def place(name, shape, dtype, off):
        per = int(np.prod(shape[1:])) * mybir.dt.size(dtype)
        assert off % 32 == 0 and off + per <= CAP, (name, off, per)
        h = nc.virtual_sbuf_tensor(name, shape, dtype)
        nc._tensor(h.name, h.shape, h.dtype, type="SB")
        m = nc.lookup_mloc(h)
        m.allocated = True
        m.addr = base + off
        return h.ap()

    def place2(name, shape, dtype, off):
        per = int(np.prod(shape[1:])) * mybir.dt.size(dtype)
        return [place(f"{name}{i}", shape, dtype, off + i * per) for i in range(2)]

    ident_f = place("ident_f", [128, 128], F32, 0)
    ident_b = place("ident_b", [128, 128], BF16, 512)
    st1 = place("st1", [128, 64], F32, 768)
    st2 = place("st2", [128, 64], F32, 1024)
    D_sb = place("D_sb", [128, 256], BF16, 34432)
    rtr = place("rtr", [128, 160], F32, 1536)
    res16 = place("res16", [128, 4 * 64], BF16, 2176)
    H_sb = place("H_sb", [128, KV], BF16, 2688)
    arT = place("arT", [128, NIC * 80], BF16, 4736)
    bb16 = place("bb16", [128, D], BF16, 9856)
    ln_sb = place("ln_sb", [128, D], F32, 18048)
    sq_sb = place("sq_sb", [128, D], BF16, 34432)
    vals_sb = place("vals_sb", [128, NH * T], BF16, 42624)
    qT = place("qT", [128, NH * T], BF16, 59008)
    P_sb = place("P_sb", [128, S], BF16, 75392)
    OB = 83584
    FR = OB + 20480

    wq_stage = place2("wq_stage", [128, 2 * KV], F32, 18048)
    wqe_sb = place("wqe_sb", [128, KV], BF16, OB)
    bw_big = place2("bw_big", [128, D], F32, FR + 65536)
    bw_out = place2("bw_out", [128, NIC * 128], BF16, 18048)
    bft_sb = place("bft_sb", [128, 8 * D], BF16, FR)
    b_stage = place2("b_stage", [128, 2048], F32, FR + 65536)
    ar_stage = place("ar_stage", [64, D], BF16, FR + 81920)
    r_stage = place("r_stage", [16, D], BF16, FR + 90112)
    h1T = place("h1T", [128, NIC * T], BF16, FR)
    x_buf = place("x_buf", [128, D], F32, FR + 65536)
    h1b = place("h1b", [128, D], BF16, FR + 81920)
    kT_own = place("kT_own", [128, NH * T], BF16, FR + 65536)
    v_own = place("v_own", [128, NTT * KV], BF16, FR + 81920)
    ws_buf = place2("ws_buf", [128, 4 * KV], BF16, 18048)
    v_sb = place("v_sb", [128, NKC * KV], BF16, FR)
    PT_half = place("PT_half", [128, NKC * 512], BF16, FR + 65536)
    kh_sb = place("kh_sb", [128, S], BF16, FR + 98304)
    wo_sb = place("wo_sb", [128, 8 * D], BF16, FR)
    atst = place2("atst", [128, D], BF16, FR + 65536)
    x5_buf = place("x5_buf", [128, D], F32, FR + 81920)
    h2b = place("h2b", [128, D], BF16, FR + 98304)
    stg_sb = place("stg_sb", [128, 4 * D], BF16, FR)
    h2T = place("h2T", [128, NIC * 512], BF16, FR + 32768)
    bwT_s = place2("bwT_s", [128, NIC * 256], BF16, FR + 65536)
    bfT_s = place2("bfT_s", [128, 8 * 256], BF16, 75392)
    x1buf = place("x1buf", [128, D], F32, 59008)
    HT_sb = place("HT_sb", [128, 8 * 512], BF16, 42624)
    assert FR + 106496 <= OB + 129248

    PS = ctx.enter_context(nc.psum_tensor("PS", [128, 4096], F32))
    _psh = nc.virtual_psum_tensor("PSb", [128, 8192], BF16)
    nc._tensor(_psh.name, _psh.shape, _psh.dtype, type="PSUM")
    _psm = nc.lookup_mloc(_psh)
    _psm.allocated = True
    _psm.addr = nc.lookup_mloc(PS.tensor).addr if hasattr(PS, "tensor") else 0
    PSb = _psh.ap()

    sems = {nm: Sem(ctx.enter_context(nc.semaphore(nm)))
            for nm in ("LD", "GD", "PE", "VE", "SE", "GE", "CC")}
    LD, GD, PE, VE, SE, GE, CC = (sems[k] for k in
                                  ("LD", "GD", "PE", "VE", "SE", "GE", "CC"))

    B = Builder()

    def _finish():
        if stage != "full":
            B.emit("gpsimd", lambda e: e.memset(ln_sb[:, :], 0.0), (), (GE, 1))
            zt = (GE, GE.n)
            for _tt in range(NTT):
                dma("gpsimd", out_d[_tt * 128:(_tt + 1) * 128, :], ln_sb[:, :], (zt,), st="g_fill")
            if debug:
                for _nm, _ap in dbg.items():
                    if _nm in dbg_written:
                        continue
                    _views = ([_ap] if _ap.shape[0] == 128 else
                              [_ap[_r * 128:(_r + 1) * 128]
                               for _r in range(_ap.shape[0] // 128)])
                    for _v in _views:
                        nfree = int(np.prod(_v.shape[1:]))
                        _off = 0
                        while _off < nfree:
                            _w = min(4096, nfree - _off)
                            dma("gpsimd", _v[:, _off:_off + _w], ln_sb[:, 0:_w], (zt,), st="g_fill")
                            _off += _w
        allw = tuple((s, s.n) for s in dstreams.values()) + ((CC, CC.n),)
        B.emit("gpsimd", None, allw)
        if any(fn is not None or ws for fn, ws, inc in B.ops["sync"]):
            B.emit("sync", None, allw)
        with nc.Block() as block:
            B.replay(block)
        ctx.close()
        return nc

    dstreams = {}

    def dsem(name):
        if name not in dstreams:
            dstreams[name] = Sem(ctx.enter_context(nc.semaphore(name)))
        return dstreams[name]

    def dma(q, out, in_, waits=(), st=None):
        s = dsem(st if st is not None else ("s_misc" if q == "sync" else "g_misc"))
        B.emit(q, lambda e, o=out, i=in_: e.dma_start(out=o, in_=i), waits, (s, 16))
        return (s, s.n)

    dma("sync", ident_f[:, :], idf_in[:, :])
    ident_ready = dma("sync", ident_b[:, :], idb_in[:, :])

    if stage == "Z0":
        return _finish()
    dma("gpsimd", bb16[:, :], bb[None, :].broadcast_to([128, D]))

    if stage == "P0a":
        return _finish()
    dma("gpsimd", ar_stage[0:64, :], A_in[:, :])
    t_r = dma("gpsimd", r_stage[0:16, :], rw[:, :])
    bb_ready = t_a = t_r
    ar_ev_tok = None
    for ic in range(NIC):
        if ar_ev_tok is not None:
            B.emit("tensor", None, (ar_ev_tok,))
        B.emit("tensor", lambda e, ic=ic: e.transpose(
            PSb[:, 0:64], ar_stage[0:64, ic * 128:(ic + 1) * 128], ident_b[0:64, 0:64]),
            (t_a, ident_ready) if ic == 0 else ())
        B.emit("tensor", lambda e, ic=ic: e.transpose(
            PSb[:, 64:80], r_stage[0:16, ic * 128:(ic + 1) * 128], ident_b[0:16, 0:16]),
            (t_r,) if ic == 0 else (), (PE, 1))
        B.emit("scalar", lambda e, ic=ic: e.copy(
            arT[:, ic * 80:(ic + 1) * 80], PSb[:, 0:80]), ((PE, PE.n),), (SE, 1))
        ar_ev_tok = (SE, SE.n)
    ar_done = ar_ev_tok

    if stage == "P0b":
        return _finish()
    b_pe = {}
    bft_last = None
    for e_i in range(16):
        erc, lane = e_i // 2, (e_i % 2) * 64
        srcv = B_in[e_i].rearrange("(ot p) r -> p ot r", p=128)
        lw = (b_pe[e_i - 2],) if e_i >= 2 else ()
        t_b = dma("sync", b_stage[e_i % 2][:, :].rearrange(
            "p (ot r) -> p ot r", ot=32), srcv, lw, st=f"s_bst{e_i % 2}")
        for grp in range(8):
            if bft_last is not None:
                B.emit("tensor", None, (bft_last,))
            for j in range(4):
                ot = grp * 4 + j
                B.emit("tensor", lambda e, e_i=e_i, ot=ot, j=j: e.transpose(
                    PS[0:64, 1024 + j * 128:1024 + (j + 1) * 128],
                    b_stage[e_i % 2][:, ot * 64:(ot + 1) * 64], ident_f[:, :]),
                    (t_b, ident_ready) if (grp == 0 and j == 0) else (),
                    (PE, 1) if j == 3 else None)
            b_pe[e_i] = (PE, PE.n)
            B.emit("vector", lambda e, erc=erc, lane=lane, grp=grp: e.tensor_copy(
                bft_sb[lane:lane + 64, erc * D + grp * 512:erc * D + (grp + 1) * 512],
                PS[0:64, 1024:1536]), ((PE, PE.n),), (VE, 1))
            bft_last = (VE, VE.n)
    bft_st = []
    for oe in range(16):
        src = bft_sb[:, :].rearrange("p (erc o) -> p erc o", erc=8)[
            :, :, oe * 256:(oe + 1) * 256]
        dst = bfT_d[oe].rearrange("(erc p) o -> p erc o", p=128)
        bft_st.append(dma("sync", dst, src, (bft_last,), st="s_bftst"))
    bft_done = bft_st[-1]

    if stage == "P0c":
        return _finish()
    fold_state = {"done": None}

    def fold_gen():
        mm_toks = []
        ev_tok = None
        st_tok = None
        u = 0
        for ic in range(NIC):
            for hh in range(2):
                bi = u % 2
                lw = (mm_toks[u - 2],) if u >= 2 else ()
                t_w = dma("sync", wq_stage[bi],
                          wq[ic * 128:(ic + 1) * 128, hh * 2048:(hh + 1) * 2048], lw,
                          st=f"s_wq{bi}")
                for hl in range(4):
                    h = 4 * hh + hl
                    for g in range(4):
                        ws = [t_w]
                        if hl == 0 and g == 0:
                            ws.append(ident_ready)
                            if ev_tok is not None and hh == 0:
                                ws.append(ev_tok)
                        B.emit("tensor", lambda e, bi=bi, hl=hl, g=g, h=h: e.matmul(
                            PS[:, 1536 + h * 128:1536 + (h + 1) * 128],
                            ident_f[:, :],
                            wq_stage[bi][:, hl * 512 + g * 128: hl * 512 + (g + 1) * 128],
                            start=(g == 0), stop=(g == 3)),
                            tuple(ws), (PE, 1) if (hl == 3 and g == 3) else None)
                mm_toks.append((PE, PE.n))
                u += 1
                yield
            ew = [(PE, PE.n)]
            if st_tok is not None:
                ew.append(st_tok)
            B.emit("vector", lambda e: e.tensor_copy(wqe_sb[:, :], PS[:, 1536:2560]),
                   tuple(ew), (VE, 1))
            ev_tok = (VE, VE.n)
            st_tok = dma("sync", wqe_d[ic * 128:(ic + 1) * 128, :], wqe_sb[:, :],
                         (ev_tok,), st="s_wqest")
        fold_state["done"] = st_tok

    fold_it = fold_gen()

    def fold_step(k=1):
        for _ in range(k):
            if fold_state["done"] is None:
                next(fold_it, None)

    fold_step(12)

    bw_state = {"done": None}

    def emit_bw_prep(region_free_tok, pe_banks_free):
        ld_toks = {}
        ev_done = {}
        st_done = {}
        for ot in range(2):
            ld_toks[ot] = dma("sync", bw_big[ot], bw[ot * 128:(ot + 1) * 128, :],
                              (region_free_tok,), st=f"s_bw{ot}")
        for ot in range(32):
            bi = ot % 2
            evs = {}
            for grp in range(8):
                bank = 3072 + (grp % 2) * 512
                gw = [ld_toks[ot], ident_ready]
                if grp >= 2:
                    gw.append(evs[grp - 2])
                elif ot >= 1:
                    gw.append(ev_done[ot - 1])
                else:
                    gw.append(pe_banks_free)
                if grp <= 1 and ot >= 2:
                    gw.append(st_done[ot - 2])
                for j in range(4):
                    ic = grp * 4 + j
                    B.emit("tensor", lambda e, ic=ic, j=j, bank=bank, bi=bi: e.transpose(
                        PS[:, bank + j * 128:bank + (j + 1) * 128],
                        bw_big[bi][:, ic * 128:(ic + 1) * 128], ident_f[:, :]),
                        tuple(gw) if j == 0 else (), (PE, 1) if j == 3 else None)
                pe_tok = (PE, PE.n)
                B.emit("vector", lambda e, grp=grp, bi=bi, bank=bank: e.tensor_copy(
                    bw_out[bi][:, grp * 512:(grp + 1) * 512],
                    PS[:, bank:bank + 512]), (pe_tok,), (VE, 1))
                evs[grp] = (VE, VE.n)
            ev_done[ot] = evs[7]
            dstv = bwT_d[ot // 2][:, (ot % 2) * 128:(ot % 2) * 128 + 128].rearrange(
                "(ic p) o -> p ic o", p=128)
            st_done[ot] = dma("sync", dstv,
                              bw_out[bi][:, :].rearrange("p (ic o) -> p ic o", ic=NIC),
                              (evs[7],), st="s_bwst")
            if ot + 2 < 32:
                ld_toks[ot + 2] = dma("sync", bw_big[bi],
                                      bw[(ot + 2) * 128:(ot + 3) * 128, :],
                                      (ev_done[ot],), st=f"s_bw{bi}")
        bw_state["done"] = st_done[31]
        bw_state["last_evac"] = ev_done[31]

    if stage == "P0":
        return _finish()
    def vdrain():
        B.emit("vector", None, ((VE, VE.n),))

    def ln_stats(src_ap):
        B.emit("vector", lambda e: e.reduce_sum(st1[:, 0:1], src_ap, axis=AXX), (), (VE, 1))
        vdrain()
        B.emit("scalar", lambda e: e.activation(
            sq_sb[:, :], src_ap, AF.Square, accum_out=st1[:, 1:2]), (), (SE, 1))
        sq_tok = (SE, SE.n)
        B.emit("vector", lambda e: e.tensor_scalar(
            st1[:, 2:3], st1[:, 0:1], -1.0 / D, None, op0=ALU.mult), (), (VE, 1))
        vdrain()
        B.emit("vector", lambda e: e.tensor_scalar(
            st1[:, 3:4], st1[:, 1:2], 1.0 / D, None, op0=ALU.mult), (sq_tok,), (VE, 1))
        vdrain()
        B.emit("vector", lambda e: e.tensor_mul(st1[:, 4:5], st1[:, 2:3], st1[:, 2:3]),
               (), (VE, 1))
        vdrain()
        B.emit("vector", lambda e: e.tensor_scalar(
            st1[:, 5:6], st1[:, 3:4], st1[:, 4:5], EPS, op0=ALU.subtract, op1=ALU.add),
            (), (VE, 1))
        vdrain()
        B.emit("vector", lambda e: e.reciprocal(st1[:, 6:7], st1[:, 5:6]), (), (VE, 1))
        B.emit("scalar", lambda e: e.activation(st1[:, 7:8], st1[:, 6:7], AF.Sqrt),
               ((VE, VE.n),), (SE, 1))
        rstd_tok = (SE, SE.n)
        B.emit("scalar", None, (rstd_tok,))
        B.emit("vector", lambda e: e.tensor_mul(st1[:, 8:9], st1[:, 2:3], st1[:, 7:8]),
               (rstd_tok,), (VE, 1))
        vdrain()
        return rstd_tok, (VE, VE.n)

    x_free = None
    h1T_last = None
    a1_bar = ([(VE, VE.n), (SE, SE.n), (GE, GE.n), (PE, PE.n)] +
              [(s, s.n) for s in dstreams.values()])
    for tt in range(NTT):
        w = ((x_free,) if x_free else tuple(a1_bar))
        t_x = dma("sync", x_buf[:, :], x_in[tt * 128:(tt + 1) * 128, :], w, st="s_x")
        B.emit("vector", None, (t_x,))
        B.emit("scalar", None, (t_x,))
        rstd_tok, nb_tok = ln_stats(x_buf[:, :])
        nw = [nb_tok]
        if tt > 0:
            nw.append(h1_pe_prev)
        B.emit("scalar", lambda e: e.activation(
            h1b[:, :], x_buf[:, :], AF.Identity, bias=st1[:, 8:9], scale=st1[:, 7:8]),
            tuple(nw), (SE, 1))
        h1_tok = (SE, SE.n)
        x_free = h1_tok
        ev2 = {}
        for grp in range(8):
            bank = (grp % 2) * 512
            gw = [h1_tok] if grp == 0 else []
            if tt == 0 and grp == 0:
                gw += list(a1_bar)
            if grp >= 2:
                gw.append(ev2[grp - 2])
            elif tt > 0:
                gw.append(h1T_prev[grp])
            pb = (grp % 2) * 1024
            for j in range(4):
                ic = grp * 4 + j
                B.emit("tensor", lambda e, ic=ic, pb=pb, j=j: e.transpose(
                    PSb[:, pb + j * 128:pb + (j + 1) * 128],
                    h1b[:, ic * 128:(ic + 1) * 128], ident_b[:, :]),
                    tuple(gw) if j == 0 else (), (PE, 1) if j == 3 else None)
            pe_tok = (PE, PE.n)
            dst = h1T[:, :].rearrange("p (ic t) -> p ic t", ic=NIC)[
                :, grp * 4:(grp + 1) * 4, tt * 128:(tt + 1) * 128]
            B.emit("vector", lambda e, dst=dst, pb=pb: e.tensor_copy(
                dst, PSb[:, pb:pb + 512].rearrange("p (a b) -> p a b", a=4)),
                (pe_tok,), (VE, 1))
            ev2[grp] = (VE, VE.n)
        h1T_prev = dict(ev2)
        h1T_prev[0] = ev2[6]
        h1T_prev[1] = ev2[7]
        h1T_last = ev2[7]
        h1_pe_prev = (PE, PE.n)
        if debug and tt == 0:
            h1_pe_prev = dma("gpsimd", dbg["dbg_h1b"], h1b[:, :], (h1_tok, ev2[7]), st="g_dbg")
            dbg_written.add("dbg_h1b")
        fold_step(7)
    fold_step(4)
    h1T_done = h1T_last
    if debug:
        for _q in range(8):
            dma("gpsimd", dbg["dbg_h1T"][:, _q * 4096:(_q + 1) * 4096],
                h1T[:, _q * 4096:(_q + 1) * 4096], (h1T_done,), st="g_dbg")
        dbg_written.add("dbg_h1T")

    if stage == "A1":
        return _finish()
    def proj_pass(kind, out_tiles, wsrc, wcols, cast, out_sb, out_off, scale=None,
                  extra_wait=(), hook=None):
        ntile = len(out_tiles)
        c0, c1 = wcols
        ncol = c1 - c0
        q = "gpsimd" if cast else "sync"
        blk_tok = {}
        blk_pe = {}

        NB = 8
        BL = 4

        def load_blk(icb, lw):
            src = wsrc.rearrange("(ic p) o -> p ic o", p=128)[
                :, icb * BL:(icb + 1) * BL, c0:c1]
            dstv = ws_buf[icb % 2][:, 0:BL * ncol].rearrange(
                "p (ic o) -> p ic o", ic=BL)
            blk_tok[icb] = dma(q, dstv, src, lw,
                               st=f"ws{'g' if cast else 's'}{icb % 2}")

        load_blk(0, extra_wait)
        load_blk(1, extra_wait)
        for ic in range(NIC):
            icb, j = ic // BL, ic % BL
            if j == 0 and hook:
                hook()
            wt = blk_tok[icb]
            for ti in range(ntile):
                if kind == "fm":
                    st_ap = ws_buf[icb % 2][:, j * ncol + ti * 128:
                                            j * ncol + (ti + 1) * 128]
                    for half in range(2):
                        last = (ic == NIC - 1 and ti == ntile - 1 and half == 1)
                        B.emit("tensor", lambda e, st_ap=st_ap, ic=ic, ti=ti, half=half: e.matmul(
                            PS[:, (2 * ti + half) * 512:(2 * ti + half + 1) * 512],
                            st_ap,
                            h1T[:, ic * T + half * 512: ic * T + (half + 1) * 512],
                            start=(ic == 0), stop=(ic == NIC - 1)),
                            ((wt, h1T_done) + tuple(extra_wait))
                            if (ic == 0 and ti == 0 and half == 0) else (wt,),
                            (PE, 1) if (last or (j == BL - 1 and ti == ntile - 1 and half == 1))
                            else None)
                else:
                    tile = out_tiles[ti]
                    last = (ic == NIC - 1 and ti == ntile - 1)
                    B.emit("tensor", lambda e, ic=ic, tile=tile, ti=ti, icb=icb, j=j: e.matmul(
                        PS[:, ti * 512:(ti + 1) * 512],
                        h1T[:, ic * T + tile * 128:ic * T + (tile + 1) * 128],
                        ws_buf[icb % 2][:, j * ncol:(j + 1) * ncol],
                        start=(ic == 0), stop=(ic == NIC - 1)),
                        ((wt, h1T_done) + tuple(extra_wait))
                        if (ic == 0 and ti == 0) else (wt,),
                        (PE, 1) if (last or (j == BL - 1 and ti == ntile - 1)) else None)
            if j == BL - 1:
                blk_pe[icb] = (PE, PE.n)
                if icb + 2 < NB:
                    load_blk(icb + 2, (blk_pe[icb],))
        mm_tok = (PE, PE.n)
        ev = None
        for ti in range(ntile):
            tile = out_tiles[ti]
            if kind == "fm":
                psl = PS[:, ti * 1024:(ti + 1) * 1024]
                dst = out_sb[:, out_off + tile * T: out_off + (tile + 1) * T]
            else:
                psl = PS[:, ti * 512:(ti + 1) * 512]
                dst = out_sb[:, tile * KV + out_off: tile * KV + out_off + ncol]
            if scale is None:
                B.emit("vector", lambda e, dst=dst, psl=psl: e.tensor_copy(dst, psl),
                       (mm_tok,), (VE, 1))
            else:
                B.emit("vector", lambda e, dst=dst, psl=psl: e.tensor_scalar(
                    dst, psl, scale, None, op0=ALU.mult), (mm_tok,), (VE, 1))
            ev = (VE, VE.n)
        return ev

    def a2_hook():
        pass

    kev = proj_pass("fm", [0, 1, 2], wk, (0, 384), True, kT_own, 0,
                    extra_wait=(fold_state["done"],), hook=a2_hook)
    kev = proj_pass("fm", [3, 4, 5], wk, (384, 768), True, kT_own, 0,
                    extra_wait=(kev,), hook=a2_hook)
    kev = proj_pass("fm", [6, 7], wk, (768, 1024), True, kT_own, 0,
                    extra_wait=(kev,), hook=a2_hook)
    vev = kev
    for tg in range(2):
        for vf in range(2):
            vev = proj_pass("tm", [tg * 4 + i for i in range(4)], wv,
                            (vf * 512, (vf + 1) * 512), True, v_own, vf * 512,
                            extra_wait=(vev,), hook=a2_hook)
    assert fold_state["done"] is not None, "fold must complete in P0+A1"
    if debug:
        for _q in range(4):
            dma("gpsimd", dbg["dbg_wqe"][_q * 1024:(_q + 1) * 1024, :],
                wqe_d[_q * 1024:(_q + 1) * 1024, :], (fold_state["done"],),
                st="g_dbg")
        dbg_written.add("dbg_wqe")
    st_toks = []
    for kc in range(8):
        st_toks.append(dma("sync", kv_in[kc * 128:(kc + 1) * 128, :],
                           kT_own[:, kc * T:(kc + 1) * T], (kev,), st="s_kvst"))
    for tt in range(8):
        st_toks.append(dma("sync", kv_in[T + tt * 128: T + (tt + 1) * 128, :],
                           v_own[:, tt * KV:(tt + 1) * KV], (vev,), st="s_kvst"))
    B.emit("gpsimd", lambda e: e.collective_compute(
        "AllGather", ALU.bypass,
        replica_groups=[[0, 1, 2, 3], [4, 5, 6, 7]],
        ins=[kv_in[:, :]], outs=[kv_out[:, :]]),
        tuple(st_toks), (CC, 1))
    cc_done = (CC, CC.n)
    qev = proj_pass("fm", [0, 1, 2], wqe_d, (0, 384), False, qT, 0, scale=QSCALE,
                    extra_wait=(vev, fold_state["done"]), hook=a2_hook)
    qev = proj_pass("fm", [3, 4, 5], wqe_d, (384, 768), False, qT, 0, scale=QSCALE,
                    extra_wait=(qev,), hook=a2_hook)
    qev = proj_pass("fm", [6, 7], wqe_d, (768, 1024), False, qT, 0, scale=QSCALE,
                    extra_wait=(qev,), hook=a2_hook)
    q_done = qev

    dbg_toks = []
    if debug:
        dbg_toks.append(dma("gpsimd", dbg["dbg_kT"], kT_own[:, :], (kev,), st="g_dbg"))
        dbg_toks.append(dma("gpsimd", dbg["dbg_v"], v_own[:, :], (vev,), st="g_dbg"))
        dbg_toks.append(dma("gpsimd", dbg["dbg_qT"], qT[:, :], (q_done,), st="g_dbg"))
        dbg_written.update(["dbg_kT", "dbg_v", "dbg_qT"])

    if stage == "A2":
        return _finish()

    emit_bw_prep(st_toks[-1], q_done)
    bwT_done = bw_state["done"]

    if stage == "AG":
        return _finish()
    v_toks = []
    for r in range(4):
        src = kv_out[2048 * r + 1024: 2048 * r + 2048, :].rearrange(
            "(kc p) f -> p kc f", p=128)
        dstv = v_sb[:, :].rearrange("p (kc f) -> p kc f", kc=NKC)[
            :, r * 8:(r + 1) * 8, :]
        v_toks.append(dma("gpsimd", dstv, src,
                          (cc_done, q_done) + tuple(dbg_toks), st="g_vsb"))
    v_ready = v_toks[-1]

    kh_tok = None
    last_smm = None
    prev_av_ev = None
    slot_free = {0: None, 1: [bw_state["last_evac"]]}
    slot_exp = {0: None, 1: None}
    prev_D_pe2 = None
    for h in range(NH):
        kw = [cc_done] if kh_tok is None else [last_smm]
        kts = []
        for r in range(4):
            kts.append(dma("gpsimd", kh_sb[:, r * 1024:(r + 1) * 1024],
                           kv_out[2048 * r + h * 128: 2048 * r + (h + 1) * 128, :],
                           tuple(kw) if r == 0 else (), st="g_kh"))
        kh_tok = kts[-1]
        for qh in range(2):
            pt_evs = []
            for qtl in range(4):
                qt = qh * 4 + qtl
                exp_toks = []
                negm_toks = []
                for c in range(2):
                    base = c * 2048
                    for m in range(4):
                        ws = [kh_tok, q_done]
                        if m == 0:
                            if slot_exp[c] is not None:
                                ws.append(slot_exp[c])
                            if slot_free[c] is not None:
                                ws += slot_free[c]
                            if c == 0 and prev_av_ev is not None:
                                ws.append(prev_av_ev)
                        B.emit("tensor", lambda e, h=h, qt=qt, base=base, m=m: e.matmul(
                            PS[:, base + m * 512:base + (m + 1) * 512],
                            qT[:, h * T + qt * 128:h * T + (qt + 1) * 128],
                            kh_sb[:, base + m * 512:base + (m + 1) * 512],
                            start=True, stop=True),
                            tuple(ws), (PE, 1) if m == 3 else None)
                    smm_c = (PE, PE.n)
                    sub = PS[:, base:base + 2048].rearrange(
                        "p (a b) -> p b a", b=8)[:, 0, :]
                    B.emit("vector", lambda e, sub=sub, c=c: e.reduce_max(
                        st2[:, c:c + 1], sub, axis=AXX, negate=True),
                        (smm_c,), (VE, 1))
                    negm_toks.append((VE, VE.n))
                    B.emit("scalar", lambda e, base=base, c=c: e.activation(
                        P_sb[:, base:base + 2048], PS[:, base:base + 2048], AF.Exp,
                        bias=st2[:, c:c + 1], scale=1.0,
                        accum_out=st2[:, 2 + c:3 + c]), (negm_toks[-1],), (SE, 1))
                    exp_toks.append((SE, SE.n))
                    slot_exp[c] = exp_toks[-1]
                last_smm = (PE, PE.n)
                B.emit("vector", lambda e: e.tensor_tensor(
                    st2[:, 4:5], st2[:, 0:1], st2[:, 1:2], op=ALU.min),
                    (negm_toks[1],), (VE, 1))
                B.emit("vector", None, ((VE, VE.n),))
                B.emit("vector", lambda e: e.tensor_scalar(
                    st2[:, 6:7], st2[:, 4:5], st2[:, 0:1], None, op0=ALU.subtract),
                    (), (VE, 1))
                B.emit("vector", None, ((VE, VE.n),))
                B.emit("vector", lambda e: e.tensor_scalar(
                    st2[:, 7:8], st2[:, 4:5], st2[:, 1:2], None, op0=ALU.subtract),
                    (), (VE, 1))
                B.emit("vector", None, ((VE, VE.n),))
                B.emit("scalar", lambda e: e.activation(
                    st2[:, 8:10], st2[:, 6:8], AF.Exp), (exp_toks[1],), (SE, 1))
                sexp_tok = (SE, SE.n)
                B.emit("vector", lambda e: e.tensor_mul(
                    st2[:, 10:11], st2[:, 2:3], st2[:, 8:9]), (sexp_tok,), (VE, 1))
                B.emit("vector", None, ((VE, VE.n),))
                B.emit("vector", lambda e: e.tensor_mul(
                    st2[:, 11:12], st2[:, 3:4], st2[:, 9:10]), (), (VE, 1))
                B.emit("vector", None, ((VE, VE.n),))
                B.emit("vector", lambda e: e.tensor_add(
                    st2[:, 12:13], st2[:, 10:11], st2[:, 11:12]), (), (VE, 1))
                B.emit("vector", None, ((VE, VE.n),))
                B.emit("vector", lambda e: e.reciprocal(
                    st2[:, 13:14], st2[:, 12:13]), (), (VE, 1))
                B.emit("vector", None, ((VE, VE.n),))
                B.emit("vector", lambda e: e.tensor_scalar(
                    st2[:, 14:15], st2[:, 8:9], st2[:, 13:14], None, op0=ALU.mult),
                    (), (VE, 1))
                B.emit("vector", None, ((VE, VE.n),))
                B.emit("vector", lambda e: e.tensor_scalar(
                    st2[:, 15:16], st2[:, 9:10], st2[:, 13:14], None, op0=ALU.mult),
                    (), (VE, 1))
                B.emit("vector", None, ((VE, VE.n),))
                dw = [prev_D_pe2] if prev_D_pe2 is not None else []
                B.emit("vector", lambda e: e.tensor_scalar(
                    D_sb[:, 0:128], ident_b[:, :], st2[:, 14:15], None, op0=ALU.mult),
                    tuple(dw), (VE, 1))
                B.emit("vector", lambda e: e.tensor_scalar(
                    D_sb[:, 128:256], ident_b[:, :], st2[:, 15:16], None, op0=ALU.mult),
                    (), (VE, 1))
                D_tok = (VE, VE.n)
                for c in range(2):
                    evs = {}
                    for grp in range(4):
                        pb = c * 2048 + (grp % 2) * 512
                        gw = [D_tok] if grp == 0 else []
                        if grp >= 2:
                            gw.append(evs[grp - 2])
                        for j in range(4):
                            kc = c * 16 + grp * 4 + j
                            B.emit("tensor", lambda e, kc=kc, pb=pb, j=j, c=c: e.matmul(
                                PS[:, pb + j * 128:pb + (j + 1) * 128],
                                P_sb[:, kc * 128:(kc + 1) * 128],
                                D_sb[:, c * 128:(c + 1) * 128],
                                start=True, stop=True),
                                tuple(gw) if j == 0 else (),
                                (PE, 1) if j == 3 else None)
                        pe_tok = (PE, PE.n)
                        dst = PT_half[:, :].rearrange("p (kc t) -> p kc t", kc=NKC)[
                            :, c * 16 + grp * 4:c * 16 + (grp + 1) * 4,
                            qtl * 128:(qtl + 1) * 128]
                        psrc = PS[:, pb:pb + 512].rearrange("p (a b) -> p a b", a=4)
                        if grp % 2 == 0:
                            B.emit("vector", lambda e, dst=dst, psrc=psrc: e.tensor_copy(
                                dst, psrc), (pe_tok,), (VE, 1))
                            evs[grp] = (VE, VE.n)
                        else:
                            B.emit("scalar", lambda e, dst=dst, psrc=psrc: e.copy(
                                dst, psrc), (pe_tok,), (SE, 1))
                            evs[grp] = (SE, SE.n)
                    slot_free[c] = [evs[2], evs[3]]
                    pt_evs += [evs[2], evs[3]]
                prev_D_pe2 = (PE, PE.n)
            pt_v_last = pt_evs[-2]
            pt_s_last = pt_evs[-1]
            for kc in range(NKC):
                ws = []
                if kc == 0:
                    ws = list(pt_evs) + [v_ready]
                    if prev_av_ev is not None:
                        ws.append(prev_av_ev)
                B.emit("tensor", lambda e, kc=kc, h=h: e.matmul(
                    PS[:, 0:512],
                    v_sb[:, kc * KV + h * 128: kc * KV + (h + 1) * 128],
                    PT_half[:, kc * 512:(kc + 1) * 512],
                    start=(kc == 0), stop=(kc == NKC - 1)),
                    tuple(ws), (PE, 1) if kc == NKC - 1 else None)
            av_tok = (PE, PE.n)
            B.emit("vector", lambda e, h=h, qh=qh: e.tensor_copy(
                vals_sb[:, h * T + qh * 512: h * T + (qh + 1) * 512], PS[:, 0:512]),
                (av_tok,), (VE, 1))
            prev_av_ev = (VE, VE.n)
    attn_done = prev_av_ev

    if debug:
        dma("gpsimd", dbg["dbg_vals"], vals_sb[:, :], (attn_done,), st="g_dbg")
        dbg_written.add("dbg_vals")

    if stage == "A4":
        return _finish()
    wo_toks = []
    for kvc in range(8):
        wo_toks.append(dma("gpsimd", wo_sb[:, kvc * D:(kvc + 1) * D],
                           wo[kvc * 128:(kvc + 1) * 128, :],
                           (attn_done,) if kvc == 0 else (wo_toks[-1],),
                           st="g_wo"))
    atst_free = [None, None]
    x5_free = None
    prev_ev_tok = None
    a5_h2_st = []
    for tt in range(NTT):
        bi = tt % 2
        for kvc in range(8):
            ws = []
            if kvc == 0:
                ws = [wo_toks[0], attn_done]
                if atst_free[bi] is not None:
                    ws.append(atst_free[bi])
                if prev_ev_tok is not None:
                    ws.append(prev_ev_tok)
            else:
                ws = [wo_toks[kvc]]
            for ob in range(8):
                B.emit("tensor", lambda e, kvc=kvc, ob=ob, tt=tt: e.matmul(
                    PS[:, ob * 512:(ob + 1) * 512],
                    vals_sb[:, kvc * T + tt * 128: kvc * T + (tt + 1) * 128],
                    wo_sb[:, kvc * D + ob * 512: kvc * D + (ob + 1) * 512],
                    start=(kvc == 0), stop=(kvc == 7)),
                    tuple(ws) if ob == 0 else (),
                    (PE, 1) if (kvc == 7 and ob == 7) else None)
        wo_tok = (PE, PE.n)
        B.emit("vector", lambda e, bi=bi: e.tensor_copy(
            atst[bi][:, 0:2048], PS[:, 0:2048]), (wo_tok,), (VE, 1))
        B.emit("vector", lambda e, bi=bi: e.tensor_copy(
            atst[bi][:, 2048:4096], PS[:, 2048:4096]), (), (VE, 1))
        ev_tok = (VE, VE.n)
        prev_ev_tok = ev_tok
        B.emit("vector", None, (ev_tok,))
        B.emit("scalar", None, (ev_tok,))
        rstd_tok, nb_tok = ln_stats(atst[bi][:, :])
        lnw = [nb_tok]
        if tt > 0:
            lnw.append(prev_x1_tok)
        else:
            lnw.append(bw_state["done"])
        B.emit("scalar", lambda e, bi=bi: e.activation(
            ln_sb[:, :], atst[bi][:, :], AF.Identity, bias=st1[:, 8:9], scale=st1[:, 7:8]),
            tuple(lnw), (SE, 1))
        ln_tok = (SE, SE.n)
        atst_free[bi] = ln_tok
        xw = [attn_done] if tt == 0 else [x5_free, prev_t_h2]
        t_x = dma("sync", x5_buf[:, :], x_in[tt * 128:(tt + 1) * 128, :], tuple(xw), st="s_x5")
        B.emit("vector", lambda e: e.tensor_add(x5_buf[:, :], x5_buf[:, :], ln_sb[:, :]),
               (t_x, ln_tok), (VE, 1))
        x1_tok = (VE, VE.n)
        prev_x1_tok = x1_tok
        dma("sync", x1sp[tt * 128:(tt + 1) * 128, :], x5_buf[:, :], (x1_tok,), st="s_sp")
        B.emit("vector", None, (x1_tok,))
        B.emit("scalar", None, (x1_tok,))
        rstd2, nb2 = ln_stats(x5_buf[:, :])
        B.emit("vector", lambda e: e.tensor_scalar(
            h2b[:, :], x5_buf[:, :], st1[:, 2:3], st1[:, 7:8],
            op0=ALU.add, op1=ALU.mult), (rstd2, nb2), (VE, 1))
        h2_tok = (VE, VE.n)
        x5_free = h2_tok
        a5_h2_st.append(dma("sync", h2sp[tt * 128:(tt + 1) * 128, :], h2b[:, :],
                            (h2_tok,), st="s_sp"))
        prev_t_h2 = a5_h2_st[-1]
        if debug:
            dma("gpsimd", dbg["dbg_h2"][tt * 128:(tt + 1) * 128, :], h2b[:, :], (h2_tok,), st="g_dbg")
            dbg_written.add("dbg_h2")
    a5_done = a5_h2_st[-1]

    if stage == "A5":
        return _finish()
    h2rd_free = None
    stage_evs = []
    ep_done = []
    slot_free = [None, None]
    x1_free = None
    for hf in range(2):
        tr_last = None
        for ttl in range(4):
            tt = hf * 4 + ttl
            w = [a5_done]
            if h2rd_free is not None:
                w.append(h2rd_free)
            t_h2 = dma("sync", h2b[:, :], h2sp[tt * 128:(tt + 1) * 128, :], tuple(w), st="s_h2rd")
            evs = {}
            for grp in range(8):
                bank = (grp % 2) * 512
                gw = [t_h2] if grp == 0 else []
                if hf == 1 and ttl == 0 and grp == 0:
                    gw.append(ep_done[-1])
                if grp >= 2:
                    gw.append(evs[grp - 2])
                elif ttl > 0:
                    gw.append(prev_evs[grp + 6])
                pb = (grp % 2) * 1024
                for j in range(4):
                    ic = grp * 4 + j
                    B.emit("tensor", lambda e, ic=ic, pb=pb, j=j: e.transpose(
                        PSb[:, pb + j * 128:pb + (j + 1) * 128],
                        h2b[:, ic * 128:(ic + 1) * 128], ident_b[:, :]),
                        tuple(gw) if j == 0 else (), (PE, 1) if j == 3 else None)
                pe_tok = (PE, PE.n)
                dst = h2T[:, :].rearrange("p (ic t) -> p ic t", ic=NIC)[
                    :, grp * 4:(grp + 1) * 4, ttl * 128:(ttl + 1) * 128]
                B.emit("vector", lambda e, dst=dst, pb=pb: e.tensor_copy(
                    dst, PSb[:, pb:pb + 512].rearrange("p (a b) -> p a b", a=4)),
                    (pe_tok,), (VE, 1))
                evs[grp] = (VE, VE.n)
            prev_evs = evs
            h2rd_free = evs[7]
            tr_last = evs[7]
        h2T_tok = tr_last

        for ttl in range(4):
            for ic in range(NIC):
                B.emit("tensor", lambda e, ic=ic, ttl=ttl: e.matmul(
                    PS[:, ttl * 512: ttl * 512 + 80],
                    h2T[:, ic * 512 + ttl * 128: ic * 512 + (ttl + 1) * 128],
                    arT[:, ic * 80:(ic + 1) * 80],
                    start=(ic == 0), stop=(ic == NIC - 1)),
                    (h2T_tok, ar_done) if (ttl == 0 and ic == 0) else (),
                    (PE, 1) if ic == NIC - 1 else None)
        ar_mm = (PE, PE.n)
        for ttl in range(4):
            B.emit("vector", lambda e, ttl=ttl: e.tensor_copy(
                res16[:, ttl * 64:(ttl + 1) * 64], PS[:, ttl * 512: ttl * 512 + 64]),
                (ar_mm,), (VE, 1))
            B.emit("vector", lambda e, ttl=ttl: e.tensor_copy(
                rtr[:, ttl * 16:(ttl + 1) * 16], PS[:, ttl * 512 + 64: ttl * 512 + 80]),
                (), (VE, 1))
        ar_ev = (VE, VE.n)
        if debug:
            dma("gpsimd", dbg["dbg_res"][hf * 512:(hf + 1) * 512, 0:64].rearrange(
                "(ttl p) r -> p ttl r", p=128),
                res16[:, :].rearrange("p (ttl r) -> p ttl r", ttl=4), (ar_ev,),
                st="g_dbg")
            dma("gpsimd", dbg["dbg_res"][hf * 512:(hf + 1) * 512, 64:80].rearrange(
                "(ttl p) r -> p ttl r", p=128),
                rtr[:, 0:64].rearrange("p (ttl r) -> p ttl r", ttl=4), (ar_ev,),
                st="g_dbg")

        ht_last = None
        for ttl in range(4):
            lg = rtr[:, ttl * 16:(ttl + 1) * 16]
            B.emit("vector", lambda e, lg=lg: e.reduce_max(
                rtr[:, 64:65], lg, axis=AXX, negate=True), (ar_ev,), (VE, 1))
            B.emit("scalar", lambda e, lg=lg: e.activation(
                rtr[:, 80:96], lg, AF.Exp, bias=rtr[:, 64:65], scale=1.0,
                accum_out=rtr[:, 65:66]), ((VE, VE.n),), (SE, 1))
            B.emit("vector", lambda e: e.reciprocal(rtr[:, 66:67], rtr[:, 65:66]),
                   ((SE, SE.n),), (VE, 1))
            B.emit("vector", None, ((VE, VE.n),))
            B.emit("vector", lambda e: e.tensor_scalar(
                rtr[:, 96:112], rtr[:, 80:96], rtr[:, 66:67], None, op0=ALU.mult),
                (), (VE, 1))
            B.emit("vector", None, ((VE, VE.n),))
            B.emit("vector", lambda e: e.reduce_max(
                rtr[:, 67:68], rtr[:, 96:112], axis=AXX), (), (VE, 1))
            B.emit("vector", None, ((VE, VE.n),))
            B.emit("vector", lambda e: e.tensor_scalar(
                rtr[:, 112:128], rtr[:, 96:112], rtr[:, 67:68], None, op0=ALU.is_equal),
                (), (VE, 1))
            B.emit("vector", None, ((VE, VE.n),))
            B.emit("vector", lambda e: e.tensor_sub(
                rtr[:, 128:144], rtr[:, 96:112], rtr[:, 112:128]), (), (VE, 1))
            B.emit("vector", None, ((VE, VE.n),))
            B.emit("vector", lambda e: e.reduce_max(
                rtr[:, 68:69], rtr[:, 128:144], axis=AXX), (), (VE, 1))
            B.emit("vector", None, ((VE, VE.n),))
            B.emit("vector", lambda e: e.tensor_scalar(
                rtr[:, 144:160], rtr[:, 128:144], rtr[:, 68:69], None, op0=ALU.is_equal),
                (), (VE, 1))
            B.emit("vector", None, ((VE, VE.n),))
            B.emit("vector", lambda e: e.tensor_scalar(
                rtr[:, 69:70], rtr[:, 67:68], rtr[:, 68:69], 1e-6,
                op0=ALU.add, op1=ALU.add), (), (VE, 1))
            B.emit("vector", None, ((VE, VE.n),))
            B.emit("vector", lambda e: e.reciprocal(rtr[:, 70:71], rtr[:, 69:70]),
                   (), (VE, 1))
            B.emit("vector", None, ((VE, VE.n),))
            B.emit("vector", lambda e: e.tensor_scalar(
                rtr[:, 71:72], rtr[:, 67:68], rtr[:, 70:71], 0.5,
                op0=ALU.mult, op1=ALU.mult), (), (VE, 1))
            B.emit("vector", None, ((VE, VE.n),))
            B.emit("vector", lambda e: e.tensor_scalar(
                rtr[:, 72:73], rtr[:, 68:69], rtr[:, 70:71], 0.5,
                op0=ALU.mult, op1=ALU.mult), (), (VE, 1))
            B.emit("vector", None, ((VE, VE.n),))
            B.emit("vector", lambda e: e.tensor_scalar(
                rtr[:, 112:128], rtr[:, 112:128], rtr[:, 71:72], None, op0=ALU.mult),
                (), (VE, 1))
            B.emit("vector", None, ((VE, VE.n),))
            B.emit("vector", lambda e: e.tensor_scalar(
                rtr[:, 144:160], rtr[:, 144:160], rtr[:, 72:73], None, op0=ALU.mult),
                (), (VE, 1))
            B.emit("vector", None, ((VE, VE.n),))
            B.emit("vector", lambda e: e.tensor_add(
                rtr[:, 112:128], rtr[:, 112:128], rtr[:, 144:160]), (), (VE, 1))
            B.emit("vector", None, ((VE, VE.n),))
            for e_i in range(16):
                B.emit("vector", lambda e, e_i=e_i, ttl=ttl: e.tensor_scalar(
                    H_sb[:, e_i * 64:(e_i + 1) * 64],
                    res16[:, ttl * 64:(ttl + 1) * 64],
                    rtr[:, 112 + e_i:113 + e_i], None, op0=ALU.mult),
                    (), (VE, 1) if e_i == 15 else None)
            h_tok = (VE, VE.n)
            if debug:
                dma("gpsimd", dbg["dbg_H"][(hf * 4 + ttl) * 128:(hf * 4 + ttl + 1) * 128, :],
                    H_sb[:, :], (h_tok,), st="g_dbg")
            for grp in range(2):
                pb = grp * 1024
                gw = [h_tok, ar_ev] if grp == 0 else []
                for j in range(4):
                    erc = grp * 4 + j
                    B.emit("tensor", lambda e, erc=erc, pb=pb, j=j: e.transpose(
                        PSb[:, pb + j * 128:pb + (j + 1) * 128],
                        H_sb[:, erc * 128:(erc + 1) * 128], ident_b[:, :]),
                        tuple(gw) if j == 0 else (), (PE, 1) if j == 3 else None)
                pe_tok = (PE, PE.n)
                dst = HT_sb[:, :].rearrange("p (erc t) -> p erc t", erc=8)[
                    :, grp * 4:(grp + 1) * 4, ttl * 128:(ttl + 1) * 128]
                B.emit("vector", lambda e, dst=dst, pb=pb: e.tensor_copy(
                    dst, PSb[:, pb:pb + 512].rearrange("p (a b) -> p a b", a=4)),
                    (pe_tok,), (VE, 1))
                ht_last = (VE, VE.n)
        ht_done = ht_last

        for oe in range(16):
            gi = hf * 16 + oe
            si = gi % 2
            w = [bwT_done]
            if slot_free[si] is not None:
                w.append(slot_free[si])
            if gi == 0:
                w.append(a5_done)
            t_w1 = dma("sync", bwT_s[si][:, :].rearrange("p (ic o) -> p ic o", ic=NIC),
                       bwT_d[oe].rearrange("(ic p) o -> p ic o", p=128), tuple(w),
                       st=f"s_sl{si}")
            t_w2 = dma("sync", bfT_s[si][:, :].rearrange("p (erc o) -> p erc o", erc=8),
                       bfT_d[oe].rearrange("(erc p) o -> p erc o", p=128),
                       (bft_done,) + ((slot_free[si],) if slot_free[si] else ()),
                       st=f"s_sl{si}")
            goff = (gi % 2) * 2048
            for ic in range(NIC):
                for ttl in range(4):
                    ws = []
                    if ic == 0 and ttl == 0:
                        ws = [t_w1, t_w2, ht_done]
                        if len(stage_evs) >= 2:
                            ws.append(stage_evs[gi - 2])
                    B.emit("tensor", lambda e, ic=ic, ttl=ttl, si=si, goff=goff: e.matmul(
                        PS[:, goff + ttl * 512: goff + ttl * 512 + 256],
                        h2T[:, ic * 512 + ttl * 128: ic * 512 + (ttl + 1) * 128],
                        bwT_s[si][:, ic * 256:(ic + 1) * 256],
                        start=(ic == 0), stop=False),
                        tuple(ws), None)
            for erc in range(8):
                for ttl in range(4):
                    ws = [t_w2] if (erc == 0 and ttl == 0) else []
                    B.emit("tensor", lambda e, erc=erc, ttl=ttl, si=si, goff=goff: e.matmul(
                        PS[:, goff + ttl * 512: goff + ttl * 512 + 256],
                        HT_sb[:, erc * 512 + ttl * 128: erc * 512 + (ttl + 1) * 128],
                        bfT_s[si][:, erc * 256:(erc + 1) * 256],
                        start=False, stop=(erc == 7)),
                        tuple(ws), (PE, 1) if (erc == 7 and ttl == 3) else None)
            mm_tok = (PE, PE.n)
            slot_free[si] = mm_tok
            for ttl in range(4):
                B.emit("vector", lambda e, ttl=ttl, oe=oe, goff=goff: e.tensor_tensor(
                    stg_sb[:, ttl * D + oe * 256: ttl * D + (oe + 1) * 256],
                    PS[:, goff + ttl * 512: goff + ttl * 512 + 256],
                    bb16[:, oe * 256:(oe + 1) * 256], op=ALU.add),
                    (mm_tok, bb_ready), (VE, 1) if ttl == 3 else None)
            stage_evs.append((VE, VE.n))
        pass_done = stage_evs[-1]

        for ttl in range(4):
            tt = hf * 4 + ttl
            stg = stg_sb[:, ttl * D:(ttl + 1) * D]
            B.emit("vector", None, (pass_done,))
            B.emit("scalar", None, (pass_done,))
            rstd3, nb3 = ln_stats(stg)
            lnw = [nb3] + ([ep_done[-1]] if ep_done else [])
            B.emit("scalar", lambda e, stg=stg: e.activation(
                ln_sb[:, :], stg, AF.Identity, bias=st1[:, 8:9], scale=st1[:, 7:8]),
                tuple(lnw), (SE, 1))
            ln_tok = (SE, SE.n)
            xw = [a5_done]
            if x1_free:
                xw.append(x1_free)
            t_x1 = dma("sync", x1buf[:, :], x1sp[tt * 128:(tt + 1) * 128, :], tuple(xw), st="s_x1rd")
            B.emit("vector", lambda e: e.tensor_add(
                x1buf[:, :], x1buf[:, :], ln_sb[:, :]), (t_x1, ln_tok), (VE, 1))
            fin_tok = (VE, VE.n)
            x1_free = dma("sync", out_d[tt * 128:(tt + 1) * 128, :], x1buf[:, :],
                          (fin_tok,), st="s_out")
            ep_done.append(fin_tok)
        if debug:
            dma("gpsimd", dbg["dbg_stage"][hf * 512:(hf + 1) * 512, :].rearrange(
                "(ttl p) o -> p ttl o", p=128),
                stg_sb[:, :].rearrange("p (ttl o) -> p ttl o", ttl=4), (pass_done,),
                st="g_dbg")

    return _finish()


_NC_CACHE = {}


def _get_nc(debug=DEBUG):
    if debug not in _NC_CACHE:
        _NC_CACHE[debug] = build_program(debug)
    return _NC_CACHE[debug]


def kernel(L0_attn_x, L0_attn_wq, L0_attn_wk, L0_attn_wv, L0_attn_wo,
           L0_ffn_base_w, L0_ffn_base_b, L0_ffn_A, L0_ffn_B, L0_ffn_router_w):
    x = np.ascontiguousarray(np.asarray(L0_attn_x, dtype=np.float32))
    wq_ = np.ascontiguousarray(np.asarray(L0_attn_wq, dtype=np.float32))
    wk_ = np.ascontiguousarray(np.asarray(L0_attn_wk, dtype=np.float32))
    wv_ = np.ascontiguousarray(np.asarray(L0_attn_wv, dtype=np.float32))
    wo_ = np.ascontiguousarray(np.asarray(L0_attn_wo, dtype=np.float32))
    bw_ = np.ascontiguousarray(np.asarray(L0_ffn_base_w, dtype=np.float32))
    bb_ = np.ascontiguousarray(np.asarray(L0_ffn_base_b, dtype=np.float32))
    A_ = np.ascontiguousarray(np.asarray(L0_ffn_A, dtype=np.float32))
    B_ = np.ascontiguousarray(np.asarray(L0_ffn_B, dtype=np.float32))
    rw_ = np.ascontiguousarray(np.asarray(L0_ffn_router_w, dtype=np.float32))

    nc = _get_nc()
    xf = x.reshape(8192, 4096)
    import ml_dtypes
    idf = np.eye(128, dtype=np.float32)
    idb = np.eye(128).astype(ml_dtypes.bfloat16)
    in_maps = [{
        "x": xf[c * 1024:(c + 1) * 1024], "wq": wq_, "wk": wk_, "wv": wv_,
        "wo": wo_, "bw": bw_, "bb": bb_, "A": A_, "B": B_, "rw": rw_,
        "idf": idf, "idb": idb,
    } for c in range(8)]
    res = run_bass_kernel_spmd(nc, in_maps, core_ids=list(range(8)))
    shards = [res.results[c]["out"] for c in range(8)]
    return np.concatenate(shards, axis=0).reshape(2, 4096, 4096).astype(np.float32)


if __name__ == "__main__":
    build_program()
    print("build ok")


# version 09
# speedup vs baseline: 1.1753x; 1.0013x over previous
import numpy as np
from contextlib import ExitStack

import concourse.bass as bass
import concourse.mybir as mybir
from concourse.bass_utils import run_bass_kernel_spmd

F32 = mybir.dt.float32
BF16 = mybir.dt.bfloat16
U8 = mybir.dt.uint8
AF = mybir.ActivationFunctionType
ALU = mybir.AluOpType
AXX = mybir.AxisListType.X

D = 4096
T = 1024
NTT = 8
NIC = 32
NH = 8
KV = 1024
S = 4096
NKC = 32
EPS = 1e-5
QSCALE = float(1.0 / np.sqrt(128.0))

DEBUG = False


class Sem:
    __slots__ = ("h", "n")

    def __init__(self, h):
        self.h = h
        self.n = 0


class Builder:
    ENGINES = ("sync", "gpsimd", "tensor", "vector", "scalar")

    def __init__(self):
        self.ops = {e: [] for e in self.ENGINES}

    def emit(self, eng, fn, waits=(), inc=None):
        ws = []
        for s, v in waits:
            assert v <= s.n, f"wait on future sem value {v} > {s.n} ({eng})"
            if v > 0:
                ws.append((s.h, v))
        if inc is not None:
            s, k = inc
            s.n += k
            inc = (s.h, k)
        self.ops[eng].append((fn, ws, inc))

    def replay(self, block):
        def mk(eng_name):
            def run(e):
                seen = {}
                for fn, ws, inc in self.ops[eng_name]:
                    merged = {}
                    for h, v in ws:
                        if v > merged.get(h.name, (None, -1))[1]:
                            merged[h.name] = (h, v)
                    for h, v in merged.values():
                        if seen.get(h.name, -1) >= v:
                            continue
                        seen[h.name] = v
                        e.wait_ge(h, v)
                    inst = fn(e) if fn is not None else None
                    if inc is not None:
                        h, k = inc
                        if inst is None:
                            inst = e.nop()
                        inst.then_inc(h, k)
            return run

        for eng_name in self.ENGINES:
            if self.ops[eng_name]:
                getattr(block, eng_name)(mk(eng_name))


def build_program(debug=DEBUG, stage="full"):
    nc = bass.Bass("TRN2", target_bir_lowering=False)

    x_in = nc.dram_tensor("x", [T, D], F32, kind="ExternalInput").ap()
    wq = nc.dram_tensor("wq", [D, D], F32, kind="ExternalInput").ap()
    wk = nc.dram_tensor("wk", [D, KV], F32, kind="ExternalInput").ap()
    wv = nc.dram_tensor("wv", [D, KV], F32, kind="ExternalInput").ap()
    wo = nc.dram_tensor("wo", [KV, D], F32, kind="ExternalInput").ap()
    bw = nc.dram_tensor("bw", [D, D], F32, kind="ExternalInput").ap()
    bb = nc.dram_tensor("bb", [D], F32, kind="ExternalInput").ap()
    A_in = nc.dram_tensor("A", [64, D], F32, kind="ExternalInput").ap()
    B_in = nc.dram_tensor("B", [16, D, 64], F32, kind="ExternalInput").ap()
    rw = nc.dram_tensor("rw", [16, D], F32, kind="ExternalInput").ap()
    idf_in = nc.dram_tensor("idf", [128, 128], F32, kind="ExternalInput").ap()
    idb_in = nc.dram_tensor("idb", [128, 128], BF16, kind="ExternalInput").ap()
    out_d = nc.dram_tensor("out", [T, D], F32, kind="ExternalOutput").ap()

    wqe_d = nc.dram_tensor("wqe_d", [D, KV], BF16, kind="Internal").ap()
    bwT_d = nc.dram_tensor("bwT_d", [16, D, 256], BF16, kind="Internal").ap()
    bfT_d = nc.dram_tensor("bfT_d", [16, KV, 256], BF16, kind="Internal").ap()
    kv_in = nc.dram_tensor("kv_in", [2 * T, KV], BF16, kind="Internal").ap()
    kv_out = nc.dram_tensor("kv_out", [8 * T, KV], BF16, kind="Internal").ap()
    x1sp = nc.dram_tensor("x1sp", [T, D], F32, kind="Internal").ap()
    h2sp = nc.dram_tensor("h2sp", [T, D], BF16, kind="Internal").ap()

    dbg = {}
    dbg_written = set()
    if debug:
        for nm, shp, dt in [
            ("dbg_qT", [128, NH * T], F32), ("dbg_kT", [128, NH * T], F32),
            ("dbg_v", [128, NTT * KV], F32), ("dbg_vals", [128, NH * T], F32),
            ("dbg_h2", [T, D], F32), ("dbg_res", [T, 80], F32),
            ("dbg_H", [T, KV], F32), ("dbg_stage", [T, D], F32),
            ("dbg_h1b", [128, D], F32), ("dbg_h1T", [128, NIC * T], F32),
            ("dbg_wqe", [D, KV], F32),
        ]:
            dbg[nm] = nc.dram_tensor(nm, shp, dt, kind="ExternalOutput").ap()

    ctx = ExitStack()
    CAP = 212_832
    arena = ctx.enter_context(nc.sbuf_tensor("arena", [128, CAP], U8))
    base = nc.lookup_mloc(arena).addr

    def place(name, shape, dtype, off):
        per = int(np.prod(shape[1:])) * mybir.dt.size(dtype)
        assert off % 32 == 0 and off + per <= CAP, (name, off, per)
        h = nc.virtual_sbuf_tensor(name, shape, dtype)
        nc._tensor(h.name, h.shape, h.dtype, type="SB")
        m = nc.lookup_mloc(h)
        m.allocated = True
        m.addr = base + off
        return h.ap()

    def place2(name, shape, dtype, off):
        per = int(np.prod(shape[1:])) * mybir.dt.size(dtype)
        return [place(f"{name}{i}", shape, dtype, off + i * per) for i in range(2)]

    ident_f = place("ident_f", [128, 128], F32, 0)
    ident_b = place("ident_b", [128, 128], BF16, 512)
    st1 = place("st1", [128, 64], F32, 768)
    st2 = place("st2", [128, 64], F32, 1024)
    D_sb = place("D_sb", [128, 256], BF16, 34432)
    rtr = place("rtr", [128, 160], F32, 1536)
    res16 = place("res16", [128, 4 * 64], BF16, 2176)
    H_sb = place("H_sb", [128, KV], BF16, 2688)
    arT = place("arT", [128, NIC * 80], BF16, 4736)
    bb16 = place("bb16", [128, D], BF16, 9856)
    ln_sb = place("ln_sb", [128, D], F32, 18048)
    sq_sb = place("sq_sb", [128, D], BF16, 34432)
    vals_sb = place("vals_sb", [128, NH * T], BF16, 42624)
    qT = place("qT", [128, NH * T], BF16, 59008)
    P_sb = place("P_sb", [128, S], BF16, 75392)
    OB = 83584
    FR = OB + 20480

    wq_stage = place2("wq_stage", [128, 2 * KV], F32, 18048)
    wqe_sb = place2("wqe_sb", [128, KV], BF16, OB)
    wq_stage2 = place2("wq_stage2", [128, 2 * KV], F32, 59008)
    bw_big = place2("bw_big", [128, D], F32, FR + 65536)
    bw_out = place2("bw_out", [128, NIC * 128], BF16, 18048)
    bft_sb = place("bft_sb", [128, 8 * D], BF16, FR)
    b_stage = place2("b_stage", [128, 2048], F32, FR + 65536)
    ar_stage = place("ar_stage", [64, D], BF16, FR + 81920)
    r_stage = place("r_stage", [16, D], BF16, FR + 90112)
    h1T = place("h1T", [128, NIC * T], BF16, FR)
    x_buf = place("x_buf", [128, D], F32, FR + 65536)
    h1b = place("h1b", [128, D], BF16, FR + 81920)
    kT_own = place("kT_own", [128, NH * T], BF16, FR + 65536)
    v_own = place("v_own", [128, NTT * KV], BF16, FR + 81920)
    ws_buf = place2("ws_buf", [128, 4 * KV], BF16, 18048)
    v_sb = place("v_sb", [128, NKC * KV], BF16, FR)
    PT_half = place("PT_half", [128, NKC * 512], BF16, FR + 65536)
    kh_sb = place("kh_sb", [128, S], BF16, FR + 98304)
    wo_sb = place("wo_sb", [128, 8 * D], BF16, FR)
    atst = place2("atst", [128, D], BF16, FR + 65536)
    x5_buf = place("x5_buf", [128, D], F32, FR + 81920)
    h2b = place("h2b", [128, D], BF16, FR + 98304)
    stg_sb = place("stg_sb", [128, 4 * D], BF16, FR)
    h2T = place("h2T", [128, NIC * 512], BF16, FR + 32768)
    bwT_s = place2("bwT_s", [128, NIC * 256], BF16, FR + 65536)
    bfT_s = place2("bfT_s", [128, 8 * 256], BF16, 75392)
    x1buf = place("x1buf", [128, D], F32, 59008)
    HT_sb = place("HT_sb", [128, 8 * 512], BF16, 42624)
    assert FR + 106496 <= OB + 129248

    PS = ctx.enter_context(nc.psum_tensor("PS", [128, 4096], F32))
    _psh = nc.virtual_psum_tensor("PSb", [128, 8192], BF16)
    nc._tensor(_psh.name, _psh.shape, _psh.dtype, type="PSUM")
    _psm = nc.lookup_mloc(_psh)
    _psm.allocated = True
    _psm.addr = nc.lookup_mloc(PS.tensor).addr if hasattr(PS, "tensor") else 0
    PSb = _psh.ap()

    sems = {nm: Sem(ctx.enter_context(nc.semaphore(nm)))
            for nm in ("LD", "GD", "PE", "VE", "SE", "GE", "CC")}
    LD, GD, PE, VE, SE, GE, CC = (sems[k] for k in
                                  ("LD", "GD", "PE", "VE", "SE", "GE", "CC"))

    B = Builder()

    def _finish():
        if stage != "full":
            B.emit("gpsimd", lambda e: e.memset(ln_sb[:, :], 0.0), (), (GE, 1))
            zt = (GE, GE.n)
            for _tt in range(NTT):
                dma("gpsimd", out_d[_tt * 128:(_tt + 1) * 128, :], ln_sb[:, :], (zt,), st="g_fill")
            if debug:
                for _nm, _ap in dbg.items():
                    if _nm in dbg_written:
                        continue
                    _views = ([_ap] if _ap.shape[0] == 128 else
                              [_ap[_r * 128:(_r + 1) * 128]
                               for _r in range(_ap.shape[0] // 128)])
                    for _v in _views:
                        nfree = int(np.prod(_v.shape[1:]))
                        _off = 0
                        while _off < nfree:
                            _w = min(4096, nfree - _off)
                            dma("gpsimd", _v[:, _off:_off + _w], ln_sb[:, 0:_w], (zt,), st="g_fill")
                            _off += _w
        allw = tuple((s, s.n) for s in dstreams.values()) + ((CC, CC.n),)
        B.emit("gpsimd", None, allw)
        if any(fn is not None or ws for fn, ws, inc in B.ops["sync"]):
            B.emit("sync", None, allw)
        with nc.Block() as block:
            B.replay(block)
        ctx.close()
        return nc

    dstreams = {}

    def dsem(name):
        if name not in dstreams:
            dstreams[name] = Sem(ctx.enter_context(nc.semaphore(name)))
        return dstreams[name]

    def dma(q, out, in_, waits=(), st=None):
        s = dsem(st if st is not None else ("s_misc" if q == "sync" else "g_misc"))
        B.emit(q, lambda e, o=out, i=in_: e.dma_start(out=o, in_=i), waits, (s, 16))
        return (s, s.n)

    dma("sync", ident_f[:, :], idf_in[:, :])
    ident_ready = dma("sync", ident_b[:, :], idb_in[:, :])

    if stage == "Z0":
        return _finish()
    dma("gpsimd", bb16[:, :], bb[None, :].broadcast_to([128, D]))

    if stage == "P0a":
        return _finish()
    dma("gpsimd", ar_stage[0:64, :], A_in[:, :])
    t_r = dma("gpsimd", r_stage[0:16, :], rw[:, :])
    bb_ready = t_a = t_r
    ar_ev_tok = None
    for ic in range(NIC):
        if ar_ev_tok is not None:
            B.emit("tensor", None, (ar_ev_tok,))
        B.emit("tensor", lambda e, ic=ic: e.transpose(
            PSb[:, 0:64], ar_stage[0:64, ic * 128:(ic + 1) * 128], ident_b[0:64, 0:64]),
            (t_a, ident_ready) if ic == 0 else ())
        B.emit("tensor", lambda e, ic=ic: e.transpose(
            PSb[:, 64:80], r_stage[0:16, ic * 128:(ic + 1) * 128], ident_b[0:16, 0:16]),
            (t_r,) if ic == 0 else (), (PE, 1))
        B.emit("scalar", lambda e, ic=ic: e.copy(
            arT[:, ic * 80:(ic + 1) * 80], PSb[:, 0:80]), ((PE, PE.n),), (SE, 1))
        ar_ev_tok = (SE, SE.n)
    ar_done = ar_ev_tok

    if stage == "P0b":
        return _finish()
    b_pe = {}
    bft_last = None
    for e_i in range(16):
        erc, lane = e_i // 2, (e_i % 2) * 64
        srcv = B_in[e_i].rearrange("(ot p) r -> p ot r", p=128)
        lw = (b_pe[e_i - 2],) if e_i >= 2 else ()
        t_b = dma("sync", b_stage[e_i % 2][:, :].rearrange(
            "p (ot r) -> p ot r", ot=32), srcv, lw, st=f"s_bst{e_i % 2}")
        for grp in range(8):
            if bft_last is not None:
                B.emit("tensor", None, (bft_last,))
            for j in range(4):
                ot = grp * 4 + j
                B.emit("tensor", lambda e, e_i=e_i, ot=ot, j=j: e.transpose(
                    PS[0:64, 1024 + j * 128:1024 + (j + 1) * 128],
                    b_stage[e_i % 2][:, ot * 64:(ot + 1) * 64], ident_f[:, :]),
                    (t_b, ident_ready) if (grp == 0 and j == 0) else (),
                    (PE, 1) if j == 3 else None)
            b_pe[e_i] = (PE, PE.n)
            B.emit("vector", lambda e, erc=erc, lane=lane, grp=grp: e.tensor_copy(
                bft_sb[lane:lane + 64, erc * D + grp * 512:erc * D + (grp + 1) * 512],
                PS[0:64, 1024:1536]), ((PE, PE.n),), (VE, 1))
            bft_last = (VE, VE.n)
    bft_st = []
    for oe in range(16):
        src = bft_sb[:, :].rearrange("p (erc o) -> p erc o", erc=8)[
            :, :, oe * 256:(oe + 1) * 256]
        dst = bfT_d[oe].rearrange("(erc p) o -> p erc o", p=128)
        bft_st.append(dma("sync", dst, src, (bft_last,), st="s_bftst"))
    bft_done = bft_st[-1]

    if stage == "P0c":
        return _finish()
    fold_state = {"done": None}

    def fold_gen():
        stages = list(wq_stage) + list(wq_stage2)
        mm_toks = []
        ev_tok = None
        st_toks = {}
        u = 0
        for ic in range(NIC):
            for hh in range(2):
                bi = u % 4
                lw = (mm_toks[u - 4],) if u >= 4 else ()
                t_w = dma("sync", stages[bi],
                          wq[ic * 128:(ic + 1) * 128, hh * 2048:(hh + 1) * 2048], lw,
                          st=f"s_wq{bi}")
                for hl in range(4):
                    h = 4 * hh + hl
                    for g in range(4):
                        ws = [t_w]
                        if hl == 0 and g == 0:
                            ws.append(ident_ready)
                            if ev_tok is not None and hh == 0:
                                ws.append(ev_tok)
                        B.emit("tensor", lambda e, bi=bi, hl=hl, g=g, h=h: e.matmul(
                            PS[:, 1536 + h * 128:1536 + (h + 1) * 128],
                            ident_f[:, :],
                            stages[bi][:, hl * 512 + g * 128: hl * 512 + (g + 1) * 128],
                            start=(g == 0), stop=(g == 3)),
                            tuple(ws), (PE, 1) if (hl == 3 and g == 3) else None)
                mm_toks.append((PE, PE.n))
                u += 1
                yield
            ew = [(PE, PE.n)]
            if ic >= 2:
                ew.append(st_toks[ic - 2])
            B.emit("vector", lambda e, wi=ic % 2: e.tensor_copy(
                wqe_sb[wi][:, :], PS[:, 1536:2560]), tuple(ew), (VE, 1))
            ev_tok = (VE, VE.n)
            st_toks[ic] = dma("sync", wqe_d[ic * 128:(ic + 1) * 128, :],
                              wqe_sb[ic % 2][:, :], (ev_tok,), st="s_wqest")
        fold_state["done"] = st_toks[NIC - 1]

    fold_it = fold_gen()

    def fold_step(k=1):
        for _ in range(k):
            if fold_state["done"] is None:
                next(fold_it, None)

    fold_step(12)

    bw_state = {"done": None}

    def emit_bw_prep(region_free_tok, pe_banks_free):
        ld_toks = {}
        ev_done = {}
        st_done = {}
        for ot in range(2):
            ld_toks[ot] = dma("sync", bw_big[ot], bw[ot * 128:(ot + 1) * 128, :],
                              (region_free_tok,), st=f"s_bw{ot}")
        for ot in range(32):
            bi = ot % 2
            evs = {}
            for grp in range(8):
                bank = 3072 + (grp % 2) * 512
                gw = [ld_toks[ot], ident_ready]
                if grp >= 2:
                    gw.append(evs[grp - 2])
                elif ot >= 1:
                    gw.append(ev_done[ot - 1])
                else:
                    gw.append(pe_banks_free)
                if grp <= 1 and ot >= 2:
                    gw.append(st_done[ot - 2])
                for j in range(4):
                    ic = grp * 4 + j
                    B.emit("tensor", lambda e, ic=ic, j=j, bank=bank, bi=bi: e.transpose(
                        PS[:, bank + j * 128:bank + (j + 1) * 128],
                        bw_big[bi][:, ic * 128:(ic + 1) * 128], ident_f[:, :]),
                        tuple(gw) if j == 0 else (), (PE, 1) if j == 3 else None)
                pe_tok = (PE, PE.n)
                B.emit("vector", lambda e, grp=grp, bi=bi, bank=bank: e.tensor_copy(
                    bw_out[bi][:, grp * 512:(grp + 1) * 512],
                    PS[:, bank:bank + 512]), (pe_tok,), (VE, 1))
                evs[grp] = (VE, VE.n)
            ev_done[ot] = evs[7]
            dstv = bwT_d[ot // 2][:, (ot % 2) * 128:(ot % 2) * 128 + 128].rearrange(
                "(ic p) o -> p ic o", p=128)
            st_done[ot] = dma("sync", dstv,
                              bw_out[bi][:, :].rearrange("p (ic o) -> p ic o", ic=NIC),
                              (evs[7],), st="s_bwst")
            if ot + 2 < 32:
                ld_toks[ot + 2] = dma("sync", bw_big[bi],
                                      bw[(ot + 2) * 128:(ot + 3) * 128, :],
                                      (ev_done[ot],), st=f"s_bw{bi}")
        bw_state["done"] = st_done[31]
        bw_state["last_evac"] = ev_done[31]

    if stage == "P0":
        return _finish()
    def vdrain():
        B.emit("vector", None, ((VE, VE.n),))

    def ln_stats(src_ap):
        B.emit("vector", lambda e: e.reduce_sum(st1[:, 0:1], src_ap, axis=AXX), (), (VE, 1))
        vdrain()
        B.emit("scalar", lambda e: e.activation(
            sq_sb[:, :], src_ap, AF.Square, accum_out=st1[:, 1:2]), (), (SE, 1))
        sq_tok = (SE, SE.n)
        B.emit("vector", lambda e: e.tensor_scalar(
            st1[:, 2:3], st1[:, 0:1], -1.0 / D, None, op0=ALU.mult), (), (VE, 1))
        vdrain()
        B.emit("vector", lambda e: e.tensor_scalar(
            st1[:, 3:4], st1[:, 1:2], 1.0 / D, None, op0=ALU.mult), (sq_tok,), (VE, 1))
        vdrain()
        B.emit("vector", lambda e: e.tensor_mul(st1[:, 4:5], st1[:, 2:3], st1[:, 2:3]),
               (), (VE, 1))
        vdrain()
        B.emit("vector", lambda e: e.tensor_scalar(
            st1[:, 5:6], st1[:, 3:4], st1[:, 4:5], EPS, op0=ALU.subtract, op1=ALU.add),
            (), (VE, 1))
        vdrain()
        B.emit("vector", lambda e: e.reciprocal(st1[:, 6:7], st1[:, 5:6]), (), (VE, 1))
        B.emit("scalar", lambda e: e.activation(st1[:, 7:8], st1[:, 6:7], AF.Sqrt),
               ((VE, VE.n),), (SE, 1))
        rstd_tok = (SE, SE.n)
        B.emit("scalar", None, (rstd_tok,))
        B.emit("vector", lambda e: e.tensor_mul(st1[:, 8:9], st1[:, 2:3], st1[:, 7:8]),
               (rstd_tok,), (VE, 1))
        vdrain()
        return rstd_tok, (VE, VE.n)

    x_free = None
    h1T_last = None
    a1_bar = ([(VE, VE.n), (SE, SE.n), (GE, GE.n), (PE, PE.n)] +
              [(s, s.n) for s in dstreams.values()])
    for tt in range(NTT):
        w = ((x_free,) if x_free else tuple(a1_bar))
        t_x = dma("sync", x_buf[:, :], x_in[tt * 128:(tt + 1) * 128, :], w, st="s_x")
        B.emit("vector", None, (t_x,))
        B.emit("scalar", None, (t_x,))
        rstd_tok, nb_tok = ln_stats(x_buf[:, :])
        nw = [nb_tok]
        if tt > 0:
            nw.append(h1_pe_prev)
        B.emit("scalar", lambda e: e.activation(
            h1b[:, :], x_buf[:, :], AF.Identity, bias=st1[:, 8:9], scale=st1[:, 7:8]),
            tuple(nw), (SE, 1))
        h1_tok = (SE, SE.n)
        x_free = h1_tok
        ev2 = {}
        for grp in range(8):
            bank = (grp % 2) * 512
            gw = [h1_tok] if grp == 0 else []
            if tt == 0 and grp == 0:
                gw += list(a1_bar)
            if grp >= 2:
                gw.append(ev2[grp - 2])
            elif tt > 0:
                gw.append(h1T_prev[grp])
            pb = (grp % 2) * 1024
            for j in range(4):
                ic = grp * 4 + j
                B.emit("tensor", lambda e, ic=ic, pb=pb, j=j: e.transpose(
                    PSb[:, pb + j * 128:pb + (j + 1) * 128],
                    h1b[:, ic * 128:(ic + 1) * 128], ident_b[:, :]),
                    tuple(gw) if j == 0 else (), (PE, 1) if j == 3 else None)
            pe_tok = (PE, PE.n)
            dst = h1T[:, :].rearrange("p (ic t) -> p ic t", ic=NIC)[
                :, grp * 4:(grp + 1) * 4, tt * 128:(tt + 1) * 128]
            B.emit("vector", lambda e, dst=dst, pb=pb: e.tensor_copy(
                dst, PSb[:, pb:pb + 512].rearrange("p (a b) -> p a b", a=4)),
                (pe_tok,), (VE, 1))
            ev2[grp] = (VE, VE.n)
        h1T_prev = dict(ev2)
        h1T_prev[0] = ev2[6]
        h1T_prev[1] = ev2[7]
        h1T_last = ev2[7]
        h1_pe_prev = (PE, PE.n)
        if debug and tt == 0:
            h1_pe_prev = dma("gpsimd", dbg["dbg_h1b"], h1b[:, :], (h1_tok, ev2[7]), st="g_dbg")
            dbg_written.add("dbg_h1b")
        fold_step(7)
    fold_step(4)
    h1T_done = h1T_last
    if debug:
        for _q in range(8):
            dma("gpsimd", dbg["dbg_h1T"][:, _q * 4096:(_q + 1) * 4096],
                h1T[:, _q * 4096:(_q + 1) * 4096], (h1T_done,), st="g_dbg")
        dbg_written.add("dbg_h1T")

    if stage == "A1":
        return _finish()
    def proj_pass(kind, out_tiles, wsrc, wcols, cast, out_sb, out_off, scale=None,
                  extra_wait=(), hook=None):
        ntile = len(out_tiles)
        c0, c1 = wcols
        ncol = c1 - c0
        q = "gpsimd" if cast else "sync"
        blk_tok = {}
        blk_pe = {}

        NB = 8
        BL = 4

        def load_blk(icb, lw):
            src = wsrc.rearrange("(ic p) o -> p ic o", p=128)[
                :, icb * BL:(icb + 1) * BL, c0:c1]
            dstv = ws_buf[icb % 2][:, 0:BL * ncol].rearrange(
                "p (ic o) -> p ic o", ic=BL)
            blk_tok[icb] = dma(q, dstv, src, lw,
                               st=f"ws{'g' if cast else 's'}{icb % 2}")

        load_blk(0, extra_wait)
        load_blk(1, extra_wait)
        for ic in range(NIC):
            icb, j = ic // BL, ic % BL
            if j == 0 and hook:
                hook()
            wt = blk_tok[icb]
            for ti in range(ntile):
                if kind == "fm":
                    st_ap = ws_buf[icb % 2][:, j * ncol + ti * 128:
                                            j * ncol + (ti + 1) * 128]
                    for half in range(2):
                        last = (ic == NIC - 1 and ti == ntile - 1 and half == 1)
                        B.emit("tensor", lambda e, st_ap=st_ap, ic=ic, ti=ti, half=half: e.matmul(
                            PS[:, (2 * ti + half) * 512:(2 * ti + half + 1) * 512],
                            st_ap,
                            h1T[:, ic * T + half * 512: ic * T + (half + 1) * 512],
                            start=(ic == 0), stop=(ic == NIC - 1)),
                            ((wt, h1T_done) + tuple(extra_wait))
                            if (ic == 0 and ti == 0 and half == 0) else (wt,),
                            (PE, 1) if (last or (j == BL - 1 and ti == ntile - 1 and half == 1))
                            else None)
                else:
                    tile = out_tiles[ti]
                    last = (ic == NIC - 1 and ti == ntile - 1)
                    B.emit("tensor", lambda e, ic=ic, tile=tile, ti=ti, icb=icb, j=j: e.matmul(
                        PS[:, ti * 512:(ti + 1) * 512],
                        h1T[:, ic * T + tile * 128:ic * T + (tile + 1) * 128],
                        ws_buf[icb % 2][:, j * ncol:(j + 1) * ncol],
                        start=(ic == 0), stop=(ic == NIC - 1)),
                        ((wt, h1T_done) + tuple(extra_wait))
                        if (ic == 0 and ti == 0) else (wt,),
                        (PE, 1) if (last or (j == BL - 1 and ti == ntile - 1)) else None)
            if j == BL - 1:
                blk_pe[icb] = (PE, PE.n)
                if icb + 2 < NB:
                    load_blk(icb + 2, (blk_pe[icb],))
        mm_tok = (PE, PE.n)
        ev = None
        for ti in range(ntile):
            tile = out_tiles[ti]
            if kind == "fm":
                psl = PS[:, ti * 1024:(ti + 1) * 1024]
                dst = out_sb[:, out_off + tile * T: out_off + (tile + 1) * T]
            else:
                psl = PS[:, ti * 512:(ti + 1) * 512]
                dst = out_sb[:, tile * KV + out_off: tile * KV + out_off + ncol]
            if scale is None:
                B.emit("vector", lambda e, dst=dst, psl=psl: e.tensor_copy(dst, psl),
                       (mm_tok,), (VE, 1))
            else:
                B.emit("vector", lambda e, dst=dst, psl=psl: e.tensor_scalar(
                    dst, psl, scale, None, op0=ALU.mult), (mm_tok,), (VE, 1))
            ev = (VE, VE.n)
        return ev

    def a2_hook():
        pass

    kev = proj_pass("fm", [0, 1, 2], wk, (0, 384), True, kT_own, 0,
                    extra_wait=(fold_state["done"],), hook=a2_hook)
    kev = proj_pass("fm", [3, 4, 5], wk, (384, 768), True, kT_own, 0,
                    extra_wait=(kev,), hook=a2_hook)
    kev = proj_pass("fm", [6, 7], wk, (768, 1024), True, kT_own, 0,
                    extra_wait=(kev,), hook=a2_hook)
    vev = kev
    for tg in range(2):
        for vf in range(2):
            vev = proj_pass("tm", [tg * 4 + i for i in range(4)], wv,
                            (vf * 512, (vf + 1) * 512), True, v_own, vf * 512,
                            extra_wait=(vev,), hook=a2_hook)
    assert fold_state["done"] is not None, "fold must complete in P0+A1"
    if debug:
        for _q in range(4):
            dma("gpsimd", dbg["dbg_wqe"][_q * 1024:(_q + 1) * 1024, :],
                wqe_d[_q * 1024:(_q + 1) * 1024, :], (fold_state["done"],),
                st="g_dbg")
        dbg_written.add("dbg_wqe")
    st_toks = []
    for kc in range(8):
        st_toks.append(dma("sync", kv_in[kc * 128:(kc + 1) * 128, :],
                           kT_own[:, kc * T:(kc + 1) * T], (kev,), st="s_kvst"))
    for tt in range(8):
        st_toks.append(dma("sync", kv_in[T + tt * 128: T + (tt + 1) * 128, :],
                           v_own[:, tt * KV:(tt + 1) * KV], (vev,), st="s_kvst"))
    B.emit("gpsimd", lambda e: e.collective_compute(
        "AllGather", ALU.bypass,
        replica_groups=[[0, 1, 2, 3], [4, 5, 6, 7]],
        ins=[kv_in[:, :]], outs=[kv_out[:, :]]),
        tuple(st_toks), (CC, 1))
    cc_done = (CC, CC.n)
    qev = proj_pass("fm", [0, 1, 2], wqe_d, (0, 384), False, qT, 0, scale=QSCALE,
                    extra_wait=(vev, fold_state["done"]), hook=a2_hook)
    qev = proj_pass("fm", [3, 4, 5], wqe_d, (384, 768), False, qT, 0, scale=QSCALE,
                    extra_wait=(qev,), hook=a2_hook)
    qev = proj_pass("fm", [6, 7], wqe_d, (768, 1024), False, qT, 0, scale=QSCALE,
                    extra_wait=(qev,), hook=a2_hook)
    q_done = qev

    dbg_toks = []
    if debug:
        dbg_toks.append(dma("gpsimd", dbg["dbg_kT"], kT_own[:, :], (kev,), st="g_dbg"))
        dbg_toks.append(dma("gpsimd", dbg["dbg_v"], v_own[:, :], (vev,), st="g_dbg"))
        dbg_toks.append(dma("gpsimd", dbg["dbg_qT"], qT[:, :], (q_done,), st="g_dbg"))
        dbg_written.update(["dbg_kT", "dbg_v", "dbg_qT"])

    if stage == "A2":
        return _finish()

    emit_bw_prep(st_toks[-1], q_done)
    bwT_done = bw_state["done"]

    if stage == "AG":
        return _finish()
    v_toks = []
    for r in range(4):
        src = kv_out[2048 * r + 1024: 2048 * r + 2048, :].rearrange(
            "(kc p) f -> p kc f", p=128)
        dstv = v_sb[:, :].rearrange("p (kc f) -> p kc f", kc=NKC)[
            :, r * 8:(r + 1) * 8, :]
        v_toks.append(dma("gpsimd", dstv, src,
                          (cc_done, q_done) + tuple(dbg_toks), st="g_vsb"))
    v_ready = v_toks[-1]

    kh_tok = None
    last_smm = None
    prev_av_ev = None
    slot_free = {0: None, 1: [bw_state["last_evac"]]}
    slot_exp = {0: None, 1: None}
    prev_D_pe2 = None
    for h in range(NH):
        kw = [cc_done] if kh_tok is None else [last_smm]
        kts = []
        for r in range(4):
            kts.append(dma("gpsimd", kh_sb[:, r * 1024:(r + 1) * 1024],
                           kv_out[2048 * r + h * 128: 2048 * r + (h + 1) * 128, :],
                           tuple(kw) if r == 0 else (), st="g_kh"))
        kh_tok = kts[-1]
        for qh in range(2):
            pt_evs = []
            for qtl in range(4):
                qt = qh * 4 + qtl
                exp_toks = []
                negm_toks = []
                for c in range(2):
                    base = c * 2048
                    for m in range(4):
                        ws = [kh_tok, q_done]
                        if m == 0:
                            if slot_exp[c] is not None:
                                ws.append(slot_exp[c])
                            if slot_free[c] is not None:
                                ws += slot_free[c]
                            if c == 0 and prev_av_ev is not None:
                                ws.append(prev_av_ev)
                        B.emit("tensor", lambda e, h=h, qt=qt, base=base, m=m: e.matmul(
                            PS[:, base + m * 512:base + (m + 1) * 512],
                            qT[:, h * T + qt * 128:h * T + (qt + 1) * 128],
                            kh_sb[:, base + m * 512:base + (m + 1) * 512],
                            start=True, stop=True),
                            tuple(ws), (PE, 1) if m == 3 else None)
                    smm_c = (PE, PE.n)
                    sub = PS[:, base:base + 2048].rearrange(
                        "p (a b) -> p b a", b=8)[:, 0, :]
                    B.emit("vector", lambda e, sub=sub, c=c: e.reduce_max(
                        st2[:, c:c + 1], sub, axis=AXX, negate=True),
                        (smm_c,), (VE, 1))
                    negm_toks.append((VE, VE.n))
                    B.emit("scalar", lambda e, base=base, c=c: e.activation(
                        P_sb[:, base:base + 2048], PS[:, base:base + 2048], AF.Exp,
                        bias=st2[:, c:c + 1], scale=1.0,
                        accum_out=st2[:, 2 + c:3 + c]), (negm_toks[-1],), (SE, 1))
                    exp_toks.append((SE, SE.n))
                    slot_exp[c] = exp_toks[-1]
                last_smm = (PE, PE.n)
                B.emit("vector", lambda e: e.tensor_tensor(
                    st2[:, 4:5], st2[:, 0:1], st2[:, 1:2], op=ALU.min),
                    (negm_toks[1],), (VE, 1))
                B.emit("vector", None, ((VE, VE.n),))
                B.emit("vector", lambda e: e.tensor_scalar(
                    st2[:, 6:8], st2[:, 0:2], st2[:, 4:5], None, op0=ALU.subtract),
                    (), (VE, 1))
                vsub_tok = (VE, VE.n)
                B.emit("scalar", lambda e: e.activation(
                    st2[:, 8:10], st2[:, 6:8], AF.Exp, scale=-1.0),
                    (vsub_tok,), (SE, 1))
                sexp_tok = (SE, SE.n)
                B.emit("vector", lambda e: e.tensor_mul(
                    st2[:, 10:12], st2[:, 2:4], st2[:, 8:10]), (sexp_tok,), (VE, 1))
                B.emit("vector", None, ((VE, VE.n),))
                B.emit("vector", lambda e: e.reduce_sum(
                    st2[:, 12:13], st2[:, 10:12], axis=AXX), (), (VE, 1))
                B.emit("vector", None, ((VE, VE.n),))
                B.emit("vector", lambda e: e.reciprocal(
                    st2[:, 13:14], st2[:, 12:13]), (), (VE, 1))
                B.emit("vector", None, ((VE, VE.n),))
                B.emit("vector", lambda e: e.tensor_scalar(
                    st2[:, 14:16], st2[:, 8:10], st2[:, 13:14], None, op0=ALU.mult),
                    (), (VE, 1))
                B.emit("vector", None, ((VE, VE.n),))
                dw = [prev_D_pe2] if prev_D_pe2 is not None else []
                B.emit("vector", lambda e: e.tensor_scalar(
                    D_sb[:, 0:128], ident_b[:, :], st2[:, 14:15], None, op0=ALU.mult),
                    tuple(dw), (VE, 1))
                B.emit("vector", lambda e: e.tensor_scalar(
                    D_sb[:, 128:256], ident_b[:, :], st2[:, 15:16], None, op0=ALU.mult),
                    (), (VE, 1))
                D_tok = (VE, VE.n)
                for c in range(2):
                    evs = {}
                    for grp in range(4):
                        pb = c * 2048 + (grp % 2) * 512
                        gw = [D_tok] if grp == 0 else []
                        if grp >= 2:
                            gw.append(evs[grp - 2])
                        for j in range(4):
                            kc = c * 16 + grp * 4 + j
                            B.emit("tensor", lambda e, kc=kc, pb=pb, j=j, c=c: e.matmul(
                                PS[:, pb + j * 128:pb + (j + 1) * 128],
                                P_sb[:, kc * 128:(kc + 1) * 128],
                                D_sb[:, c * 128:(c + 1) * 128],
                                start=True, stop=True),
                                tuple(gw) if j == 0 else (),
                                (PE, 1) if j == 3 else None)
                        pe_tok = (PE, PE.n)
                        dst = PT_half[:, :].rearrange("p (kc t) -> p kc t", kc=NKC)[
                            :, c * 16 + grp * 4:c * 16 + (grp + 1) * 4,
                            qtl * 128:(qtl + 1) * 128]
                        psrc = PS[:, pb:pb + 512].rearrange("p (a b) -> p a b", a=4)
                        if grp % 2 == 0:
                            B.emit("vector", lambda e, dst=dst, psrc=psrc: e.tensor_copy(
                                dst, psrc), (pe_tok,), (VE, 1))
                            evs[grp] = (VE, VE.n)
                        else:
                            B.emit("scalar", lambda e, dst=dst, psrc=psrc: e.copy(
                                dst, psrc), (pe_tok,), (SE, 1))
                            evs[grp] = (SE, SE.n)
                    slot_free[c] = [evs[2], evs[3]]
                    pt_evs += [evs[2], evs[3]]
                prev_D_pe2 = (PE, PE.n)
            pt_v_last = pt_evs[-2]
            pt_s_last = pt_evs[-1]
            for kc in range(NKC):
                ws = []
                if kc == 0:
                    ws = list(pt_evs) + [v_ready]
                    if prev_av_ev is not None:
                        ws.append(prev_av_ev)
                B.emit("tensor", lambda e, kc=kc, h=h: e.matmul(
                    PS[:, 0:512],
                    v_sb[:, kc * KV + h * 128: kc * KV + (h + 1) * 128],
                    PT_half[:, kc * 512:(kc + 1) * 512],
                    start=(kc == 0), stop=(kc == NKC - 1)),
                    tuple(ws), (PE, 1) if kc == NKC - 1 else None)
            av_tok = (PE, PE.n)
            B.emit("vector", lambda e, h=h, qh=qh: e.tensor_copy(
                vals_sb[:, h * T + qh * 512: h * T + (qh + 1) * 512], PS[:, 0:512]),
                (av_tok,), (VE, 1))
            prev_av_ev = (VE, VE.n)
    attn_done = prev_av_ev

    if debug:
        dma("gpsimd", dbg["dbg_vals"], vals_sb[:, :], (attn_done,), st="g_dbg")
        dbg_written.add("dbg_vals")

    if stage == "A4":
        return _finish()
    wo_toks = []
    for kvc in range(8):
        wo_toks.append(dma("gpsimd", wo_sb[:, kvc * D:(kvc + 1) * D],
                           wo[kvc * 128:(kvc + 1) * 128, :],
                           (attn_done,) if kvc == 0 else (wo_toks[-1],),
                           st="g_wo"))
    atst_free = [None, None]
    x5_free = None
    prev_ev_tok = None
    a5_h2_st = []
    for tt in range(NTT):
        bi = tt % 2
        for kvc in range(8):
            ws = []
            if kvc == 0:
                ws = [wo_toks[0], attn_done]
                if atst_free[bi] is not None:
                    ws.append(atst_free[bi])
                if prev_ev_tok is not None:
                    ws.append(prev_ev_tok)
            else:
                ws = [wo_toks[kvc]]
            for ob in range(8):
                B.emit("tensor", lambda e, kvc=kvc, ob=ob, tt=tt: e.matmul(
                    PS[:, ob * 512:(ob + 1) * 512],
                    vals_sb[:, kvc * T + tt * 128: kvc * T + (tt + 1) * 128],
                    wo_sb[:, kvc * D + ob * 512: kvc * D + (ob + 1) * 512],
                    start=(kvc == 0), stop=(kvc == 7)),
                    tuple(ws) if ob == 0 else (),
                    (PE, 1) if (kvc == 7 and ob == 7) else None)
        wo_tok = (PE, PE.n)
        B.emit("vector", lambda e, bi=bi: e.tensor_copy(
            atst[bi][:, 0:2048], PS[:, 0:2048]), (wo_tok,), (VE, 1))
        B.emit("vector", lambda e, bi=bi: e.tensor_copy(
            atst[bi][:, 2048:4096], PS[:, 2048:4096]), (), (VE, 1))
        ev_tok = (VE, VE.n)
        prev_ev_tok = ev_tok
        B.emit("vector", None, (ev_tok,))
        B.emit("scalar", None, (ev_tok,))
        rstd_tok, nb_tok = ln_stats(atst[bi][:, :])
        lnw = [nb_tok]
        if tt > 0:
            lnw.append(prev_x1_tok)
        else:
            lnw.append(bw_state["done"])
        B.emit("scalar", lambda e, bi=bi: e.activation(
            ln_sb[:, :], atst[bi][:, :], AF.Identity, bias=st1[:, 8:9], scale=st1[:, 7:8]),
            tuple(lnw), (SE, 1))
        ln_tok = (SE, SE.n)
        atst_free[bi] = ln_tok
        xw = [attn_done] if tt == 0 else [x5_free, prev_t_h2]
        t_x = dma("sync", x5_buf[:, :], x_in[tt * 128:(tt + 1) * 128, :], tuple(xw), st="s_x5")
        B.emit("vector", lambda e: e.tensor_add(x5_buf[:, :], x5_buf[:, :], ln_sb[:, :]),
               (t_x, ln_tok), (VE, 1))
        x1_tok = (VE, VE.n)
        prev_x1_tok = x1_tok
        dma("sync", x1sp[tt * 128:(tt + 1) * 128, :], x5_buf[:, :], (x1_tok,), st="s_sp")
        B.emit("vector", None, (x1_tok,))
        B.emit("scalar", None, (x1_tok,))
        rstd2, nb2 = ln_stats(x5_buf[:, :])
        B.emit("vector", lambda e: e.tensor_scalar(
            h2b[:, :], x5_buf[:, :], st1[:, 2:3], st1[:, 7:8],
            op0=ALU.add, op1=ALU.mult), (rstd2, nb2), (VE, 1))
        h2_tok = (VE, VE.n)
        x5_free = h2_tok
        a5_h2_st.append(dma("sync", h2sp[tt * 128:(tt + 1) * 128, :], h2b[:, :],
                            (h2_tok,), st="s_sp"))
        prev_t_h2 = a5_h2_st[-1]
        if debug:
            dma("gpsimd", dbg["dbg_h2"][tt * 128:(tt + 1) * 128, :], h2b[:, :], (h2_tok,), st="g_dbg")
            dbg_written.add("dbg_h2")
    a5_done = a5_h2_st[-1]

    if stage == "A5":
        return _finish()
    h2rd_free = None
    stage_evs = []
    ep_done = []
    slot_free = [None, None]
    x1_free = None
    for hf in range(2):
        tr_last = None
        for ttl in range(4):
            tt = hf * 4 + ttl
            w = [a5_done]
            if h2rd_free is not None:
                w.append(h2rd_free)
            t_h2 = dma("sync", h2b[:, :], h2sp[tt * 128:(tt + 1) * 128, :], tuple(w), st="s_h2rd")
            evs = {}
            for grp in range(8):
                bank = (grp % 2) * 512
                gw = [t_h2] if grp == 0 else []
                if hf == 1 and ttl == 0 and grp == 0:
                    gw.append(ep_done[-1])
                if grp >= 2:
                    gw.append(evs[grp - 2])
                elif ttl > 0:
                    gw.append(prev_evs[grp + 6])
                pb = (grp % 2) * 1024
                for j in range(4):
                    ic = grp * 4 + j
                    B.emit("tensor", lambda e, ic=ic, pb=pb, j=j: e.transpose(
                        PSb[:, pb + j * 128:pb + (j + 1) * 128],
                        h2b[:, ic * 128:(ic + 1) * 128], ident_b[:, :]),
                        tuple(gw) if j == 0 else (), (PE, 1) if j == 3 else None)
                pe_tok = (PE, PE.n)
                dst = h2T[:, :].rearrange("p (ic t) -> p ic t", ic=NIC)[
                    :, grp * 4:(grp + 1) * 4, ttl * 128:(ttl + 1) * 128]
                B.emit("vector", lambda e, dst=dst, pb=pb: e.tensor_copy(
                    dst, PSb[:, pb:pb + 512].rearrange("p (a b) -> p a b", a=4)),
                    (pe_tok,), (VE, 1))
                evs[grp] = (VE, VE.n)
            prev_evs = evs
            h2rd_free = evs[7]
            tr_last = evs[7]
        h2T_tok = tr_last

        for ttl in range(4):
            for ic in range(NIC):
                B.emit("tensor", lambda e, ic=ic, ttl=ttl: e.matmul(
                    PS[:, ttl * 512: ttl * 512 + 80],
                    h2T[:, ic * 512 + ttl * 128: ic * 512 + (ttl + 1) * 128],
                    arT[:, ic * 80:(ic + 1) * 80],
                    start=(ic == 0), stop=(ic == NIC - 1)),
                    (h2T_tok, ar_done) if (ttl == 0 and ic == 0) else (),
                    (PE, 1) if ic == NIC - 1 else None)
        ar_mm = (PE, PE.n)
        for ttl in range(4):
            B.emit("vector", lambda e, ttl=ttl: e.tensor_copy(
                res16[:, ttl * 64:(ttl + 1) * 64], PS[:, ttl * 512: ttl * 512 + 64]),
                (ar_mm,), (VE, 1))
            B.emit("vector", lambda e, ttl=ttl: e.tensor_copy(
                rtr[:, ttl * 16:(ttl + 1) * 16], PS[:, ttl * 512 + 64: ttl * 512 + 80]),
                (), (VE, 1))
        ar_ev = (VE, VE.n)
        if debug:
            dma("gpsimd", dbg["dbg_res"][hf * 512:(hf + 1) * 512, 0:64].rearrange(
                "(ttl p) r -> p ttl r", p=128),
                res16[:, :].rearrange("p (ttl r) -> p ttl r", ttl=4), (ar_ev,),
                st="g_dbg")
            dma("gpsimd", dbg["dbg_res"][hf * 512:(hf + 1) * 512, 64:80].rearrange(
                "(ttl p) r -> p ttl r", p=128),
                rtr[:, 0:64].rearrange("p (ttl r) -> p ttl r", ttl=4), (ar_ev,),
                st="g_dbg")

        ht_last = None
        for ttl in range(4):
            lg = rtr[:, ttl * 16:(ttl + 1) * 16]
            B.emit("vector", lambda e, lg=lg: e.reduce_max(
                rtr[:, 64:65], lg, axis=AXX, negate=True), (ar_ev,), (VE, 1))
            B.emit("scalar", lambda e, lg=lg: e.activation(
                rtr[:, 80:96], lg, AF.Exp, bias=rtr[:, 64:65], scale=1.0,
                accum_out=rtr[:, 65:66]), ((VE, VE.n),), (SE, 1))
            B.emit("vector", lambda e: e.reciprocal(rtr[:, 66:67], rtr[:, 65:66]),
                   ((SE, SE.n),), (VE, 1))
            B.emit("vector", None, ((VE, VE.n),))
            B.emit("vector", lambda e: e.tensor_scalar(
                rtr[:, 96:112], rtr[:, 80:96], rtr[:, 66:67], None, op0=ALU.mult),
                (), (VE, 1))
            B.emit("vector", None, ((VE, VE.n),))
            B.emit("vector", lambda e: e.reduce_max(
                rtr[:, 67:68], rtr[:, 96:112], axis=AXX), (), (VE, 1))
            B.emit("vector", None, ((VE, VE.n),))
            B.emit("vector", lambda e: e.tensor_scalar(
                rtr[:, 112:128], rtr[:, 96:112], rtr[:, 67:68], None, op0=ALU.is_equal),
                (), (VE, 1))
            B.emit("vector", None, ((VE, VE.n),))
            B.emit("vector", lambda e: e.tensor_sub(
                rtr[:, 128:144], rtr[:, 96:112], rtr[:, 112:128]), (), (VE, 1))
            B.emit("vector", None, ((VE, VE.n),))
            B.emit("vector", lambda e: e.reduce_max(
                rtr[:, 68:69], rtr[:, 128:144], axis=AXX), (), (VE, 1))
            B.emit("vector", None, ((VE, VE.n),))
            B.emit("vector", lambda e: e.tensor_scalar(
                rtr[:, 144:160], rtr[:, 128:144], rtr[:, 68:69], None, op0=ALU.is_equal),
                (), (VE, 1))
            B.emit("vector", None, ((VE, VE.n),))
            B.emit("vector", lambda e: e.tensor_scalar(
                rtr[:, 69:70], rtr[:, 67:68], rtr[:, 68:69], 1e-6,
                op0=ALU.add, op1=ALU.add), (), (VE, 1))
            B.emit("vector", None, ((VE, VE.n),))
            B.emit("vector", lambda e: e.reciprocal(rtr[:, 70:71], rtr[:, 69:70]),
                   (), (VE, 1))
            B.emit("vector", None, ((VE, VE.n),))
            B.emit("vector", lambda e: e.tensor_scalar(
                rtr[:, 71:72], rtr[:, 67:68], rtr[:, 70:71], 0.5,
                op0=ALU.mult, op1=ALU.mult), (), (VE, 1))
            B.emit("vector", None, ((VE, VE.n),))
            B.emit("vector", lambda e: e.tensor_scalar(
                rtr[:, 72:73], rtr[:, 68:69], rtr[:, 70:71], 0.5,
                op0=ALU.mult, op1=ALU.mult), (), (VE, 1))
            B.emit("vector", None, ((VE, VE.n),))
            B.emit("vector", lambda e: e.tensor_scalar(
                rtr[:, 112:128], rtr[:, 112:128], rtr[:, 71:72], None, op0=ALU.mult),
                (), (VE, 1))
            B.emit("vector", None, ((VE, VE.n),))
            B.emit("vector", lambda e: e.tensor_scalar(
                rtr[:, 144:160], rtr[:, 144:160], rtr[:, 72:73], None, op0=ALU.mult),
                (), (VE, 1))
            B.emit("vector", None, ((VE, VE.n),))
            B.emit("vector", lambda e: e.tensor_add(
                rtr[:, 112:128], rtr[:, 112:128], rtr[:, 144:160]), (), (VE, 1))
            B.emit("vector", None, ((VE, VE.n),))
            for e_i in range(16):
                B.emit("vector", lambda e, e_i=e_i, ttl=ttl: e.tensor_scalar(
                    H_sb[:, e_i * 64:(e_i + 1) * 64],
                    res16[:, ttl * 64:(ttl + 1) * 64],
                    rtr[:, 112 + e_i:113 + e_i], None, op0=ALU.mult),
                    (), (VE, 1) if e_i == 15 else None)
            h_tok = (VE, VE.n)
            if debug:
                dma("gpsimd", dbg["dbg_H"][(hf * 4 + ttl) * 128:(hf * 4 + ttl + 1) * 128, :],
                    H_sb[:, :], (h_tok,), st="g_dbg")
            for grp in range(2):
                pb = grp * 1024
                gw = [h_tok, ar_ev] if grp == 0 else []
                for j in range(4):
                    erc = grp * 4 + j
                    B.emit("tensor", lambda e, erc=erc, pb=pb, j=j: e.transpose(
                        PSb[:, pb + j * 128:pb + (j + 1) * 128],
                        H_sb[:, erc * 128:(erc + 1) * 128], ident_b[:, :]),
                        tuple(gw) if j == 0 else (), (PE, 1) if j == 3 else None)
                pe_tok = (PE, PE.n)
                dst = HT_sb[:, :].rearrange("p (erc t) -> p erc t", erc=8)[
                    :, grp * 4:(grp + 1) * 4, ttl * 128:(ttl + 1) * 128]
                B.emit("vector", lambda e, dst=dst, pb=pb: e.tensor_copy(
                    dst, PSb[:, pb:pb + 512].rearrange("p (a b) -> p a b", a=4)),
                    (pe_tok,), (VE, 1))
                ht_last = (VE, VE.n)
        ht_done = ht_last

        for oe in range(16):
            gi = hf * 16 + oe
            si = gi % 2
            w = [bwT_done]
            if slot_free[si] is not None:
                w.append(slot_free[si])
            if gi == 0:
                w.append(a5_done)
            t_w1 = dma("sync", bwT_s[si][:, :].rearrange("p (ic o) -> p ic o", ic=NIC),
                       bwT_d[oe].rearrange("(ic p) o -> p ic o", p=128), tuple(w),
                       st=f"s_sl{si}")
            t_w2 = dma("sync", bfT_s[si][:, :].rearrange("p (erc o) -> p erc o", erc=8),
                       bfT_d[oe].rearrange("(erc p) o -> p erc o", p=128),
                       (bft_done,) + ((slot_free[si],) if slot_free[si] else ()),
                       st=f"s_sl{si}")
            goff = (gi % 2) * 2048
            for ic in range(NIC):
                for ttl in range(4):
                    ws = []
                    if ic == 0 and ttl == 0:
                        ws = [t_w1, t_w2, ht_done]
                        if len(stage_evs) >= 2:
                            ws.append(stage_evs[gi - 2])
                    B.emit("tensor", lambda e, ic=ic, ttl=ttl, si=si, goff=goff: e.matmul(
                        PS[:, goff + ttl * 512: goff + ttl * 512 + 256],
                        h2T[:, ic * 512 + ttl * 128: ic * 512 + (ttl + 1) * 128],
                        bwT_s[si][:, ic * 256:(ic + 1) * 256],
                        start=(ic == 0), stop=False),
                        tuple(ws), None)
            for erc in range(8):
                for ttl in range(4):
                    ws = [t_w2] if (erc == 0 and ttl == 0) else []
                    B.emit("tensor", lambda e, erc=erc, ttl=ttl, si=si, goff=goff: e.matmul(
                        PS[:, goff + ttl * 512: goff + ttl * 512 + 256],
                        HT_sb[:, erc * 512 + ttl * 128: erc * 512 + (ttl + 1) * 128],
                        bfT_s[si][:, erc * 256:(erc + 1) * 256],
                        start=False, stop=(erc == 7)),
                        tuple(ws), (PE, 1) if (erc == 7 and ttl == 3) else None)
            mm_tok = (PE, PE.n)
            slot_free[si] = mm_tok
            for ttl in range(4):
                B.emit("vector", lambda e, ttl=ttl, oe=oe, goff=goff: e.tensor_tensor(
                    stg_sb[:, ttl * D + oe * 256: ttl * D + (oe + 1) * 256],
                    PS[:, goff + ttl * 512: goff + ttl * 512 + 256],
                    bb16[:, oe * 256:(oe + 1) * 256], op=ALU.add),
                    (mm_tok, bb_ready), (VE, 1) if ttl == 3 else None)
            stage_evs.append((VE, VE.n))
        pass_done = stage_evs[-1]

        for ttl in range(4):
            tt = hf * 4 + ttl
            stg = stg_sb[:, ttl * D:(ttl + 1) * D]
            B.emit("vector", None, (pass_done,))
            B.emit("scalar", None, (pass_done,))
            rstd3, nb3 = ln_stats(stg)
            lnw = [nb3] + ([ep_done[-1]] if ep_done else [])
            B.emit("scalar", lambda e, stg=stg: e.activation(
                ln_sb[:, :], stg, AF.Identity, bias=st1[:, 8:9], scale=st1[:, 7:8]),
                tuple(lnw), (SE, 1))
            ln_tok = (SE, SE.n)
            xw = [a5_done]
            if x1_free:
                xw.append(x1_free)
            t_x1 = dma("sync", x1buf[:, :], x1sp[tt * 128:(tt + 1) * 128, :], tuple(xw), st="s_x1rd")
            B.emit("vector", lambda e: e.tensor_add(
                x1buf[:, :], x1buf[:, :], ln_sb[:, :]), (t_x1, ln_tok), (VE, 1))
            fin_tok = (VE, VE.n)
            x1_free = dma("sync", out_d[tt * 128:(tt + 1) * 128, :], x1buf[:, :],
                          (fin_tok,), st="s_out")
            ep_done.append(fin_tok)
        if debug:
            dma("gpsimd", dbg["dbg_stage"][hf * 512:(hf + 1) * 512, :].rearrange(
                "(ttl p) o -> p ttl o", p=128),
                stg_sb[:, :].rearrange("p (ttl o) -> p ttl o", ttl=4), (pass_done,),
                st="g_dbg")

    return _finish()


_NC_CACHE = {}


def _get_nc(debug=DEBUG):
    if debug not in _NC_CACHE:
        _NC_CACHE[debug] = build_program(debug)
    return _NC_CACHE[debug]


def kernel(L0_attn_x, L0_attn_wq, L0_attn_wk, L0_attn_wv, L0_attn_wo,
           L0_ffn_base_w, L0_ffn_base_b, L0_ffn_A, L0_ffn_B, L0_ffn_router_w):
    x = np.ascontiguousarray(np.asarray(L0_attn_x, dtype=np.float32))
    wq_ = np.ascontiguousarray(np.asarray(L0_attn_wq, dtype=np.float32))
    wk_ = np.ascontiguousarray(np.asarray(L0_attn_wk, dtype=np.float32))
    wv_ = np.ascontiguousarray(np.asarray(L0_attn_wv, dtype=np.float32))
    wo_ = np.ascontiguousarray(np.asarray(L0_attn_wo, dtype=np.float32))
    bw_ = np.ascontiguousarray(np.asarray(L0_ffn_base_w, dtype=np.float32))
    bb_ = np.ascontiguousarray(np.asarray(L0_ffn_base_b, dtype=np.float32))
    A_ = np.ascontiguousarray(np.asarray(L0_ffn_A, dtype=np.float32))
    B_ = np.ascontiguousarray(np.asarray(L0_ffn_B, dtype=np.float32))
    rw_ = np.ascontiguousarray(np.asarray(L0_ffn_router_w, dtype=np.float32))

    nc = _get_nc()
    xf = x.reshape(8192, 4096)
    import ml_dtypes
    idf = np.eye(128, dtype=np.float32)
    idb = np.eye(128).astype(ml_dtypes.bfloat16)
    in_maps = [{
        "x": xf[c * 1024:(c + 1) * 1024], "wq": wq_, "wk": wk_, "wv": wv_,
        "wo": wo_, "bw": bw_, "bb": bb_, "A": A_, "B": B_, "rw": rw_,
        "idf": idf, "idb": idb,
    } for c in range(8)]
    res = run_bass_kernel_spmd(nc, in_maps, core_ids=list(range(8)))
    shards = [res.results[c]["out"] for c in range(8)]
    return np.concatenate(shards, axis=0).reshape(2, 4096, 4096).astype(np.float32)


if __name__ == "__main__":
    build_program()
    print("build ok")


# version 11 (final)
# speedup vs baseline: 1.1934x; 1.0154x over previous
import numpy as np
from contextlib import ExitStack

import concourse.bass as bass
import concourse.mybir as mybir
from concourse.bass_utils import run_bass_kernel_spmd

F32 = mybir.dt.float32
BF16 = mybir.dt.bfloat16
U8 = mybir.dt.uint8
AF = mybir.ActivationFunctionType
ALU = mybir.AluOpType
AXX = mybir.AxisListType.X

D = 4096
T = 1024
NTT = 8
NIC = 32
NH = 8
KV = 1024
S = 4096
NKC = 32
EPS = 1e-5
QSCALE = float(1.0 / np.sqrt(128.0))

DEBUG = False


class Sem:
    __slots__ = ("h", "n")

    def __init__(self, h):
        self.h = h
        self.n = 0


class Builder:
    ENGINES = ("sync", "gpsimd", "tensor", "vector", "scalar")

    def __init__(self):
        self.ops = {e: [] for e in self.ENGINES}

    def emit(self, eng, fn, waits=(), inc=None):
        ws = []
        for s, v in waits:
            assert v <= s.n, f"wait on future sem value {v} > {s.n} ({eng})"
            if v > 0:
                ws.append((s.h, v))
        if inc is not None:
            s, k = inc
            s.n += k
            inc = (s.h, k)
        self.ops[eng].append((fn, ws, inc))

    def replay(self, block):
        def mk(eng_name):
            def run(e):
                seen = {}
                for fn, ws, inc in self.ops[eng_name]:
                    merged = {}
                    for h, v in ws:
                        if v > merged.get(h.name, (None, -1))[1]:
                            merged[h.name] = (h, v)
                    for h, v in merged.values():
                        if seen.get(h.name, -1) >= v:
                            continue
                        seen[h.name] = v
                        e.wait_ge(h, v)
                    inst = fn(e) if fn is not None else None
                    if inc is not None:
                        h, k = inc
                        if inst is None:
                            inst = e.nop()
                        inst.then_inc(h, k)
            return run

        for eng_name in self.ENGINES:
            if self.ops[eng_name]:
                getattr(block, eng_name)(mk(eng_name))


def build_program(debug=DEBUG, stage="full"):
    nc = bass.Bass("TRN2", target_bir_lowering=False)

    x_in = nc.dram_tensor("x", [T, D], F32, kind="ExternalInput").ap()
    wq = nc.dram_tensor("wq", [D, D], F32, kind="ExternalInput").ap()
    wk = nc.dram_tensor("wk", [D, KV], F32, kind="ExternalInput").ap()
    wv = nc.dram_tensor("wv", [D, KV], F32, kind="ExternalInput").ap()
    wo = nc.dram_tensor("wo", [KV, D], F32, kind="ExternalInput").ap()
    bw = nc.dram_tensor("bw", [D, D], F32, kind="ExternalInput").ap()
    bb = nc.dram_tensor("bb", [D], F32, kind="ExternalInput").ap()
    A_in = nc.dram_tensor("A", [64, D], F32, kind="ExternalInput").ap()
    B_in = nc.dram_tensor("B", [16, D, 64], F32, kind="ExternalInput").ap()
    rw = nc.dram_tensor("rw", [16, D], F32, kind="ExternalInput").ap()
    idf_in = nc.dram_tensor("idf", [128, 128], F32, kind="ExternalInput").ap()
    idb_in = nc.dram_tensor("idb", [128, 128], BF16, kind="ExternalInput").ap()
    out_d = nc.dram_tensor("out", [T, D], F32, kind="ExternalOutput").ap()

    wqe_d = nc.dram_tensor("wqe_d", [D, KV], BF16, kind="Internal").ap()
    bwT_d = nc.dram_tensor("bwT_d", [16, D, 256], BF16, kind="Internal").ap()
    bfT_d = nc.dram_tensor("bfT_d", [16, KV, 256], BF16, kind="Internal").ap()
    kv_in = nc.dram_tensor("kv_in", [2 * T, KV], BF16, kind="Internal").ap()
    kv_out = nc.dram_tensor("kv_out", [8 * T, KV], BF16, kind="Internal").ap()
    x1sp = nc.dram_tensor("x1sp", [T, D], F32, kind="Internal").ap()
    h2sp = nc.dram_tensor("h2sp", [T, D], BF16, kind="Internal").ap()

    dbg = {}
    dbg_written = set()
    if debug:
        for nm, shp, dt in [
            ("dbg_qT", [128, NH * T], F32), ("dbg_kT", [128, NH * T], F32),
            ("dbg_v", [128, NTT * KV], F32), ("dbg_vals", [128, NH * T], F32),
            ("dbg_h2", [T, D], F32), ("dbg_res", [T, 80], F32),
            ("dbg_H", [T, KV], F32), ("dbg_stage", [T, D], F32),
            ("dbg_h1b", [128, D], F32), ("dbg_h1T", [128, NIC * T], F32),
            ("dbg_wqe", [D, KV], F32),
        ]:
            dbg[nm] = nc.dram_tensor(nm, shp, dt, kind="ExternalOutput").ap()

    ctx = ExitStack()
    CAP = 212_832
    arena = ctx.enter_context(nc.sbuf_tensor("arena", [128, CAP], U8))
    base = nc.lookup_mloc(arena).addr

    def place(name, shape, dtype, off):
        per = int(np.prod(shape[1:])) * mybir.dt.size(dtype)
        assert off % 32 == 0 and off + per <= CAP, (name, off, per)
        h = nc.virtual_sbuf_tensor(name, shape, dtype)
        nc._tensor(h.name, h.shape, h.dtype, type="SB")
        m = nc.lookup_mloc(h)
        m.allocated = True
        m.addr = base + off
        return h.ap()

    def place2(name, shape, dtype, off):
        per = int(np.prod(shape[1:])) * mybir.dt.size(dtype)
        return [place(f"{name}{i}", shape, dtype, off + i * per) for i in range(2)]

    ident_f = place("ident_f", [128, 128], F32, 0)
    ident_b = place("ident_b", [128, 128], BF16, 512)
    st1 = place("st1", [128, 64], F32, 768)
    st2 = place("st2", [128, 64], F32, 1024)
    D_sb = place("D_sb", [128, 256], BF16, 34432)
    rtr = place("rtr", [128, 160], F32, 1536)
    res16 = place("res16", [128, 4 * 64], BF16, 2176)
    H_sb = place("H_sb", [128, KV], BF16, 2688)
    arT = place("arT", [128, NIC * 80], BF16, 4736)
    bb16 = place("bb16", [128, D], BF16, 9856)
    ln_sb = place("ln_sb", [128, D], F32, 18048)
    sq_sb = place("sq_sb", [128, D], BF16, 34432)
    vals_sb = place("vals_sb", [128, NH * T], BF16, 42624)
    qT = place("qT", [128, NH * T], BF16, 59008)
    P_sb = place("P_sb", [128, S], BF16, 75392)
    OB = 83584
    FR = OB + 20480

    wq_stage = place2("wq_stage", [128, 2 * KV], F32, 18048)
    wqe_sb = place("wqe_sb", [128, KV], BF16, OB)
    bw_big = place2("bw_big", [128, D], F32, FR + 65536)
    bw_out = place2("bw_out", [128, NIC * 128], BF16, 18048)
    bft_sb = place("bft_sb", [128, 8 * D], BF16, FR)
    b_stage = place2("b_stage", [128, 2048], F32, FR + 65536)
    ar_stage = place("ar_stage", [64, D], BF16, FR + 81920)
    r_stage = place("r_stage", [16, D], BF16, FR + 90112)
    h1T = place("h1T", [128, NIC * T], BF16, FR)
    x_buf = place("x_buf", [128, D], F32, FR + 65536)
    h1b = place("h1b", [128, D], BF16, FR + 81920)
    kT_own = place("kT_own", [128, NH * T], BF16, FR + 65536)
    v_own = place("v_own", [128, NTT * KV], BF16, FR + 81920)
    ws_buf = place2("ws_buf", [128, 4 * KV], BF16, 18048)
    ws_buf2 = place2("ws_buf2", [128, 4 * KV], BF16, 59008)
    ws_buf3 = place2("ws_buf3", [128, 4 * KV], BF16, 42624)
    v_sb = place("v_sb", [128, NKC * KV], BF16, FR)
    PT_half = place("PT_half", [128, NKC * 512], BF16, FR + 65536)
    kh_sb = place("kh_sb", [128, S], BF16, FR + 98304)
    wo_sb = place("wo_sb", [128, 8 * D], BF16, FR)
    atst = place2("atst", [128, D], BF16, FR + 65536)
    x5_buf = place("x5_buf", [128, D], F32, FR + 81920)
    h2b = place("h2b", [128, D], BF16, FR + 98304)
    stg_sb = place("stg_sb", [128, 4 * D], BF16, FR)
    h2T = place("h2T", [128, NIC * 512], BF16, FR + 32768)
    bwT_s = place2("bwT_s", [128, NIC * 256], BF16, FR + 65536)
    bfT_s = place2("bfT_s", [128, 8 * 256], BF16, 75392)
    x1buf = place("x1buf", [128, D], F32, 59008)
    HT_sb = place("HT_sb", [128, 8 * 512], BF16, 42624)
    assert FR + 106496 <= OB + 129248

    PS = ctx.enter_context(nc.psum_tensor("PS", [128, 4096], F32))
    _psh = nc.virtual_psum_tensor("PSb", [128, 8192], BF16)
    nc._tensor(_psh.name, _psh.shape, _psh.dtype, type="PSUM")
    _psm = nc.lookup_mloc(_psh)
    _psm.allocated = True
    _psm.addr = nc.lookup_mloc(PS.tensor).addr if hasattr(PS, "tensor") else 0
    PSb = _psh.ap()

    sems = {nm: Sem(ctx.enter_context(nc.semaphore(nm)))
            for nm in ("LD", "GD", "PE", "VE", "SE", "GE", "CC")}
    LD, GD, PE, VE, SE, GE, CC = (sems[k] for k in
                                  ("LD", "GD", "PE", "VE", "SE", "GE", "CC"))

    B = Builder()

    def _finish():
        if stage != "full":
            B.emit("gpsimd", lambda e: e.memset(ln_sb[:, :], 0.0), (), (GE, 1))
            zt = (GE, GE.n)
            for _tt in range(NTT):
                dma("gpsimd", out_d[_tt * 128:(_tt + 1) * 128, :], ln_sb[:, :], (zt,), st="g_fill")
            if debug:
                for _nm, _ap in dbg.items():
                    if _nm in dbg_written:
                        continue
                    _views = ([_ap] if _ap.shape[0] == 128 else
                              [_ap[_r * 128:(_r + 1) * 128]
                               for _r in range(_ap.shape[0] // 128)])
                    for _v in _views:
                        nfree = int(np.prod(_v.shape[1:]))
                        _off = 0
                        while _off < nfree:
                            _w = min(4096, nfree - _off)
                            dma("gpsimd", _v[:, _off:_off + _w], ln_sb[:, 0:_w], (zt,), st="g_fill")
                            _off += _w
        allw = tuple((s, s.n) for s in dstreams.values()) + ((CC, CC.n),)
        B.emit("gpsimd", None, allw)
        if any(fn is not None or ws for fn, ws, inc in B.ops["sync"]):
            B.emit("sync", None, allw)
        with nc.Block() as block:
            B.replay(block)
        ctx.close()
        return nc

    dstreams = {}

    def dsem(name):
        if name not in dstreams:
            dstreams[name] = Sem(ctx.enter_context(nc.semaphore(name)))
        return dstreams[name]

    def dma(q, out, in_, waits=(), st=None):
        s = dsem(st if st is not None else ("s_misc" if q == "sync" else "g_misc"))
        B.emit(q, lambda e, o=out, i=in_: e.dma_start(out=o, in_=i), waits, (s, 16))
        return (s, s.n)

    dma("sync", ident_f[:, :], idf_in[:, :])
    ident_ready = dma("sync", ident_b[:, :], idb_in[:, :])

    if stage == "Z0":
        return _finish()
    dma("gpsimd", bb16[:, :], bb[None, :].broadcast_to([128, D]))

    if stage == "P0a":
        return _finish()
    dma("gpsimd", ar_stage[0:64, :], A_in[:, :])
    t_r = dma("gpsimd", r_stage[0:16, :], rw[:, :])
    bb_ready = t_a = t_r
    ar_ev_tok = None
    for ic in range(NIC):
        if ar_ev_tok is not None:
            B.emit("tensor", None, (ar_ev_tok,))
        B.emit("tensor", lambda e, ic=ic: e.transpose(
            PSb[:, 0:64], ar_stage[0:64, ic * 128:(ic + 1) * 128], ident_b[0:64, 0:64]),
            (t_a, ident_ready) if ic == 0 else ())
        B.emit("tensor", lambda e, ic=ic: e.transpose(
            PSb[:, 64:80], r_stage[0:16, ic * 128:(ic + 1) * 128], ident_b[0:16, 0:16]),
            (t_r,) if ic == 0 else (), (PE, 1))
        B.emit("scalar", lambda e, ic=ic: e.copy(
            arT[:, ic * 80:(ic + 1) * 80], PSb[:, 0:80]), ((PE, PE.n),), (SE, 1))
        ar_ev_tok = (SE, SE.n)
    ar_done = ar_ev_tok

    if stage == "P0b":
        return _finish()
    b_pe = {}
    bft_last = None
    for e_i in range(16):
        erc, lane = e_i // 2, (e_i % 2) * 64
        srcv = B_in[e_i].rearrange("(ot p) r -> p ot r", p=128)
        lw = (b_pe[e_i - 2],) if e_i >= 2 else ()
        t_b = dma("sync", b_stage[e_i % 2][:, :].rearrange(
            "p (ot r) -> p ot r", ot=32), srcv, lw, st=f"s_bst{e_i % 2}")
        for grp in range(8):
            if bft_last is not None:
                B.emit("tensor", None, (bft_last,))
            for j in range(4):
                ot = grp * 4 + j
                B.emit("tensor", lambda e, e_i=e_i, ot=ot, j=j: e.transpose(
                    PS[0:64, 1024 + j * 128:1024 + (j + 1) * 128],
                    b_stage[e_i % 2][:, ot * 64:(ot + 1) * 64], ident_f[:, :]),
                    (t_b, ident_ready) if (grp == 0 and j == 0) else (),
                    (PE, 1) if j == 3 else None)
            b_pe[e_i] = (PE, PE.n)
            B.emit("vector", lambda e, erc=erc, lane=lane, grp=grp: e.tensor_copy(
                bft_sb[lane:lane + 64, erc * D + grp * 512:erc * D + (grp + 1) * 512],
                PS[0:64, 1024:1536]), ((PE, PE.n),), (VE, 1))
            bft_last = (VE, VE.n)
    bft_st = []
    for oe in range(16):
        src = bft_sb[:, :].rearrange("p (erc o) -> p erc o", erc=8)[
            :, :, oe * 256:(oe + 1) * 256]
        dst = bfT_d[oe].rearrange("(erc p) o -> p erc o", p=128)
        bft_st.append(dma("sync", dst, src, (bft_last,), st="s_bftst"))
    bft_done = bft_st[-1]

    if stage == "P0c":
        return _finish()
    fold_state = {"done": None}

    def fold_gen():
        mm_toks = []
        ev_tok = None
        st_tok = None
        u = 0
        for ic in range(NIC):
            for hh in range(2):
                bi = u % 2
                lw = (mm_toks[u - 2],) if u >= 2 else ()
                t_w = dma("sync", wq_stage[bi],
                          wq[ic * 128:(ic + 1) * 128, hh * 2048:(hh + 1) * 2048], lw,
                          st=f"s_wq{bi}")
                for hl in range(4):
                    h = 4 * hh + hl
                    for g in range(4):
                        ws = [t_w]
                        if hl == 0 and g == 0:
                            ws.append(ident_ready)
                            if ev_tok is not None and hh == 0:
                                ws.append(ev_tok)
                        B.emit("tensor", lambda e, bi=bi, hl=hl, g=g, h=h: e.matmul(
                            PS[:, 1536 + h * 128:1536 + (h + 1) * 128],
                            ident_f[:, :],
                            wq_stage[bi][:, hl * 512 + g * 128: hl * 512 + (g + 1) * 128],
                            start=(g == 0), stop=(g == 3)),
                            tuple(ws), (PE, 1) if (hl == 3 and g == 3) else None)
                mm_toks.append((PE, PE.n))
                u += 1
                yield
            ew = [(PE, PE.n)]
            if st_tok is not None:
                ew.append(st_tok)
            B.emit("vector", lambda e: e.tensor_copy(wqe_sb[:, :], PS[:, 1536:2560]),
                   tuple(ew), (VE, 1))
            ev_tok = (VE, VE.n)
            st_tok = dma("sync", wqe_d[ic * 128:(ic + 1) * 128, :], wqe_sb[:, :],
                         (ev_tok,), st="s_wqest")
        fold_state["done"] = st_tok

    fold_it = fold_gen()

    def fold_step(k=1):
        for _ in range(k):
            if fold_state["done"] is None:
                next(fold_it, None)

    fold_step(12)

    bw_state = {"done": None}

    def emit_bw_prep(region_free_tok, pe_banks_free):
        ld_toks = {}
        ev_done = {}
        st_done = {}
        for ot in range(2):
            ld_toks[ot] = dma("sync", bw_big[ot], bw[ot * 128:(ot + 1) * 128, :],
                              (region_free_tok,), st=f"s_bw{ot}")
        for ot in range(32):
            bi = ot % 2
            evs = {}
            for grp in range(8):
                bank = 3072 + (grp % 2) * 512
                gw = [ld_toks[ot], ident_ready]
                if grp >= 2:
                    gw.append(evs[grp - 2])
                elif ot >= 1:
                    gw.append(ev_done[ot - 1])
                else:
                    gw.append(pe_banks_free)
                if grp <= 1 and ot >= 2:
                    gw.append(st_done[ot - 2])
                for j in range(4):
                    ic = grp * 4 + j
                    B.emit("tensor", lambda e, ic=ic, j=j, bank=bank, bi=bi: e.transpose(
                        PS[:, bank + j * 128:bank + (j + 1) * 128],
                        bw_big[bi][:, ic * 128:(ic + 1) * 128], ident_f[:, :]),
                        tuple(gw) if j == 0 else (), (PE, 1) if j == 3 else None)
                pe_tok = (PE, PE.n)
                B.emit("vector", lambda e, grp=grp, bi=bi, bank=bank: e.tensor_copy(
                    bw_out[bi][:, grp * 512:(grp + 1) * 512],
                    PS[:, bank:bank + 512]), (pe_tok,), (VE, 1))
                evs[grp] = (VE, VE.n)
            ev_done[ot] = evs[7]
            dstv = bwT_d[ot // 2][:, (ot % 2) * 128:(ot % 2) * 128 + 128].rearrange(
                "(ic p) o -> p ic o", p=128)
            st_done[ot] = dma("sync", dstv,
                              bw_out[bi][:, :].rearrange("p (ic o) -> p ic o", ic=NIC),
                              (evs[7],), st="s_bwst")
            if ot + 2 < 32:
                ld_toks[ot + 2] = dma("sync", bw_big[bi],
                                      bw[(ot + 2) * 128:(ot + 3) * 128, :],
                                      (ev_done[ot],), st=f"s_bw{bi}")
        bw_state["done"] = st_done[31]
        bw_state["last_evac"] = ev_done[31]

    if stage == "P0":
        return _finish()
    def vdrain():
        B.emit("vector", None, ((VE, VE.n),))

    def ln_stats(src_ap):
        B.emit("vector", lambda e: e.reduce_sum(st1[:, 0:1], src_ap, axis=AXX), (), (VE, 1))
        vdrain()
        B.emit("scalar", lambda e: e.activation(
            sq_sb[:, :], src_ap, AF.Square, accum_out=st1[:, 1:2]), (), (SE, 1))
        sq_tok = (SE, SE.n)
        B.emit("vector", lambda e: e.tensor_scalar(
            st1[:, 2:3], st1[:, 0:1], -1.0 / D, None, op0=ALU.mult), (), (VE, 1))
        vdrain()
        B.emit("vector", lambda e: e.tensor_scalar(
            st1[:, 3:4], st1[:, 1:2], 1.0 / D, None, op0=ALU.mult), (sq_tok,), (VE, 1))
        vdrain()
        B.emit("vector", lambda e: e.tensor_mul(st1[:, 4:5], st1[:, 2:3], st1[:, 2:3]),
               (), (VE, 1))
        vdrain()
        B.emit("vector", lambda e: e.tensor_scalar(
            st1[:, 5:6], st1[:, 3:4], st1[:, 4:5], EPS, op0=ALU.subtract, op1=ALU.add),
            (), (VE, 1))
        vdrain()
        B.emit("vector", lambda e: e.reciprocal(st1[:, 6:7], st1[:, 5:6]), (), (VE, 1))
        B.emit("scalar", lambda e: e.activation(st1[:, 7:8], st1[:, 6:7], AF.Sqrt),
               ((VE, VE.n),), (SE, 1))
        rstd_tok = (SE, SE.n)
        B.emit("scalar", None, (rstd_tok,))
        B.emit("vector", lambda e: e.tensor_mul(st1[:, 8:9], st1[:, 2:3], st1[:, 7:8]),
               (rstd_tok,), (VE, 1))
        vdrain()
        return rstd_tok, (VE, VE.n)

    x_free = None
    h1T_last = None
    a1_bar = ([(VE, VE.n), (SE, SE.n), (GE, GE.n), (PE, PE.n)] +
              [(s, s.n) for s in dstreams.values()])
    for tt in range(NTT):
        w = ((x_free,) if x_free else tuple(a1_bar))
        t_x = dma("sync", x_buf[:, :], x_in[tt * 128:(tt + 1) * 128, :], w, st="s_x")
        B.emit("vector", None, (t_x,))
        B.emit("scalar", None, (t_x,))
        rstd_tok, nb_tok = ln_stats(x_buf[:, :])
        nw = [nb_tok]
        if tt > 0:
            nw.append(h1_pe_prev)
        B.emit("scalar", lambda e: e.activation(
            h1b[:, :], x_buf[:, :], AF.Identity, bias=st1[:, 8:9], scale=st1[:, 7:8]),
            tuple(nw), (SE, 1))
        h1_tok = (SE, SE.n)
        x_free = h1_tok
        ev2 = {}
        for grp in range(8):
            bank = (grp % 2) * 512
            gw = [h1_tok] if grp == 0 else []
            if tt == 0 and grp == 0:
                gw += list(a1_bar)
            if grp >= 2:
                gw.append(ev2[grp - 2])
            elif tt > 0:
                gw.append(h1T_prev[grp])
            pb = (grp % 2) * 1024
            for j in range(4):
                ic = grp * 4 + j
                B.emit("tensor", lambda e, ic=ic, pb=pb, j=j: e.transpose(
                    PSb[:, pb + j * 128:pb + (j + 1) * 128],
                    h1b[:, ic * 128:(ic + 1) * 128], ident_b[:, :]),
                    tuple(gw) if j == 0 else (), (PE, 1) if j == 3 else None)
            pe_tok = (PE, PE.n)
            dst = h1T[:, :].rearrange("p (ic t) -> p ic t", ic=NIC)[
                :, grp * 4:(grp + 1) * 4, tt * 128:(tt + 1) * 128]
            B.emit("vector", lambda e, dst=dst, pb=pb: e.tensor_copy(
                dst, PSb[:, pb:pb + 512].rearrange("p (a b) -> p a b", a=4)),
                (pe_tok,), (VE, 1))
            ev2[grp] = (VE, VE.n)
        h1T_prev = dict(ev2)
        h1T_prev[0] = ev2[6]
        h1T_prev[1] = ev2[7]
        h1T_last = ev2[7]
        h1_pe_prev = (PE, PE.n)
        if debug and tt == 0:
            h1_pe_prev = dma("gpsimd", dbg["dbg_h1b"], h1b[:, :], (h1_tok, ev2[7]), st="g_dbg")
            dbg_written.add("dbg_h1b")
        fold_step(7)
    fold_step(4)
    h1T_done = h1T_last
    if debug:
        for _q in range(8):
            dma("gpsimd", dbg["dbg_h1T"][:, _q * 4096:(_q + 1) * 4096],
                h1T[:, _q * 4096:(_q + 1) * 4096], (h1T_done,), st="g_dbg")
        dbg_written.add("dbg_h1T")

    if stage == "A1":
        return _finish()
    def proj_pass(kind, out_tiles, wsrc, wcols, cast, out_sb, out_off, scale=None,
                  extra_wait=(), hook=None, deep=False):
        ntile = len(out_tiles)
        c0, c1 = wcols
        ncol = c1 - c0
        q = "gpsimd" if cast else "sync"
        bufs = list(ws_buf) + list(deep or [])
        nbuf = len(bufs)
        blk_tok = {}
        blk_pe = {}

        NB = 8
        BL = 4

        def load_blk(icb, lw):
            src = wsrc.rearrange("(ic p) o -> p ic o", p=128)[
                :, icb * BL:(icb + 1) * BL, c0:c1]
            dstv = bufs[icb % nbuf][:, 0:BL * ncol].rearrange(
                "p (ic o) -> p ic o", ic=BL)
            blk_tok[icb] = dma(q, dstv, src, lw,
                               st=f"ws{'g' if cast else 's'}{icb % nbuf}")

        for pre in range(min(nbuf, NB)):
            load_blk(pre, extra_wait)
        for ic in range(NIC):
            icb, j = ic // BL, ic % BL
            if j == 0 and hook:
                hook()
            wt = blk_tok[icb]
            for ti in range(ntile):
                if kind == "fm":
                    st_ap = bufs[icb % nbuf][:, j * ncol + ti * 128:
                                             j * ncol + (ti + 1) * 128]
                    for half in range(2):
                        last = (ic == NIC - 1 and ti == ntile - 1 and half == 1)
                        B.emit("tensor", lambda e, st_ap=st_ap, ic=ic, ti=ti, half=half: e.matmul(
                            PS[:, (2 * ti + half) * 512:(2 * ti + half + 1) * 512],
                            st_ap,
                            h1T[:, ic * T + half * 512: ic * T + (half + 1) * 512],
                            start=(ic == 0), stop=(ic == NIC - 1)),
                            ((wt, h1T_done) + tuple(extra_wait))
                            if (ic == 0 and ti == 0 and half == 0) else (wt,),
                            (PE, 1) if (last or (j == BL - 1 and ti == ntile - 1 and half == 1))
                            else None)
                else:
                    tile = out_tiles[ti]
                    last = (ic == NIC - 1 and ti == ntile - 1)
                    B.emit("tensor", lambda e, ic=ic, tile=tile, ti=ti, icb=icb, j=j: e.matmul(
                        PS[:, ti * 512:(ti + 1) * 512],
                        h1T[:, ic * T + tile * 128:ic * T + (tile + 1) * 128],
                        bufs[icb % nbuf][:, j * ncol:(j + 1) * ncol],
                        start=(ic == 0), stop=(ic == NIC - 1)),
                        ((wt, h1T_done) + tuple(extra_wait))
                        if (ic == 0 and ti == 0) else (wt,),
                        (PE, 1) if (last or (j == BL - 1 and ti == ntile - 1)) else None)
            if j == BL - 1:
                blk_pe[icb] = (PE, PE.n)
                if icb + nbuf < NB:
                    load_blk(icb + nbuf, (blk_pe[icb],))
        mm_tok = (PE, PE.n)
        ev = None
        for ti in range(ntile):
            tile = out_tiles[ti]
            if kind == "fm":
                psl = PS[:, ti * 1024:(ti + 1) * 1024]
                dst = out_sb[:, out_off + tile * T: out_off + (tile + 1) * T]
            else:
                psl = PS[:, ti * 512:(ti + 1) * 512]
                dst = out_sb[:, tile * KV + out_off: tile * KV + out_off + ncol]
            if scale is None:
                B.emit("vector", lambda e, dst=dst, psl=psl: e.tensor_copy(dst, psl),
                       (mm_tok,), (VE, 1))
            else:
                B.emit("vector", lambda e, dst=dst, psl=psl: e.tensor_scalar(
                    dst, psl, scale, None, op0=ALU.mult), (mm_tok,), (VE, 1))
            ev = (VE, VE.n)
        return ev

    def a2_hook():
        pass

    kev = proj_pass("fm", [0, 1, 2], wk, (0, 384), True, kT_own, 0,
                    extra_wait=(fold_state["done"],), hook=a2_hook, deep=list(ws_buf2) + list(ws_buf3))
    kev = proj_pass("fm", [3, 4, 5], wk, (384, 768), True, kT_own, 0,
                    extra_wait=(kev,), hook=a2_hook, deep=list(ws_buf2) + list(ws_buf3))
    kev = proj_pass("fm", [6, 7], wk, (768, 1024), True, kT_own, 0,
                    extra_wait=(kev,), hook=a2_hook, deep=list(ws_buf2) + list(ws_buf3))
    vev = kev
    for tg in range(2):
        for vf in range(2):
            vev = proj_pass("tm", [tg * 4 + i for i in range(4)], wv,
                            (vf * 512, (vf + 1) * 512), True, v_own, vf * 512,
                            extra_wait=(vev,), hook=a2_hook, deep=list(ws_buf2) + list(ws_buf3))
    assert fold_state["done"] is not None, "fold must complete in P0+A1"
    if debug:
        for _q in range(4):
            dma("gpsimd", dbg["dbg_wqe"][_q * 1024:(_q + 1) * 1024, :],
                wqe_d[_q * 1024:(_q + 1) * 1024, :], (fold_state["done"],),
                st="g_dbg")
        dbg_written.add("dbg_wqe")
    st_toks = []
    for kc in range(8):
        st_toks.append(dma("sync", kv_in[kc * 128:(kc + 1) * 128, :],
                           kT_own[:, kc * T:(kc + 1) * T], (kev,), st="s_kvst"))
    for tt in range(8):
        st_toks.append(dma("sync", kv_in[T + tt * 128: T + (tt + 1) * 128, :],
                           v_own[:, tt * KV:(tt + 1) * KV], (vev,), st="s_kvst"))
    B.emit("gpsimd", lambda e: e.collective_compute(
        "AllGather", ALU.bypass,
        replica_groups=[[0, 1, 2, 3], [4, 5, 6, 7]],
        ins=[kv_in[:, :]], outs=[kv_out[:, :]]),
        tuple(st_toks), (CC, 1))
    cc_done = (CC, CC.n)
    qev = proj_pass("fm", [0, 1, 2], wqe_d, (0, 384), False, qT, 0, scale=QSCALE,
                    extra_wait=(vev, fold_state["done"]), hook=a2_hook, deep=ws_buf3)
    qev = proj_pass("fm", [3, 4, 5], wqe_d, (384, 768), False, qT, 0, scale=QSCALE,
                    extra_wait=(qev,), hook=a2_hook, deep=ws_buf3)
    qev = proj_pass("fm", [6, 7], wqe_d, (768, 1024), False, qT, 0, scale=QSCALE,
                    extra_wait=(qev,), hook=a2_hook, deep=ws_buf3)
    q_done = qev

    dbg_toks = []
    if debug:
        dbg_toks.append(dma("gpsimd", dbg["dbg_kT"], kT_own[:, :], (kev,), st="g_dbg"))
        dbg_toks.append(dma("gpsimd", dbg["dbg_v"], v_own[:, :], (vev,), st="g_dbg"))
        dbg_toks.append(dma("gpsimd", dbg["dbg_qT"], qT[:, :], (q_done,), st="g_dbg"))
        dbg_written.update(["dbg_kT", "dbg_v", "dbg_qT"])

    if stage == "A2":
        return _finish()

    emit_bw_prep(st_toks[-1], q_done)
    bwT_done = bw_state["done"]

    if stage == "AG":
        return _finish()
    v_toks = []
    for r in range(4):
        src = kv_out[2048 * r + 1024: 2048 * r + 2048, :].rearrange(
            "(kc p) f -> p kc f", p=128)
        dstv = v_sb[:, :].rearrange("p (kc f) -> p kc f", kc=NKC)[
            :, r * 8:(r + 1) * 8, :]
        v_toks.append(dma("gpsimd", dstv, src,
                          (cc_done, q_done) + tuple(dbg_toks), st="g_vsb"))
    v_ready = v_toks[-1]

    kh_tok = None
    last_smm = None
    prev_av_ev = None
    slot_free = {0: None, 1: [bw_state["last_evac"]]}
    slot_exp = {0: None, 1: None}
    prev_D_pe2 = None
    for h in range(NH):
        kw = [cc_done] if kh_tok is None else [last_smm]
        kts = []
        for r in range(4):
            kts.append(dma("gpsimd", kh_sb[:, r * 1024:(r + 1) * 1024],
                           kv_out[2048 * r + h * 128: 2048 * r + (h + 1) * 128, :],
                           tuple(kw) if r == 0 else (), st="g_kh"))
        kh_tok = kts[-1]
        for qh in range(2):
            pt_evs = []
            for qtl in range(4):
                qt = qh * 4 + qtl
                exp_toks = []
                negm_toks = []
                for c in range(2):
                    base = c * 2048
                    for m in range(4):
                        ws = [kh_tok, q_done]
                        if m == 0:
                            if slot_exp[c] is not None:
                                ws.append(slot_exp[c])
                            if slot_free[c] is not None:
                                ws += slot_free[c]
                            if c == 0 and prev_av_ev is not None:
                                ws.append(prev_av_ev)
                        B.emit("tensor", lambda e, h=h, qt=qt, base=base, m=m: e.matmul(
                            PS[:, base + m * 512:base + (m + 1) * 512],
                            qT[:, h * T + qt * 128:h * T + (qt + 1) * 128],
                            kh_sb[:, base + m * 512:base + (m + 1) * 512],
                            start=True, stop=True),
                            tuple(ws), (PE, 1) if m == 3 else None)
                    smm_c = (PE, PE.n)
                    sub = PS[:, base:base + 2048].rearrange(
                        "p (a b) -> p b a", b=8)[:, 0, :]
                    B.emit("vector", lambda e, sub=sub, c=c: e.reduce_max(
                        st2[:, c:c + 1], sub, axis=AXX, negate=True),
                        (smm_c,), (VE, 1))
                    negm_toks.append((VE, VE.n))
                    B.emit("scalar", lambda e, base=base, c=c: e.activation(
                        P_sb[:, base:base + 2048], PS[:, base:base + 2048], AF.Exp,
                        bias=st2[:, c:c + 1], scale=1.0,
                        accum_out=st2[:, 2 + c:3 + c]), (negm_toks[-1],), (SE, 1))
                    exp_toks.append((SE, SE.n))
                    slot_exp[c] = exp_toks[-1]
                last_smm = (PE, PE.n)
                B.emit("vector", lambda e: e.tensor_tensor(
                    st2[:, 4:5], st2[:, 0:1], st2[:, 1:2], op=ALU.min),
                    (negm_toks[1],), (VE, 1))
                B.emit("vector", None, ((VE, VE.n),))
                B.emit("vector", lambda e: e.tensor_scalar(
                    st2[:, 6:7], st2[:, 4:5], st2[:, 0:1], None, op0=ALU.subtract),
                    (), (VE, 1))
                B.emit("vector", None, ((VE, VE.n),))
                B.emit("vector", lambda e: e.tensor_scalar(
                    st2[:, 7:8], st2[:, 4:5], st2[:, 1:2], None, op0=ALU.subtract),
                    (), (VE, 1))
                B.emit("vector", None, ((VE, VE.n),))
                B.emit("scalar", lambda e: e.activation(
                    st2[:, 8:10], st2[:, 6:8], AF.Exp), (exp_toks[1],), (SE, 1))
                sexp_tok = (SE, SE.n)
                B.emit("vector", lambda e: e.tensor_mul(
                    st2[:, 10:11], st2[:, 2:3], st2[:, 8:9]), (sexp_tok,), (VE, 1))
                B.emit("vector", None, ((VE, VE.n),))
                B.emit("vector", lambda e: e.tensor_mul(
                    st2[:, 11:12], st2[:, 3:4], st2[:, 9:10]), (), (VE, 1))
                B.emit("vector", None, ((VE, VE.n),))
                B.emit("vector", lambda e: e.tensor_add(
                    st2[:, 12:13], st2[:, 10:11], st2[:, 11:12]), (), (VE, 1))
                B.emit("vector", None, ((VE, VE.n),))
                B.emit("vector", lambda e: e.reciprocal(
                    st2[:, 13:14], st2[:, 12:13]), (), (VE, 1))
                B.emit("vector", None, ((VE, VE.n),))
                B.emit("vector", lambda e: e.tensor_scalar(
                    st2[:, 14:15], st2[:, 8:9], st2[:, 13:14], None, op0=ALU.mult),
                    (), (VE, 1))
                B.emit("vector", None, ((VE, VE.n),))
                B.emit("vector", lambda e: e.tensor_scalar(
                    st2[:, 15:16], st2[:, 9:10], st2[:, 13:14], None, op0=ALU.mult),
                    (), (VE, 1))
                B.emit("vector", None, ((VE, VE.n),))
                dw = [prev_D_pe2] if prev_D_pe2 is not None else []
                B.emit("vector", lambda e: e.tensor_scalar(
                    D_sb[:, 0:128], ident_b[:, :], st2[:, 14:15], None, op0=ALU.mult),
                    tuple(dw), (VE, 1))
                B.emit("vector", lambda e: e.tensor_scalar(
                    D_sb[:, 128:256], ident_b[:, :], st2[:, 15:16], None, op0=ALU.mult),
                    (), (VE, 1))
                D_tok = (VE, VE.n)
                for c in range(2):
                    evs = {}
                    for grp in range(4):
                        pb = c * 2048 + (grp % 2) * 512
                        gw = [D_tok] if grp == 0 else []
                        if grp >= 2:
                            gw.append(evs[grp - 2])
                        for j in range(4):
                            kc = c * 16 + grp * 4 + j
                            B.emit("tensor", lambda e, kc=kc, pb=pb, j=j, c=c: e.matmul(
                                PS[:, pb + j * 128:pb + (j + 1) * 128],
                                P_sb[:, kc * 128:(kc + 1) * 128],
                                D_sb[:, c * 128:(c + 1) * 128],
                                start=True, stop=True),
                                tuple(gw) if j == 0 else (),
                                (PE, 1) if j == 3 else None)
                        pe_tok = (PE, PE.n)
                        dst = PT_half[:, :].rearrange("p (kc t) -> p kc t", kc=NKC)[
                            :, c * 16 + grp * 4:c * 16 + (grp + 1) * 4,
                            qtl * 128:(qtl + 1) * 128]
                        psrc = PS[:, pb:pb + 512].rearrange("p (a b) -> p a b", a=4)
                        if grp % 2 == 0:
                            B.emit("vector", lambda e, dst=dst, psrc=psrc: e.tensor_copy(
                                dst, psrc), (pe_tok,), (VE, 1))
                            evs[grp] = (VE, VE.n)
                        else:
                            B.emit("scalar", lambda e, dst=dst, psrc=psrc: e.copy(
                                dst, psrc), (pe_tok,), (SE, 1))
                            evs[grp] = (SE, SE.n)
                    slot_free[c] = [evs[2], evs[3]]
                    pt_evs += [evs[2], evs[3]]
                prev_D_pe2 = (PE, PE.n)
            pt_v_last = pt_evs[-2]
            pt_s_last = pt_evs[-1]
            for kc in range(NKC):
                ws = []
                if kc == 0:
                    ws = list(pt_evs) + [v_ready]
                    if prev_av_ev is not None:
                        ws.append(prev_av_ev)
                B.emit("tensor", lambda e, kc=kc, h=h: e.matmul(
                    PS[:, 0:512],
                    v_sb[:, kc * KV + h * 128: kc * KV + (h + 1) * 128],
                    PT_half[:, kc * 512:(kc + 1) * 512],
                    start=(kc == 0), stop=(kc == NKC - 1)),
                    tuple(ws), (PE, 1) if kc == NKC - 1 else None)
            av_tok = (PE, PE.n)
            B.emit("vector", lambda e, h=h, qh=qh: e.tensor_copy(
                vals_sb[:, h * T + qh * 512: h * T + (qh + 1) * 512], PS[:, 0:512]),
                (av_tok,), (VE, 1))
            prev_av_ev = (VE, VE.n)
    attn_done = prev_av_ev

    if debug:
        dma("gpsimd", dbg["dbg_vals"], vals_sb[:, :], (attn_done,), st="g_dbg")
        dbg_written.add("dbg_vals")

    if stage == "A4":
        return _finish()
    wo_toks = []
    for kvc in range(8):
        wo_toks.append(dma("gpsimd", wo_sb[:, kvc * D:(kvc + 1) * D],
                           wo[kvc * 128:(kvc + 1) * 128, :],
                           (attn_done,) if kvc == 0 else (wo_toks[-1],),
                           st="g_wo"))
    atst_free = [None, None]
    x5_free = None
    prev_ev_tok = None
    a5_h2_st = []
    for tt in range(NTT):
        bi = tt % 2
        for kvc in range(8):
            ws = []
            if kvc == 0:
                ws = [wo_toks[0], attn_done]
                if atst_free[bi] is not None:
                    ws.append(atst_free[bi])
                if prev_ev_tok is not None:
                    ws.append(prev_ev_tok)
            else:
                ws = [wo_toks[kvc]]
            for ob in range(8):
                B.emit("tensor", lambda e, kvc=kvc, ob=ob, tt=tt: e.matmul(
                    PS[:, ob * 512:(ob + 1) * 512],
                    vals_sb[:, kvc * T + tt * 128: kvc * T + (tt + 1) * 128],
                    wo_sb[:, kvc * D + ob * 512: kvc * D + (ob + 1) * 512],
                    start=(kvc == 0), stop=(kvc == 7)),
                    tuple(ws) if ob == 0 else (),
                    (PE, 1) if (kvc == 7 and ob == 7) else None)
        wo_tok = (PE, PE.n)
        B.emit("vector", lambda e, bi=bi: e.tensor_copy(
            atst[bi][:, 0:2048], PS[:, 0:2048]), (wo_tok,), (VE, 1))
        B.emit("vector", lambda e, bi=bi: e.tensor_copy(
            atst[bi][:, 2048:4096], PS[:, 2048:4096]), (), (VE, 1))
        ev_tok = (VE, VE.n)
        prev_ev_tok = ev_tok
        B.emit("vector", None, (ev_tok,))
        B.emit("scalar", None, (ev_tok,))
        rstd_tok, nb_tok = ln_stats(atst[bi][:, :])
        lnw = [nb_tok]
        if tt > 0:
            lnw.append(prev_x1_tok)
        else:
            lnw.append(bw_state["done"])
        B.emit("scalar", lambda e, bi=bi: e.activation(
            ln_sb[:, :], atst[bi][:, :], AF.Identity, bias=st1[:, 8:9], scale=st1[:, 7:8]),
            tuple(lnw), (SE, 1))
        ln_tok = (SE, SE.n)
        atst_free[bi] = ln_tok
        xw = [attn_done] if tt == 0 else [x5_free, prev_t_h2]
        t_x = dma("sync", x5_buf[:, :], x_in[tt * 128:(tt + 1) * 128, :], tuple(xw), st="s_x5")
        B.emit("vector", lambda e: e.tensor_add(x5_buf[:, :], x5_buf[:, :], ln_sb[:, :]),
               (t_x, ln_tok), (VE, 1))
        x1_tok = (VE, VE.n)
        prev_x1_tok = x1_tok
        dma("sync", x1sp[tt * 128:(tt + 1) * 128, :], x5_buf[:, :], (x1_tok,), st="s_sp")
        B.emit("vector", None, (x1_tok,))
        B.emit("scalar", None, (x1_tok,))
        rstd2, nb2 = ln_stats(x5_buf[:, :])
        B.emit("vector", lambda e: e.tensor_scalar(
            h2b[:, :], x5_buf[:, :], st1[:, 2:3], st1[:, 7:8],
            op0=ALU.add, op1=ALU.mult), (rstd2, nb2), (VE, 1))
        h2_tok = (VE, VE.n)
        x5_free = h2_tok
        a5_h2_st.append(dma("sync", h2sp[tt * 128:(tt + 1) * 128, :], h2b[:, :],
                            (h2_tok,), st="s_sp"))
        prev_t_h2 = a5_h2_st[-1]
        if debug:
            dma("gpsimd", dbg["dbg_h2"][tt * 128:(tt + 1) * 128, :], h2b[:, :], (h2_tok,), st="g_dbg")
            dbg_written.add("dbg_h2")
    a5_done = a5_h2_st[-1]

    if stage == "A5":
        return _finish()
    h2rd_free = None
    stage_evs = []
    ep_done = []
    slot_free = [None, None]
    x1_free = None
    for hf in range(2):
        tr_last = None
        for ttl in range(4):
            tt = hf * 4 + ttl
            w = [a5_done]
            if h2rd_free is not None:
                w.append(h2rd_free)
            t_h2 = dma("sync", h2b[:, :], h2sp[tt * 128:(tt + 1) * 128, :], tuple(w), st="s_h2rd")
            evs = {}
            for grp in range(8):
                bank = (grp % 2) * 512
                gw = [t_h2] if grp == 0 else []
                if hf == 1 and ttl == 0 and grp == 0:
                    gw.append(ep_done[-1])
                if grp >= 2:
                    gw.append(evs[grp - 2])
                elif ttl > 0:
                    gw.append(prev_evs[grp + 6])
                pb = (grp % 2) * 1024
                for j in range(4):
                    ic = grp * 4 + j
                    B.emit("tensor", lambda e, ic=ic, pb=pb, j=j: e.transpose(
                        PSb[:, pb + j * 128:pb + (j + 1) * 128],
                        h2b[:, ic * 128:(ic + 1) * 128], ident_b[:, :]),
                        tuple(gw) if j == 0 else (), (PE, 1) if j == 3 else None)
                pe_tok = (PE, PE.n)
                dst = h2T[:, :].rearrange("p (ic t) -> p ic t", ic=NIC)[
                    :, grp * 4:(grp + 1) * 4, ttl * 128:(ttl + 1) * 128]
                B.emit("vector", lambda e, dst=dst, pb=pb: e.tensor_copy(
                    dst, PSb[:, pb:pb + 512].rearrange("p (a b) -> p a b", a=4)),
                    (pe_tok,), (VE, 1))
                evs[grp] = (VE, VE.n)
            prev_evs = evs
            h2rd_free = evs[7]
            tr_last = evs[7]
        h2T_tok = tr_last

        for ttl in range(4):
            for ic in range(NIC):
                B.emit("tensor", lambda e, ic=ic, ttl=ttl: e.matmul(
                    PS[:, ttl * 512: ttl * 512 + 80],
                    h2T[:, ic * 512 + ttl * 128: ic * 512 + (ttl + 1) * 128],
                    arT[:, ic * 80:(ic + 1) * 80],
                    start=(ic == 0), stop=(ic == NIC - 1)),
                    (h2T_tok, ar_done) if (ttl == 0 and ic == 0) else (),
                    (PE, 1) if ic == NIC - 1 else None)
        ar_mm = (PE, PE.n)
        for ttl in range(4):
            B.emit("vector", lambda e, ttl=ttl: e.tensor_copy(
                res16[:, ttl * 64:(ttl + 1) * 64], PS[:, ttl * 512: ttl * 512 + 64]),
                (ar_mm,), (VE, 1))
            B.emit("vector", lambda e, ttl=ttl: e.tensor_copy(
                rtr[:, ttl * 16:(ttl + 1) * 16], PS[:, ttl * 512 + 64: ttl * 512 + 80]),
                (), (VE, 1))
        ar_ev = (VE, VE.n)
        if debug:
            dma("gpsimd", dbg["dbg_res"][hf * 512:(hf + 1) * 512, 0:64].rearrange(
                "(ttl p) r -> p ttl r", p=128),
                res16[:, :].rearrange("p (ttl r) -> p ttl r", ttl=4), (ar_ev,),
                st="g_dbg")
            dma("gpsimd", dbg["dbg_res"][hf * 512:(hf + 1) * 512, 64:80].rearrange(
                "(ttl p) r -> p ttl r", p=128),
                rtr[:, 0:64].rearrange("p (ttl r) -> p ttl r", ttl=4), (ar_ev,),
                st="g_dbg")

        ht_last = None
        for ttl in range(4):
            lg = rtr[:, ttl * 16:(ttl + 1) * 16]
            B.emit("vector", lambda e, lg=lg: e.reduce_max(
                rtr[:, 64:65], lg, axis=AXX, negate=True), (ar_ev,), (VE, 1))
            B.emit("scalar", lambda e, lg=lg: e.activation(
                rtr[:, 80:96], lg, AF.Exp, bias=rtr[:, 64:65], scale=1.0,
                accum_out=rtr[:, 65:66]), ((VE, VE.n),), (SE, 1))
            B.emit("vector", lambda e: e.reciprocal(rtr[:, 66:67], rtr[:, 65:66]),
                   ((SE, SE.n),), (VE, 1))
            B.emit("vector", None, ((VE, VE.n),))
            B.emit("vector", lambda e: e.tensor_scalar(
                rtr[:, 96:112], rtr[:, 80:96], rtr[:, 66:67], None, op0=ALU.mult),
                (), (VE, 1))
            B.emit("vector", None, ((VE, VE.n),))
            B.emit("vector", lambda e: e.reduce_max(
                rtr[:, 67:68], rtr[:, 96:112], axis=AXX), (), (VE, 1))
            B.emit("vector", None, ((VE, VE.n),))
            B.emit("vector", lambda e: e.tensor_scalar(
                rtr[:, 112:128], rtr[:, 96:112], rtr[:, 67:68], None, op0=ALU.is_equal),
                (), (VE, 1))
            B.emit("vector", None, ((VE, VE.n),))
            B.emit("vector", lambda e: e.tensor_sub(
                rtr[:, 128:144], rtr[:, 96:112], rtr[:, 112:128]), (), (VE, 1))
            B.emit("vector", None, ((VE, VE.n),))
            B.emit("vector", lambda e: e.reduce_max(
                rtr[:, 68:69], rtr[:, 128:144], axis=AXX), (), (VE, 1))
            B.emit("vector", None, ((VE, VE.n),))
            B.emit("vector", lambda e: e.tensor_scalar(
                rtr[:, 144:160], rtr[:, 128:144], rtr[:, 68:69], None, op0=ALU.is_equal),
                (), (VE, 1))
            B.emit("vector", None, ((VE, VE.n),))
            B.emit("vector", lambda e: e.tensor_scalar(
                rtr[:, 69:70], rtr[:, 67:68], rtr[:, 68:69], 1e-6,
                op0=ALU.add, op1=ALU.add), (), (VE, 1))
            B.emit("vector", None, ((VE, VE.n),))
            B.emit("vector", lambda e: e.reciprocal(rtr[:, 70:71], rtr[:, 69:70]),
                   (), (VE, 1))
            B.emit("vector", None, ((VE, VE.n),))
            B.emit("vector", lambda e: e.tensor_scalar(
                rtr[:, 71:72], rtr[:, 67:68], rtr[:, 70:71], 0.5,
                op0=ALU.mult, op1=ALU.mult), (), (VE, 1))
            B.emit("vector", None, ((VE, VE.n),))
            B.emit("vector", lambda e: e.tensor_scalar(
                rtr[:, 72:73], rtr[:, 68:69], rtr[:, 70:71], 0.5,
                op0=ALU.mult, op1=ALU.mult), (), (VE, 1))
            B.emit("vector", None, ((VE, VE.n),))
            B.emit("vector", lambda e: e.tensor_scalar(
                rtr[:, 112:128], rtr[:, 112:128], rtr[:, 71:72], None, op0=ALU.mult),
                (), (VE, 1))
            B.emit("vector", None, ((VE, VE.n),))
            B.emit("vector", lambda e: e.tensor_scalar(
                rtr[:, 144:160], rtr[:, 144:160], rtr[:, 72:73], None, op0=ALU.mult),
                (), (VE, 1))
            B.emit("vector", None, ((VE, VE.n),))
            B.emit("vector", lambda e: e.tensor_add(
                rtr[:, 112:128], rtr[:, 112:128], rtr[:, 144:160]), (), (VE, 1))
            B.emit("vector", None, ((VE, VE.n),))
            for e_i in range(16):
                B.emit("vector", lambda e, e_i=e_i, ttl=ttl: e.tensor_scalar(
                    H_sb[:, e_i * 64:(e_i + 1) * 64],
                    res16[:, ttl * 64:(ttl + 1) * 64],
                    rtr[:, 112 + e_i:113 + e_i], None, op0=ALU.mult),
                    (), (VE, 1) if e_i == 15 else None)
            h_tok = (VE, VE.n)
            if debug:
                dma("gpsimd", dbg["dbg_H"][(hf * 4 + ttl) * 128:(hf * 4 + ttl + 1) * 128, :],
                    H_sb[:, :], (h_tok,), st="g_dbg")
            for grp in range(2):
                pb = grp * 1024
                gw = [h_tok, ar_ev] if grp == 0 else []
                for j in range(4):
                    erc = grp * 4 + j
                    B.emit("tensor", lambda e, erc=erc, pb=pb, j=j: e.transpose(
                        PSb[:, pb + j * 128:pb + (j + 1) * 128],
                        H_sb[:, erc * 128:(erc + 1) * 128], ident_b[:, :]),
                        tuple(gw) if j == 0 else (), (PE, 1) if j == 3 else None)
                pe_tok = (PE, PE.n)
                dst = HT_sb[:, :].rearrange("p (erc t) -> p erc t", erc=8)[
                    :, grp * 4:(grp + 1) * 4, ttl * 128:(ttl + 1) * 128]
                B.emit("vector", lambda e, dst=dst, pb=pb: e.tensor_copy(
                    dst, PSb[:, pb:pb + 512].rearrange("p (a b) -> p a b", a=4)),
                    (pe_tok,), (VE, 1))
                ht_last = (VE, VE.n)
        ht_done = ht_last

        for oe in range(16):
            gi = hf * 16 + oe
            si = gi % 2
            w = [bwT_done]
            if slot_free[si] is not None:
                w.append(slot_free[si])
            if gi == 0:
                w.append(a5_done)
            t_w1 = dma("sync", bwT_s[si][:, :].rearrange("p (ic o) -> p ic o", ic=NIC),
                       bwT_d[oe].rearrange("(ic p) o -> p ic o", p=128), tuple(w),
                       st=f"s_sl{si}")
            t_w2 = dma("sync", bfT_s[si][:, :].rearrange("p (erc o) -> p erc o", erc=8),
                       bfT_d[oe].rearrange("(erc p) o -> p erc o", p=128),
                       (bft_done,) + ((slot_free[si],) if slot_free[si] else ()),
                       st=f"s_sl{si}")
            goff = (gi % 2) * 2048
            for ic in range(NIC):
                for ttl in range(4):
                    ws = []
                    if ic == 0 and ttl == 0:
                        ws = [t_w1, t_w2, ht_done]
                        if len(stage_evs) >= 2:
                            ws.append(stage_evs[gi - 2])
                    B.emit("tensor", lambda e, ic=ic, ttl=ttl, si=si, goff=goff: e.matmul(
                        PS[:, goff + ttl * 512: goff + ttl * 512 + 256],
                        h2T[:, ic * 512 + ttl * 128: ic * 512 + (ttl + 1) * 128],
                        bwT_s[si][:, ic * 256:(ic + 1) * 256],
                        start=(ic == 0), stop=False),
                        tuple(ws), None)
            for erc in range(8):
                for ttl in range(4):
                    ws = [t_w2] if (erc == 0 and ttl == 0) else []
                    B.emit("tensor", lambda e, erc=erc, ttl=ttl, si=si, goff=goff: e.matmul(
                        PS[:, goff + ttl * 512: goff + ttl * 512 + 256],
                        HT_sb[:, erc * 512 + ttl * 128: erc * 512 + (ttl + 1) * 128],
                        bfT_s[si][:, erc * 256:(erc + 1) * 256],
                        start=False, stop=(erc == 7)),
                        tuple(ws), (PE, 1) if (erc == 7 and ttl == 3) else None)
            mm_tok = (PE, PE.n)
            slot_free[si] = mm_tok
            for ttl in range(4):
                B.emit("vector", lambda e, ttl=ttl, oe=oe, goff=goff: e.tensor_tensor(
                    stg_sb[:, ttl * D + oe * 256: ttl * D + (oe + 1) * 256],
                    PS[:, goff + ttl * 512: goff + ttl * 512 + 256],
                    bb16[:, oe * 256:(oe + 1) * 256], op=ALU.add),
                    (mm_tok, bb_ready), (VE, 1) if ttl == 3 else None)
            stage_evs.append((VE, VE.n))
        pass_done = stage_evs[-1]

        for ttl in range(4):
            tt = hf * 4 + ttl
            stg = stg_sb[:, ttl * D:(ttl + 1) * D]
            B.emit("vector", None, (pass_done,))
            B.emit("scalar", None, (pass_done,))
            rstd3, nb3 = ln_stats(stg)
            lnw = [nb3] + ([ep_done[-1]] if ep_done else [])
            B.emit("scalar", lambda e, stg=stg: e.activation(
                ln_sb[:, :], stg, AF.Identity, bias=st1[:, 8:9], scale=st1[:, 7:8]),
                tuple(lnw), (SE, 1))
            ln_tok = (SE, SE.n)
            xw = [a5_done]
            if x1_free:
                xw.append(x1_free)
            t_x1 = dma("sync", x1buf[:, :], x1sp[tt * 128:(tt + 1) * 128, :], tuple(xw), st="s_x1rd")
            B.emit("vector", lambda e: e.tensor_add(
                x1buf[:, :], x1buf[:, :], ln_sb[:, :]), (t_x1, ln_tok), (VE, 1))
            fin_tok = (VE, VE.n)
            x1_free = dma("sync", out_d[tt * 128:(tt + 1) * 128, :], x1buf[:, :],
                          (fin_tok,), st="s_out")
            ep_done.append(fin_tok)
        if debug:
            dma("gpsimd", dbg["dbg_stage"][hf * 512:(hf + 1) * 512, :].rearrange(
                "(ttl p) o -> p ttl o", p=128),
                stg_sb[:, :].rearrange("p (ttl o) -> p ttl o", ttl=4), (pass_done,),
                st="g_dbg")

    return _finish()


_NC_CACHE = {}


def _get_nc(debug=DEBUG):
    if debug not in _NC_CACHE:
        _NC_CACHE[debug] = build_program(debug)
    return _NC_CACHE[debug]


def kernel(L0_attn_x, L0_attn_wq, L0_attn_wk, L0_attn_wv, L0_attn_wo,
           L0_ffn_base_w, L0_ffn_base_b, L0_ffn_A, L0_ffn_B, L0_ffn_router_w):
    x = np.ascontiguousarray(np.asarray(L0_attn_x, dtype=np.float32))
    wq_ = np.ascontiguousarray(np.asarray(L0_attn_wq, dtype=np.float32))
    wk_ = np.ascontiguousarray(np.asarray(L0_attn_wk, dtype=np.float32))
    wv_ = np.ascontiguousarray(np.asarray(L0_attn_wv, dtype=np.float32))
    wo_ = np.ascontiguousarray(np.asarray(L0_attn_wo, dtype=np.float32))
    bw_ = np.ascontiguousarray(np.asarray(L0_ffn_base_w, dtype=np.float32))
    bb_ = np.ascontiguousarray(np.asarray(L0_ffn_base_b, dtype=np.float32))
    A_ = np.ascontiguousarray(np.asarray(L0_ffn_A, dtype=np.float32))
    B_ = np.ascontiguousarray(np.asarray(L0_ffn_B, dtype=np.float32))
    rw_ = np.ascontiguousarray(np.asarray(L0_ffn_router_w, dtype=np.float32))

    nc = _get_nc()
    xf = x.reshape(8192, 4096)
    import ml_dtypes
    idf = np.eye(128, dtype=np.float32)
    idb = np.eye(128).astype(ml_dtypes.bfloat16)
    in_maps = [{
        "x": xf[c * 1024:(c + 1) * 1024], "wq": wq_, "wk": wk_, "wv": wv_,
        "wo": wo_, "bw": bw_, "bb": bb_, "A": A_, "B": B_, "rw": rw_,
        "idf": idf, "idb": idb,
    } for c in range(8)]
    res = run_bass_kernel_spmd(nc, in_maps, core_ids=list(range(8)))
    shards = [res.results[c]["out"] for c in range(8)]
    return np.concatenate(shards, axis=0).reshape(2, 4096, 4096).astype(np.float32)


if __name__ == "__main__":
    build_program()
    print("build ok")
